# version 02
# baseline (speedup 1.0000x reference)
import sys
sys.path.insert(0, '/opt/trn_rl_repo')
from contextlib import ExitStack

import numpy as np
import ml_dtypes
import concourse.bass as bass
import concourse.mybir as mybir
from concourse.bass_utils import run_bass_kernel_spmd

F32 = mybir.dt.float32
BF16 = mybir.dt.bfloat16
AF = mybir.ActivationFunctionType
ALU = mybir.AluOpType
AX = mybir.AxisListType

P = 128
T = 2048
TH = 2051
NCORES = 8
D = 1024
DC = 8
DI = 2048
DIC = 16
NH = 32
HD = 64
DS = 128
NTC = 16
BF = ml_dtypes.bfloat16


class Prog:

    ENGINES = ("pe", "act", "dve", "pool", "sy")

    def __init__(self, nc, ctx):
        self.nc = nc
        self.ctx = ctx
        self.steps = {e: [] for e in self.ENGINES}
        self.sems = {}
        self.cnt = {}
        for s in ("ld", "pld", "st", "S_pe", "S_act", "S_dve", "S_pool", "S_sy", "cc"):
            self.sems[s] = ctx.enter_context(nc.semaphore(s))
            self.cnt[s] = 0

    def now(self, sem):
        return self.cnt[sem]

    def op(self, eng, emit, waits=(), incs=(), settle=None):
        waits = tuple(waits)
        incs = tuple(incs)
        if settle is None:
            if eng in ("act", "dve", "pool"):
                settle = emit is not None
            else:
                settle = eng == "pe" and any(s.startswith("S_") for s, _ in incs)
        for s, a in incs:
            self.cnt[s] += a
        if settle and emit is not None:
            self.steps[eng].append((emit, waits, ()))
            self.steps[eng].append((lambda o: o.drain(), (), incs))
        else:
            self.steps[eng].append((emit, waits, incs))

    def dma(self, qeng, out, in_, sem=None, waits=()):
        if sem is None:
            sem = "pld" if qeng == "pool" else ("ld" if qeng == "sy" else "ld")
        self.op(qeng, lambda o, out=out, in_=in_: o.dma_start(out=out, in_=in_),
                waits=waits, incs=[(sem, 16)], settle=False)
        return self.cnt[sem]

    def barrier(self):
        marks = {}
        for e in self.ENGINES:
            s = {"pe": "S_pe", "act": "S_act", "dve": "S_dve", "pool": "S_pool", "sy": "S_sy"}[e]
            if e in ("act", "dve", "pool"):
                self.op(e, lambda o: o.drain(), incs=[(s, 1)], settle=False)
            else:
                self.op(e, lambda o: o.nop(), incs=[(s, 1)], settle=False)
            marks[s] = self.cnt[s]
        dma_marks = [(s, self.cnt[s]) for s in ("ld", "pld", "st")]
        waits = [(s, v) for s, v in marks.items()] + dma_marks
        for e in self.ENGINES:
            self.op(e, None, waits=waits)

    def finalize(self):
        nc = self.nc
        block = self.ctx.enter_context(nc.Block())
        prog = self

        def runner(eng_name):
            def _(obj):
                for emit, waits, incs in prog.steps[eng_name]:
                    for s, v in waits:
                        if v > 0:
                            obj.wait_ge(prog.sems[s], v)
                    if emit is None:
                        if incs:
                            inst = obj.nop()
                            for s, a in incs:
                                inst.then_inc(prog.sems[s], a)
                        continue
                    inst = emit(obj)
                    for s, a in incs:
                        inst.then_inc(prog.sems[s], a)
            return _

        block.tensor(runner("pe"))
        block.scalar(runner("act"))
        block.vector(runner("dve"))
        block.gpsimd(runner("pool"))
        block.sync(runner("sy"))


def ceil_div(a, b):
    return -(-a // b)


def col_groups(total, width=512):
    out = []
    a = 0
    while a < total:
        out.append((a, min(a + width, total)))
        a = out[-1][1]
    return out


def _dump_A(p, xnT, xres, xfm, xnbf):
    for c in range(DC):
        p.dma("sy", xres[c * P:(c + 1) * P, 0:T], xfm[:, c * TH + 3: c * TH + 3 + T], sem="st")
    p.op("sy", None, waits=[("st", p.now("st"))])


def _dump_rows_A(p, xres, r1, r2):
    p.dma("sy", xres[0:1, 0:T], r1[0:1, 3:3 + T], sem="st")
    p.dma("sy", xres[1:2, 0:T], r2[0:1, 3:3 + T], sem="st")
    p.op("sy", None, waits=[("st", p.now("st"))])


def emit_A(nc, p, ctx, io, stage=99):
    E = ctx.enter_context
    x_in = io["x_in"]
    identf = io["identf"]
    xnT = io["xnT"]
    xres = io["xres"]
    xfm = E(nc.sbuf_tensor("xfm", [P, DC * TH], F32))
    xnbf = E(nc.sbuf_tensor("xnbf", [P, DC * TH], BF16))
    xt0 = E(nc.sbuf_tensor("xt0", [P, D], F32))
    xt1 = E(nc.sbuf_tensor("xt1", [P, D], F32))
    ident = E(nc.sbuf_tensor("ident", [P, P], F32))
    sq = E(nc.sbuf_tensor("sq", [P, 1024], F32))
    onescol = E(nc.sbuf_tensor("onescol", [P, 1], F32))
    onesrow = E(nc.sbuf_tensor("onesrow", [1, P], F32))
    s1r = E(nc.sbuf_tensor("s1r", [1, TH], F32))
    s2r = E(nc.sbuf_tensor("s2r", [1, TH], F32))
    varr = E(nc.sbuf_tensor("varr", [1, TH], F32))
    rrow = E(nc.sbuf_tensor("rrow", [1, TH], F32))
    mrow = E(nc.sbuf_tensor("mrow", [1, TH], F32))
    epsc = E(nc.sbuf_tensor("epsc", [1, 1], F32))
    bc1 = E(nc.sbuf_tensor("bc1", [P, TH], F32))
    bc2 = E(nc.sbuf_tensor("bc2", [P, TH], F32))
    tmp = E(nc.sbuf_tensor("tmpx", [P, 512], F32))
    xrst = E(nc.sbuf_tensor("xrst", [P, 512], F32))
    pst = [E(nc.psum_tensor(f"pst{i}", [P, 512], F32)) for i in range(2)]
    psA = E(nc.psum_tensor("psA", [P, 512], F32))
    psB = E(nc.psum_tensor("psB", [P, 512], F32))
    psC = E(nc.psum_tensor("psC", [P, 512], F32))

    p.dma("sy", ident[:, :], identf)
    p.op("pool", lambda o: o.memset(onescol[:, :], 1.0))
    p.op("pool", lambda o: o.memset(onesrow[:, :], 1.0))
    p.op("pool", lambda o: o.memset(epsc[:, :], 1e-5), incs=[("S_pool", 1)])
    ldw = p.now("ld")

    tiles = [(3, 0)] + [(128, 3 + 128 * i) for i in range(NTC)]
    tbuf = [xt0, xt1]
    tld = [0, 0]
    tfree = [0, 0]
    ti = 0
    for rows, coff in tiles:
        b = ti % 2
        src = x_in[0:3, :] if rows == 3 else x_in[coff:coff + 128, :]
        tld[b] = p.dma("sy", tbuf[b][0:rows, :], src,
                       waits=[("S_act", tfree[b]), ("ld", ldw)])
        act_base = p.now("S_act")
        for c in range(DC):
            pb = pst[c % 2]
            need_act = max(0, act_base + c - 1)
            p.op("pe", lambda o, b=b, c=c, rows=rows, pb=pb:
                 o.transpose(pb[:, 0:rows], tbuf[b][0:rows, c * P:(c + 1) * P],
                             ident[0:rows, 0:rows]),
                 waits=[("ld", tld[b]), ("S_act", need_act)],
                 incs=[("S_pe", 1)])
            p.op("act", lambda o, c=c, rows=rows, coff=coff, pb=pb:
                 o.copy(xfm[:, c * TH + coff: c * TH + coff + rows], pb[:, 0:rows]),
                 waits=[("S_pe", p.now("S_pe"))], incs=[("S_act", 1)])
        tfree[b] = p.now("S_act")
        ti += 1
    p.barrier()
    if stage == 1:
        _dump_A(p, xnT, xres, xfm, xnbf)
        return

    evacA = p.now("S_act")
    evacB = p.now("S_act")
    for a, bnd in col_groups(TH):
        w = bnd - a
        sq_done = []
        mmB_done = []
        for c in range(DC):
            p.op("pe", lambda o, c=c, a=a, w=w:
                 o.matmul(psA[0:1, 0:w], onescol[:, :], xfm[:, c * TH + a: c * TH + a + w],
                          start=(c == 0), stop=(c == DC - 1)),
                 waits=[("S_act", evacA if c == 0 else 0)], incs=[("S_pe", 1)])
            sqh = (c % 2) * 512
            p.op("act", lambda o, c=c, a=a, w=w, sqh=sqh:
                 o.activation(sq[:, sqh:sqh + w], xfm[:, c * TH + a: c * TH + a + w], AF.Square),
                 waits=[("S_pe", mmB_done[c - 2] if c >= 2 else 0)],
                 incs=[("S_act", 1)])
            sq_done.append(p.now("S_act"))
            p.op("pe", lambda o, c=c, a=a, w=w, sqh=sqh:
                 o.matmul(psB[0:1, 0:w], onescol[:, :], sq[:, sqh:sqh + w],
                          start=(c == 0), stop=(c == DC - 1)),
                 waits=[("S_act", sq_done[c]), ("S_act", evacB if c == 0 else 0)],
                 incs=[("S_pe", 1)])
            mmB_done.append(p.now("S_pe"))
        p.op("act", lambda o, a=a, w=w: o.copy(s1r[:, a:a + w], psA[0:1, 0:w]),
             waits=[("S_pe", p.now("S_pe"))], incs=[("S_act", 1)])
        evacA = p.now("S_act")
        p.op("act", lambda o, a=a, w=w: o.copy(s2r[:, a:a + w], psB[0:1, 0:w]),
             incs=[("S_act", 1)])
        evacB = p.now("S_act")
    p.barrier()
    if stage == 2:
        _dump_rows_A(p, xres, s1r, s2r)
        return

    p.op("dve", lambda o: o.tensor_scalar(mrow[:, :], s1r[:, :], 1.0 / D, 0.0, ALU.mult, ALU.add))
    p.op("dve", lambda o: o.tensor_scalar(varr[:, :], s2r[:, :], 1.0 / D, 0.0, ALU.mult, ALU.add), incs=[("S_dve", 1)])
    p.op("act", lambda o: o.activation(s2r[:, :], mrow[:, :], AF.Square),
         waits=[("S_dve", p.now("S_dve"))], incs=[("S_act", 1)])
    p.op("dve", lambda o: o.tensor_tensor(s1r[:, :], varr[:, :], s2r[:, :], ALU.subtract),
         waits=[("S_act", p.now("S_act"))], incs=[("S_dve", 1)])
    p.op("act", lambda o: o.activation(varr[:, :], s1r[:, :], AF.Sqrt, bias=epsc[0:1, 0:1], scale=1.0),
         waits=[("S_dve", p.now("S_dve")), ("S_pool", p.now("S_pool"))], incs=[("S_act", 1)])
    p.op("dve", lambda o: o.reciprocal(rrow[:, :], varr[:, :]),
         waits=[("S_act", p.now("S_act"))], incs=[("S_dve", 1)])
    p.op("dve", lambda o: o.scalar_tensor_tensor(s2r[:, :], mrow[:, :], -1.0, rrow[:, :], ALU.mult, ALU.mult),
         incs=[("S_dve", 1)])
    p.op("dve", lambda o: o.tensor_copy(mrow[:, :], s2r[:, :]), incs=[("S_dve", 1)])
    p.barrier()
    if stage == 4:
        pass
    if stage == 3:
        p.dma("sy", xres[0:1, 0:T], rrow[0:1, 3:3 + T], sem="st")
        p.dma("sy", xres[1:2, 0:T], mrow[0:1, 3:3 + T], sem="st")
        p.dma("sy", xres[2:3, 0:T], s1r[0:1, 3:3 + T], sem="st")
        p.dma("sy", xres[3:4, 0:T], varr[0:1, 3:3 + T], sem="st")
        p.dma("sy", xres[4:5, 0:T], s2r[0:1, 3:3 + T], sem="st")
        p.op("sy", None, waits=[("st", p.now("st"))])
        return

    for a, bnd in col_groups(TH):
        w = bnd - a
        p.op("pe", lambda o, a=a, w=w:
             o.matmul(psC[:, 0:w], onesrow[0:1, :], rrow[0:1, a:a + w], start=True, stop=True),
             waits=[("S_act", p.now("S_act"))], incs=[("S_pe", 1)])
        p.op("act", lambda o, a=a, w=w: o.copy(bc1[:, a:a + w], psC[:, 0:w]),
             waits=[("S_pe", p.now("S_pe"))], incs=[("S_act", 1)])
        p.op("pe", lambda o, a=a, w=w:
             o.matmul(psC[:, 0:w], onesrow[0:1, :], mrow[0:1, a:a + w], start=True, stop=True),
             waits=[("S_act", p.now("S_act"))], incs=[("S_pe", 1)])
        p.op("act", lambda o, a=a, w=w: o.copy(bc2[:, a:a + w], psC[:, 0:w]),
             waits=[("S_pe", p.now("S_pe"))], incs=[("S_act", 1)])
    p.barrier()
    if stage == 4:
        for c in range(4):
            p.dma("sy", xres[c * P:(c + 1) * P, 0:T], bc1[:, 3:3 + T], sem="st")
            p.dma("sy", xres[(4 + c) * P:(5 + c) * P, 0:T], bc2[:, 3:3 + T], sem="st")
        p.op("sy", None, waits=[("st", p.now("st"))])
        return

    for c in range(DC):
        for a, bnd in col_groups(TH):
            w = bnd - a
            p.op("dve", lambda o, c=c, a=a, w=w:
                 o.tensor_tensor(tmp[:, 0:w], xfm[:, c * TH + a: c * TH + a + w], bc1[:, a:a + w], ALU.mult),
                 waits=[("st", p.now("st"))], incs=[("S_dve", 1)])
            p.op("dve", lambda o, c=c, a=a, w=w:
                 o.tensor_tensor(xnbf[:, c * TH + a: c * TH + a + w], tmp[:, 0:w], bc2[:, a:a + w], ALU.add),
                 incs=[("S_dve", 1)])
        p.dma("sy", xnT[c * P:(c + 1) * P, :], xnbf[:, c * TH: (c + 1) * TH],
              sem="st", waits=[("S_dve", p.now("S_dve"))])
    p.barrier()
    for c in range(DC):
        for a in range(0, T, 512):
            w = 512
            p.op("dve", lambda o, c=c, a=a, w=w:
                 o.tensor_tensor(xrst[:, 0:w], xfm[:, c * TH + 3 + a: c * TH + 3 + a + w],
                                 xnbf[:, c * TH + 3 + a: c * TH + 3 + a + w], ALU.add),
                 waits=[("st", p.now("st"))], incs=[("S_dve", 1)])
            p.dma("sy", xres[c * P:(c + 1) * P, a:a + w], xrst[:, 0:w],
                  sem="st", waits=[("S_dve", p.now("S_dve"))])
    p.op("sy", None, waits=[("st", p.now("st"))])



def build_launch_A(stage=99):
    nc = bass.Bass("TRN2", target_bir_lowering=False)
    x_in = nc.dram_tensor("x_in", [TH, D], F32, kind="ExternalInput").ap()
    identf = nc.dram_tensor("identf", [P, P], F32, kind="ExternalInput").ap()
    xnT = nc.dram_tensor("xnT", [D, TH], BF16, kind="ExternalOutput").ap()
    xres = nc.dram_tensor("xres", [D, T], F32, kind="ExternalOutput").ap()

    io = {}
    io["x_in"] = x_in
    io["identf"] = identf
    io["xnT"] = xnT
    io["xres"] = xres
    with ExitStack() as ctx:
        p = Prog(nc, ctx)
        emit_A(nc, p, ctx, io, stage)
        p.finalize()
    return nc


def emit_B(nc, p, ctx, io, stage=99):
    E = ctx.enter_context
    xnT = io["xnT"]
    winT = io["winT"]
    convw = io["convw"]
    convb = io["convb"]
    alog = io["alog"]
    dtb = io["dtb"]
    nw = io["nw"]
    cmask = io["cmask"]
    identf = io["identf"]
    szT = io["szT"]
    xcvT = io["xcvT"]
    BTo = io["BTo"]
    CTo = io["CTo"]
    cumo = io["cumo"]
    dtfo = io["dtfo"]
    hino = io["hino"]
    cc_in_s = io["cc_in_s"]
    cc_out_s = io["cc_out_s"]
    cc_in_l = io["cc_in_l"]
    cc_out_l = io["cc_out_l"]
    big = E(nc.sbuf_tensor("bigB", [P, DC * 4384], BF16))
    xnTs = big[:, 0:DC * TH]
    winTs = big[:, :]
    xcv = big[:, 0:DIC * T]
    big2 = E(nc.sbuf_tensor("big2B", [P, DIC * TH], BF16))
    xssmT = big2[:, :]
    hTa = big2[:, 0:4096].bitcast(F32)
    hTb = big2[:, 4096:8192].bitcast(F32)
    xctok = big2[:, 8192:10240]
    dtx = big2[:, 10240:12288]
    ddtx = big2[:, 12288:14336]
    tmpd = big2[:, 14336:15360].bitcast(F32)
    sgr = big2[:, 15360:19456].bitcast(F32)
    hins = big2[:, 19456:23552].bitcast(F32)
    xn2T = E(nc.sbuf_tensor("xn2T", [P, DC * TH], BF16))
    BTs = E(nc.sbuf_tensor("BTs", [P, TH], BF16))
    ident = E(nc.sbuf_tensor("identB", [P, P], F32))
    identb = E(nc.sbuf_tensor("identbB", [P, P], BF16))
    onescol = E(nc.sbuf_tensor("onescolB", [P, 1], F32))
    onesrow = E(nc.sbuf_tensor("onesrowB", [1, P], F32))
    dtraw = E(nc.sbuf_tensor("dtrawB", [NH, TH], F32))
    dtfs = E(nc.sbuf_tensor("dtfsB", [NH, T], F32))
    cums = E(nc.sbuf_tensor("cumsB", [NH, T], F32))
    rowsA = xn2T[0:32, 0:4096].bitcast(F32)
    rowsB = xn2T[0:32, 4096:8192].bitcast(F32)
    onesTrow = E(nc.sbuf_tensor("onesTrowB", [1, T], F32))
    onecol32 = E(nc.sbuf_tensor("onecol32B", [NH, 1], F32))
    zcol32 = E(nc.sbuf_tensor("zcol32B", [NH, 1], F32))
    nwcol = E(nc.sbuf_tensor("nwcolB", [P, DC], F32))
    wcv = E(nc.sbuf_tensor("wcvB", [P, DIC * 4], F32))
    cbcol = E(nc.sbuf_tensor("cbcolB", [P, DIC], F32))
    dtbc = E(nc.sbuf_tensor("dtbcB", [NH, 1], F32))
    expac = E(nc.sbuf_tensor("expacB", [NH, 1], F32))
    expa2 = E(nc.sbuf_tensor("expa2B", [NH, 1], F32))
    s2r = big2[0:1, 0:4102].bitcast(F32)
    varr = big2[0:1, 4102:8204].bitcast(F32)
    rrow = big2[0:1, 8204:12306].bitcast(F32)
    sq = xn2T[:, 4 * T:4 * T + 2048].bitcast(F32)
    stg = [E(nc.sbuf_tensor(f"stgB{i}", [P, 512], BF16)) for i in range(2)]
    btok = big2[:, 23552:23680]
    dttok = big2[:, 23680:23744].bitcast(F32)
    edtok = big2[:, 23744:23808].bitcast(F32)
    edf = big2[0:32, 23808:24064].bitcast(F32)
    lamrow = big2[0:1, 24064:24128].bitcast(F32)
    lamexp = xn2T[0:1, 10240:14336].bitcast(F32)
    wrow = big2[0:1, 0:4096].bitcast(F32)
    ccrow = big2[0:1, 4096:4608].bitcast(F32)
    trow = big2[0:1, 4608:5120].bitcast(F32)
    prow = big2[0:1, 5120:5632].bitcast(F32)
    cmrow = E(nc.sbuf_tensor("cmrowB", [1, NCORES], F32))
    invrow = E(nc.sbuf_tensor("invrowB", [1, NCORES], F32))
    nwr = big[0:DC, 0:256].bitcast(F32)
    cbr = big[0:DIC, 256:512].bitcast(F32)
    cv1 = xn2T[:, 0:2 * T].bitcast(F32)
    cv2 = xn2T[:, 2 * T:4 * T].bitcast(F32)
    ps4 = [E(nc.psum_tensor(f"ps4_{i}", [P, 512], F32)) for i in range(4)]
    psL = E(nc.psum_tensor("psL", [P, 512], F32))
    psF = E(nc.psum_tensor("psF", [P, 512], F32))
    psT = [E(nc.psum_tensor(f"psT{i}", [P, 1024], BF16)) for i in range(2)]

    p.dma("sy", ident[:, :], identf)
    p.dma("pool", identb[:, :], identf)
    p.dma("pool", wcv[:, :].rearrange("p (c k) -> p c k", k=4),
          convw.rearrange("(c p) k -> p c k", p=P))
    p.dma("sy", cmrow[:, :], cmask)
    p.op("pool", lambda o: o.memset(onescol[:, :], 1.0))
    p.op("pool", lambda o: o.memset(onesrow[:, :], 1.0))
    p.op("pool", lambda o: o.memset(onesTrow[:, :], 1.0))
    p.op("pool", lambda o: o.memset(onecol32[:, :], 1.0))
    p.op("pool", lambda o: o.memset(zcol32[:, :], 0.0), incs=[("S_pool", 1)])
    p.dma("sy", nwr[:, :], nw.rearrange("(c p) -> c p", p=P))
    p.dma("sy", cbr[:, :], convb.rearrange("(c p) -> c p", p=P))
    p.dma("sy", dtbc[:, :], dtb.unsqueeze(1))
    p.dma("sy", expac[:, :], alog.unsqueeze(1))
    ld0 = p.now("ld")
    pld0 = p.now("pld")
    p.op("pe", lambda o: o.transpose(psF[0:P, 0:DC], nwr[:, :], ident[0:DC, 0:DC]),
         waits=[("ld", ld0), ("pld", pld0), ("S_pool", p.now("S_pool"))], incs=[("S_pe", 1)])
    p.op("act", lambda o: o.copy(nwcol[:, :], psF[:, 0:DC]),
         waits=[("S_pe", p.now("S_pe"))], incs=[("S_act", 1)])
    p.op("pe", lambda o: o.transpose(psF[:, 64:64 + DIC], cbr[:, :], ident[0:DIC, 0:DIC]),
         waits=[("S_act", p.now("S_act"))], incs=[("S_pe", 1)])
    p.op("act", lambda o: o.copy(cbcol[:, :], psF[:, 64:64 + DIC]),
         waits=[("S_pe", p.now("S_pe"))], incs=[("S_act", 1)])
    p.op("act", lambda o: o.activation(expa2[:, :], expac[:, :], AF.Exp),
         waits=[("ld", ld0)], incs=[("S_act", 1)])
    p.barrier()

    for c in range(DC):
        p.dma("sy", xnTs[:, c * TH:(c + 1) * TH], xnT[c * P:(c + 1) * P, :])
    ld1 = p.now("ld")
    evacB = 0
    for a, bnd in col_groups(TH):
        w = bnd - a
        sq_done = []
        mmB_done = []
        for c in range(DC):
            sqh = (c % 2) * 512
            p.op("act", lambda o, c=c, a=a, w=w, sqh=sqh:
                 o.activation(sq[:, sqh:sqh + w], xnTs[:, c * TH + a: c * TH + a + w], AF.Square),
                 waits=[("ld", ld1), ("S_pe", mmB_done[c - 2] if c >= 2 else 0)],
                 incs=[("S_act", 1)])
            sq_done.append(p.now("S_act"))
            p.op("pe", lambda o, c=c, a=a, w=w, sqh=sqh:
                 o.matmul(ps4[0][0:1, 0:w], onescol[:, :], sq[:, sqh:sqh + w],
                          start=(c == 0), stop=(c == DC - 1)),
                 waits=[("S_act", sq_done[c]), ("S_act", evacB if c == 0 else 0)],
                 incs=[("S_pe", 1)])
            mmB_done.append(p.now("S_pe"))
        p.op("act", lambda o, a=a, w=w: o.copy(s2r[:, a:a + w], ps4[0][0:1, 0:w]),
             waits=[("S_pe", p.now("S_pe"))], incs=[("S_act", 1)])
        evacB = p.now("S_act")
    p.barrier()
    p.op("dve", lambda o: o.tensor_scalar(varr[:, :], s2r[:, :], 1.0 / D, 1e-6, ALU.mult, ALU.add),
         incs=[("S_dve", 1)])
    p.op("act", lambda o: o.activation(s2r[:, :], varr[:, :], AF.Sqrt, bias=0.0, scale=1.0),
         waits=[("S_dve", p.now("S_dve"))], incs=[("S_act", 1)])
    p.op("dve", lambda o: o.reciprocal(rrow[:, :], s2r[:, :]),
         waits=[("S_act", p.now("S_act"))], incs=[("S_dve", 1)])
    p.barrier()
    for a, bnd in col_groups(TH):
        w = bnd - a
        p.op("pe", lambda o, a=a, w=w:
             o.matmul(psL[:, 0:w], onesrow[0:1, :], rrow[0:1, a:a + w], start=True, stop=True),
             waits=[("S_dve", p.now("S_dve"))], incs=[("S_pe", 1)])
        mmd = p.now("S_pe")
        for c in range(DC):
            p.op("dve", lambda o, c=c, a=a, w=w:
                 o.scalar_tensor_tensor(xn2T[:, c * TH + a: c * TH + a + w],
                                        xnTs[:, c * TH + a: c * TH + a + w],
                                        nwcol[:, c:c + 1], psL[:, 0:w], ALU.mult, ALU.mult),
                 waits=[("S_pe", mmd)], incs=[("S_dve", 1)] if c == DC - 1 else [])
    p.barrier()
    if stage == 1:
        for c in range(DC):
            p.dma("sy", szT[c * P:(c + 1) * P, 0:T], xn2T[:, c * TH + 3: c * TH + 3 + T], sem="st")
        p.op("sy", None, waits=[("st", p.now("st"))])
        return

    for k in range(DC):
        p.dma("pool", winTs[:, k * 4384:(k + 1) * 4384], winT[k * P:(k + 1) * P, :])
    ld2 = p.now("pld")
    bank_free = [("S_act", 0)] * 4
    bi = 0
    for a, bnd in col_groups(TH):
        w = bnd - a
        for m in range(35):
            rows = 32 if m == 34 else P
            bank = ps4[bi % 4]
            bwait = bank_free[bi % 4]
            for k in range(DC):
                p.op("pe", lambda o, m=m, k=k, a=a, w=w, rows=rows, bank=bank:
                     o.matmul(bank[0:rows, 0:w],
                              winTs[:, k * 4384 + m * P: k * 4384 + m * P + rows],
                              xn2T[:, k * TH + a: k * TH + a + w],
                              start=(k == 0), stop=(k == DC - 1)),
                     waits=([bwait, ("pld", ld2)] if k == 0 else []),
                     incs=([("S_pe", 1)] if k == DC - 1 else []))
            mmdone = p.now("S_pe")
            sg = stg[bi % 2]
            lo = max(a, 3)
            if m < 16:
                p.op("act", lambda o, w=w, bank=bank, sg=sg:
                     o.activation(sg[:, 0:w], bank[:, 0:w], AF.Silu),
                     waits=[("S_pe", mmdone), ("st", p.now("st"))], incs=[("S_act", 1)])
                if lo < bnd:
                    p.dma("sy", szT[m * P:(m + 1) * P, lo - 3:bnd - 3], sg[:, lo - a:w],
                          sem="st", waits=[("S_act", p.now("S_act"))])
            elif m < 32:
                p.op("act", lambda o, m=m, a=a, w=w, bank=bank:
                     o.activation(xssmT[:, (m - 16) * TH + a:(m - 16) * TH + a + w],
                                  bank[:, 0:w], AF.Identity, bias=0.0, scale=1.0),
                     waits=[("S_pe", mmdone)], incs=[("S_act", 1)])
            elif m == 32:
                p.op("act", lambda o, a=a, w=w, bank=bank:
                     o.activation(BTs[:, a:a + w], bank[:, 0:w], AF.Identity, bias=0.0, scale=1.0),
                     waits=[("S_pe", mmdone), ("st", p.now("st"))], incs=[("S_act", 1)])
                if lo < bnd:
                    p.dma("sy", BTo[:, lo - 3:bnd - 3], BTs[:, lo:bnd],
                          sem="st", waits=[("S_act", p.now("S_act"))])
            elif m == 33:
                p.op("act", lambda o, a=a, w=w, bank=bank, sg=sg:
                     o.activation(sg[:, 0:w], bank[:, 0:w], AF.Identity, bias=0.0, scale=1.0),
                     waits=[("S_pe", mmdone), ("st", p.now("st"))], incs=[("S_act", 1)])
                if lo < bnd:
                    p.dma("sy", CTo[:, lo - 3:bnd - 3], sg[:, lo - a:w],
                          sem="st", waits=[("S_act", p.now("S_act"))])
            else:
                p.op("act", lambda o, a=a, w=w, bank=bank:
                     o.copy(dtraw[:, a:a + w], bank[0:32, 0:w]),
                     waits=[("S_pe", mmdone)], incs=[("S_act", 1)])
            bank_free[bi % 4] = ("S_act", p.now("S_act"))
            bi += 1
    p.barrier()
    if stage == 2:
        for c in range(DIC):
            p.dma("sy", xcvT[c * P:(c + 1) * P, :], xssmT[:, c * TH + 3: c * TH + 3 + T], sem="st")
        p.dma("sy", cumo[:, :], dtraw[:, 3:], sem="st")
        p.op("sy", None, waits=[("st", p.now("st"))])
        return

    for c in range(DIC):
        base = c * TH
        p.op("dve", lambda o, c=c, base=base:
             o.tensor_scalar(cv1[:, :], xssmT[:, base:base + T], wcv[:, 4 * c:4 * c + 1],
                             0.0, ALU.mult, ALU.add))
        p.op("dve", lambda o, c=c, base=base:
             o.scalar_tensor_tensor(cv2[:, :], xssmT[:, base + 1:base + 1 + T],
                                    wcv[:, 4 * c + 1:4 * c + 2], cv1[:, :], ALU.mult, ALU.add))
        p.op("dve", lambda o, c=c, base=base:
             o.scalar_tensor_tensor(cv1[:, :], xssmT[:, base + 2:base + 2 + T],
                                    wcv[:, 4 * c + 2:4 * c + 3], cv2[:, :], ALU.mult, ALU.add))
        p.op("dve", lambda o, c=c, base=base:
             o.scalar_tensor_tensor(cv2[:, :], xssmT[:, base + 3:base + 3 + T],
                                    wcv[:, 4 * c + 3:4 * c + 4], cv1[:, :], ALU.mult, ALU.add),
             incs=[("S_dve", 1)])
        p.op("act", lambda o, c=c:
             o.activation(xcv[:, c * T:(c + 1) * T], cv2[:, :], AF.Silu,
                          bias=cbcol[:, c:c + 1], scale=1.0),
             waits=[("S_dve", p.now("S_dve")), ("st", p.now("st"))], incs=[("S_act", 1)])
        p.dma("sy", xcvT[c * P:(c + 1) * P, :], xcv[:, c * T:(c + 1) * T],
              sem="st", waits=[("S_act", p.now("S_act"))])
        p.op("dve", None, waits=[("S_act", p.now("S_act"))])
    p.barrier()

    p.op("dve", lambda o: o.tensor_scalar(rowsA[:, :], dtraw[:, 3:], dtbc[:, 0:1], 0.0, ALU.add, ALU.add),
         incs=[("S_dve", 1)])
    p.op("act", lambda o: o.activation(rowsB[:, :], rowsA[:, :], AF.Abs),
         waits=[("S_dve", p.now("S_dve"))], incs=[("S_act", 1)])
    p.op("act", lambda o: o.activation(rowsB[:, :], rowsB[:, :], AF.Exp, bias=0.0, scale=-1.0),
         incs=[("S_act", 1)])
    p.op("act", lambda o: o.activation(rowsB[:, :], rowsB[:, :], AF.Ln, bias=onecol32[:, 0:1], scale=1.0),
         incs=[("S_act", 1)])
    p.op("dve", lambda o: o.tensor_scalar(rowsA[:, :], rowsA[:, :], 0.0, 0.0, ALU.max, ALU.add),
         waits=[("S_act", p.now("S_act"))])
    p.op("dve", lambda o: o.tensor_tensor(dtfs[:, :], rowsA[:, :], rowsB[:, :], ALU.add),
         incs=[("S_dve", 1)])
    p.dma("sy", dtfo[:, :], dtfs[:, :], sem="st", waits=[("S_dve", p.now("S_dve"))])
    p.op("dve", lambda o: o.tensor_scalar(rowsA[:, :], dtfs[:, :], expa2[:, 0:1], -1.0, ALU.mult, ALU.mult),
         incs=[("S_dve", 1)])
    p.op("dve", lambda o: o.tensor_tensor_scan(cums[:, :], rowsA[:, :],
                                               rowsA[:, :], 0.0, ALU.add, ALU.bypass),
         incs=[("S_dve", 1)])
    p.dma("sy", cumo[:, :], cums[:, :], sem="st", waits=[("S_dve", p.now("S_dve"))])
    p.barrier()
    if stage == 3:
        p.op("sy", None, waits=[("st", p.now("st"))])
        return

    p.op("pool", lambda o: o.memset(hTa[:, :], 0.0))
    p.op("pool", lambda o: o.memset(hTb[:, :], 0.0), incs=[("S_pool", 1)])
    hbufs = [hTa, hTb]
    for c in range(NTC):
        t0 = c * P
        hold = hbufs[c % 2]
        hnew = hbufs[(c + 1) % 2]
        base_act = p.now("S_act")
        for q in range(DIC):
            pb = psT[q % 2]
            p.op("pe", lambda o, q=q, t0=t0, pb=pb:
                 o.transpose(pb[:, 0:P], xcv[:, q * T + t0: q * T + t0 + P], identb[:, :]),
                 waits=[("S_act", max(0, base_act + q - 1)), ("S_pool", p.now("S_pool"))],
                 incs=[("S_pe", 1)])
            p.op("act", lambda o, q=q, pb=pb:
                 o.copy(xctok[:, q * P:(q + 1) * P], pb[:, 0:P]),
                 waits=[("S_pe", p.now("S_pe"))], incs=[("S_act", 1)])
        p.op("pe", lambda o, t0=t0:
             o.transpose(psT[0][:, 0:P], BTs[:, 3 + t0: 3 + t0 + P], identb[:, :]),
             waits=[("S_act", p.now("S_act"))], incs=[("S_pe", 1)])
        p.op("act", lambda o: o.copy(btok[:, :], psT[0][:, 0:P]),
             waits=[("S_pe", p.now("S_pe"))], incs=[("S_act", 1)])
        p.op("act", lambda o, t0=t0:
             o.activation(edf[:, :], cums[:, t0:t0 + P], AF.Exp,
                          bias=cums[:, t0 + P - 1:t0 + P], scale=-1.0),
             incs=[("S_act", 1)])
        p.op("pe", lambda o, t0=t0:
             o.transpose(psF[:, 0:NH], dtfs[:, t0:t0 + P], ident[0:NH, 0:NH]),
             waits=[("S_act", p.now("S_act"))], incs=[("S_pe", 1)])
        p.op("act", lambda o: o.copy(dttok[:, :], psF[:, 0:NH]),
             waits=[("S_pe", p.now("S_pe"))], incs=[("S_act", 1)])
        p.op("pe", lambda o:
             o.transpose(psF[:, 64:64 + NH], edf[:, :], ident[0:NH, 0:NH]),
             waits=[("S_act", p.now("S_act"))], incs=[("S_pe", 1)])
        p.op("act", lambda o: o.copy(edtok[:, :], psF[:, 64:64 + NH]),
             waits=[("S_pe", p.now("S_pe"))], incs=[("S_act", 1)])
        p.op("dve", lambda o:
             o.tensor_tensor(dtx[:, :].rearrange("p (h d) -> p h d", d=HD),
                             xctok[:, :].rearrange("p (h d) -> p h d", d=HD),
                             dttok[:, :].unsqueeze(2).broadcast_to([P, NH, HD]), ALU.mult),
             waits=[("S_act", p.now("S_act"))], incs=[("S_dve", 1)])
        p.op("dve", lambda o:
             o.tensor_tensor(ddtx[:, :].rearrange("p (h d) -> p h d", d=HD),
                             dtx[:, :].rearrange("p (h d) -> p h d", d=HD),
                             edtok[:, :].unsqueeze(2).broadcast_to([P, NH, HD]), ALU.mult),
             incs=[("S_dve", 1)])
        dd = p.now("S_dve")
        for g in range(4):
            p.op("pe", lambda o, g=g:
                 o.matmul(ps4[g][:, :], btok[:, :], ddtx[:, g * 512:(g + 1) * 512],
                          start=True, stop=True),
                 waits=[("S_dve", dd)],
                 incs=([("S_pe", 1)] if g == 3 else []))
        inj_done = p.now("S_pe")
        p.op("dve", lambda o, c=c, t0=t0:
             o.tensor_scalar(rowsB[:, 0:1], (cums[:, t0 - 1:t0] if c > 0 else zcol32[:, 0:1]),
                             -1.0, 0.0, ALU.mult, ALU.add), incs=[("S_dve", 1)])
        p.op("act", lambda o, t0=t0:
             o.activation(rowsB[:, 1:2], cums[:, t0 + P - 1:t0 + P], AF.Exp,
                          bias=rowsB[:, 0:1], scale=1.0),
             waits=[("S_dve", p.now("S_dve"))], incs=[("S_act", 1)])
        p.op("pe", lambda o:
             o.transpose(psF[0:1, 64:64 + NH], rowsB[:, 1:2], ident[0:NH, 0:NH]),
             waits=[("S_act", p.now("S_act"))], incs=[("S_pe", 1)])
        p.op("act", lambda o: o.copy(lamrow[0:1, :], psF[0:1, 64:64 + NH]),
             waits=[("S_pe", p.now("S_pe"))], incs=[("S_act", 1)])
        p.op("dve", lambda o:
             o.tensor_tensor(lamexp[:, :].rearrange("o (h d) -> o h d", d=HD),
                             onesTrow[0:1, 0:DI].rearrange("o (h d) -> o h d", d=HD),
                             lamrow[0:1, :].unsqueeze(2).broadcast_to([1, NH, HD]), ALU.mult),
             waits=[("S_act", p.now("S_act"))], incs=[("S_dve", 1)])
        lam_done = p.now("S_dve")
        for g in range(4):
            p.op("pe", lambda o, g=g:
                 o.matmul(psL[:, :], onesrow[0:1, :], lamexp[0:1, g * 512:(g + 1) * 512],
                          start=True, stop=True),
                 waits=[("S_dve", lam_done), ("S_dve", p.now("S_dve"))],
                 incs=[("S_pe", 1)])
            p.op("dve", lambda o, g=g, hold=hold:
                 o.tensor_tensor(tmpd[:, :], hold[:, g * 512:(g + 1) * 512], psL[:, :], ALU.mult),
                 waits=[("S_pe", p.now("S_pe")), ("S_pool", p.now("S_pool"))], incs=[("S_dve", 1)])
            p.op("dve", lambda o, g=g, hnew=hnew:
                 o.tensor_tensor(hnew[:, g * 512:(g + 1) * 512], tmpd[:, :], ps4[g][:, :], ALU.add),
                 waits=[("S_pe", inj_done)], incs=[("S_dve", 1)])
        p.barrier()
    hfin = hbufs[NTC % 2]
    p.op("act", lambda o: o.activation(rowsB[:, 2:3], cums[:, T - 1:T], AF.Exp),
         incs=[("S_act", 1)])
    p.dma("sy", cc_in_s.ap(), hfin[:, :], sem="st", waits=[("S_dve", p.now("S_dve"))])
    p.dma("sy", cc_in_l.ap(), rowsB[:, 2:3], sem="st", waits=[("S_act", p.now("S_act"))])
    p.op("pool", lambda o: o.collective_compute(
        "AllGather", ALU.bypass, replica_groups=[list(range(NCORES))],
        ins=[cc_in_s.ap().opt()], outs=[cc_out_s.ap().opt()]),
        waits=[("st", p.now("st"))], incs=[("cc", 1)], settle=False)
    p.op("pool", lambda o: o.collective_compute(
        "AllGather", ALU.bypass, replica_groups=[list(range(NCORES))],
        ins=[cc_in_l.ap().opt()], outs=[cc_out_l.ap().opt()]),
        incs=[("cc", 1)], settle=False)
    if stage == 4:
        for g in range(4):
            p.dma("sy", hino[:, g * 512:(g + 1) * 512], hfin[:, g * 512:(g + 1) * 512], sem="st")
        p.op("sy", None, waits=[("st", p.now("st")), ("cc", p.now("cc"))])
        return

    p.dma("sy", ccrow[:, :], cc_out_l.ap().rearrange("(r h) one -> one (r h)", h=NH),
          waits=[("cc", p.now("cc"))])
    ld3 = p.now("ld")
    p.op("pool", lambda o: o.memset(hins[:, :], 0.0), incs=[("S_pool", 1)])
    p.op("dve", lambda o: o.tensor_scalar(invrow[:, :], cmrow[:, :], -1.0, 1.0, ALU.mult, ALU.add),
         incs=[("S_dve", 1)])
    for q in range(NCORES):
        p.op("dve", lambda o, q=q:
             o.tensor_scalar(trow[:, q * NH:(q + 1) * NH], ccrow[:, q * NH:(q + 1) * NH],
                             cmrow[0:1, q:q + 1], invrow[0:1, q:q + 1], ALU.mult, ALU.add),
             waits=[("ld", ld3)], incs=[("S_dve", 1)])
    p.op("dve", lambda o: o.tensor_scalar(prow[:, 7 * NH:8 * NH], onesTrow[0:1, 0:NH], 1.0, 0.0,
                                          ALU.mult, ALU.add), incs=[("S_dve", 1)])
    for r in range(NCORES - 2, -1, -1):
        p.op("dve", lambda o, r=r:
             o.tensor_tensor(prow[:, r * NH:(r + 1) * NH], trow[:, (r + 1) * NH:(r + 2) * NH],
                             prow[:, (r + 1) * NH:(r + 2) * NH], ALU.mult),
             incs=[("S_dve", 1)])
    p.barrier()
    for r in range(NCORES):
        p.dma("sy", sgr[:, :], cc_out_s.ap()[r * DS:(r + 1) * DS, :],
              waits=[("S_dve", p.now("S_dve"))])
        ldr = p.now("ld")
        p.op("dve", lambda o, r=r:
             o.tensor_scalar(prow[:, r * NH:(r + 1) * NH], prow[:, r * NH:(r + 1) * NH],
                             cmrow[0:1, r:r + 1], 0.0, ALU.mult, ALU.add), incs=[("S_dve", 1)])
        p.op("dve", lambda o, r=r:
             o.tensor_tensor(wrow[:, :].rearrange("o (h d) -> o h d", d=HD),
                             onesTrow[0:1, 0:DI].rearrange("o (h d) -> o h d", d=HD),
                             prow[:, r * NH:(r + 1) * NH].unsqueeze(2).broadcast_to([1, NH, HD]),
                             ALU.mult), incs=[("S_dve", 1)])
        wd = p.now("S_dve")
        for g in range(4):
            p.op("pe", lambda o, g=g:
                 o.matmul(psL[:, :], onesrow[0:1, :], wrow[0:1, g * 512:(g + 1) * 512],
                          start=True, stop=True),
                 waits=[("S_dve", wd), ("S_dve", p.now("S_dve"))], incs=[("S_pe", 1)])
            p.op("dve", lambda o, g=g:
                 o.tensor_tensor(tmpd[:, :], sgr[:, g * 512:(g + 1) * 512], psL[:, :], ALU.mult),
                 waits=[("S_pe", p.now("S_pe")), ("ld", ldr), ("S_pool", p.now("S_pool"))],
                 incs=[("S_dve", 1)])
            p.op("dve", lambda o, g=g:
                 o.tensor_tensor(hins[:, g * 512:(g + 1) * 512], hins[:, g * 512:(g + 1) * 512],
                                 tmpd[:, :], ALU.add),
                 incs=[("S_dve", 1)])
    p.dma("sy", hino, hins[:, :], sem="st", waits=[("S_dve", p.now("S_dve"))])
    p.op("sy", None, waits=[("st", p.now("st"))])



def build_launch_B(stage=99):
    nc = bass.Bass("TRN2", target_bir_lowering=False)
    xnT = nc.dram_tensor("xnT", [D, TH], BF16, kind="ExternalInput").ap()
    winT = nc.dram_tensor("winT", [D, 4384], F32, kind="ExternalInput").ap()
    convw = nc.dram_tensor("convw", [DI, 4], F32, kind="ExternalInput").ap()
    convb = nc.dram_tensor("convb", [DI], F32, kind="ExternalInput").ap()
    alog = nc.dram_tensor("alog", [NH], F32, kind="ExternalInput").ap()
    dtb = nc.dram_tensor("dtb", [NH], F32, kind="ExternalInput").ap()
    nw = nc.dram_tensor("nw", [D], F32, kind="ExternalInput").ap()
    cmask = nc.dram_tensor("cmask", [1, NCORES], F32, kind="ExternalInput").ap()
    identf = nc.dram_tensor("identf", [P, P], F32, kind="ExternalInput").ap()
    szT = nc.dram_tensor("szT", [DI, T], BF16, kind="ExternalOutput").ap()
    xcvT = nc.dram_tensor("xcvT", [DI, T], BF16, kind="ExternalOutput").ap()
    BTo = nc.dram_tensor("BTo", [DS, T], BF16, kind="ExternalOutput").ap()
    CTo = nc.dram_tensor("CTo", [DS, T], BF16, kind="ExternalOutput").ap()
    cumo = nc.dram_tensor("cumo", [NH, T], F32, kind="ExternalOutput").ap()
    dtfo = nc.dram_tensor("dtfo", [NH, T], F32, kind="ExternalOutput").ap()
    hino = nc.dram_tensor("hino", [DS, NH * HD], F32, kind="ExternalOutput").ap()
    cc_in_s = nc.dram_tensor("cc_in_s", [DS, NH * HD], F32)
    cc_out_s = nc.dram_tensor("cc_out_s", [NCORES * DS, NH * HD], F32, addr_space="Shared")
    cc_in_l = nc.dram_tensor("cc_in_l", [NH, 1], F32)
    cc_out_l = nc.dram_tensor("cc_out_l", [NCORES * NH, 1], F32, addr_space="Shared")

    io = {}
    io["xnT"] = xnT
    io["winT"] = winT
    io["convw"] = convw
    io["convb"] = convb
    io["alog"] = alog
    io["dtb"] = dtb
    io["nw"] = nw
    io["cmask"] = cmask
    io["identf"] = identf
    io["szT"] = szT
    io["xcvT"] = xcvT
    io["BTo"] = BTo
    io["CTo"] = CTo
    io["cumo"] = cumo
    io["dtfo"] = dtfo
    io["hino"] = hino
    io["cc_in_s"] = cc_in_s
    io["cc_out_s"] = cc_out_s
    io["cc_in_l"] = cc_in_l
    io["cc_out_l"] = cc_out_l
    with ExitStack() as ctx:
        p = Prog(nc, ctx)
        emit_B(nc, p, ctx, io, stage)
        p.finalize()
    return nc




def emit_C(nc, p, ctx, io, stage=99):
    E = ctx.enter_context
    xnT = io["xnT"]
    xres = io["xres"]
    gatew = io["gatew"]
    gateb = io["gateb"]
    w1 = io["w1"]
    w2 = io["w2"]
    wg = io["wg"]
    w3 = io["w3"]
    b1 = io["b1"]
    b2 = io["b2"]
    bg = io["bg"]
    b3 = io["b3"]
    identf = io["identf"]
    xres2 = io["xres2"]
    wesc = io["wesc"]
    wbig = E(nc.sbuf_tensor("wbigC", [P, 2 * DC * DI], BF16))
    w1s = wbig[:, 0:DC * DI]
    w2s = wbig[:, DC * DI:2 * DC * DI]
    wgs = wbig[:, :]
    zT = E(nc.sbuf_tensor("zTC", [P, DIC * 1024], BF16))
    xracc = E(nc.sbuf_tensor("xraccC", [P, DC * T], F32))
    swT = E(nc.sbuf_tensor("swTC", [P, DIC * 512], BF16))
    xnr = E(nc.sbuf_tensor("xnrC", [P, DC * 512], BF16))
    w3r = [E(nc.sbuf_tensor(f"w3rC{i}", [P, D], BF16)) for i in range(2)]
    ident = E(nc.sbuf_tensor("identC", [P, P], F32))
    onesrow = E(nc.sbuf_tensor("onesrowC", [1, P], F32))
    gw = E(nc.sbuf_tensor("gwC", [P, DC * 8], BF16))
    gbr = E(nc.sbuf_tensor("gbrC", [1, 8], F32))
    b3row = E(nc.sbuf_tensor("b3rowC", [1, D], F32))
    bcolr = E(nc.sbuf_tensor("bcolrC", [P, P], F32))
    b1col = E(nc.sbuf_tensor("b1colC", [P, P], F32))
    b2col = E(nc.sbuf_tensor("b2colC", [P, P], F32))
    bgcol = E(nc.sbuf_tensor("bgcolC", [P, P], F32))
    wefm = zT[0:8, 0:4096].bitcast(F32)
    lg = E(nc.sbuf_tensor("lgC", [P, 8], F32))
    pex = E(nc.sbuf_tensor("pexC", [P, 8], F32))
    m1 = E(nc.sbuf_tensor("m1C", [P, 1], F32))
    t1g = E(nc.sbuf_tensor("t1gC", [P, 8], F32))
    wet = E(nc.sbuf_tensor("wetC", [P, 8], F32))
    werow = E(nc.sbuf_tensor("werowC", [1, T], F32))
    x1tmp = [swT[:, i * 1024:(i + 1) * 1024].bitcast(F32) for i in range(2)]
    stmp = xnr[:, 0:1024].bitcast(F32)
    ps8 = [E(nc.psum_tensor(f"ps8C{i}", [P, 512], F32)) for i in range(8)]

    p.dma("sy", ident[:, :], identf)
    p.op("pool", lambda o: o.memset(onesrow[:, :], 1.0), incs=[("S_pool", 1)])
    p.dma("pool", gw[:, :].rearrange("p (k e) -> p k e", e=8), gatew.rearrange("(k p) e -> p k e", p=P))
    p.dma("sy", gbr[:, :], gateb)
    for name, src_ap, dst in ((0, b1, b1col), (1, b2, b2col), (2, bg, bgcol)):
        p.dma("sy", bcolr[:, :], src_ap.rearrange("e (c p) -> (e c) p", p=P),
              waits=[("S_act", p.now("S_act"))])
        p.op("pe", lambda o: o.transpose(ps8[0][:, 0:P], bcolr[:, :], ident[:, :]),
             waits=[("ld", p.now("ld"))], incs=[("S_pe", 1)])
        p.op("act", lambda o, dst=dst: o.copy(dst[:, :], ps8[0][:, 0:P]),
             waits=[("S_pe", p.now("S_pe"))], incs=[("S_act", 1)])
    for c in range(DC):
        p.dma("sy", xracc[:, c * T:(c + 1) * T], xres[c * P:(c + 1) * P, :])
    xg = [zT[:, 4096 + i * DC * P:4096 + (i + 1) * DC * P] for i in range(2)]
    if True:
        ldg = p.now("ld")
        for tt in range(NTC):
            t0 = tt * P
            xgb = xg[tt % 2]
            p.dma("pool", xgb[:, :].rearrange("p (k t) -> p k t", t=P),
                  xnT.rearrange("(k p) t -> p k t", p=P)[:, :, 3 + t0:3 + t0 + P],
                  waits=[("S_pe", p.now("S_pe"))])
            ldx = p.now("pld")
            for k in range(DC):
                p.op("pe", lambda o, k=k, xgb=xgb:
                     o.matmul(ps8[1][0:P, 0:8], xgb[:, k * P:(k + 1) * P],
                              gw[:, k * 8:(k + 1) * 8], start=(k == 0), stop=False),
                     waits=([("pld", ldx), ("ld", ldg), ("S_act", p.now("S_act"))] if k == 0 else []))
            p.op("pe", lambda o, t0=t0:
                 o.matmul(ps8[1][0:P, 0:8], onesrow[0:1, 0:P], gbr[0:1, :], start=False, stop=True),
                 incs=[("S_pe", 1)])
            p.op("act", lambda o: o.copy(lg[:, :], ps8[1][0:P, 0:8]),
                 waits=[("S_pe", p.now("S_pe"))], incs=[("S_act", 1)])
            p.op("dve", lambda o: o.reduce_max(m1[:, :], lg[:, :], AX.X),
                 waits=[("S_act", p.now("S_act"))])
            p.op("dve", lambda o: o.tensor_scalar(m1[:, :], m1[:, :], -1.0, 0.0, ALU.mult, ALU.add),
                 incs=[("S_dve", 1)])
            p.op("act", lambda o: o.activation(pex[:, :], lg[:, :], AF.Exp, bias=m1[:, 0:1], scale=1.0),
                 waits=[("S_dve", p.now("S_dve"))], incs=[("S_act", 1)])
            p.op("dve", lambda o: o.reduce_max(m1[:, :], pex[:, :], AX.X),
                 waits=[("S_act", p.now("S_act"))])
            p.op("dve", lambda o: o.tensor_scalar(t1g[:, :], pex[:, :], m1[:, 0:1], 0.0, ALU.is_ge, ALU.add))
            p.op("dve", lambda o: o.tensor_scalar(t1g[:, :], t1g[:, :], -1.0, 1.0, ALU.mult, ALU.add))
            p.op("dve", lambda o: o.tensor_tensor(t1g[:, :], pex[:, :], t1g[:, :], ALU.mult))
            p.op("dve", lambda o: o.reduce_max(m1[:, :], t1g[:, :], AX.X))
            p.op("dve", lambda o: o.tensor_scalar(t1g[:, :], pex[:, :], m1[:, 0:1], 0.0, ALU.is_ge, ALU.add))
            p.op("dve", lambda o: o.tensor_tensor(wet[:, :], pex[:, :], t1g[:, :], ALU.mult))
            p.op("dve", lambda o: o.reduce_sum(m1[:, :], wet[:, :], AX.X))
            p.op("dve", lambda o: o.reciprocal(m1[:, :], m1[:, :]))
            p.op("dve", lambda o: o.tensor_scalar(wet[:, :], wet[:, :], m1[:, 0:1], 0.0, ALU.mult, ALU.add),
                 incs=[("S_dve", 1)])
            p.op("pe", lambda o: o.transpose(ps8[2][0:8, 0:P], wet[:, :], ident[:, :]),
                 waits=[("S_dve", p.now("S_dve"))], incs=[("S_pe", 1)])
            p.op("act", lambda o, t0=t0: o.copy(wefm[:, t0:t0 + P], ps8[2][0:8, 0:P]),
                 waits=[("S_pe", p.now("S_pe"))], incs=[("S_act", 1)])
            if stage == 0 and t0 == 0:
                p.barrier()
                p.dma("sy", xres2[0:P, 0:8], lg[:, :], sem="st")
                p.dma("sy", xres2[0:P, 8:16], pex[:, :], sem="st")
                p.dma("sy", xres2[0:P, 16:24], wet[:, :], sem="st")
                p.dma("sy", xres2[128:136, 24:152], wefm[0:8, 0:128], sem="st")
                p.op("sy", None, waits=[("st", p.now("st"))])
                return
        p.barrier()
        p.dma("sy", wesc.ap(), wefm[:, :], sem="st")
        p.op("sy", None, waits=[("st", p.now("st"))])
        p.barrier()
        if stage == 1:
            for c in range(DC):
                p.dma("sy", xres2[c * P:(c + 1) * P, :], xracc[:, c * T:(c + 1) * T], sem="st")
            p.dma("sy", xres2[0:8, 0:T], wefm[:, :], sem="st")
            p.op("sy", None, waits=[("st", p.now("st"))])
            return

        for e in range(8):
            p.dma("sy", werow[:, :], wesc.ap()[e:e + 1, :], waits=[("S_pe", p.now("S_pe"))])
            p.dma("sy", b3row[:, :], b3[e:e + 1, :])
            for th in range(2):
                p.dma("pool", w1s[:, :].rearrange("p (k f) -> p k f", f=DI),
                      w1.rearrange("e (k p) f -> e p k f", p=P)[e],
                      waits=[("S_act", p.now("S_act")), ("S_dve", p.now("S_dve")),
                             ("S_pe", p.now("S_pe"))])
                p.dma("pool", w2s[:, :].rearrange("p (k f) -> p k f", f=DI),
                      w2.rearrange("e (k p) f -> e p k f", p=P)[e])
                ldw = p.now("pld")
                for ng in range(2):
                    n0 = th * 1024 + ng * 512
                    nn = ng * 512
                    p.dma("pool", xnr[:, :].rearrange("p (k n) -> p k n", n=512),
                          xnT.rearrange("(k p) t -> p k t", p=P)[:, :, 3 + n0:3 + n0 + 512],
                          waits=[("S_pe", p.now("S_pe"))])
                    ldn = p.now("pld")
                    for fb in range(DIC):
                        bnkA = ps8[(fb % 2) * 2]
                        bnkB = ps8[(fb % 2) * 2 + 1]
                        xt = x1tmp[fb % 2]
                        for k in range(DC):
                            p.op("pe", lambda o, k=k, fb=fb, bnkA=bnkA:
                                 o.matmul(bnkA[:, :], w1s[:, k * DI + fb * P: k * DI + (fb + 1) * P],
                                          xnr[:, k * 512:(k + 1) * 512],
                                          start=(k == 0), stop=(k == DC - 1)),
                                 waits=([("pld", ldw), ("pld", ldn), ("S_dve", p.now("S_dve"))] if k == 0 else []))
                        for k in range(DC):
                            p.op("pe", lambda o, k=k, fb=fb, bnkB=bnkB:
                                 o.matmul(bnkB[:, :], w2s[:, k * DI + fb * P: k * DI + (fb + 1) * P],
                                          xnr[:, k * 512:(k + 1) * 512],
                                          start=(k == 0), stop=(k == DC - 1)),
                                 incs=([("S_pe", 1)] if k == DC - 1 else []))
                        mmd = p.now("S_pe")
                        p.op("act", lambda o, e=e, fb=fb, bnkA=bnkA, xt=xt:
                             o.activation(xt[:, :], bnkA[:, :], AF.Identity,
                                          bias=b1col[:, e * DIC + fb:e * DIC + fb + 1], scale=1.0),
                             waits=[("S_pe", mmd)], incs=[("S_act", 1)])
                        p.op("dve", lambda o, e=e, fb=fb, nn=nn, bnkB=bnkB, xt=xt:
                             o.scalar_tensor_tensor(zT[:, fb * 1024 + nn: fb * 1024 + nn + 512],
                                                    bnkB[:, :], b2col[:, e * DIC + fb:e * DIC + fb + 1],
                                                    xt[:, :], ALU.add, ALU.mult),
                             waits=[("S_act", p.now("S_act"))], incs=[("S_dve", 1)])
                p.barrier()
                if stage == 2 and e == 0 and th == 0:
                    for fb in range(DIC):
                        p.dma("pool", xres2[(fb % 8) * P:(fb % 8 + 1) * P,
                                          (fb // 8) * 1024:(fb // 8 + 1) * 1024],
                              zT[:, fb * 1024:(fb + 1) * 1024], sem="st")
                    p.op("sy", None, waits=[("st", p.now("st"))])
                    return
                p.dma("pool", wgs[:, :].rearrange("p (k f) -> p k f", f=DI),
                      wg.rearrange("e (k p) f -> e p k f", p=P)[e])
                ldw2 = p.now("pld")
                for ng in range(2):
                    n0 = th * 1024 + ng * 512
                    nn = ng * 512
                    p.op("pe", lambda o, n0=n0:
                         o.matmul(ps8[7][:, :], onesrow[0:1, :], werow[0:1, n0:n0 + 512],
                                  start=True, stop=True),
                         waits=[("S_dve", p.now("S_dve")), ("ld", p.now("ld"))], incs=[("S_pe", 1)])
                    webd = p.now("S_pe")
                    for gb in range(DIC):
                        bnk = ps8[4 + (gb % 2)]
                        for k in range(DIC):
                            p.op("pe", lambda o, k=k, gb=gb, nn=nn, bnk=bnk:
                                 o.matmul(bnk[:, :], wgs[:, k * DI + gb * P: k * DI + (gb + 1) * P],
                                          zT[:, k * 1024 + nn: k * 1024 + nn + 512],
                                          start=(k == 0), stop=(k == DIC - 1)),
                                 waits=([("pld", ldw2), ("S_dve", p.now("S_dve"))] if k == 0 else []),
                                 incs=([("S_pe", 1)] if k == DIC - 1 else []))
                        mmd = p.now("S_pe")
                        p.op("act", lambda o, e=e, gb=gb, bnk=bnk:
                             o.activation(stmp[:, :], bnk[:, :], AF.Silu,
                                          bias=bgcol[:, e * DIC + gb:e * DIC + gb + 1], scale=1.0),
                             waits=[("S_pe", mmd)], incs=[("S_act", 1)])
                        p.op("dve", lambda o, gb=gb:
                             o.tensor_tensor(swT[:, gb * 512:(gb + 1) * 512], stmp[:, :], ps8[7][:, :], ALU.mult),
                             waits=[("S_act", p.now("S_act")), ("S_pe", webd)], incs=[("S_dve", 1)])
                    swd = p.now("S_dve")
                    if stage == 3 and e == 0 and th == 0 and ng == 0:
                        for gb in range(DIC):
                            p.dma("pool", xres2[(gb % 8) * P:(gb % 8 + 1) * P,
                                              (gb // 8) * 512:(gb // 8 + 1) * 512],
                                  swT[:, gb * 512:(gb + 1) * 512], sem="st",
                                  waits=[("S_dve", swd)])
                        p.op("sy", None, waits=[("st", p.now("st"))])
                        return
                    gb_done = {}
                    for obh in range(2):
                        evw = p.now("S_dve")
                        for gb in range(DIC):
                            wr = w3r[gb % 2]
                            wwait = [("S_pe", gb_done.get((obh, gb - 2), gb_done.get((obh - 1, DIC - 1), 0)))]
                            p.dma("pool", wr[:, :], w3[e, gb * P:(gb + 1) * P, :], waits=wwait)
                            ldw3 = p.now("pld")
                            for ob in range(4):
                                w_ = [("pld", ldw3)] if ob == 0 else []
                                if gb == 0:
                                    w_ = w_ + [("S_dve", evw if obh > 0 else swd)]
                                p.op("pe", lambda o, gb=gb, ob=ob, obh=obh, wr=wr:
                                     o.matmul(ps8[ob][:, :], wr[:, (obh * 4 + ob) * P:(obh * 4 + ob + 1) * P],
                                              swT[:, gb * 512:(gb + 1) * 512],
                                              start=(gb == 0), stop=False),
                                     waits=w_,
                                     incs=([("S_pe", 1)] if ob == 3 else []), settle=False)
                            gb_done[(obh, gb)] = p.now("S_pe")
                        for ob in range(4):
                            p.op("pe", lambda o, ob=ob, obh=obh, n0=n0:
                                 o.matmul(ps8[ob][:, :], b3row[0:1, (obh * 4 + ob) * P:(obh * 4 + ob + 1) * P],
                                          werow[0:1, n0:n0 + 512],
                                          start=False, stop=True),
                                 incs=([("S_pe", 1)] if ob == 3 else []))
                        mmd = p.now("S_pe")
                        for ob in range(4):
                            c = obh * 4 + ob
                            p.op("dve", lambda o, c=c, ob=ob, n0=n0:
                                 o.tensor_tensor(xracc[:, c * T + n0: c * T + n0 + 512],
                                                 xracc[:, c * T + n0: c * T + n0 + 512],
                                                 ps8[ob][:, :], ALU.add),
                                 waits=[("S_pe", mmd)], incs=[("S_dve", 1)])
                p.barrier()
    for c in range(DC):
        p.dma("sy", xres2[c * P:(c + 1) * P, :], xracc[:, c * T:(c + 1) * T], sem="st",
              waits=[("S_dve", p.now("S_dve"))])
    p.op("sy", None, waits=[("st", p.now("st"))])



def build_launch_C(stage=99):
    nc = bass.Bass("TRN2", target_bir_lowering=False)
    xnT = nc.dram_tensor("xnT", [D, TH], BF16, kind="ExternalInput").ap()
    xres = nc.dram_tensor("xres", [D, T], F32, kind="ExternalInput").ap()
    gatew = nc.dram_tensor("gatew", [D, 8], F32, kind="ExternalInput").ap()
    gateb = nc.dram_tensor("gateb", [1, 8], F32, kind="ExternalInput").ap()
    w1 = nc.dram_tensor("w1", [8, D, DI], F32, kind="ExternalInput").ap()
    w2 = nc.dram_tensor("w2", [8, D, DI], F32, kind="ExternalInput").ap()
    wg = nc.dram_tensor("wg", [8, DI, DI], F32, kind="ExternalInput").ap()
    w3 = nc.dram_tensor("w3", [8, DI, D], F32, kind="ExternalInput").ap()
    b1 = nc.dram_tensor("b1", [8, DI], F32, kind="ExternalInput").ap()
    b2 = nc.dram_tensor("b2", [8, DI], F32, kind="ExternalInput").ap()
    bg = nc.dram_tensor("bg", [8, DI], F32, kind="ExternalInput").ap()
    b3 = nc.dram_tensor("b3", [8, D], F32, kind="ExternalInput").ap()
    identf = nc.dram_tensor("identf", [P, P], F32, kind="ExternalInput").ap()
    xres2 = nc.dram_tensor("xres2", [D, T], F32, kind="ExternalOutput").ap()
    wesc = nc.dram_tensor("wesc", [8, T], F32)

    io = {}
    io["xnT"] = xnT
    io["xres"] = xres
    io["gatew"] = gatew
    io["gateb"] = gateb
    io["w1"] = w1
    io["w2"] = w2
    io["wg"] = wg
    io["w3"] = w3
    io["b1"] = b1
    io["b2"] = b2
    io["bg"] = bg
    io["b3"] = b3
    io["identf"] = identf
    io["xres2"] = xres2
    io["wesc"] = wesc
    with ExitStack() as ctx:
        p = Prog(nc, ctx)
        emit_C(nc, p, ctx, io, stage)
        p.finalize()
    return nc




def emit_D(nc, p, ctx, io, stage=99, _DBG_CHUNK=-1):
    E = ctx.enter_context
    szT = io["szT"]
    xcvT = io["xcvT"]
    BTo = io["BTo"]
    CTo = io["CTo"]
    cumo = io["cumo"]
    dtfo = io["dtfo"]
    hino = io["hino"]
    xres2 = io["xres2"]
    woT = io["woT"]
    dvec = io["dvec"]
    identf = io["identf"]
    mask01T = io["mask01T"]
    selT = io["selT"]
    sel2 = io["sel2"]
    xres3 = io["xres3"]
    scr1 = io["scr1"]
    scr2 = io["scr2"]
    woTs = E(nc.sbuf_tensor("woTsD", [P, DIC * D], BF16))
    ygT = E(nc.sbuf_tensor("ygTD", [P, DIC * 512], BF16))
    STb = E(nc.sbuf_tensor("STbD", [P, NH * P], BF16))
    CtT = E(nc.sbuf_tensor("CtTD", [P, NH * P], BF16))
    szc = [E(nc.sbuf_tensor(f"szcD{i}", [P, DIC * P], BF16)) for i in range(2)]
    xcvc = [E(nc.sbuf_tensor(f"xcvcD{i}", [P, DIC * P], BF16)) for i in range(2)]
    BTs = E(nc.sbuf_tensor("BTsD", [P, T], BF16))
    CTs = E(nc.sbuf_tensor("CTsD", [P, T], BF16))
    cums = E(nc.sbuf_tensor("cumsD", [NH, T], F32))
    dtfs = E(nc.sbuf_tensor("dtfsD", [NH, T], F32))
    hT = E(nc.sbuf_tensor("hTD", [P, NH * HD], F32))
    xctok = E(nc.sbuf_tensor("xctokD", [P, DI], BF16))
    dtx = E(nc.sbuf_tensor("dtxD", [P, DI], BF16))
    ddtx = E(nc.sbuf_tensor("ddtxD", [P, DI], BF16))
    btok = E(nc.sbuf_tensor("btokD", [P, P], BF16))
    cumtok = E(nc.sbuf_tensor("cumtokD", [P, NH], F32))
    dttok = E(nc.sbuf_tensor("dttokD", [P, NH], F32))
    edtok = E(nc.sbuf_tensor("edtokD", [P, NH], F32))
    edf2 = E(nc.sbuf_tensor("edf2D", [NH, P], F32))
    edfB = E(nc.sbuf_tensor("edfBD", [NH, P], F32))
    cumflat = E(nc.sbuf_tensor("cumflatD", [1, NH * P], F32))
    ecumflat = E(nc.sbuf_tensor("ecumflatD", [1, NH * P], F32))
    lamexp = E(nc.sbuf_tensor("lamexpD", [1, DI], F32))
    lamrow = E(nc.sbuf_tensor("lamrowD", [1, NH], F32))
    negpc = E(nc.sbuf_tensor("negpcD", [NH, 1], F32))
    zc32 = E(nc.sbuf_tensor("zc32D", [NH, 1], F32))
    GmT = E(nc.sbuf_tensor("GmTD", [P, P], F32))
    t1s = E(nc.sbuf_tensor("t1sD", [P, 512], F32))
    t2s = E(nc.sbuf_tensor("t2sD", [P, 512], F32))
    ytmp = E(nc.sbuf_tensor("ytmpD", [P, P], F32))
    mask = E(nc.sbuf_tensor("maskD", [P, P], F32))
    selTs = E(nc.sbuf_tensor("selTsD", [NH, P], F32))
    sel2s = E(nc.sbuf_tensor("sel2sD", [NH, 16], F32))
    dsel = E(nc.sbuf_tensor("dselD", [NH, 16], F32))
    dcol2 = E(nc.sbuf_tensor("dcol2D", [NH, 1], F32))
    Dcol = E(nc.sbuf_tensor("DcolD", [P, 16], F32))
    ident = E(nc.sbuf_tensor("identD", [P, P], F32))
    identb = E(nc.sbuf_tensor("identbD", [P, P], BF16))
    onesrow = E(nc.sbuf_tensor("onesrowD", [1, P], F32))
    onesDI = E(nc.sbuf_tensor("onesDID", [1, DI], F32))
    xrs = [E(nc.sbuf_tensor(f"xrsD{i}", [P, 512], F32)) for i in range(2)]
    hT2 = E(nc.sbuf_tensor("hT2D", [P, NH * HD], F32))
    hb = E(nc.sbuf_tensor("hbD", [P, NH * HD], BF16))
    tmpd = E(nc.sbuf_tensor("tmpdD", [P, 512], F32))
    ps4 = [E(nc.psum_tensor(f"ps4D{i}", [P, 512], F32)) for i in range(4)]
    psF = E(nc.psum_tensor("psFD", [P, 512], F32))
    psB = E(nc.psum_tensor("psBD", [P, 512], F32))
    psT = [E(nc.psum_tensor(f"psTD{i}", [P, 1024], BF16)) for i in range(2)]

    p.dma("sy", ident[:, :], identf)
    p.dma("pool", identb[:, :], identf)
    p.dma("sy", mask[:, :], mask01T)
    p.dma("sy", selTs[:, :], selT)
    p.dma("sy", sel2s[:, :], sel2)
    p.dma("sy", dcol2[:, :], dvec.unsqueeze(1))
    p.dma("sy", BTs[:, :], BTo)
    p.dma("sy", CTs[:, :], CTo)
    p.dma("sy", cums[:, :], cumo)
    p.dma("sy", dtfs[:, :], dtfo)
    p.dma("sy", hT[:, :], hino)
    p.op("pool", lambda o: o.memset(onesrow[:, :], 1.0))
    p.op("pool", lambda o: o.memset(onesDI[:, :], 1.0))
    p.op("pool", lambda o: o.memset(zc32[:, :], 0.0))
    p.dma("pool", woTs[:, :].rearrange("p (k o) -> p k o", o=D),
          woT.rearrange("(k p) o -> p k o", p=P))
    ld0 = p.now("ld")
    p.op("dve", lambda o: o.tensor_tensor(dsel[:, :], sel2s[:, :],
                                          dcol2[:, 0:1].broadcast_to([NH, 16]), ALU.mult),
         waits=[("ld", ld0)], incs=[("S_dve", 1)])
    p.op("pe", lambda o: o.matmul(psF[:, 0:16], selTs[:, :], dsel[:, :], start=True, stop=True),
         waits=[("S_dve", p.now("S_dve"))], incs=[("S_pe", 1)])
    p.op("act", lambda o: o.copy(Dcol[:, :], psF[:, 0:16]),
         waits=[("S_pe", p.now("S_pe"))], incs=[("S_act", 1)])
    p.op("dve", lambda o: o.tensor_copy(hb[:, :], hT[:, :]),
         waits=[("ld", p.now("ld"))], incs=[("S_dve", 1)])
    p.barrier()

    for c in range(NTC):
        t0 = c * P
        szb = szc[c % 2]
        xcb = xcvc[c % 2]
        p.dma("pool", szb[:, :].rearrange("p (q t) -> p q t", t=P),
              szT.rearrange("(q p) t -> p q t", p=P)[:, :, t0:t0 + P],
              waits=[("S_dve", p.now("S_dve"))])
        p.dma("pool", xcb[:, :].rearrange("p (q t) -> p q t", t=P),
              xcvT.rearrange("(q p) t -> p q t", p=P)[:, :, t0:t0 + P])
        ldc = p.now("pld")
        p.op("dve", lambda o, c=c, t0=t0:
             o.tensor_scalar(negpc[:, 0:1], (cums[:, t0 - 1:t0] if c > 0 else zc32[:, 0:1]),
                             -1.0, 0.0, ALU.mult, ALU.add), incs=[("S_dve", 1)])
        p.op("act", lambda o, t0=t0:
             o.activation(edf2[:, :], cums[:, t0:t0 + P], AF.Exp, bias=negpc[:, 0:1], scale=1.0),
             waits=[("S_dve", p.now("S_dve"))], incs=[("S_act", 1)])
        p.dma("sy", scr1.ap(), cums[:, t0:t0 + P], sem="st")
        p.dma("sy", scr2.ap(), edf2[:, :], sem="st", waits=[("S_act", p.now("S_act"))])
        p.op("sy", None, waits=[("st", p.now("st"))])
        p.dma("sy", cumflat[:, :], scr1.ap().rearrange("h t -> (h t)").unsqueeze(0))
        p.dma("sy", ecumflat[:, :], scr2.ap().rearrange("h t -> (h t)").unsqueeze(0))
        ldflat = p.now("ld")
        base_act = p.now("S_act")
        for q in range(DIC):
            pb = psT[q % 2]
            p.op("pe", lambda o, q=q, pb=pb, xcb=xcb:
                 o.transpose(pb[:, 0:P], xcb[:, q * P:(q + 1) * P], identb[:, :]),
                 waits=[("S_act", max(0, base_act + q - 1)), ("pld", ldc)], incs=[("S_pe", 1)])
            p.op("act", lambda o, q=q, pb=pb:
                 o.copy(xctok[:, q * P:(q + 1) * P], pb[:, 0:P]),
                 waits=[("S_pe", p.now("S_pe"))], incs=[("S_act", 1)])
        p.op("pe", lambda o, t0=t0:
             o.transpose(psT[0][:, 0:P], BTs[:, t0:t0 + P], identb[:, :]),
             waits=[("S_act", p.now("S_act"))], incs=[("S_pe", 1)])
        p.op("act", lambda o: o.copy(btok[:, :], psT[0][:, 0:P]),
             waits=[("S_pe", p.now("S_pe"))], incs=[("S_act", 1)])
        p.op("act", lambda o, t0=t0:
             o.activation(edfB[:, :], cums[:, t0:t0 + P], AF.Exp,
                          bias=cums[:, t0 + P - 1:t0 + P], scale=-1.0),
             incs=[("S_act", 1)])
        for (dst, srcrow) in ((cumtok, 0), (dttok, 1), (edtok, 2)):
            src = [cums, dtfs, None][srcrow]
            if srcrow < 2:
                p.op("pe", lambda o, t0=t0, src=src:
                     o.transpose(psF[:, 0:NH], src[:, t0:t0 + P], ident[0:NH, 0:NH]),
                     waits=[("S_act", p.now("S_act"))], incs=[("S_pe", 1)])
            else:
                p.op("pe", lambda o:
                     o.transpose(psF[:, 0:NH], edfB[:, :], ident[0:NH, 0:NH]),
                     waits=[("S_act", p.now("S_act"))], incs=[("S_pe", 1)])
            p.op("act", lambda o, dst=dst: o.copy(dst[:, :], psF[:, 0:NH]),
                 waits=[("S_pe", p.now("S_pe"))], incs=[("S_act", 1)])
        p.op("dve", lambda o:
             o.tensor_tensor(dtx[:, :].rearrange("p (h d) -> p h d", d=HD),
                             xctok[:, :].rearrange("p (h d) -> p h d", d=HD),
                             dttok[:, :].unsqueeze(2).broadcast_to([P, NH, HD]), ALU.mult),
             waits=[("S_act", p.now("S_act"))], incs=[("S_dve", 1)])
        p.op("dve", lambda o:
             o.tensor_tensor(ddtx[:, :].rearrange("p (h d) -> p h d", d=HD),
                             dtx[:, :].rearrange("p (h d) -> p h d", d=HD),
                             edtok[:, :].unsqueeze(2).broadcast_to([P, NH, HD]), ALU.mult),
             incs=[("S_dve", 1)])
        p.op("pe", lambda o, t0=t0:
             o.matmul(psF[:, 0:P], BTs[:, t0:t0 + P], CTs[:, t0:t0 + P], start=True, stop=True),
             waits=[("S_act", p.now("S_act"))], incs=[("S_pe", 1)])
        p.op("dve", lambda o: o.tensor_tensor(GmT[:, :], psF[:, 0:P], mask[:, :], ALU.mult),
             waits=[("S_pe", p.now("S_pe"))], incs=[("S_dve", 1)])
        for s in range(8):
            f0 = s * 512
            h0 = s * 4
            p.op("pe", lambda o, f0=f0:
                 o.matmul(psB[:, :], onesrow[0:1, :], cumflat[0:1, f0:f0 + 512], start=True, stop=True),
                 waits=[("ld", ldflat), ("S_dve", p.now("S_dve"))], incs=[("S_pe", 1)])
            p.op("dve", lambda o, h0=h0:
                 o.tensor_tensor(t1s[:, :].rearrange("p (a b) -> p a b", a=4),
                                 psB[:, :].rearrange("p (a b) -> p a b", a=4),
                                 cumtok[:, h0:h0 + 4].unsqueeze(2).broadcast_to([P, 4, P]),
                                 ALU.subtract),
                 waits=[("S_pe", p.now("S_pe"))], incs=[("S_dve", 1)])
            p.op("dve", lambda o:
                 o.tensor_tensor(t2s[:, :].rearrange("p (a b) -> p a b", a=4),
                                 t1s[:, :].rearrange("p (a b) -> p a b", a=4),
                                 mask[:, :].unsqueeze(1).broadcast_to([P, 4, P]), ALU.mult),
                 incs=[("S_dve", 1)])
            p.op("act", lambda o: o.activation(t1s[:, :], t2s[:, :], AF.Exp),
                 waits=[("S_dve", p.now("S_dve"))], incs=[("S_act", 1)])
            p.op("dve", lambda o, f0=f0:
                 o.tensor_tensor(STb[:, f0:f0 + 512].rearrange("p (a b) -> p a b", a=4),
                                 t1s[:, :].rearrange("p (a b) -> p a b", a=4),
                                 GmT[:, :].unsqueeze(1).broadcast_to([P, 4, P]), ALU.mult),
                 waits=[("S_act", p.now("S_act"))], incs=[("S_dve", 1)])
            p.op("pe", lambda o, f0=f0:
                 o.matmul(psB[:, :], onesrow[0:1, :], ecumflat[0:1, f0:f0 + 512], start=True, stop=True),
                 waits=[("S_dve", p.now("S_dve"))], incs=[("S_pe", 1)])
            p.op("dve", lambda o, f0=f0, t0=t0:
                 o.tensor_tensor(CtT[:, f0:f0 + 512].rearrange("p (a b) -> p a b", a=4),
                                 psB[:, :].rearrange("p (a b) -> p a b", a=4),
                                 CTs[:, t0:t0 + P].unsqueeze(1).broadcast_to([P, 4, P]), ALU.mult),
                 waits=[("S_pe", p.now("S_pe"))], incs=[("S_dve", 1)])
        std = p.now("S_dve")
        if stage == 1 and c == 0:
            for q in range(DIC):
                p.dma("pool", xres3[(q % 8) * P:(q % 8 + 1) * P, (q // 8) * 256:(q // 8) * 256 + 128],
                      STb[:, q * P:(q + 1) * P], sem="st", waits=[("S_dve", std)])
                p.dma("pool", xres3[(q % 8) * P:(q % 8 + 1) * P, 512 + (q // 8) * 256:512 + (q // 8) * 256 + 128],
                      CtT[:, q * P:(q + 1) * P], sem="st")
            p.op("sy", None, waits=[("st", p.now("st"))])
            return
        for h in range(NH):
            bank = ps4[(h // 2) % 4]
            coff = ((h // 2) // 4) * P
            g = h // 2
            bank = ps4[g // 4]
            coff = (g % 4) * P
            r0 = (h % 2) * HD
            p.op("pe", lambda o, h=h, bank=bank, coff=coff, r0=r0:
                 o.matmul(bank[r0:r0 + HD, coff:coff + P], dtx[:, h * HD:(h + 1) * HD],
                          STb[:, h * P:(h + 1) * P], start=True, stop=False),
                 waits=([("S_dve", std)] if h == 0 else []))
            p.op("pe", lambda o, h=h, bank=bank, coff=coff, r0=r0:
                 o.matmul(bank[r0:r0 + HD, coff:coff + P], hb[:, h * HD:(h + 1) * HD],
                          CtT[:, h * P:(h + 1) * P], start=False, stop=True),
                 incs=([("S_pe", 1)] if h == NH - 1 else []))
        yd = p.now("S_pe")
        for q in range(DIC):
            bank = ps4[q // 4]
            coff = (q % 4) * P
            p.op("dve", lambda o, q=q, bank=bank, coff=coff, xcb=xcb:
                 o.scalar_tensor_tensor(ytmp[:, :], xcb[:, q * P:(q + 1) * P], Dcol[:, q:q + 1],
                                        bank[:, coff:coff + P], ALU.mult, ALU.add),
                 waits=[("S_pe", yd)], incs=[("S_dve", 1)])
            p.op("dve", lambda o, q=q, c=c, szb=szb:
                 o.tensor_tensor(ygT[:, q * 512 + (c % 4) * P: q * 512 + (c % 4) * P + P],
                                 ytmp[:, :], szb[:, q * P:(q + 1) * P], ALU.mult),
                 incs=[("S_dve", 1)])
        epd = p.now("S_dve")
        if stage == 2 and c == _DBG_CHUNK:
            for q in range(DIC):
                p.dma("pool", xres3[(q % 8) * P:(q % 8 + 1) * P, (q // 8) * 128:(q // 8) * 128 + P],
                      ygT[:, q * 512:q * 512 + P], sem="st", waits=[("S_dve", epd)])
            p.op("sy", None, waits=[("st", p.now("st"))])
            return
        for g in range(4):
            p.op("pe", lambda o, g=g:
                 o.matmul(ps4[g][:, :], btok[:, :], ddtx[:, g * 512:(g + 1) * 512],
                          start=True, stop=True),
                 waits=[("S_dve", epd)],
                 incs=([("S_pe", 1)] if g == 3 else []))
        injd = p.now("S_pe")
        p.op("act", lambda o, t0=t0:
             o.activation(negpc[:, 0:1], cums[:, t0 + P - 1:t0 + P], AF.Exp,
                          bias=negpc[:, 0:1], scale=1.0), incs=[("S_act", 1)])
        p.op("pe", lambda o:
             o.transpose(psF[0:1, 64:64 + NH], negpc[:, 0:1], ident[0:NH, 0:NH]),
             waits=[("S_act", p.now("S_act"))], incs=[("S_pe", 1)])
        p.op("act", lambda o: o.copy(lamrow[0:1, :], psF[0:1, 64:64 + NH]),
             waits=[("S_pe", p.now("S_pe"))], incs=[("S_act", 1)])
        p.op("dve", lambda o:
             o.tensor_tensor(lamexp[:, :].rearrange("o (h d) -> o h d", d=HD),
                             onesDI[0:1, :].rearrange("o (h d) -> o h d", d=HD),
                             lamrow[0:1, :].unsqueeze(2).broadcast_to([1, NH, HD]), ALU.mult),
             waits=[("S_act", p.now("S_act"))], incs=[("S_dve", 1)])
        lamd = p.now("S_dve")
        for g in range(4):
            p.op("pe", lambda o, g=g:
                 o.matmul(psB[:, :], onesrow[0:1, :], lamexp[0:1, g * 512:(g + 1) * 512],
                          start=True, stop=True),
                 waits=[("S_dve", lamd), ("S_dve", p.now("S_dve"))], incs=[("S_pe", 1)])
            p.op("dve", lambda o, g=g:
                 o.tensor_tensor(tmpd[:, :], hT[:, g * 512:(g + 1) * 512], psB[:, :], ALU.mult),
                 waits=[("S_pe", p.now("S_pe"))], incs=[("S_dve", 1)])
            p.op("dve", lambda o, g=g:
                 o.tensor_tensor(hT2[:, g * 512:(g + 1) * 512], tmpd[:, :], ps4[g][:, :], ALU.add),
                 waits=[("S_pe", injd)], incs=[("S_dve", 1)])
        p.op("dve", lambda o: o.tensor_copy(hT[:, :], hT2[:, :]), incs=[("S_dve", 1)])
        p.op("dve", lambda o: o.tensor_copy(hb[:, :], hT2[:, :]), incs=[("S_dve", 1)])
        if c % 4 == 3:
            grp = c // 4
            gt0 = grp * 512
            for oc in range(DC):
                bank = ps4[oc % 2]
                xb = xrs[oc % 2]
                p.dma("sy", xb[:, :], xres2[oc * P:(oc + 1) * P, gt0:gt0 + 512],
                      waits=[("st", p.now("st"))])
                ldx = p.now("ld")
                for q in range(DIC):
                    p.op("pe", lambda o, oc=oc, q=q, bank=bank:
                         o.matmul(bank[:, :], woTs[:, q * D + oc * P: q * D + (oc + 1) * P],
                                  ygT[:, q * 512:(q + 1) * 512],
                                  start=(q == 0), stop=(q == DIC - 1)),
                         waits=([("S_dve", p.now("S_dve")), ("S_act", p.now("S_act"))] if q == 0 else []),
                         incs=([("S_pe", 1)] if q == DIC - 1 else []))
                mmd = p.now("S_pe")
                p.op("dve", lambda o, bank=bank, xb=xb:
                     o.tensor_tensor(xb[:, :], xb[:, :], bank[:, :], ALU.add),
                     waits=[("S_pe", mmd), ("ld", ldx)], incs=[("S_dve", 1)])
                p.dma("sy", xres3[oc * P:(oc + 1) * P, gt0:gt0 + 512], xb[:, :],
                      sem="st", waits=[("S_dve", p.now("S_dve"))])
        p.barrier()
    p.op("sy", None, waits=[("st", p.now("st"))])



def build_launch_D(stage=99, _DBG_CHUNK=0):
    nc = bass.Bass("TRN2", target_bir_lowering=False)
    szT = nc.dram_tensor("szT", [DI, T], BF16, kind="ExternalInput").ap()
    xcvT = nc.dram_tensor("xcvT", [DI, T], BF16, kind="ExternalInput").ap()
    BTo = nc.dram_tensor("BTo", [DS, T], BF16, kind="ExternalInput").ap()
    CTo = nc.dram_tensor("CTo", [DS, T], BF16, kind="ExternalInput").ap()
    cumo = nc.dram_tensor("cumo", [NH, T], F32, kind="ExternalInput").ap()
    dtfo = nc.dram_tensor("dtfo", [NH, T], F32, kind="ExternalInput").ap()
    hino = nc.dram_tensor("hino", [DS, NH * HD], F32, kind="ExternalInput").ap()
    xres2 = nc.dram_tensor("xres2", [D, T], F32, kind="ExternalInput").ap()
    woT = nc.dram_tensor("woT", [DI, D], F32, kind="ExternalInput").ap()
    dvec = nc.dram_tensor("dvec", [NH], F32, kind="ExternalInput").ap()
    identf = nc.dram_tensor("identf", [P, P], F32, kind="ExternalInput").ap()
    mask01T = nc.dram_tensor("mask01T", [P, P], F32, kind="ExternalInput").ap()
    selT = nc.dram_tensor("selT", [NH, P], F32, kind="ExternalInput").ap()
    sel2 = nc.dram_tensor("sel2", [NH, 16], F32, kind="ExternalInput").ap()
    xres3 = nc.dram_tensor("xres3", [D, T], F32, kind="ExternalOutput").ap()
    scr1 = nc.dram_tensor("scr1D", [NH, P], F32)
    scr2 = nc.dram_tensor("scr2D", [NH, P], F32)

    io = {}
    io["szT"] = szT
    io["xcvT"] = xcvT
    io["BTo"] = BTo
    io["CTo"] = CTo
    io["cumo"] = cumo
    io["dtfo"] = dtfo
    io["hino"] = hino
    io["xres2"] = xres2
    io["woT"] = woT
    io["dvec"] = dvec
    io["identf"] = identf
    io["mask01T"] = mask01T
    io["selT"] = selT
    io["sel2"] = sel2
    io["xres3"] = xres3
    io["scr1"] = scr1
    io["scr2"] = scr2
    with ExitStack() as ctx:
        p = Prog(nc, ctx)
        emit_D(nc, p, ctx, io, stage, _DBG_CHUNK)
        p.finalize()
    return nc




def emit_E(nc, p, ctx, io, stage=99):
    E = ctx.enter_context
    xres3 = io["xres3"]
    xn1T = io["xn1T"]
    xres4 = io["xres4"]
    xs = E(nc.sbuf_tensor("xsE", [P, DC * T], F32))
    sq = E(nc.sbuf_tensor("sqE", [P, 1024], F32))
    onescol = E(nc.sbuf_tensor("onescolE", [P, 1], F32))
    onesrow = E(nc.sbuf_tensor("onesrowE", [1, P], F32))
    s1r = E(nc.sbuf_tensor("s1rE", [1, T], F32))
    s2r = E(nc.sbuf_tensor("s2rE", [1, T], F32))
    varr = E(nc.sbuf_tensor("varrE", [1, T], F32))
    rrow = E(nc.sbuf_tensor("rrowE", [1, T], F32))
    mrow = E(nc.sbuf_tensor("mrowE", [1, T], F32))
    epsc = E(nc.sbuf_tensor("epscE", [1, 1], F32))
    tmp = E(nc.sbuf_tensor("tmpE", [P, 512], F32))
    xnf = E(nc.sbuf_tensor("xnfE", [P, 512], F32))
    xnb = E(nc.sbuf_tensor("xnbE", [P, 512], BF16))
    psA = E(nc.psum_tensor("psAE", [P, 512], F32))
    psB = E(nc.psum_tensor("psBE", [P, 512], F32))
    psC = E(nc.psum_tensor("psCE", [P, 512], F32))
    psD_ = E(nc.psum_tensor("psDE", [P, 512], F32))

    for c in range(DC):
        p.dma("sy", xs[:, c * T:(c + 1) * T], xres3[c * P:(c + 1) * P, :])
    ld0 = p.now("ld")
    p.op("pool", lambda o: o.memset(onescol[:, :], 1.0))
    p.op("pool", lambda o: o.memset(onesrow[:, :], 1.0))
    p.op("pool", lambda o: o.memset(epsc[:, :], 1e-5), incs=[("S_pool", 1)])
    evacA = evacB = 0
    for a in range(0, T, 512):
        w = 512
        sq_done = []
        mmB_done = []
        for c in range(DC):
            sqh = (c % 2) * 512
            p.op("pe", lambda o, c=c, a=a, w=w:
                 o.matmul(psA[0:1, 0:w], onescol[:, :], xs[:, c * T + a: c * T + a + w],
                          start=(c == 0), stop=(c == DC - 1)),
                 waits=[("ld", ld0), ("S_act", evacA if c == 0 else 0)], incs=[("S_pe", 1)])
            p.op("act", lambda o, c=c, a=a, w=w, sqh=sqh:
                 o.activation(sq[:, sqh:sqh + w], xs[:, c * T + a: c * T + a + w], AF.Square),
                 waits=[("ld", ld0), ("S_pe", mmB_done[c - 2] if c >= 2 else 0)],
                 incs=[("S_act", 1)])
            sq_done.append(p.now("S_act"))
            p.op("pe", lambda o, c=c, a=a, w=w, sqh=sqh:
                 o.matmul(psB[0:1, 0:w], onescol[:, :], sq[:, sqh:sqh + w],
                          start=(c == 0), stop=(c == DC - 1)),
                 waits=[("S_act", sq_done[c]), ("S_act", evacB if c == 0 else 0)],
                 incs=[("S_pe", 1)])
            mmB_done.append(p.now("S_pe"))
        p.op("act", lambda o, a=a, w=w: o.copy(s1r[:, a:a + w], psA[0:1, 0:w]),
             waits=[("S_pe", p.now("S_pe"))], incs=[("S_act", 1)])
        evacA = p.now("S_act")
        p.op("act", lambda o, a=a, w=w: o.copy(s2r[:, a:a + w], psB[0:1, 0:w]),
             incs=[("S_act", 1)])
        evacB = p.now("S_act")
    p.barrier()
    if stage == 4:
        p.dma("sy", xres4[0:1, :], s1r[0:1, :], sem="st")
        p.dma("sy", xres4[1:2, :], s2r[0:1, :], sem="st")
        p.op("sy", None, waits=[("st", p.now("st"))])
        return
    p.op("dve", lambda o: o.tensor_scalar(mrow[:, :], s1r[:, :], 1.0 / D, 0.0, ALU.mult, ALU.add))
    p.op("dve", lambda o: o.tensor_scalar(varr[:, :], s2r[:, :], 1.0 / D, 0.0, ALU.mult, ALU.add),
         incs=[("S_dve", 1)])
    p.op("act", lambda o: o.activation(s2r[:, :], mrow[:, :], AF.Square),
         waits=[("S_dve", p.now("S_dve"))], incs=[("S_act", 1)])
    p.op("dve", lambda o: o.tensor_tensor(s1r[:, :], varr[:, :], s2r[:, :], ALU.subtract),
         waits=[("S_act", p.now("S_act")), ("S_pool", p.now("S_pool"))], incs=[("S_dve", 1)])
    p.op("act", lambda o: o.activation(varr[:, :], s1r[:, :], AF.Sqrt, bias=epsc[0:1, 0:1], scale=1.0),
         waits=[("S_dve", p.now("S_dve"))], incs=[("S_act", 1)])
    p.op("dve", lambda o: o.reciprocal(rrow[:, :], varr[:, :]),
         waits=[("S_act", p.now("S_act"))], incs=[("S_dve", 1)])
    p.op("dve", lambda o: o.scalar_tensor_tensor(s2r[:, :], mrow[:, :], -1.0, rrow[:, :],
                                                 ALU.mult, ALU.mult), incs=[("S_dve", 1)])
    p.op("dve", lambda o: o.tensor_copy(mrow[:, :], s2r[:, :]), incs=[("S_dve", 1)])
    p.barrier()
    if stage == 4:
        pass
    if stage == 3:
        p.dma("sy", xres4[0:1, :], rrow[0:1, :], sem="st")
        p.dma("sy", xres4[1:2, :], mrow[0:1, :], sem="st")
        p.dma("sy", xres4[2:3, :], s1r[0:1, :], sem="st")
        p.dma("sy", xres4[3:4, :], varr[0:1, :], sem="st")
        p.op("sy", None, waits=[("st", p.now("st"))])
        return
    for a in range(0, T, 512):
        w = 512
        p.op("pe", lambda o, a=a:
             o.matmul(psC[:, :], onesrow[0:1, :], rrow[0:1, a:a + 512], start=True, stop=True),
             waits=[("st", p.now("st")), ("S_dve", p.now("S_dve"))], incs=[("S_pe", 1)])
        p.op("pe", lambda o, a=a:
             o.matmul(psD_[:, :], onesrow[0:1, :], mrow[0:1, a:a + 512], start=True, stop=True),
             incs=[("S_pe", 1)])
        bd = p.now("S_pe")
        for c in range(DC):
            p.op("dve", lambda o, c=c, a=a:
                 o.tensor_tensor(tmp[:, :], xs[:, c * T + a: c * T + a + 512], psC[:, :], ALU.mult),
                 waits=[("S_pe", bd), ("st", p.now("st"))], incs=[("S_dve", 1)])
            p.op("dve", lambda o:
                 o.tensor_tensor(xnf[:, :], tmp[:, :], psD_[:, :], ALU.add), incs=[("S_dve", 1)])
            p.op("dve", lambda o:
                 o.tensor_copy(xnb[:, :], xnf[:, :]), incs=[("S_dve", 1)])
            p.op("dve", lambda o, c=c, a=a:
                 o.tensor_tensor(tmp[:, :], xs[:, c * T + a: c * T + a + 512], xnf[:, :], ALU.add),
                 incs=[("S_dve", 1)])
            p.dma("sy", xn1T[c * P:(c + 1) * P, a:a + 512], xnb[:, :], sem="st",
                  waits=[("S_dve", p.now("S_dve"))])
            p.dma("sy", xres4[c * P:(c + 1) * P, a:a + 512], tmp[:, :], sem="st")
    p.op("sy", None, waits=[("st", p.now("st"))])



def build_launch_E(stage=99):
    nc = bass.Bass("TRN2", target_bir_lowering=False)
    xres3 = nc.dram_tensor("xres3", [D, T], F32, kind="ExternalInput").ap()
    xn1T = nc.dram_tensor("xn1T", [D, T], BF16, kind="ExternalOutput").ap()
    xres4 = nc.dram_tensor("xres4", [D, T], F32, kind="ExternalOutput").ap()
    io = {}
    io["xres3"] = xres3
    io["xn1T"] = xn1T
    io["xres4"] = xres4
    with ExitStack() as ctx:
        p = Prog(nc, ctx)
        emit_E(nc, p, ctx, io, stage)
        p.finalize()
    return nc


def emit_F(nc, p, ctx, io, stage=99):
    E = ctx.enter_context
    xn1T = io["xn1T"]
    xres4 = io["xres4"]
    projT = io["projT"]
    fc1T = io["fc1T"]
    fc2T = io["fc2T"]
    vecs8 = io["vecs8"]
    vecs32 = io["vecs32"]
    identf = io["identf"]
    xres5 = io["xres5"]
    xr4b = io["xr4b"]
    HID = 4 * D
    HC = 32
    fc1s = E(nc.sbuf_tensor("fc1sF", [P, DC * HID], BF16))
    n2T = E(nc.sbuf_tensor("n2TF", [P, DC * T], BF16))
    hTt = E(nc.sbuf_tensor("hTtF", [P, HC * 512], BF16))
    hnTt = hTt
    xn1s = E(nc.sbuf_tensor("xn1sF", [P, DC * T], BF16))
    x1Ts = fc1s[:, 0:DC * T]
    projTs = fc1s[:, DC * T:DC * T + DC * D]
    fc2r = [E(nc.sbuf_tensor(f"fc2rF{i}", [P, D], BF16)) for i in range(2)]
    vc = E(nc.sbuf_tensor("vcF", [P, 64], F32))
    hc = E(nc.sbuf_tensor("hcF", [P, 96], F32))
    v8 = E(nc.sbuf_tensor("v8F", [64, P], F32))
    v32 = E(nc.sbuf_tensor("v32F", [96, P], F32))
    vcb = E(nc.sbuf_tensor("vcbF", [P, 64], BF16))
    hcb = E(nc.sbuf_tensor("hcbF", [P, 96], BF16))
    bias1 = E(nc.sbuf_tensor("bias1F", [P, 8], F32))
    bias2 = E(nc.sbuf_tensor("bias2F", [P, 32], F32))
    bias3 = E(nc.sbuf_tensor("bias3F", [P, 8], F32))
    ident = E(nc.sbuf_tensor("identF", [P, P], F32))
    onescol = E(nc.sbuf_tensor("onescolF", [P, 1], F32))
    onescolb = E(nc.sbuf_tensor("onescolbF", [P, 1], BF16))
    onesrow = E(nc.sbuf_tensor("onesrowF", [1, P], F32))
    sq = E(nc.sbuf_tensor("sqF", [P, 1024], F32))
    rbase = DC * T + DC * D
    s1r = fc1s[0:1, rbase:rbase + 4096].bitcast(F32)
    s2r = fc1s[0:1, rbase + 4096:rbase + 8192].bitcast(F32)
    varr = E(nc.sbuf_tensor("varrF", [1, T], F32))
    rrow = E(nc.sbuf_tensor("rrowF", [1, T], F32))
    mrow = E(nc.sbuf_tensor("mrowF", [1, T], F32))
    srow = E(nc.sbuf_tensor("srowF", [1, 512], F32))
    srow2 = E(nc.sbuf_tensor("srow2F", [1, 512], F32))
    epsc = E(nc.sbuf_tensor("epscF", [1, 1], F32))
    a1s = E(nc.sbuf_tensor("a1sF", [P, 512], F32))
    xrs = [E(nc.sbuf_tensor(f"xrsF{i}", [P, 512], F32)) for i in range(2)]
    tmp = E(nc.sbuf_tensor("tmpF", [P, 512], F32))
    ps8 = [E(nc.psum_tensor(f"ps8F{i}", [P, 512], F32)) for i in range(8)]

    p.dma("sy", ident[:, :], identf)
    p.dma("sy", v8[:, :], vecs8)
    p.dma("sy", v32[:, :], vecs32)
    p.op("pool", lambda o: o.memset(onescol[:, :], 1.0))
    p.op("pool", lambda o: o.memset(onescolb[:, :], 1.0))
    p.op("pool", lambda o: o.memset(onesrow[:, :], 1.0))
    p.op("pool", lambda o: o.memset(epsc[:, :], 1e-5), incs=[("S_pool", 1)])
    for c in range(DC):
        p.dma("pool", xn1s[:, c * T:(c + 1) * T], xn1T[c * P:(c + 1) * P, :])
    p.dma("pool", projTs[:, :].rearrange("p (k o) -> p k o", o=D),
          projT.rearrange("(k p) o -> p k o", p=P))
    ld0 = p.now("ld")
    pld0 = p.now("pld")
    p.op("pe", lambda o: o.transpose(ps8[0][:, 0:64], v8[:, :], ident[0:64, 0:64]),
         waits=[("ld", ld0)], incs=[("S_pe", 1)])
    p.op("act", lambda o: o.copy(vc[:, :], ps8[0][:, 0:64]),
         waits=[("S_pe", p.now("S_pe"))], incs=[("S_act", 1)])
    p.op("pe", lambda o: o.transpose(ps8[0][:, 64:64 + 96], v32[:, :], ident[0:96, 0:96]),
         waits=[("S_act", p.now("S_act"))], incs=[("S_pe", 1)])
    p.op("act", lambda o: o.copy(hc[:, :], ps8[0][:, 64:64 + 96]),
         waits=[("S_pe", p.now("S_pe"))], incs=[("S_act", 1)])
    p.op("dve", lambda o: o.tensor_copy(vcb[:, :], vc[:, :]),
         waits=[("S_act", p.now("S_act"))], incs=[("S_dve", 1)])
    p.op("dve", lambda o: o.tensor_copy(hcb[:, :], hc[:, :]), incs=[("S_dve", 1)])
    p.barrier()

    def ln_stats_bf(src, colw, with_mean=True):
        evacA = [0]
        evacB = [0]
        for a in range(0, T, 512):
            w = 512
            mmB_done = []
            for c in range(DC):
                sqh = (c % 2) * 512
                if with_mean:
                    p.op("pe", lambda o, c=c, a=a, w=w, src=src:
                         o.matmul(ps8[6][0:1, 0:w], onescolb[:, :], src[:, c * T + a: c * T + a + w],
                                  start=(c == 0), stop=(c == DC - 1)),
                         waits=[("S_act", evacA[0] if c == 0 else 0)], incs=[("S_pe", 1)])
                p.op("act", lambda o, c=c, a=a, w=w, sqh=sqh, src=src:
                     o.activation(sq[:, sqh:sqh + w], src[:, c * T + a: c * T + a + w], AF.Square),
                     waits=[("S_pe", mmB_done[c - 2] if c >= 2 else 0)], incs=[("S_act", 1)])
                sd = p.now("S_act")
                p.op("pe", lambda o, c=c, a=a, w=w, sqh=sqh:
                     o.matmul(ps8[7][0:1, 0:w], onescol[:, :], sq[:, sqh:sqh + w],
                              start=(c == 0), stop=(c == DC - 1)),
                     waits=[("S_act", sd), ("S_act", evacB[0] if c == 0 else 0)],
                     incs=[("S_pe", 1)])
                mmB_done.append(p.now("S_pe"))
            if with_mean:
                p.op("act", lambda o, a=a, w=w: o.copy(s1r[:, a:a + w], ps8[6][0:1, 0:w]),
                     waits=[("S_pe", p.now("S_pe"))], incs=[("S_act", 1)])
                evacA[0] = p.now("S_act")
            p.op("act", lambda o, a=a, w=w: o.copy(s2r[:, a:a + w], ps8[7][0:1, 0:w]),
                 waits=[("S_pe", p.now("S_pe"))], incs=[("S_act", 1)])
            evacB[0] = p.now("S_act")
        p.barrier()
        p.op("dve", lambda o: o.tensor_scalar(mrow[:, :], s1r[:, :], 1.0 / D, 0.0, ALU.mult, ALU.add))
        p.op("dve", lambda o: o.tensor_scalar(varr[:, :], s2r[:, :], 1.0 / D, 0.0, ALU.mult, ALU.add),
             incs=[("S_dve", 1)])
        p.op("act", lambda o: o.activation(s2r[:, :], mrow[:, :], AF.Square),
             waits=[("S_dve", p.now("S_dve"))], incs=[("S_act", 1)])
        p.op("dve", lambda o: o.tensor_tensor(s1r[:, :], varr[:, :], s2r[:, :], ALU.subtract),
             waits=[("S_act", p.now("S_act"))], incs=[("S_dve", 1)])
        p.op("act", lambda o: o.activation(varr[:, :], s1r[:, :], AF.Sqrt, bias=epsc[0:1, 0:1], scale=1.0),
             waits=[("S_dve", p.now("S_dve")), ("S_pool", p.now("S_pool"))], incs=[("S_act", 1)])
        p.op("dve", lambda o: o.reciprocal(rrow[:, :], varr[:, :]),
             waits=[("S_act", p.now("S_act"))], incs=[("S_dve", 1)])
        p.op("dve", lambda o: o.scalar_tensor_tensor(s2r[:, :], mrow[:, :], -1.0, rrow[:, :],
                                                     ALU.mult, ALU.mult), incs=[("S_dve", 1)])
        p.op("dve", lambda o: o.tensor_copy(mrow[:, :], s2r[:, :]), incs=[("S_dve", 1)])
        p.barrier()

    def ln_apply_bf(src, dst):
        for a in range(0, T, 512):
            p.op("pe", lambda o, a=a:
                 o.matmul(ps8[6][:, :], onesrow[0:1, :], rrow[0:1, a:a + 512], start=True, stop=True),
                 waits=[("S_dve", p.now("S_dve"))], incs=[("S_pe", 1)])
            p.op("pe", lambda o, a=a:
                 o.matmul(ps8[7][:, :], onesrow[0:1, :], mrow[0:1, a:a + 512], start=True, stop=True),
                 incs=[("S_pe", 1)])
            bd = p.now("S_pe")
            for c in range(DC):
                p.op("dve", lambda o, c=c, a=a, src=src:
                     o.tensor_tensor(tmp[:, :], src[:, c * T + a: c * T + a + 512], ps8[6][:, :], ALU.mult),
                     waits=[("S_pe", bd)], incs=[("S_dve", 1)])
                p.op("dve", lambda o, c=c, a=a, dst=dst:
                     o.tensor_tensor(dst[:, c * T + a: c * T + a + 512], tmp[:, :], ps8[7][:, :], ALU.add),
                     incs=[("S_dve", 1)])

    ln_stats_bf(xn1s, None)
    n1T = hTt[:, 0:DC * T // 2]
    n1T = hTt
    ln_apply_bf(xn1s, n1T)
    p.barrier()

    for ob in range(DC):
        for k in range(DC):
            p.op("pe", lambda o, ob=ob, k=k:
                 o.matmul(ps8[0][0:P, ob:ob + 1], projTs[:, k * D + ob * P:k * D + (ob + 1) * P],
                          vcb[:, 8 + k:8 + k + 1], start=(k == 0), stop=(k == DC - 1)),
                 waits=([("pld", pld0), ("S_act", p.now("S_act"))] if ob == 0 and k == 0 else []),
                 incs=([("S_pe", 1)] if k == DC - 1 else []))
    p.op("act", lambda o: o.copy(bias1[:, :], ps8[0][:, 0:8]),
         waits=[("S_pe", p.now("S_pe"))], incs=[("S_act", 1)])
    p.op("dve", lambda o: o.tensor_tensor(bias1[:, :], bias1[:, :], vc[:, 48:56], ALU.add),
         waits=[("S_act", p.now("S_act"))], incs=[("S_dve", 1)])
    for k in range(DC):
        p.op("dve", lambda o, k=k:
             o.tensor_scalar(projTs[:, k * D:(k + 1) * D], projTs[:, k * D:(k + 1) * D],
                             vc[:, k:k + 1], 0.0, ALU.mult, ALU.add), incs=[("S_dve", 1)])
    p.barrier()
    for oc in range(DC):
        for a in range(0, T, 512):
            bank = ps8[(a // 512) % 2]
            xb = xrs[(a // 512) % 2]
            p.dma("sy", xb[:, :], xres4[oc * P:(oc + 1) * P, a:a + 512],
                  waits=[("st", p.now("st"))])
            ldx = p.now("ld")
            for k in range(DC):
                p.op("pe", lambda o, oc=oc, a=a, k=k, bank=bank:
                     o.matmul(bank[:, :], projTs[:, k * D + oc * P:k * D + (oc + 1) * P],
                              n1T[:, k * T + a:k * T + a + 512], start=(k == 0), stop=(k == DC - 1)),
                     waits=([("S_dve", p.now("S_dve")), ("S_act", p.now("S_act"))] if k == 0 else []),
                     incs=([("S_pe", 1)] if k == DC - 1 else []))
            mmd = p.now("S_pe")
            p.op("act", lambda o, oc=oc, bank=bank:
                 o.activation(a1s[:, :], bank[:, :], AF.Identity, bias=bias1[:, oc:oc + 1], scale=1.0),
                 waits=[("S_pe", mmd)], incs=[("S_act", 1)])
            p.op("dve", lambda o, oc=oc, a=a:
                 o.scalar_tensor_tensor(x1Ts[:, oc * T + a:oc * T + a + 512], a1s[:, :],
                                        vc[:, 32 + oc:32 + oc + 1], xn1s[:, oc * T + a:oc * T + a + 512],
                                        ALU.mult, ALU.add),
                 waits=[("S_act", p.now("S_act"))], incs=[("S_dve", 1)])
            p.op("dve", lambda o, oc=oc, xb=xb:
                 o.scalar_tensor_tensor(xb[:, :], a1s[:, :], vc[:, 32 + oc:32 + oc + 1], xb[:, :],
                                        ALU.mult, ALU.add),
                 waits=[("ld", ldx)], incs=[("S_dve", 1)])
            p.dma("sy", xr4b.ap()[oc * P:(oc + 1) * P, a:a + 512], xb[:, :], sem="st",
                  waits=[("S_dve", p.now("S_dve"))])
    p.barrier()
    if stage == 1:
        for c in range(DC):
            p.dma("pool", xres5[c * P:(c + 1) * P, :], x1Ts[:, c * T:(c + 1) * T], sem="st")
        p.op("sy", None, waits=[("st", p.now("st"))])
        return

    ln_stats_bf(x1Ts, None)
    ln_apply_bf(x1Ts, n2T)
    p.barrier()

    for k in range(DC):
        p.dma("pool", fc1s[:, k * HID:(k + 1) * HID], fc1T[k * P:(k + 1) * P, :])
    pldf = p.now("pld")
    p.barrier()
    if stage == 2:
        for c in range(DC):
            p.dma("pool", xres5[c * P:(c + 1) * P, :], n2T[:, c * T:(c + 1) * T], sem="st")
        p.op("sy", None, waits=[("st", p.now("st"))])
        return

    for hb in range(HC):
        for k in range(DC):
            p.op("pe", lambda o, hb=hb, k=k:
                 o.matmul(ps8[5][0:P, hb:hb + 1], fc1s[:, k * HID + hb * P:k * HID + (hb + 1) * P],
                          vcb[:, 24 + k:24 + k + 1], start=(k == 0), stop=(k == DC - 1)),
                 waits=([("pld", pldf)] if hb == 0 and k == 0 else []),
                 incs=([("S_pe", 1)] if k == DC - 1 else []))
    p.op("act", lambda o: o.copy(bias2[:, :], ps8[5][:, 0:32]),
         waits=[("S_pe", p.now("S_pe"))], incs=[("S_act", 1)])
    p.op("dve", lambda o: o.tensor_tensor(bias2[:, :], bias2[:, :], hc[:, 64:96], ALU.add),
         waits=[("S_act", p.now("S_act"))], incs=[("S_dve", 1)])
    for k in range(DC):
        p.op("dve", lambda o, k=k:
             o.tensor_scalar(fc1s[:, k * HID:(k + 1) * HID], fc1s[:, k * HID:(k + 1) * HID],
                             vc[:, 16 + k:16 + k + 1], 0.0, ALU.mult, ALU.add),
             waits=[("S_pe", p.now("S_pe"))], incs=[("S_dve", 1)])
    p.barrier()

    for k in range(HC):
        wr = fc2r[k % 2]
        p.dma("pool", wr[:, :], fc2T[k * P:(k + 1) * P, :],
              waits=[("S_pe", p.now("S_pe"))] if k >= 2 else [])
        ldk = p.now("pld")
        for ob in range(DC):
            p.op("pe", lambda o, k=k, ob=ob, wr=wr:
                 o.matmul(ps8[6][0:P, ob:ob + 1], wr[:, ob * P:(ob + 1) * P],
                          hcb[:, 32 + k:32 + k + 1], start=(k == 0), stop=(k == HC - 1)),
                 waits=([("pld", ldk)] if ob == 0 else []),
                 incs=([("S_pe", 1)] if ob == DC - 1 else []))
    p.op("pe", lambda o: o.drain(), incs=[("S_pe", 1)], settle=False)
    p.op("act", lambda o: o.copy(bias3[:, :], ps8[6][:, 0:8]),
         waits=[("S_pe", p.now("S_pe"))], incs=[("S_act", 1)])
    p.op("dve", lambda o: o.tensor_tensor(bias3[:, :], bias3[:, :], vc[:, 56:64], ALU.add),
         waits=[("S_act", p.now("S_act"))], incs=[("S_dve", 1)])
    p.barrier()

    for tg in range(4):
        a = tg * 512
        for hb in range(HC):
            bank = ps8[hb % 2]
            for k in range(DC):
                p.op("pe", lambda o, hb=hb, k=k, a=a, bank=bank:
                     o.matmul(bank[:, :], fc1s[:, k * HID + hb * P:k * HID + (hb + 1) * P],
                              n2T[:, k * T + a:k * T + a + 512], start=(k == 0), stop=(k == DC - 1)),
                     waits=([("S_act", p.now("S_act")), ("S_dve", p.now("S_dve"))] if k == 0 else []),
                     incs=([("S_pe", 1)] if k == DC - 1 else []))
            mmd = p.now("S_pe")
            p.op("act", lambda o, hb=hb, bank=bank:
                 o.activation(hTt[:, hb * 512:(hb + 1) * 512], bank[:, :], AF.Gelu,
                              bias=bias2[:, hb:hb + 1], scale=1.0),
                 waits=[("S_pe", mmd)], incs=[("S_act", 1)])
        p.barrier()
        mmB_done = []
        for k in range(HC):
            sqh = (k % 2) * 512
            p.op("pe", lambda o, k=k:
                 o.matmul(ps8[6][0:1, 0:512], onescolb[:, :], hTt[:, k * 512:(k + 1) * 512],
                          start=(k == 0), stop=(k == HC - 1)), incs=[("S_pe", 1)])
            p.op("act", lambda o, k=k, sqh=sqh:
                 o.activation(sq[:, sqh:sqh + 512], hTt[:, k * 512:(k + 1) * 512], AF.Square),
                 waits=[("S_pe", mmB_done[k - 2] if k >= 2 else 0)], incs=[("S_act", 1)])
            sd = p.now("S_act")
            p.op("pe", lambda o, k=k, sqh=sqh:
                 o.matmul(ps8[7][0:1, 0:512], onescol[:, :], sq[:, sqh:sqh + 512],
                          start=(k == 0), stop=(k == HC - 1)),
                 waits=[("S_act", sd)], incs=[("S_pe", 1)])
            mmB_done.append(p.now("S_pe"))
        p.op("act", lambda o: o.copy(srow[:, :], ps8[6][0:1, 0:512]),
             waits=[("S_pe", p.now("S_pe"))], incs=[("S_act", 1)])
        p.op("act", lambda o: o.copy(srow2[:, :], ps8[7][0:1, 0:512]), incs=[("S_act", 1)])
        p.op("dve", lambda o: o.tensor_scalar(srow[:, :], srow[:, :], 1.0 / HID, 0.0, ALU.mult, ALU.add),
             waits=[("S_act", p.now("S_act"))], incs=[("S_dve", 1)])
        p.op("dve", lambda o: o.tensor_scalar(srow2[:, :], srow2[:, :], 1.0 / HID, 0.0, ALU.mult, ALU.add),
             incs=[("S_dve", 1)])
        p.op("act", lambda o: o.activation(sq[0:1, 0:512], srow[:, :], AF.Square),
             waits=[("S_dve", p.now("S_dve"))], incs=[("S_act", 1)])
        p.op("dve", lambda o: o.tensor_tensor(srow2[:, :], srow2[:, :], sq[0:1, 0:512], ALU.subtract),
             waits=[("S_act", p.now("S_act"))], incs=[("S_dve", 1)])
        p.op("act", lambda o: o.activation(srow2[:, :], srow2[:, :], AF.Sqrt, bias=epsc[0:1, 0:1], scale=1.0),
             incs=[("S_act", 1)])
        p.op("dve", lambda o: o.reciprocal(sq[0:1, 512:1024], srow2[:, :]),
             waits=[("S_act", p.now("S_act"))], incs=[("S_dve", 1)])
        p.op("dve", lambda o: o.scalar_tensor_tensor(srow[:, :], srow[:, :], -1.0, sq[0:1, 512:1024],
                                                     ALU.mult, ALU.mult), incs=[("S_dve", 1)])
        p.barrier()
        p.op("pe", lambda o: o.matmul(ps8[6][:, :], onesrow[0:1, :], sq[0:1, 512:1024],
                                      start=True, stop=True), incs=[("S_pe", 1)])
        p.op("pe", lambda o: o.matmul(ps8[7][:, :], onesrow[0:1, :], srow[0:1, :],
                                      start=True, stop=True), incs=[("S_pe", 1)])
        bd = p.now("S_pe")
        for k in range(HC):
            p.op("dve", lambda o, k=k:
                 o.tensor_tensor(tmp[:, :], hTt[:, k * 512:(k + 1) * 512], ps8[6][:, :], ALU.mult),
                 waits=[("S_pe", bd)], incs=[("S_dve", 1)])
            p.op("dve", lambda o, k=k:
                 o.tensor_tensor(hTt[:, k * 512:(k + 1) * 512], tmp[:, :], ps8[7][:, :], ALU.add),
                 incs=[("S_dve", 1)])
            p.op("dve", lambda o, k=k:
                 o.tensor_scalar(hTt[:, k * 512:(k + 1) * 512], hTt[:, k * 512:(k + 1) * 512],
                                 hc[:, k:k + 1], 0.0, ALU.mult, ALU.add), incs=[("S_dve", 1)])
        p.barrier()
        hd_ = p.now("S_dve")
        for k in range(HC):
            wr = fc2r[k % 2]
            p.dma("pool", wr[:, :], fc2T[k * P:(k + 1) * P, :],
                  waits=[("S_pe", p.now("S_pe"))] if k >= 2 else [("S_dve", hd_)])
            ldk = p.now("pld")
            for ob in range(DC):
                p.op("pe", lambda o, k=k, ob=ob, wr=wr:
                     o.matmul(ps8[ob][:, :], wr[:, ob * P:(ob + 1) * P], hTt[:, k * 512:(k + 1) * 512],
                              start=(k == 0), stop=(k == HC - 1)),
                     waits=([("pld", ldk), ("S_dve", hd_)] if ob == 0 else []),
                     incs=([("S_pe", 1)] if ob == DC - 1 else []))
        mmd = p.now("S_pe")
        for ob in range(DC):
            xb = xrs[ob % 2]
            p.dma("sy", xb[:, :], xr4b.ap()[ob * P:(ob + 1) * P, a:a + 512],
                  waits=[("st", p.now("st"))])
            ldx = p.now("ld")
            p.op("act", lambda o, ob=ob:
                 o.activation(a1s[:, :], ps8[ob][:, :], AF.Identity, bias=bias3[:, ob:ob + 1], scale=1.0),
                 waits=[("S_pe", mmd)], incs=[("S_act", 1)])
            p.op("dve", lambda o, ob=ob, xb=xb:
                 o.scalar_tensor_tensor(xb[:, :], a1s[:, :], vc[:, 40 + ob:40 + ob + 1], xb[:, :],
                                        ALU.mult, ALU.add),
                 waits=[("ld", ldx), ("S_act", p.now("S_act"))], incs=[("S_dve", 1)])
            p.dma("sy", xres5[ob * P:(ob + 1) * P, a:a + 512], xb[:, :], sem="st",
                  waits=[("S_dve", p.now("S_dve"))])
        p.barrier()
    p.op("sy", None, waits=[("st", p.now("st"))])



def build_launch_F(stage=99):
    nc = bass.Bass("TRN2", target_bir_lowering=False)
    xn1T = nc.dram_tensor("xn1T", [D, T], BF16, kind="ExternalInput").ap()
    xres4 = nc.dram_tensor("xres4", [D, T], F32, kind="ExternalInput").ap()
    projT = nc.dram_tensor("projT", [D, D], F32, kind="ExternalInput").ap()
    fc1T = nc.dram_tensor("fc1T", [D, 4 * D], F32, kind="ExternalInput").ap()
    fc2T = nc.dram_tensor("fc2T", [4 * D, D], F32, kind="ExternalInput").ap()
    vecs8 = nc.dram_tensor("vecs8", [64, P], F32, kind="ExternalInput").ap()
    vecs32 = nc.dram_tensor("vecs32", [96, P], F32, kind="ExternalInput").ap()
    identf = nc.dram_tensor("identf", [P, P], F32, kind="ExternalInput").ap()
    xres5 = nc.dram_tensor("xres5", [D, T], F32, kind="ExternalOutput").ap()
    xr4b = nc.dram_tensor("xr4bF", [D, T], F32)

    HID = 4 * D
    HC = 32
    io = {}
    io["xn1T"] = xn1T
    io["xres4"] = xres4
    io["projT"] = projT
    io["fc1T"] = fc1T
    io["fc2T"] = fc2T
    io["vecs8"] = vecs8
    io["vecs32"] = vecs32
    io["identf"] = identf
    io["xres5"] = xres5
    io["xr4b"] = xr4b
    with ExitStack() as ctx:
        p = Prog(nc, ctx)
        emit_F(nc, p, ctx, io, stage)
        p.finalize()
    return nc


def emit_G(nc, p, ctx, io, stage=99):
    E = ctx.enter_context
    xn1T = io["xn1T"]
    xres5 = io["xres5"]
    Wg = io["Wg"]
    bg = io["bg1"]
    WeT = io["WeT"]
    be = io["be"]
    identf = io["identf"]
    yout = io["yout"]
    wesc = io["wesc"]
    xn1s = E(nc.sbuf_tensor("xn1sG", [P, DC * T], BF16))
    macc = E(nc.sbuf_tensor("maccG", [P, DC * T], F32))
    xnw = E(nc.sbuf_tensor("xnwG", [P, DC * T], BF16))
    wes = E(nc.sbuf_tensor("wesG", [P, DC * D], BF16))
    ident = E(nc.sbuf_tensor("identG", [P, P], F32))
    onesrow = E(nc.sbuf_tensor("onesrowG", [1, P], F32))
    gw = E(nc.sbuf_tensor("gwG", [P, DC * 8], BF16))
    gbr = E(nc.sbuf_tensor("gbrG", [1, 8], F32))
    bes = E(nc.sbuf_tensor("besG", [8, D], F32))
    wefm = E(nc.sbuf_tensor("wefmG", [8, T], F32))
    werow = E(nc.sbuf_tensor("werowG", [1, T], F32))
    lg = E(nc.sbuf_tensor("lgG", [P, 8], F32))
    pex = E(nc.sbuf_tensor("pexG", [P, 8], F32))
    m1 = E(nc.sbuf_tensor("m1G", [P, 1], F32))
    t1g = E(nc.sbuf_tensor("t1gG", [P, 8], F32))
    wet = E(nc.sbuf_tensor("wetG", [P, 8], F32))
    xg = [E(nc.sbuf_tensor(f"xgG{i}", [P, DC * P], BF16)) for i in range(2)]
    outst = [E(nc.sbuf_tensor(f"outstG{i}", [P, D], F32)) for i in range(2)]
    ps8 = [E(nc.psum_tensor(f"ps8G{i}", [P, 512], F32)) for i in range(8)]

    p.dma("sy", ident[:, :], identf)
    p.op("pool", lambda o: o.memset(onesrow[:, :], 1.0), incs=[("S_pool", 1)])
    p.dma("pool", gw[:, :].rearrange("p (k e) -> p k e", e=8), Wg.rearrange("(k p) e -> p k e", p=P))
    p.dma("sy", gbr[:, :], bg)
    p.dma("sy", bes[:, :], be)
    for c in range(DC):
        p.dma("pool", xn1s[:, c * T:(c + 1) * T], xn1T[c * P:(c + 1) * P, :])
        p.dma("sy", macc[:, c * T:(c + 1) * T], xres5[c * P:(c + 1) * P, :])
    ldg = p.now("ld")
    pldg = p.now("pld")
    for tt in range(NTC):
        t0 = tt * P
        for k in range(DC):
            p.op("pe", lambda o, k=k, t0=t0:
                 o.matmul(ps8[1][0:P, 0:8], xn1s[:, k * T + t0:k * T + t0 + P],
                          gw[:, k * 8:(k + 1) * 8], start=(k == 0), stop=False),
                 waits=([("pld", pldg), ("ld", ldg), ("S_act", p.now("S_act"))] if k == 0 else []))
        p.op("pe", lambda o:
             o.matmul(ps8[1][0:P, 0:8], onesrow[0:1, 0:P], gbr[0:1, :], start=False, stop=True),
             incs=[("S_pe", 1)])
        p.op("act", lambda o: o.copy(lg[:, :], ps8[1][0:P, 0:8]),
             waits=[("S_pe", p.now("S_pe"))], incs=[("S_act", 1)])
        p.op("dve", lambda o: o.reduce_max(m1[:, :], lg[:, :], AX.X),
             waits=[("S_act", p.now("S_act"))], incs=[("S_dve", 1)])
        p.op("dve", lambda o: o.tensor_scalar(m1[:, :], m1[:, :], -1.0, 0.0, ALU.mult, ALU.add),
             incs=[("S_dve", 1)])
        p.op("act", lambda o: o.activation(pex[:, :], lg[:, :], AF.Exp, bias=m1[:, 0:1], scale=1.0),
             waits=[("S_dve", p.now("S_dve"))], incs=[("S_act", 1)])
        p.op("dve", lambda o: o.reduce_max(m1[:, :], pex[:, :], AX.X),
             waits=[("S_act", p.now("S_act"))], incs=[("S_dve", 1)])
        p.op("dve", lambda o: o.tensor_scalar(t1g[:, :], pex[:, :], m1[:, 0:1], 0.0, ALU.is_ge, ALU.add),
             incs=[("S_dve", 1)])
        p.op("dve", lambda o: o.tensor_scalar(t1g[:, :], t1g[:, :], -1.0, 1.0, ALU.mult, ALU.add),
             incs=[("S_dve", 1)])
        p.op("dve", lambda o: o.tensor_tensor(t1g[:, :], pex[:, :], t1g[:, :], ALU.mult),
             incs=[("S_dve", 1)])
        p.op("dve", lambda o: o.reduce_max(m1[:, :], t1g[:, :], AX.X), incs=[("S_dve", 1)])
        p.op("dve", lambda o: o.tensor_scalar(t1g[:, :], pex[:, :], m1[:, 0:1], 0.0, ALU.is_ge, ALU.add),
             incs=[("S_dve", 1)])
        p.op("dve", lambda o: o.tensor_tensor(wet[:, :], pex[:, :], t1g[:, :], ALU.mult),
             incs=[("S_dve", 1)])
        p.op("dve", lambda o: o.reduce_sum(m1[:, :], wet[:, :], AX.X), incs=[("S_dve", 1)])
        p.op("dve", lambda o: o.reciprocal(m1[:, :], m1[:, :]), incs=[("S_dve", 1)])
        p.op("dve", lambda o: o.tensor_scalar(wet[:, :], wet[:, :], m1[:, 0:1], 0.0, ALU.mult, ALU.add),
             incs=[("S_dve", 1)])
        p.op("pe", lambda o: o.transpose(ps8[2][0:8, 0:P], wet[:, :], ident[:, :]),
             waits=[("S_dve", p.now("S_dve"))], incs=[("S_pe", 1)])
        p.op("act", lambda o, t0=t0: o.copy(wefm[:, t0:t0 + P], ps8[2][0:8, 0:P]),
             waits=[("S_pe", p.now("S_pe"))], incs=[("S_act", 1)])
    p.barrier()
    p.dma("sy", wesc.ap(), wefm[:, :], sem="st")
    p.op("sy", None, waits=[("st", p.now("st"))])
    for ob in range(DC):
        for a in range(0, T, 512):
            bank = ps8[(a // 512) % 2]
            p.op("pe", lambda o, ob=ob, a=a, bank=bank:
                 o.matmul(bank[:, :], bes[:, ob * P:(ob + 1) * P], wefm[0:8, a:a + 512],
                          start=True, stop=True),
                 waits=[("S_dve", p.now("S_dve"))], incs=[("S_pe", 1)])
            p.op("dve", lambda o, ob=ob, a=a, bank=bank:
                 o.tensor_tensor(macc[:, ob * T + a:ob * T + a + 512],
                                 macc[:, ob * T + a:ob * T + a + 512], bank[:, :], ALU.add),
                 waits=[("S_pe", p.now("S_pe")), ("ld", p.now("ld"))], incs=[("S_dve", 1)])
    p.barrier()
    if stage == 1:
        for c in range(DC):
            p.dma("sy", yout.rearrange("t o -> o t")[c * P:(c + 1) * P, :],
                  macc[:, c * T:(c + 1) * T], sem="st")
        p.op("sy", None, waits=[("st", p.now("st"))])
        return

    for e in range(8):
        p.dma("sy", werow[:, :], wesc.ap()[e:e + 1, :], waits=[("S_pe", p.now("S_pe"))])
        p.dma("pool", wes[:, :].rearrange("p (k o) -> p k o", o=D),
              WeT.rearrange("e (k p) o -> e p k o", p=P)[e],
              waits=[("S_pe", p.now("S_pe"))])
        lde = p.now("pld")
        ldw = p.now("ld")
        for a in range(0, T, 512):
            p.op("pe", lambda o, a=a:
                 o.matmul(ps8[7][:, :], onesrow[0:1, :], werow[0:1, a:a + 512], start=True, stop=True),
                 waits=[("ld", ldw), ("S_dve", p.now("S_dve"))], incs=[("S_pe", 1)])
            wd = p.now("S_pe")
            for k in range(DC):
                p.op("dve", lambda o, k=k, a=a:
                     o.tensor_tensor(xnw[:, k * T + a:k * T + a + 512],
                                     xn1s[:, k * T + a:k * T + a + 512], ps8[7][:, :], ALU.mult),
                     waits=[("S_pe", wd)], incs=[("S_dve", 1)])
        xd = p.now("S_dve")
        for oc in range(DC):
            for a in range(0, T, 512):
                bank = ps8[(oc * 4 + a // 512) % 2]
                for k in range(DC):
                    p.op("pe", lambda o, oc=oc, a=a, k=k, bank=bank:
                         o.matmul(bank[:, :], wes[:, k * D + oc * P:k * D + (oc + 1) * P],
                                  xnw[:, k * T + a:k * T + a + 512], start=(k == 0), stop=(k == DC - 1)),
                         waits=([("pld", lde), ("S_dve", p.now("S_dve"))] if k == 0 else []),
                         incs=([("S_pe", 1)] if k == DC - 1 else []))
                mmd = p.now("S_pe")
                p.op("dve", lambda o, oc=oc, a=a, bank=bank:
                     o.tensor_tensor(macc[:, oc * T + a:oc * T + a + 512],
                                     macc[:, oc * T + a:oc * T + a + 512], bank[:, :], ALU.add),
                     waits=[("S_pe", mmd)], incs=[("S_dve", 1)])
        p.barrier()
    for tt in range(NTC):
        t0 = tt * P
        ob_ = outst[tt % 2]
        for oc in range(DC):
            bank = ps8[oc % 2]
            p.op("pe", lambda o, oc=oc, t0=t0, bank=bank:
                 o.transpose(bank[:, 0:P], macc[:, oc * T + t0:oc * T + t0 + P], ident[:, :]),
                 waits=[("S_act", p.now("S_act")), ("st", p.now("st"))], incs=[("S_pe", 1)])
            p.op("act", lambda o, oc=oc, bank=bank, ob_=ob_:
                 o.copy(ob_[:, oc * P:(oc + 1) * P], bank[:, 0:P]),
                 waits=[("S_pe", p.now("S_pe"))], incs=[("S_act", 1)])
        p.dma("sy", yout[t0:t0 + P, :], ob_[:, :], sem="st",
              waits=[("S_act", p.now("S_act"))])
    p.op("sy", None, waits=[("st", p.now("st"))])



def build_launch_G(stage=99):
    nc = bass.Bass("TRN2", target_bir_lowering=False)
    xn1T = nc.dram_tensor("xn1T", [D, T], BF16, kind="ExternalInput").ap()
    xres5 = nc.dram_tensor("xres5", [D, T], F32, kind="ExternalInput").ap()
    Wg = nc.dram_tensor("Wg", [D, 8], F32, kind="ExternalInput").ap()
    bg = nc.dram_tensor("bg", [1, 8], F32, kind="ExternalInput").ap()
    WeT = nc.dram_tensor("WeT", [8, D, D], F32, kind="ExternalInput").ap()
    be = nc.dram_tensor("be", [8, D], F32, kind="ExternalInput").ap()
    identf = nc.dram_tensor("identf", [P, P], F32, kind="ExternalInput").ap()
    yout = nc.dram_tensor("yout", [T, D], F32, kind="ExternalOutput").ap()
    wesc = nc.dram_tensor("wescG", [8, T], F32)

    io = {}
    io["xn1T"] = xn1T
    io["xres5"] = xres5
    io["Wg"] = Wg
    io["bg1"] = bg
    io["WeT"] = WeT
    io["be"] = be
    io["identf"] = identf
    io["yout"] = yout
    io["wesc"] = wesc
    with ExitStack() as ctx:
        p = Prog(nc, ctx)
        emit_G(nc, p, ctx, io, stage)
        p.finalize()
    return nc




def build_fused():
    nc = bass.Bass("TRN2", target_bir_lowering=False)
    io = {}

    def ein(n, s, dt=F32):
        io[n] = nc.dram_tensor(n, s, dt, kind="ExternalInput").ap()

    ein("x_in", [TH, D]); ein("identf", [P, P]); ein("mask01T", [P, P])
    ein("selT", [NH, P]); ein("sel2", [NH, 16]); ein("cmask", [1, NCORES])
    ein("winT", [D, 4384]); ein("convw", [DI, 4]); ein("convb", [DI])
    ein("alog", [NH]); ein("dtb", [NH]); ein("nw", [D]); ein("woT", [DI, D])
    ein("dvec", [NH])
    ein("gatew", [D, 8]); ein("gateb", [1, 8])
    ein("w1", [8, D, DI]); ein("w2", [8, D, DI]); ein("wg", [8, DI, DI]); ein("w3", [8, DI, D])
    ein("b1", [8, DI]); ein("b2", [8, DI]); ein("bg", [8, DI]); ein("b3", [8, D])
    ein("projT", [D, D]); ein("fc1T", [D, 4 * D]); ein("fc2T", [4 * D, D])
    ein("vecs8", [64, P]); ein("vecs32", [96, P])
    ein("Wg", [D, 8]); ein("bg1", [1, 8]); ein("WeT", [8, D, D]); ein("be", [8, D])
    io["xnT"] = nc.dram_tensor("i_xnT", [D, TH], BF16).ap()
    io["xres"] = nc.dram_tensor("i_xres", [D, T], F32).ap()
    io["szT"] = nc.dram_tensor("i_szT", [DI, T], BF16).ap()
    io["xcvT"] = nc.dram_tensor("i_xcvT", [DI, T], BF16).ap()
    io["BTo"] = nc.dram_tensor("i_BTo", [DS, T], BF16).ap()
    io["CTo"] = nc.dram_tensor("i_CTo", [DS, T], BF16).ap()
    io["cumo"] = nc.dram_tensor("i_cumo", [NH, T], F32).ap()
    io["dtfo"] = nc.dram_tensor("i_dtfo", [NH, T], F32).ap()
    io["hino"] = nc.dram_tensor("i_hino", [DS, NH * HD], F32).ap()
    io["xres2"] = nc.dram_tensor("i_xres2", [D, T], F32).ap()
    io["xres3"] = nc.dram_tensor("i_xres3", [D, T], F32).ap()
    io["xn1T"] = nc.dram_tensor("i_xn1T", [D, T], BF16).ap()
    io["xres4"] = nc.dram_tensor("i_xres4", [D, T], F32).ap()
    io["xres5"] = nc.dram_tensor("i_xres5", [D, T], F32).ap()
    io["xr4b"] = nc.dram_tensor("i_xr4b", [D, T], F32)
    io["wesc"] = nc.dram_tensor("i_wesc", [8, T], F32)
    io["scr1"] = nc.dram_tensor("i_scr1", [NH, P], F32)
    io["scr2"] = nc.dram_tensor("i_scr2", [NH, P], F32)
    io["cc_in_s"] = nc.dram_tensor("cc_in_s", [DS, NH * HD], F32)
    io["cc_out_s"] = nc.dram_tensor("cc_out_s", [NCORES * DS, NH * HD], F32, addr_space="Shared")
    io["cc_in_l"] = nc.dram_tensor("cc_in_l", [NH, 1], F32)
    io["cc_out_l"] = nc.dram_tensor("cc_out_l", [NCORES * NH, 1], F32, addr_space="Shared")
    io["yout"] = nc.dram_tensor("yout", [T, D], F32, kind="ExternalOutput").ap()

    with ExitStack() as ctx:
        p = Prog(nc, ctx)
        for emit in (emit_A, emit_B, emit_C, emit_D, emit_E, emit_F, emit_G):
            with ExitStack() as c2:
                emit(nc, p, c2, io)
            p.barrier()
        p.op("sy", None, waits=[("st", p.now("st")), ("cc", p.now("cc"))])
        p.finalize()
    return nc


def shard_inputs(L0_attn_x):
    x = np.asarray(L0_attn_x).reshape(16384, D)
    shards = []
    for i in range(NCORES):
        lo = i * T
        halo = np.zeros((3, D), np.float32) if i == 0 else x[lo - 3:lo]
        shards.append(np.ascontiguousarray(np.concatenate([halo, x[lo:lo + T]], axis=0)))
    return shards


def host_consts():
    identf = np.eye(P, dtype=np.float32)
    mask01T = np.triu(np.ones((P, P), np.float32))
    selT = np.zeros((NH, P), np.float32)
    for h in range(NH):
        selT[h, (h % 2) * HD:(h % 2 + 1) * HD] = 1.0
    sel2 = np.zeros((NH, 16), np.float32)
    for h in range(NH):
        sel2[h, h // 2] = 1.0
    return dict(identf=identf, mask01T=mask01T, selT=selT, sel2=sel2)


def _f32(a):
    return np.ascontiguousarray(np.asarray(a, dtype=np.float32))


def _pack_rows(inputs, names):
    return np.ascontiguousarray(np.concatenate(
        [_f32(inputs[n]).reshape(-1, P) for n in names], 0))


def kernel(**inputs):
    consts = host_consts()
    identf = consts["identf"]
    shards = shard_inputs(inputs["L0_attn_x"])
    cores = list(range(NCORES))

    winT = np.ascontiguousarray(_f32(inputs["L0_attn_in_proj_w"]).T)
    convw = _f32(inputs["L0_attn_conv_w"]).reshape(DI, 4)
    woT = np.ascontiguousarray(_f32(inputs["L0_attn_out_proj_w"]).T)
    projT = np.ascontiguousarray(_f32(inputs["L1_attn_proj_w"]).T)
    fc1T = np.ascontiguousarray(_f32(inputs["L1_attn_fc1_w"]).T)
    fc2T = np.ascontiguousarray(_f32(inputs["L1_attn_fc2_w"]).T)
    WeT = np.ascontiguousarray(np.transpose(_f32(inputs["L1_ffn_We"]), (0, 2, 1)))
    vecs8 = _pack_rows(inputs, ["L1_attn_ln1_g", "L1_attn_ln1_b", "L1_attn_ln2_g",
                                "L1_attn_ln2_b", "L1_attn_gamma1", "L1_attn_gamma2",
                                "L1_attn_proj_b", "L1_attn_fc2_b"])
    vecs32 = _pack_rows(inputs, ["L1_attn_lnh_g", "L1_attn_lnh_b", "L1_attn_fc1_b"])

    nc = build_fused()
    in_maps = []
    for i in cores:
        cm = np.zeros((1, NCORES), np.float32)
        cm[0, :i] = 1.0
        in_maps.append(dict(
            x_in=shards[i], identf=identf, mask01T=consts["mask01T"],
            selT=consts["selT"], sel2=consts["sel2"], cmask=cm,
            winT=winT, convw=convw, convb=_f32(inputs["L0_attn_conv_b"]),
            alog=_f32(inputs["L0_attn_A_log"]), dtb=_f32(inputs["L0_attn_dt_bias"]),
            nw=_f32(inputs["L0_attn_norm_w"]), woT=woT, dvec=_f32(inputs["L0_attn_D"]),
            gatew=_f32(inputs["L0_ffn_gate_w"]), gateb=_f32(inputs["L0_ffn_gate_b"]).reshape(1, 8),
            w1=_f32(inputs["L0_ffn_w1"]), w2=_f32(inputs["L0_ffn_w2"]),
            wg=_f32(inputs["L0_ffn_wg"]), w3=_f32(inputs["L0_ffn_w3"]),
            b1=_f32(inputs["L0_ffn_b1"]), b2=_f32(inputs["L0_ffn_b2"]),
            bg=_f32(inputs["L0_ffn_bg"]), b3=_f32(inputs["L0_ffn_b3"]),
            projT=projT, fc1T=fc1T, fc2T=fc2T, vecs8=vecs8, vecs32=vecs32,
            Wg=_f32(inputs["L1_ffn_Wg"]), bg1=_f32(inputs["L1_ffn_bg"]).reshape(1, 8),
            WeT=WeT, be=_f32(inputs["L1_ffn_be"])))
    res = run_bass_kernel_spmd(nc, in_maps, core_ids=cores)
    out = np.concatenate([np.asarray(res.results[i]["yout"]) for i in cores], axis=0)
    return out.reshape(1, 16384, D).astype(np.float32)


# version 06
# speedup vs baseline: 1.1145x; 1.1145x over previous
import sys
sys.path.insert(0, '/opt/trn_rl_repo')
from contextlib import ExitStack

import numpy as np
import ml_dtypes
import concourse.bass as bass
import concourse.mybir as mybir
from concourse.bass_utils import run_bass_kernel_spmd

F32 = mybir.dt.float32
BF16 = mybir.dt.bfloat16
AF = mybir.ActivationFunctionType
ALU = mybir.AluOpType
AX = mybir.AxisListType

P = 128
T = 2048
TH = 2051
NCORES = 8
D = 1024
DC = 8
DI = 2048
DIC = 16
NH = 32
HD = 64
DS = 128
NTC = 16
BF = ml_dtypes.bfloat16


class Prog:

    ENGINES = ("pe", "act", "dve", "pool", "sy")

    def __init__(self, nc, ctx):
        self.nc = nc
        self.ctx = ctx
        self.steps = {e: [] for e in self.ENGINES}
        self.sems = {}
        self.cnt = {}
        for s in ("ld", "pld", "st", "S_pe", "S_act", "S_dve", "S_pool", "S_sy", "cc"):
            self.sems[s] = ctx.enter_context(nc.semaphore(s))
            self.cnt[s] = 0

    def now(self, sem):
        return self.cnt[sem]

    def op(self, eng, emit, waits=(), incs=(), settle=None):
        waits = tuple(waits)
        incs = tuple(incs)
        if settle is None:
            if eng in ("act", "dve", "pool"):
                settle = emit is not None
            else:
                settle = any(s.startswith("S_") for s, _ in incs)
        for s, a in incs:
            self.cnt[s] += a
        if settle and emit is not None:
            self.steps[eng].append((emit, waits, ()))
            self.steps[eng].append((lambda o: o.drain(), (), incs))
        else:
            self.steps[eng].append((emit, waits, incs))

    def dma(self, qeng, out, in_, sem=None, waits=()):
        if sem is None:
            sem = "pld" if qeng == "pool" else ("ld" if qeng == "sy" else "ld")
        self.op(qeng, lambda o, out=out, in_=in_: o.dma_start(out=out, in_=in_),
                waits=waits, incs=[(sem, 16)], settle=False)
        return self.cnt[sem]

    def barrier(self):
        marks = {}
        for e in self.ENGINES:
            s = {"pe": "S_pe", "act": "S_act", "dve": "S_dve", "pool": "S_pool", "sy": "S_sy"}[e]
            if e in ("act", "dve", "pool", "pe"):
                self.op(e, lambda o: o.drain(), incs=[(s, 1)], settle=False)
            else:
                self.op(e, lambda o: o.nop(), incs=[(s, 1)], settle=False)
            marks[s] = self.cnt[s]
        dma_marks = [(s, self.cnt[s]) for s in ("ld", "pld", "st")]
        waits = [(s, v) for s, v in marks.items()] + dma_marks
        for e in self.ENGINES:
            self.op(e, None, waits=waits)

    def finalize(self):
        nc = self.nc
        block = self.ctx.enter_context(nc.Block())
        prog = self

        def runner(eng_name):
            def _(obj):
                for emit, waits, incs in prog.steps[eng_name]:
                    for s, v in waits:
                        if v > 0:
                            obj.wait_ge(prog.sems[s], v)
                    if emit is None:
                        if incs:
                            inst = obj.nop()
                            for s, a in incs:
                                inst.then_inc(prog.sems[s], a)
                        continue
                    inst = emit(obj)
                    for s, a in incs:
                        inst.then_inc(prog.sems[s], a)
            return _

        block.tensor(runner("pe"))
        block.scalar(runner("act"))
        block.vector(runner("dve"))
        block.gpsimd(runner("pool"))
        block.sync(runner("sy"))


def ceil_div(a, b):
    return -(-a // b)


def col_groups(total, width=512):
    out = []
    a = 0
    while a < total:
        out.append((a, min(a + width, total)))
        a = out[-1][1]
    return out


def _dump_A(p, xnT, xres, xfm, xnbf):
    for c in range(DC):
        p.dma("sy", xres[c * P:(c + 1) * P, 0:T], xfm[:, c * TH + 3: c * TH + 3 + T], sem="st")
    p.op("sy", None, waits=[("st", p.now("st"))])


def _dump_rows_A(p, xres, r1, r2):
    p.dma("sy", xres[0:1, 0:T], r1[0:1, 3:3 + T], sem="st")
    p.dma("sy", xres[1:2, 0:T], r2[0:1, 3:3 + T], sem="st")
    p.op("sy", None, waits=[("st", p.now("st"))])


def emit_A(nc, p, ctx, io, stage=99):
    E = ctx.enter_context
    x_in = io["x_in"]
    identf = io["identf"]
    xnT = io["xnT"]
    xres = io["xres"]
    xfm = E(nc.sbuf_tensor("xfm", [P, DC * TH], F32))
    xnbf = E(nc.sbuf_tensor("xnbf", [P, DC * TH], BF16))
    xt0 = E(nc.sbuf_tensor("xt0", [P, D], F32))
    xt1 = E(nc.sbuf_tensor("xt1", [P, D], F32))
    ident = E(nc.sbuf_tensor("ident", [P, P], F32))
    sq = E(nc.sbuf_tensor("sq", [P, 1024], F32))
    onescol = E(nc.sbuf_tensor("onescol", [P, 1], F32))
    onesrow = E(nc.sbuf_tensor("onesrow", [1, P], F32))
    s1r = E(nc.sbuf_tensor("s1r", [1, TH], F32))
    s2r = E(nc.sbuf_tensor("s2r", [1, TH], F32))
    varr = E(nc.sbuf_tensor("varr", [1, TH], F32))
    rrow = E(nc.sbuf_tensor("rrow", [1, TH], F32))
    mrow = E(nc.sbuf_tensor("mrow", [1, TH], F32))
    epsc = E(nc.sbuf_tensor("epsc", [1, 1], F32))
    bc1 = E(nc.sbuf_tensor("bc1", [P, TH], F32))
    bc2 = E(nc.sbuf_tensor("bc2", [P, TH], F32))
    tmp = E(nc.sbuf_tensor("tmpx", [P, 512], F32))
    xrst = E(nc.sbuf_tensor("xrst", [P, 512], F32))
    pst = [E(nc.psum_tensor(f"pst{i}", [P, 512], F32)) for i in range(2)]
    psA = E(nc.psum_tensor("psA", [P, 512], F32))
    psB = E(nc.psum_tensor("psB", [P, 512], F32))
    psC = E(nc.psum_tensor("psC", [P, 512], F32))

    p.dma("sy", ident[:, :], identf)
    p.op("pool", lambda o: o.memset(onescol[:, :], 1.0))
    p.op("pool", lambda o: o.memset(onesrow[:, :], 1.0))
    p.op("pool", lambda o: o.memset(epsc[:, :], 1e-5), incs=[("S_pool", 1)])
    ldw = p.now("ld")

    tiles = [(3, 0)] + [(128, 3 + 128 * i) for i in range(NTC)]
    tbuf = [xt0, xt1]
    tld = [0, 0]
    tfree = [0, 0]
    ti = 0
    for rows, coff in tiles:
        b = ti % 2
        src = x_in[0:3, :] if rows == 3 else x_in[coff:coff + 128, :]
        tld[b] = p.dma("sy", tbuf[b][0:rows, :], src,
                       waits=[("S_act", tfree[b]), ("ld", ldw)])
        act_base = p.now("S_act")
        for c in range(DC):
            pb = pst[c % 2]
            need_act = max(0, act_base + c - 1)
            p.op("pe", lambda o, b=b, c=c, rows=rows, pb=pb:
                 o.transpose(pb[:, 0:rows], tbuf[b][0:rows, c * P:(c + 1) * P],
                             ident[0:rows, 0:rows]),
                 waits=[("ld", tld[b]), ("S_act", need_act)],
                 incs=[("S_pe", 1)], settle=(rows < 64))
            p.op("act", lambda o, c=c, rows=rows, coff=coff, pb=pb:
                 o.copy(xfm[:, c * TH + coff: c * TH + coff + rows], pb[:, 0:rows]),
                 waits=[("S_pe", p.now("S_pe"))], incs=[("S_act", 1)])
        tfree[b] = p.now("S_act")
        ti += 1
    p.barrier()
    if stage == 1:
        _dump_A(p, xnT, xres, xfm, xnbf)
        return

    evacA = p.now("S_act")
    evacB = p.now("S_act")
    for a, bnd in col_groups(TH):
        w = bnd - a
        sq_done = []
        mmB_done = []
        for c in range(DC):
            p.op("pe", lambda o, c=c, a=a, w=w:
                 o.matmul(psA[0:1, 0:w], onescol[:, :], xfm[:, c * TH + a: c * TH + a + w],
                          start=(c == 0), stop=(c == DC - 1)),
                 waits=[("S_act", evacA if c == 0 else 0)], incs=[("S_pe", 1)],
                 settle=(w < 64))
            sqh = (c % 2) * 512
            p.op("act", lambda o, c=c, a=a, w=w, sqh=sqh:
                 o.activation(sq[:, sqh:sqh + w], xfm[:, c * TH + a: c * TH + a + w], AF.Square),
                 waits=[("S_pe", mmB_done[c - 2] if c >= 2 else 0)],
                 incs=[("S_act", 1)])
            sq_done.append(p.now("S_act"))
            p.op("pe", lambda o, c=c, a=a, w=w, sqh=sqh:
                 o.matmul(psB[0:1, 0:w], onescol[:, :], sq[:, sqh:sqh + w],
                          start=(c == 0), stop=(c == DC - 1)),
                 waits=[("S_act", sq_done[c]), ("S_act", evacB if c == 0 else 0)],
                 incs=[("S_pe", 1)], settle=(w < 64))
            mmB_done.append(p.now("S_pe"))
        p.op("act", lambda o, a=a, w=w: o.copy(s1r[:, a:a + w], psA[0:1, 0:w]),
             waits=[("S_pe", p.now("S_pe"))], incs=[("S_act", 1)])
        evacA = p.now("S_act")
        p.op("act", lambda o, a=a, w=w: o.copy(s2r[:, a:a + w], psB[0:1, 0:w]),
             incs=[("S_act", 1)])
        evacB = p.now("S_act")
    p.barrier()
    if stage == 2:
        _dump_rows_A(p, xres, s1r, s2r)
        return

    p.op("dve", lambda o: o.tensor_scalar(mrow[:, :], s1r[:, :], 1.0 / D, 0.0, ALU.mult, ALU.add))
    p.op("dve", lambda o: o.tensor_scalar(varr[:, :], s2r[:, :], 1.0 / D, 0.0, ALU.mult, ALU.add), incs=[("S_dve", 1)])
    p.op("act", lambda o: o.activation(s2r[:, :], mrow[:, :], AF.Square),
         waits=[("S_dve", p.now("S_dve"))], incs=[("S_act", 1)])
    p.op("dve", lambda o: o.tensor_tensor(s1r[:, :], varr[:, :], s2r[:, :], ALU.subtract),
         waits=[("S_act", p.now("S_act"))], incs=[("S_dve", 1)])
    p.op("act", lambda o: o.activation(varr[:, :], s1r[:, :], AF.Sqrt, bias=epsc[0:1, 0:1], scale=1.0),
         waits=[("S_dve", p.now("S_dve")), ("S_pool", p.now("S_pool"))], incs=[("S_act", 1)])
    p.op("dve", lambda o: o.reciprocal(rrow[:, :], varr[:, :]),
         waits=[("S_act", p.now("S_act"))], incs=[("S_dve", 1)])
    p.op("dve", lambda o: o.scalar_tensor_tensor(s2r[:, :], mrow[:, :], -1.0, rrow[:, :], ALU.mult, ALU.mult),
         incs=[("S_dve", 1)])
    p.op("dve", lambda o: o.tensor_copy(mrow[:, :], s2r[:, :]), incs=[("S_dve", 1)])
    p.barrier()
    if stage == 4:
        pass
    if stage == 3:
        p.dma("sy", xres[0:1, 0:T], rrow[0:1, 3:3 + T], sem="st")
        p.dma("sy", xres[1:2, 0:T], mrow[0:1, 3:3 + T], sem="st")
        p.dma("sy", xres[2:3, 0:T], s1r[0:1, 3:3 + T], sem="st")
        p.dma("sy", xres[3:4, 0:T], varr[0:1, 3:3 + T], sem="st")
        p.dma("sy", xres[4:5, 0:T], s2r[0:1, 3:3 + T], sem="st")
        p.op("sy", None, waits=[("st", p.now("st"))])
        return

    for a, bnd in col_groups(TH):
        w = bnd - a
        p.op("pe", lambda o, a=a, w=w:
             o.matmul(psC[:, 0:w], onesrow[0:1, :], rrow[0:1, a:a + w], start=True, stop=True),
             waits=[("S_act", p.now("S_act"))], incs=[("S_pe", 1)], settle=(w < 64))
        p.op("act", lambda o, a=a, w=w: o.copy(bc1[:, a:a + w], psC[:, 0:w]),
             waits=[("S_pe", p.now("S_pe"))], incs=[("S_act", 1)])
        p.op("pe", lambda o, a=a, w=w:
             o.matmul(psC[:, 0:w], onesrow[0:1, :], mrow[0:1, a:a + w], start=True, stop=True),
             waits=[("S_act", p.now("S_act"))], incs=[("S_pe", 1)], settle=(w < 64))
        p.op("act", lambda o, a=a, w=w: o.copy(bc2[:, a:a + w], psC[:, 0:w]),
             waits=[("S_pe", p.now("S_pe"))], incs=[("S_act", 1)])
    p.barrier()
    if stage == 4:
        for c in range(4):
            p.dma("sy", xres[c * P:(c + 1) * P, 0:T], bc1[:, 3:3 + T], sem="st")
            p.dma("sy", xres[(4 + c) * P:(5 + c) * P, 0:T], bc2[:, 3:3 + T], sem="st")
        p.op("sy", None, waits=[("st", p.now("st"))])
        return

    for c in range(DC):
        for a, bnd in col_groups(TH):
            w = bnd - a
            p.op("dve", lambda o, c=c, a=a, w=w:
                 o.tensor_tensor(tmp[:, 0:w], xfm[:, c * TH + a: c * TH + a + w], bc1[:, a:a + w], ALU.mult),
                 waits=[("st", p.now("st"))], incs=[("S_dve", 1)])
            p.op("dve", lambda o, c=c, a=a, w=w:
                 o.tensor_tensor(xnbf[:, c * TH + a: c * TH + a + w], tmp[:, 0:w], bc2[:, a:a + w], ALU.add),
                 incs=[("S_dve", 1)])
        p.dma("sy", xnT[c * P:(c + 1) * P, :], xnbf[:, c * TH: (c + 1) * TH],
              sem="st", waits=[("S_dve", p.now("S_dve"))])
    p.barrier()
    for c in range(DC):
        for a in range(0, T, 512):
            w = 512
            p.op("dve", lambda o, c=c, a=a, w=w:
                 o.tensor_tensor(xrst[:, 0:w], xfm[:, c * TH + 3 + a: c * TH + 3 + a + w],
                                 xnbf[:, c * TH + 3 + a: c * TH + 3 + a + w], ALU.add),
                 waits=[("st", p.now("st"))], incs=[("S_dve", 1)])
            p.dma("sy", xres[c * P:(c + 1) * P, a:a + w], xrst[:, 0:w],
                  sem="st", waits=[("S_dve", p.now("S_dve"))])
    p.op("sy", None, waits=[("st", p.now("st"))])



def build_launch_A(stage=99):
    nc = bass.Bass("TRN2", target_bir_lowering=False)
    x_in = nc.dram_tensor("x_in", [TH, D], F32, kind="ExternalInput").ap()
    identf = nc.dram_tensor("identf", [P, P], F32, kind="ExternalInput").ap()
    xnT = nc.dram_tensor("xnT", [D, TH], BF16, kind="ExternalOutput").ap()
    xres = nc.dram_tensor("xres", [D, T], F32, kind="ExternalOutput").ap()

    io = {}
    io["x_in"] = x_in
    io["identf"] = identf
    io["xnT"] = xnT
    io["xres"] = xres
    with ExitStack() as ctx:
        p = Prog(nc, ctx)
        emit_A(nc, p, ctx, io, stage)
        p.finalize()
    return nc


def emit_B(nc, p, ctx, io, stage=99):
    E = ctx.enter_context
    xnT = io["xnT"]
    winT = io["winT"]
    convw = io["convw"]
    convb = io["convb"]
    alog = io["alog"]
    dtb = io["dtb"]
    nw = io["nw"]
    cmask = io["cmask"]
    identf = io["identf"]
    szT = io["szT"]
    xcvT = io["xcvT"]
    BTo = io["BTo"]
    CTo = io["CTo"]
    cumo = io["cumo"]
    dtfo = io["dtfo"]
    hino = io["hino"]
    cc_in_s = io["cc_in_s"]
    cc_out_s = io["cc_out_s"]
    cc_in_l = io["cc_in_l"]
    cc_out_l = io["cc_out_l"]
    big = E(nc.sbuf_tensor("bigB", [P, DC * 4384], BF16))
    xnTs = big[:, 0:DC * TH]
    winTs = big[:, :]
    xcv = big[:, 0:DIC * T]
    big2 = E(nc.sbuf_tensor("big2B", [P, DIC * TH], BF16))
    xssmT = big2[:, :]
    hTa = big2[:, 0:4096].bitcast(F32)
    hTb = big2[:, 4096:8192].bitcast(F32)
    xctok = big2[:, 8192:10240]
    dtx = big2[:, 10240:12288]
    ddtx = big2[:, 12288:14336]
    tmpd = big2[:, 14336:15360].bitcast(F32)
    sgr = big2[:, 15360:19456].bitcast(F32)
    hins = big2[:, 19456:23552].bitcast(F32)
    xn2T = E(nc.sbuf_tensor("xn2T", [P, DC * TH], BF16))
    BTs = E(nc.sbuf_tensor("BTs", [P, TH], BF16))
    ident = E(nc.sbuf_tensor("identB", [P, P], F32))
    identb = E(nc.sbuf_tensor("identbB", [P, P], BF16))
    onescol = E(nc.sbuf_tensor("onescolB", [P, 1], F32))
    onesrow = E(nc.sbuf_tensor("onesrowB", [1, P], F32))
    dtraw = E(nc.sbuf_tensor("dtrawB", [NH, TH], F32))
    dtfs = E(nc.sbuf_tensor("dtfsB", [NH, T], F32))
    cums = E(nc.sbuf_tensor("cumsB", [NH, T], F32))
    rowsA = xn2T[0:32, 0:4096].bitcast(F32)
    rowsB = xn2T[0:32, 4096:8192].bitcast(F32)
    onesTrow = E(nc.sbuf_tensor("onesTrowB", [1, T], F32))
    onecol32 = E(nc.sbuf_tensor("onecol32B", [NH, 1], F32))
    zcol32 = E(nc.sbuf_tensor("zcol32B", [NH, 1], F32))
    nwcol = E(nc.sbuf_tensor("nwcolB", [P, DC], F32))
    wcv = E(nc.sbuf_tensor("wcvB", [P, DIC * 4], F32))
    cbcol = E(nc.sbuf_tensor("cbcolB", [P, DIC], F32))
    dtbc = E(nc.sbuf_tensor("dtbcB", [NH, 1], F32))
    expac = E(nc.sbuf_tensor("expacB", [NH, 1], F32))
    expa2 = E(nc.sbuf_tensor("expa2B", [NH, 1], F32))
    s2r = big2[0:1, 0:4102].bitcast(F32)
    varr = big2[0:1, 4102:8204].bitcast(F32)
    rrow = big2[0:1, 8204:12306].bitcast(F32)
    sq = xn2T[:, 4 * T:4 * T + 2048].bitcast(F32)
    stg = [E(nc.sbuf_tensor(f"stgB{i}", [P, 512], BF16)) for i in range(2)]
    btok = big2[:, 23552:23680]
    dttok = big2[:, 23680:23744].bitcast(F32)
    edtok = big2[:, 23744:23808].bitcast(F32)
    edf = big2[0:32, 23808:24064].bitcast(F32)
    lamrow = big2[0:1, 24064:24128].bitcast(F32)
    lamexp = xn2T[0:1, 10240:14336].bitcast(F32)
    wrow = big2[0:1, 0:4096].bitcast(F32)
    ccrow = big2[0:1, 4096:4608].bitcast(F32)
    trow = big2[0:1, 4608:5120].bitcast(F32)
    prow = big2[0:1, 5120:5632].bitcast(F32)
    cmrow = E(nc.sbuf_tensor("cmrowB", [1, NCORES], F32))
    invrow = E(nc.sbuf_tensor("invrowB", [1, NCORES], F32))
    nwr = big[0:DC, 0:256].bitcast(F32)
    cbr = big[0:DIC, 256:512].bitcast(F32)
    cv1 = xn2T[:, 0:2 * T].bitcast(F32)
    cv2 = xn2T[:, 2 * T:4 * T].bitcast(F32)
    ps4 = [E(nc.psum_tensor(f"ps4_{i}", [P, 512], F32)) for i in range(4)]
    psL = E(nc.psum_tensor("psL", [P, 512], F32))
    psF = E(nc.psum_tensor("psF", [P, 512], F32))
    psT = [E(nc.psum_tensor(f"psT{i}", [P, 1024], BF16)) for i in range(2)]

    p.dma("sy", ident[:, :], identf)
    p.dma("pool", identb[:, :], identf)
    p.dma("pool", wcv[:, :].rearrange("p (c k) -> p c k", k=4),
          convw.rearrange("(c p) k -> p c k", p=P))
    p.dma("sy", cmrow[:, :], cmask)
    p.op("pool", lambda o: o.memset(onescol[:, :], 1.0))
    p.op("pool", lambda o: o.memset(onesrow[:, :], 1.0))
    p.op("pool", lambda o: o.memset(onesTrow[:, :], 1.0))
    p.op("pool", lambda o: o.memset(onecol32[:, :], 1.0))
    p.op("pool", lambda o: o.memset(zcol32[:, :], 0.0), incs=[("S_pool", 1)])
    p.dma("sy", nwr[:, :], nw.rearrange("(c p) -> c p", p=P))
    p.dma("sy", cbr[:, :], convb.rearrange("(c p) -> c p", p=P))
    p.dma("sy", dtbc[:, :], dtb.unsqueeze(1))
    p.dma("sy", expac[:, :], alog.unsqueeze(1))
    ld0 = p.now("ld")
    pld0 = p.now("pld")
    p.op("pe", lambda o: o.transpose(psF[0:P, 0:DC], nwr[:, :], ident[0:DC, 0:DC]),
         waits=[("ld", ld0), ("pld", pld0), ("S_pool", p.now("S_pool"))], incs=[("S_pe", 1)],
         settle=True)
    p.op("act", lambda o: o.copy(nwcol[:, :], psF[:, 0:DC]),
         waits=[("S_pe", p.now("S_pe"))], incs=[("S_act", 1)])
    p.op("pe", lambda o: o.transpose(psF[:, 64:64 + DIC], cbr[:, :], ident[0:DIC, 0:DIC]),
         waits=[("S_act", p.now("S_act"))], incs=[("S_pe", 1)], settle=True)
    p.op("act", lambda o: o.copy(cbcol[:, :], psF[:, 64:64 + DIC]),
         waits=[("S_pe", p.now("S_pe"))], incs=[("S_act", 1)])
    p.op("act", lambda o: o.activation(expa2[:, :], expac[:, :], AF.Exp),
         waits=[("ld", ld0)], incs=[("S_act", 1)])
    p.barrier()

    for c in range(DC):
        p.dma("sy", xnTs[:, c * TH:(c + 1) * TH], xnT[c * P:(c + 1) * P, :])
    ld1 = p.now("ld")
    evacB = 0
    for a, bnd in col_groups(TH):
        w = bnd - a
        sq_done = []
        mmB_done = []
        for c in range(DC):
            sqh = (c % 2) * 512
            p.op("act", lambda o, c=c, a=a, w=w, sqh=sqh:
                 o.activation(sq[:, sqh:sqh + w], xnTs[:, c * TH + a: c * TH + a + w], AF.Square),
                 waits=[("ld", ld1), ("S_pe", mmB_done[c - 2] if c >= 2 else 0)],
                 incs=[("S_act", 1)])
            sq_done.append(p.now("S_act"))
            p.op("pe", lambda o, c=c, a=a, w=w, sqh=sqh:
                 o.matmul(ps4[0][0:1, 0:w], onescol[:, :], sq[:, sqh:sqh + w],
                          start=(c == 0), stop=(c == DC - 1)),
                 waits=[("S_act", sq_done[c]), ("S_act", evacB if c == 0 else 0)],
                 incs=[("S_pe", 1)], settle=(w < 64))
            mmB_done.append(p.now("S_pe"))
        p.op("act", lambda o, a=a, w=w: o.copy(s2r[:, a:a + w], ps4[0][0:1, 0:w]),
             waits=[("S_pe", p.now("S_pe"))], incs=[("S_act", 1)])
        evacB = p.now("S_act")
    p.barrier()
    p.op("dve", lambda o: o.tensor_scalar(varr[:, :], s2r[:, :], 1.0 / D, 1e-6, ALU.mult, ALU.add),
         incs=[("S_dve", 1)])
    p.op("act", lambda o: o.activation(s2r[:, :], varr[:, :], AF.Sqrt, bias=0.0, scale=1.0),
         waits=[("S_dve", p.now("S_dve"))], incs=[("S_act", 1)])
    p.op("dve", lambda o: o.reciprocal(rrow[:, :], s2r[:, :]),
         waits=[("S_act", p.now("S_act"))], incs=[("S_dve", 1)])
    p.barrier()
    for a, bnd in col_groups(TH):
        w = bnd - a
        p.op("pe", lambda o, a=a, w=w:
             o.matmul(psL[:, 0:w], onesrow[0:1, :], rrow[0:1, a:a + w], start=True, stop=True),
             waits=[("S_dve", p.now("S_dve"))], incs=[("S_pe", 1)], settle=(w < 64))
        mmd = p.now("S_pe")
        for c in range(DC):
            p.op("dve", lambda o, c=c, a=a, w=w:
                 o.scalar_tensor_tensor(xn2T[:, c * TH + a: c * TH + a + w],
                                        xnTs[:, c * TH + a: c * TH + a + w],
                                        nwcol[:, c:c + 1], psL[:, 0:w], ALU.mult, ALU.mult),
                 waits=[("S_pe", mmd)], incs=[("S_dve", 1)] if c == DC - 1 else [])
    p.barrier()
    if stage == 1:
        for c in range(DC):
            p.dma("sy", szT[c * P:(c + 1) * P, 0:T], xn2T[:, c * TH + 3: c * TH + 3 + T], sem="st")
        p.op("sy", None, waits=[("st", p.now("st"))])
        return

    for k in range(DC):
        p.dma("pool", winTs[:, k * 4384:(k + 1) * 4384], winT[k * P:(k + 1) * P, :])
    ld2 = p.now("pld")
    bank_free = [("S_act", 0)] * 4
    bi = 0
    for a, bnd in col_groups(TH):
        w = bnd - a
        for m in range(35):
            rows = 32 if m == 34 else P
            bank = ps4[bi % 4]
            bwait = bank_free[bi % 4]
            for k in range(DC):
                p.op("pe", lambda o, m=m, k=k, a=a, w=w, rows=rows, bank=bank:
                     o.matmul(bank[0:rows, 0:w],
                              winTs[:, k * 4384 + m * P: k * 4384 + m * P + rows],
                              xn2T[:, k * TH + a: k * TH + a + w],
                              start=(k == 0), stop=(k == DC - 1)),
                     waits=([bwait, ("pld", ld2)] if k == 0 else []),
                     incs=([("S_pe", 1)] if k == DC - 1 else []),
                     settle=(w < 64 and k == DC - 1))
            mmdone = p.now("S_pe")
            sg = stg[bi % 2]
            lo = max(a, 3)
            if m < 16:
                p.op("act", lambda o, w=w, bank=bank, sg=sg:
                     o.activation(sg[:, 0:w], bank[:, 0:w], AF.Silu),
                     waits=[("S_pe", mmdone), ("st", p.now("st"))], incs=[("S_act", 1)])
                if lo < bnd:
                    p.dma("sy", szT[m * P:(m + 1) * P, lo - 3:bnd - 3], sg[:, lo - a:w],
                          sem="st", waits=[("S_act", p.now("S_act"))])
            elif m < 32:
                p.op("act", lambda o, m=m, a=a, w=w, bank=bank:
                     o.activation(xssmT[:, (m - 16) * TH + a:(m - 16) * TH + a + w],
                                  bank[:, 0:w], AF.Identity, bias=0.0, scale=1.0),
                     waits=[("S_pe", mmdone)], incs=[("S_act", 1)])
            elif m == 32:
                p.op("act", lambda o, a=a, w=w, bank=bank:
                     o.activation(BTs[:, a:a + w], bank[:, 0:w], AF.Identity, bias=0.0, scale=1.0),
                     waits=[("S_pe", mmdone), ("st", p.now("st"))], incs=[("S_act", 1)])
                if lo < bnd:
                    p.dma("sy", BTo[:, lo - 3:bnd - 3], BTs[:, lo:bnd],
                          sem="st", waits=[("S_act", p.now("S_act"))])
            elif m == 33:
                p.op("act", lambda o, a=a, w=w, bank=bank, sg=sg:
                     o.activation(sg[:, 0:w], bank[:, 0:w], AF.Identity, bias=0.0, scale=1.0),
                     waits=[("S_pe", mmdone), ("st", p.now("st"))], incs=[("S_act", 1)])
                if lo < bnd:
                    p.dma("sy", CTo[:, lo - 3:bnd - 3], sg[:, lo - a:w],
                          sem="st", waits=[("S_act", p.now("S_act"))])
            else:
                p.op("act", lambda o, a=a, w=w, bank=bank:
                     o.copy(dtraw[:, a:a + w], bank[0:32, 0:w]),
                     waits=[("S_pe", mmdone)], incs=[("S_act", 1)])
            bank_free[bi % 4] = ("S_act", p.now("S_act"))
            bi += 1
    p.barrier()
    if stage == 2:
        for c in range(DIC):
            p.dma("sy", xcvT[c * P:(c + 1) * P, :], xssmT[:, c * TH + 3: c * TH + 3 + T], sem="st")
        p.dma("sy", cumo[:, :], dtraw[:, 3:], sem="st")
        p.op("sy", None, waits=[("st", p.now("st"))])
        return

    for c in range(DIC):
        base = c * TH
        p.op("dve", lambda o, c=c, base=base:
             o.tensor_scalar(cv1[:, :], xssmT[:, base:base + T], wcv[:, 4 * c:4 * c + 1],
                             0.0, ALU.mult, ALU.add))
        p.op("dve", lambda o, c=c, base=base:
             o.scalar_tensor_tensor(cv2[:, :], xssmT[:, base + 1:base + 1 + T],
                                    wcv[:, 4 * c + 1:4 * c + 2], cv1[:, :], ALU.mult, ALU.add))
        p.op("dve", lambda o, c=c, base=base:
             o.scalar_tensor_tensor(cv1[:, :], xssmT[:, base + 2:base + 2 + T],
                                    wcv[:, 4 * c + 2:4 * c + 3], cv2[:, :], ALU.mult, ALU.add))
        p.op("dve", lambda o, c=c, base=base:
             o.scalar_tensor_tensor(cv2[:, :], xssmT[:, base + 3:base + 3 + T],
                                    wcv[:, 4 * c + 3:4 * c + 4], cv1[:, :], ALU.mult, ALU.add),
             incs=[("S_dve", 1)])
        p.op("act", lambda o, c=c:
             o.activation(xcv[:, c * T:(c + 1) * T], cv2[:, :], AF.Silu,
                          bias=cbcol[:, c:c + 1], scale=1.0),
             waits=[("S_dve", p.now("S_dve")), ("st", p.now("st"))], incs=[("S_act", 1)])
        p.dma("sy", xcvT[c * P:(c + 1) * P, :], xcv[:, c * T:(c + 1) * T],
              sem="st", waits=[("S_act", p.now("S_act"))])
        p.op("dve", None, waits=[("S_act", p.now("S_act"))])
    p.barrier()

    p.op("dve", lambda o: o.tensor_scalar(rowsA[:, :], dtraw[:, 3:], dtbc[:, 0:1], 0.0, ALU.add, ALU.add),
         incs=[("S_dve", 1)])
    p.op("act", lambda o: o.activation(rowsB[:, :], rowsA[:, :], AF.Abs),
         waits=[("S_dve", p.now("S_dve"))], incs=[("S_act", 1)])
    p.op("act", lambda o: o.activation(rowsB[:, :], rowsB[:, :], AF.Exp, bias=0.0, scale=-1.0),
         incs=[("S_act", 1)])
    p.op("act", lambda o: o.activation(rowsB[:, :], rowsB[:, :], AF.Ln, bias=onecol32[:, 0:1], scale=1.0),
         incs=[("S_act", 1)])
    p.op("dve", lambda o: o.tensor_scalar(rowsA[:, :], rowsA[:, :], 0.0, 0.0, ALU.max, ALU.add),
         waits=[("S_act", p.now("S_act"))])
    p.op("dve", lambda o: o.tensor_tensor(dtfs[:, :], rowsA[:, :], rowsB[:, :], ALU.add),
         incs=[("S_dve", 1)])
    p.dma("sy", dtfo[:, :], dtfs[:, :], sem="st", waits=[("S_dve", p.now("S_dve"))])
    p.op("dve", lambda o: o.tensor_scalar(rowsA[:, :], dtfs[:, :], expa2[:, 0:1], -1.0, ALU.mult, ALU.mult),
         incs=[("S_dve", 1)])
    p.op("dve", lambda o: o.tensor_tensor_scan(cums[:, :], rowsA[:, :],
                                               rowsA[:, :], 0.0, ALU.add, ALU.bypass),
         incs=[("S_dve", 1)])
    p.dma("sy", cumo[:, :], cums[:, :], sem="st", waits=[("S_dve", p.now("S_dve"))])
    p.barrier()
    if stage == 3:
        p.op("sy", None, waits=[("st", p.now("st"))])
        return

    p.op("pool", lambda o: o.memset(hTa[:, :], 0.0))
    p.op("pool", lambda o: o.memset(hTb[:, :], 0.0), incs=[("S_pool", 1)])
    hbufs = [hTa, hTb]
    for c in range(NTC):
        t0 = c * P
        hold = hbufs[c % 2]
        hnew = hbufs[(c + 1) % 2]
        base_act = p.now("S_act")
        for q in range(DIC):
            pb = psT[q % 2]
            p.op("pe", lambda o, q=q, t0=t0, pb=pb:
                 o.transpose(pb[:, 0:P], xcv[:, q * T + t0: q * T + t0 + P], identb[:, :]),
                 waits=[("S_act", max(0, base_act + q - 1)), ("S_pool", p.now("S_pool"))],
                 incs=[("S_pe", 1)])
            p.op("act", lambda o, q=q, pb=pb:
                 o.copy(xctok[:, q * P:(q + 1) * P], pb[:, 0:P]),
                 waits=[("S_pe", p.now("S_pe"))], incs=[("S_act", 1)])
        p.op("pe", lambda o, t0=t0:
             o.transpose(psT[0][:, 0:P], BTs[:, 3 + t0: 3 + t0 + P], identb[:, :]),
             waits=[("S_act", p.now("S_act"))], incs=[("S_pe", 1)])
        p.op("act", lambda o: o.copy(btok[:, :], psT[0][:, 0:P]),
             waits=[("S_pe", p.now("S_pe"))], incs=[("S_act", 1)])
        p.op("act", lambda o, t0=t0:
             o.activation(edf[:, :], cums[:, t0:t0 + P], AF.Exp,
                          bias=cums[:, t0 + P - 1:t0 + P], scale=-1.0),
             incs=[("S_act", 1)])
        p.op("pe", lambda o, t0=t0:
             o.transpose(psF[:, 0:NH], dtfs[:, t0:t0 + P], ident[0:NH, 0:NH]),
             waits=[("S_act", p.now("S_act"))], incs=[("S_pe", 1)], settle=True)
        p.op("act", lambda o: o.copy(dttok[:, :], psF[:, 0:NH]),
             waits=[("S_pe", p.now("S_pe"))], incs=[("S_act", 1)])
        p.op("pe", lambda o:
             o.transpose(psF[:, 64:64 + NH], edf[:, :], ident[0:NH, 0:NH]),
             waits=[("S_act", p.now("S_act"))], incs=[("S_pe", 1)], settle=True)
        p.op("act", lambda o: o.copy(edtok[:, :], psF[:, 64:64 + NH]),
             waits=[("S_pe", p.now("S_pe"))], incs=[("S_act", 1)])
        p.op("dve", lambda o:
             o.tensor_tensor(dtx[:, :].rearrange("p (h d) -> p h d", d=HD),
                             xctok[:, :].rearrange("p (h d) -> p h d", d=HD),
                             dttok[:, :].unsqueeze(2).broadcast_to([P, NH, HD]), ALU.mult),
             waits=[("S_act", p.now("S_act"))], incs=[("S_dve", 1)])
        p.op("dve", lambda o:
             o.tensor_tensor(ddtx[:, :].rearrange("p (h d) -> p h d", d=HD),
                             dtx[:, :].rearrange("p (h d) -> p h d", d=HD),
                             edtok[:, :].unsqueeze(2).broadcast_to([P, NH, HD]), ALU.mult),
             incs=[("S_dve", 1)])
        dd = p.now("S_dve")
        for g in range(4):
            p.op("pe", lambda o, g=g:
                 o.matmul(ps4[g][:, :], btok[:, :], ddtx[:, g * 512:(g + 1) * 512],
                          start=True, stop=True),
                 waits=[("S_dve", dd)],
                 incs=([("S_pe", 1)] if g == 3 else []), settle=(g == 3))
        inj_done = p.now("S_pe")
        p.op("dve", lambda o, c=c, t0=t0:
             o.tensor_scalar(rowsB[:, 0:1], (cums[:, t0 - 1:t0] if c > 0 else zcol32[:, 0:1]),
                             -1.0, 0.0, ALU.mult, ALU.add), incs=[("S_dve", 1)])
        p.op("act", lambda o, t0=t0:
             o.activation(rowsB[:, 1:2], cums[:, t0 + P - 1:t0 + P], AF.Exp,
                          bias=rowsB[:, 0:1], scale=1.0),
             waits=[("S_dve", p.now("S_dve"))], incs=[("S_act", 1)])
        p.op("pe", lambda o:
             o.transpose(psF[0:1, 64:64 + NH], rowsB[:, 1:2], ident[0:NH, 0:NH]),
             waits=[("S_act", p.now("S_act"))], incs=[("S_pe", 1)], settle=True)
        p.op("act", lambda o: o.copy(lamrow[0:1, :], psF[0:1, 64:64 + NH]),
             waits=[("S_pe", p.now("S_pe"))], incs=[("S_act", 1)])
        p.op("dve", lambda o:
             o.tensor_tensor(lamexp[:, :].rearrange("o (h d) -> o h d", d=HD),
                             onesTrow[0:1, 0:DI].rearrange("o (h d) -> o h d", d=HD),
                             lamrow[0:1, :].unsqueeze(2).broadcast_to([1, NH, HD]), ALU.mult),
             waits=[("S_act", p.now("S_act"))], incs=[("S_dve", 1)])
        lam_done = p.now("S_dve")
        for g in range(4):
            p.op("pe", lambda o, g=g:
                 o.matmul(psL[:, :], onesrow[0:1, :], lamexp[0:1, g * 512:(g + 1) * 512],
                          start=True, stop=True),
                 waits=[("S_dve", lam_done), ("S_dve", p.now("S_dve"))],
                 incs=[("S_pe", 1)], settle=True)
            p.op("dve", lambda o, g=g, hold=hold:
                 o.tensor_tensor(tmpd[:, :], hold[:, g * 512:(g + 1) * 512], psL[:, :], ALU.mult),
                 waits=[("S_pe", p.now("S_pe")), ("S_pool", p.now("S_pool"))], incs=[("S_dve", 1)])
            p.op("dve", lambda o, g=g, hnew=hnew:
                 o.tensor_tensor(hnew[:, g * 512:(g + 1) * 512], tmpd[:, :], ps4[g][:, :], ALU.add),
                 waits=[("S_pe", inj_done)], incs=[("S_dve", 1)])
        p.barrier()
    hfin = hbufs[NTC % 2]
    p.op("act", lambda o: o.activation(rowsB[:, 2:3], cums[:, T - 1:T], AF.Exp),
         incs=[("S_act", 1)])
    p.dma("sy", cc_in_s.ap(), hfin[:, :], sem="st", waits=[("S_dve", p.now("S_dve"))])
    p.dma("sy", cc_in_l.ap(), rowsB[:, 2:3], sem="st", waits=[("S_act", p.now("S_act"))])
    p.op("pool", lambda o: o.collective_compute(
        "AllGather", ALU.bypass, replica_groups=[list(range(NCORES))],
        ins=[cc_in_s.ap().opt()], outs=[cc_out_s.ap().opt()]),
        waits=[("st", p.now("st"))], incs=[("cc", 1)], settle=False)
    p.op("pool", lambda o: o.collective_compute(
        "AllGather", ALU.bypass, replica_groups=[list(range(NCORES))],
        ins=[cc_in_l.ap().opt()], outs=[cc_out_l.ap().opt()]),
        incs=[("cc", 1)], settle=False)
    if stage == 4:
        for g in range(4):
            p.dma("sy", hino[:, g * 512:(g + 1) * 512], hfin[:, g * 512:(g + 1) * 512], sem="st")
        p.op("sy", None, waits=[("st", p.now("st")), ("cc", p.now("cc"))])
        return

    p.dma("sy", ccrow[:, :], cc_out_l.ap().rearrange("(r h) one -> one (r h)", h=NH),
          waits=[("cc", p.now("cc"))])
    ld3 = p.now("ld")
    p.op("pool", lambda o: o.memset(hins[:, :], 0.0), incs=[("S_pool", 1)])
    p.op("dve", lambda o: o.tensor_scalar(invrow[:, :], cmrow[:, :], -1.0, 1.0, ALU.mult, ALU.add),
         incs=[("S_dve", 1)])
    for q in range(NCORES):
        p.op("dve", lambda o, q=q:
             o.tensor_scalar(trow[:, q * NH:(q + 1) * NH], ccrow[:, q * NH:(q + 1) * NH],
                             cmrow[0:1, q:q + 1], invrow[0:1, q:q + 1], ALU.mult, ALU.add),
             waits=[("ld", ld3)], incs=[("S_dve", 1)])
    p.op("dve", lambda o: o.tensor_scalar(prow[:, 7 * NH:8 * NH], onesTrow[0:1, 0:NH], 1.0, 0.0,
                                          ALU.mult, ALU.add), incs=[("S_dve", 1)])
    for r in range(NCORES - 2, -1, -1):
        p.op("dve", lambda o, r=r:
             o.tensor_tensor(prow[:, r * NH:(r + 1) * NH], trow[:, (r + 1) * NH:(r + 2) * NH],
                             prow[:, (r + 1) * NH:(r + 2) * NH], ALU.mult),
             incs=[("S_dve", 1)])
    p.barrier()
    for r in range(NCORES):
        p.dma("sy", sgr[:, :], cc_out_s.ap()[r * DS:(r + 1) * DS, :],
              waits=[("S_dve", p.now("S_dve"))])
        ldr = p.now("ld")
        p.op("dve", lambda o, r=r:
             o.tensor_scalar(prow[:, r * NH:(r + 1) * NH], prow[:, r * NH:(r + 1) * NH],
                             cmrow[0:1, r:r + 1], 0.0, ALU.mult, ALU.add), incs=[("S_dve", 1)])
        p.op("dve", lambda o, r=r:
             o.tensor_tensor(wrow[:, :].rearrange("o (h d) -> o h d", d=HD),
                             onesTrow[0:1, 0:DI].rearrange("o (h d) -> o h d", d=HD),
                             prow[:, r * NH:(r + 1) * NH].unsqueeze(2).broadcast_to([1, NH, HD]),
                             ALU.mult), incs=[("S_dve", 1)])
        wd = p.now("S_dve")
        for g in range(4):
            p.op("pe", lambda o, g=g:
                 o.matmul(psL[:, :], onesrow[0:1, :], wrow[0:1, g * 512:(g + 1) * 512],
                          start=True, stop=True),
                 waits=[("S_dve", wd), ("S_dve", p.now("S_dve"))], incs=[("S_pe", 1)])
            p.op("dve", lambda o, g=g:
                 o.tensor_tensor(tmpd[:, :], sgr[:, g * 512:(g + 1) * 512], psL[:, :], ALU.mult),
                 waits=[("S_pe", p.now("S_pe")), ("ld", ldr), ("S_pool", p.now("S_pool"))],
                 incs=[("S_dve", 1)])
            p.op("dve", lambda o, g=g:
                 o.tensor_tensor(hins[:, g * 512:(g + 1) * 512], hins[:, g * 512:(g + 1) * 512],
                                 tmpd[:, :], ALU.add),
                 incs=[("S_dve", 1)])
    p.dma("sy", hino, hins[:, :], sem="st", waits=[("S_dve", p.now("S_dve"))])
    p.op("sy", None, waits=[("st", p.now("st"))])



def build_launch_B(stage=99):
    nc = bass.Bass("TRN2", target_bir_lowering=False)
    xnT = nc.dram_tensor("xnT", [D, TH], BF16, kind="ExternalInput").ap()
    winT = nc.dram_tensor("winT", [D, 4384], F32, kind="ExternalInput").ap()
    convw = nc.dram_tensor("convw", [DI, 4], F32, kind="ExternalInput").ap()
    convb = nc.dram_tensor("convb", [DI], F32, kind="ExternalInput").ap()
    alog = nc.dram_tensor("alog", [NH], F32, kind="ExternalInput").ap()
    dtb = nc.dram_tensor("dtb", [NH], F32, kind="ExternalInput").ap()
    nw = nc.dram_tensor("nw", [D], F32, kind="ExternalInput").ap()
    cmask = nc.dram_tensor("cmask", [1, NCORES], F32, kind="ExternalInput").ap()
    identf = nc.dram_tensor("identf", [P, P], F32, kind="ExternalInput").ap()
    szT = nc.dram_tensor("szT", [DI, T], BF16, kind="ExternalOutput").ap()
    xcvT = nc.dram_tensor("xcvT", [DI, T], BF16, kind="ExternalOutput").ap()
    BTo = nc.dram_tensor("BTo", [DS, T], BF16, kind="ExternalOutput").ap()
    CTo = nc.dram_tensor("CTo", [DS, T], BF16, kind="ExternalOutput").ap()
    cumo = nc.dram_tensor("cumo", [NH, T], F32, kind="ExternalOutput").ap()
    dtfo = nc.dram_tensor("dtfo", [NH, T], F32, kind="ExternalOutput").ap()
    hino = nc.dram_tensor("hino", [DS, NH * HD], F32, kind="ExternalOutput").ap()
    cc_in_s = nc.dram_tensor("cc_in_s", [DS, NH * HD], F32)
    cc_out_s = nc.dram_tensor("cc_out_s", [NCORES * DS, NH * HD], F32, addr_space="Shared")
    cc_in_l = nc.dram_tensor("cc_in_l", [NH, 1], F32)
    cc_out_l = nc.dram_tensor("cc_out_l", [NCORES * NH, 1], F32, addr_space="Shared")

    io = {}
    io["xnT"] = xnT
    io["winT"] = winT
    io["convw"] = convw
    io["convb"] = convb
    io["alog"] = alog
    io["dtb"] = dtb
    io["nw"] = nw
    io["cmask"] = cmask
    io["identf"] = identf
    io["szT"] = szT
    io["xcvT"] = xcvT
    io["BTo"] = BTo
    io["CTo"] = CTo
    io["cumo"] = cumo
    io["dtfo"] = dtfo
    io["hino"] = hino
    io["cc_in_s"] = cc_in_s
    io["cc_out_s"] = cc_out_s
    io["cc_in_l"] = cc_in_l
    io["cc_out_l"] = cc_out_l
    with ExitStack() as ctx:
        p = Prog(nc, ctx)
        emit_B(nc, p, ctx, io, stage)
        p.finalize()
    return nc




def emit_C(nc, p, ctx, io, stage=99):
    E = ctx.enter_context
    xnT = io["xnT"]
    xres = io["xres"]
    gatew = io["gatew"]
    gateb = io["gateb"]
    w1 = io["w1"]
    w2 = io["w2"]
    wg = io["wg"]
    w3 = io["w3"]
    b1 = io["b1"]
    b2 = io["b2"]
    bg = io["bg"]
    b3 = io["b3"]
    identf = io["identf"]
    xres2 = io["xres2"]
    wesc = io["wesc"]
    wbig = E(nc.sbuf_tensor("wbigC", [P, 2 * DC * DI], BF16))
    w1s = wbig[:, 0:DC * DI]
    w2s = wbig[:, DC * DI:2 * DC * DI]
    wgs = wbig[:, :]
    zT = E(nc.sbuf_tensor("zTC", [P, DIC * 1024], BF16))
    xnTs = E(nc.sbuf_tensor("xnTsC", [P, DC * T], BF16))
    w3s = E(nc.sbuf_tensor("w3sC", [P, DIC * D], BF16))
    swT = E(nc.sbuf_tensor("swTC", [P, DIC * 512], BF16))
    xrb = E(nc.sbuf_tensor("xrbC", [P, 4 * 512], F32))
    ident = E(nc.sbuf_tensor("identC", [P, P], F32))
    onesrow = E(nc.sbuf_tensor("onesrowC", [1, P], F32))
    gw = E(nc.sbuf_tensor("gwC", [P, DC * 8], BF16))
    gbr = E(nc.sbuf_tensor("gbrC", [1, 8], F32))
    b3row = E(nc.sbuf_tensor("b3rowC", [1, D], F32))
    bcolr = E(nc.sbuf_tensor("bcolrC", [P, P], F32))
    b1col = E(nc.sbuf_tensor("b1colC", [P, P], F32))
    b2col = E(nc.sbuf_tensor("b2colC", [P, P], F32))
    bgcol = E(nc.sbuf_tensor("bgcolC", [P, P], F32))
    wefm = zT[0:8, 0:4096].bitcast(F32)
    lg = E(nc.sbuf_tensor("lgC", [P, 8], F32))
    pex = E(nc.sbuf_tensor("pexC", [P, 8], F32))
    m1 = E(nc.sbuf_tensor("m1C", [P, 1], F32))
    t1g = E(nc.sbuf_tensor("t1gC", [P, 8], F32))
    wet = E(nc.sbuf_tensor("wetC", [P, 8], F32))
    werow = E(nc.sbuf_tensor("werowC", [1, T], F32))
    x1tmp = [swT[:, i * 1024:(i + 1) * 1024].bitcast(F32) for i in range(3)]
    stmp = E(nc.sbuf_tensor("stmpC", [P, 512], F32))
    ps8 = [E(nc.psum_tensor(f"ps8C{i}", [P, 512], F32)) for i in range(8)]

    p.dma("sy", ident[:, :], identf)
    p.op("pool", lambda o: o.memset(onesrow[:, :], 1.0), incs=[("S_pool", 1)])
    p.dma("pool", gw[:, :].rearrange("p (k e) -> p k e", e=8), gatew.rearrange("(k p) e -> p k e", p=P))
    p.dma("sy", gbr[:, :], gateb)
    for name, src_ap, dst in ((0, b1, b1col), (1, b2, b2col), (2, bg, bgcol)):
        p.dma("sy", bcolr[:, :], src_ap.rearrange("e (c p) -> (e c) p", p=P),
              waits=[("S_act", p.now("S_act"))])
        p.op("pe", lambda o: o.transpose(ps8[0][:, 0:P], bcolr[:, :], ident[:, :]),
             waits=[("ld", p.now("ld")), ("S_act", p.now("S_act"))], incs=[("S_pe", 1)])
        p.op("act", lambda o, dst=dst: o.copy(dst[:, :], ps8[0][:, 0:P]),
             waits=[("S_pe", p.now("S_pe"))], incs=[("S_act", 1)])
    for c in range(DC):
        p.dma("pool", xnTs[:, c * T:(c + 1) * T], xnT[c * P:(c + 1) * P, 3:])
    xg = [zT[:, 4096 + i * DC * P:4096 + (i + 1) * DC * P] for i in range(2)]
    if True:
        ldg = p.now("ld")
        for tt in range(NTC):
            t0 = tt * P
            xgb = xg[tt % 2]
            p.dma("pool", xgb[:, :].rearrange("p (k t) -> p k t", t=P),
                  xnT.rearrange("(k p) t -> p k t", p=P)[:, :, 3 + t0:3 + t0 + P],
                  waits=[("S_pe", p.now("S_pe"))])
            ldx = p.now("pld")
            for k in range(DC):
                p.op("pe", lambda o, k=k, xgb=xgb:
                     o.matmul(ps8[1][0:P, 0:8], xgb[:, k * P:(k + 1) * P],
                              gw[:, k * 8:(k + 1) * 8], start=(k == 0), stop=False),
                     waits=([("pld", ldx), ("ld", ldg), ("S_act", p.now("S_act"))] if k == 0 else []))
            p.op("pe", lambda o, t0=t0:
                 o.matmul(ps8[1][0:P, 0:8], onesrow[0:1, 0:P], gbr[0:1, :], start=False, stop=True),
                 incs=[("S_pe", 1)])
            p.op("act", lambda o: o.copy(lg[:, :], ps8[1][0:P, 0:8]),
                 waits=[("S_pe", p.now("S_pe"))], incs=[("S_act", 1)])
            p.op("dve", lambda o: o.reduce_max(m1[:, :], lg[:, :], AX.X),
                 waits=[("S_act", p.now("S_act"))])
            p.op("dve", lambda o: o.tensor_scalar(m1[:, :], m1[:, :], -1.0, 0.0, ALU.mult, ALU.add),
                 incs=[("S_dve", 1)])
            p.op("act", lambda o: o.activation(pex[:, :], lg[:, :], AF.Exp, bias=m1[:, 0:1], scale=1.0),
                 waits=[("S_dve", p.now("S_dve"))], incs=[("S_act", 1)])
            p.op("dve", lambda o: o.reduce_max(m1[:, :], pex[:, :], AX.X),
                 waits=[("S_act", p.now("S_act"))])
            p.op("dve", lambda o: o.tensor_scalar(t1g[:, :], pex[:, :], m1[:, 0:1], 0.0, ALU.is_ge, ALU.add))
            p.op("dve", lambda o: o.tensor_scalar(t1g[:, :], t1g[:, :], -1.0, 1.0, ALU.mult, ALU.add))
            p.op("dve", lambda o: o.tensor_tensor(t1g[:, :], pex[:, :], t1g[:, :], ALU.mult))
            p.op("dve", lambda o: o.reduce_max(m1[:, :], t1g[:, :], AX.X))
            p.op("dve", lambda o: o.tensor_scalar(t1g[:, :], pex[:, :], m1[:, 0:1], 0.0, ALU.is_ge, ALU.add))
            p.op("dve", lambda o: o.tensor_tensor(wet[:, :], pex[:, :], t1g[:, :], ALU.mult))
            p.op("dve", lambda o: o.reduce_sum(m1[:, :], wet[:, :], AX.X))
            p.op("dve", lambda o: o.reciprocal(m1[:, :], m1[:, :]))
            p.op("dve", lambda o: o.tensor_scalar(wet[:, :], wet[:, :], m1[:, 0:1], 0.0, ALU.mult, ALU.add),
                 incs=[("S_dve", 1)])
            p.op("pe", lambda o: o.transpose(ps8[2][0:8, 0:P], wet[:, :], ident[:, :]),
                 waits=[("S_dve", p.now("S_dve")), ("S_act", p.now("S_act"))], incs=[("S_pe", 1)])
            p.op("act", lambda o, t0=t0: o.copy(wefm[:, t0:t0 + P], ps8[2][0:8, 0:P]),
                 waits=[("S_pe", p.now("S_pe"))], incs=[("S_act", 1)])
            if stage == 0 and t0 == 0:
                p.barrier()
                p.dma("sy", xres2[0:P, 0:8], lg[:, :], sem="st")
                p.dma("sy", xres2[0:P, 8:16], pex[:, :], sem="st")
                p.dma("sy", xres2[0:P, 16:24], wet[:, :], sem="st")
                p.dma("sy", xres2[128:136, 24:152], wefm[0:8, 0:128], sem="st")
                p.op("sy", None, waits=[("st", p.now("st"))])
                return
        p.barrier()
        p.dma("sy", wesc.ap(), wefm[:, :], sem="st")
        p.op("sy", None, waits=[("st", p.now("st"))])
        p.barrier()
        if stage == 1:
            for c in range(DC):
                p.dma("sy", xres2[c * P:(c + 1) * P, :], xracc[:, c * T:(c + 1) * T], sem="st")
            p.dma("sy", xres2[0:8, 0:T], wefm[:, :], sem="st")
            p.op("sy", None, waits=[("st", p.now("st"))])
            return

        for e in range(8):
            p.dma("sy", werow[:, :], wesc.ap()[e:e + 1, :], waits=[("S_pe", p.now("S_pe"))])
            p.dma("sy", b3row[:, :], b3[e:e + 1, :])
            for th in range(2):
                wmark = {}
                wbfree = getattr(p, "_c_wg_end", p.now("S_pe"))
                for k in range(DC):
                    p.dma("pool", w1s[:, k * DI:(k + 1) * DI], w1[e, k * P:(k + 1) * P, :],
                          waits=[("S_pe", wbfree)] if k == 0 else [])
                    wmark[("w1", k)] = p.now("pld")
                    p.dma("pool", w2s[:, k * DI:(k + 1) * DI], w2[e, k * P:(k + 1) * P, :])
                    wmark[("w2", k)] = p.now("pld")
                if th == 0:
                    w3mark = {}
                    w3free = p.now("S_pe")
                    for gb in range(DIC):
                        p.dma("pool", w3s[:, gb * D:(gb + 1) * D], w3[e, gb * P:(gb + 1) * P, :],
                              waits=[("S_pe", w3free)] if gb == 0 else [])
                        w3mark[gb] = p.now("pld")
                for ng in range(2):
                    n0 = th * 1024 + ng * 512
                    nn = ng * 512
                    for fb in range(DIC):
                        bnkA = ps8[(fb % 3) * 2]
                        bnkB = ps8[(fb % 3) * 2 + 1]
                        xt = x1tmp[fb % 3]
                        for k in range(DC):
                            w_ = [("pld", wmark[("w1", k)])] if fb == 0 else []
                            if k == 0:
                                w_ = w_ + [("S_dve", p.now("S_dve"))]
                            p.op("pe", lambda o, k=k, fb=fb, bnkA=bnkA, n0=n0:
                                 o.matmul(bnkA[:, :], w1s[:, k * DI + fb * P: k * DI + (fb + 1) * P],
                                          xnTs[:, k * T + n0:k * T + n0 + 512],
                                          start=(k == 0), stop=(k == DC - 1)),
                                 waits=w_)
                        for k in range(DC):
                            p.op("pe", lambda o, k=k, fb=fb, bnkB=bnkB, n0=n0:
                                 o.matmul(bnkB[:, :], w2s[:, k * DI + fb * P: k * DI + (fb + 1) * P],
                                          xnTs[:, k * T + n0:k * T + n0 + 512],
                                          start=(k == 0), stop=(k == DC - 1)),
                                 waits=([("pld", wmark[("w2", k)])] if fb == 0 else []),
                                 incs=([("S_pe", 1)] if k == DC - 1 else []))
                        mmd = p.now("S_pe")
                        p.op("act", lambda o, e=e, fb=fb, bnkA=bnkA, xt=xt:
                             o.activation(xt[:, :], bnkA[:, :], AF.Identity,
                                          bias=b1col[:, e * DIC + fb:e * DIC + fb + 1], scale=1.0),
                             waits=[("S_pe", mmd)], incs=[("S_act", 1)])
                        p.op("dve", lambda o, e=e, fb=fb, nn=nn, bnkB=bnkB, xt=xt:
                             o.scalar_tensor_tensor(zT[:, fb * 1024 + nn: fb * 1024 + nn + 512],
                                                    bnkB[:, :], b2col[:, e * DIC + fb:e * DIC + fb + 1],
                                                    xt[:, :], ALU.add, ALU.mult),
                             waits=[("S_act", p.now("S_act"))], incs=[("S_dve", 1)])
                if stage == 2 and e == 0 and th == 0:
                    p.barrier()
                    for fb in range(DIC):
                        p.dma("pool", xres2[(fb % 8) * P:(fb % 8 + 1) * P,
                                          (fb // 8) * 1024:(fb // 8 + 1) * 1024],
                              zT[:, fb * 1024:(fb + 1) * 1024], sem="st")
                    p.op("sy", None, waits=[("st", p.now("st"))])
                    return
                gmark = {}
                wgfree = p.now("S_pe")
                for k in range(DIC):
                    p.dma("pool", wgs[:, k * DI:(k + 1) * DI], wg[e, k * P:(k + 1) * P, :],
                          waits=[("S_pe", wgfree)] if k == 0 else [])
                    gmark[k] = p.now("pld")
                for ng in range(2):
                    n0 = th * 1024 + ng * 512
                    nn = ng * 512
                    p.op("pe", lambda o, n0=n0:
                         o.matmul(ps8[7][:, :], onesrow[0:1, :], werow[0:1, n0:n0 + 512],
                                  start=True, stop=True),
                         waits=[("S_dve", p.now("S_dve")), ("ld", p.now("ld"))], incs=[("S_pe", 1)])
                    webd = p.now("S_pe")
                    for gb in range(DIC):
                        bnk = ps8[4 + (gb % 3)]
                        for k in range(DIC):
                            w_ = [("pld", gmark[k])] if gb == 0 and ng == 0 else []
                            if k == 0:
                                w_ = w_ + [("S_dve", p.now("S_dve"))]
                            p.op("pe", lambda o, k=k, gb=gb, nn=nn, bnk=bnk:
                                 o.matmul(bnk[:, :], wgs[:, k * DI + gb * P: k * DI + (gb + 1) * P],
                                          zT[:, k * 1024 + nn: k * 1024 + nn + 512],
                                          start=(k == 0), stop=(k == DIC - 1)),
                                 waits=w_,
                                 incs=([("S_pe", 1)] if k == DIC - 1 else []))
                        mmd = p.now("S_pe")
                        p.op("act", lambda o, e=e, gb=gb, bnk=bnk:
                             o.activation(stmp[:, :], bnk[:, :], AF.Silu,
                                          bias=bgcol[:, e * DIC + gb:e * DIC + gb + 1], scale=1.0),
                             waits=[("S_pe", mmd)], incs=[("S_act", 1)])
                        p.op("dve", lambda o, gb=gb:
                             o.tensor_tensor(swT[:, gb * 512:(gb + 1) * 512], stmp[:, :], ps8[7][:, :], ALU.mult),
                             waits=[("S_act", p.now("S_act")), ("S_pe", webd)], incs=[("S_dve", 1)])
                    if ng == 1:
                        p._c_wg_end = p.now("S_pe")
                    swd = p.now("S_dve")
                    if stage == 3 and e == 0 and th == 0 and ng == 0:
                        for gb in range(DIC):
                            p.dma("pool", xres2[(gb % 8) * P:(gb % 8 + 1) * P,
                                              (gb // 8) * 512:(gb // 8 + 1) * 512],
                                  swT[:, gb * 512:(gb + 1) * 512], sem="st",
                                  waits=[("S_dve", swd)])
                        p.op("sy", None, waits=[("st", p.now("st"))])
                        return
                    xsrc = xres if e == 0 else xres2
                    for ob in range(4):
                        p.dma("sy", xrb[:, ob * 512:(ob + 1) * 512],
                              xsrc[ob * P:(ob + 1) * P, n0:n0 + 512],
                              waits=[("st", p.now("st"))] if ob == 0 else [])
                    for gb in range(DIC):
                        for ob in range(8):
                            w_ = [("pld", w3mark[gb])] if ob == 0 else []
                            if gb == 0:
                                w_ = w_ + [("S_dve", swd)]
                            p.op("pe", lambda o, gb=gb, ob=ob:
                                 o.matmul(ps8[ob][:, :], w3s[:, gb * D + ob * P:gb * D + (ob + 1) * P],
                                          swT[:, gb * 512:(gb + 1) * 512],
                                          start=(gb == 0), stop=False),
                                 waits=w_, settle=False)
                    for ob in range(8):
                        p.op("pe", lambda o, ob=ob, n0=n0:
                             o.matmul(ps8[ob][:, :], b3row[0:1, ob * P:(ob + 1) * P],
                                      werow[0:1, n0:n0 + 512],
                                      start=False, stop=True),
                             incs=([("S_pe", 1)] if ob == 7 else []))
                    mmd = p.now("S_pe")
                    for ob in range(8):
                        sl = ob % 4
                        ldx = p.now("ld")
                        p.op("dve", lambda o, ob=ob, sl=sl:
                             o.tensor_tensor(xrb[:, sl * 512:(sl + 1) * 512],
                                             xrb[:, sl * 512:(sl + 1) * 512],
                                             ps8[ob][:, :], ALU.add),
                             waits=[("S_pe", mmd), ("ld", ldx)], incs=[("S_dve", 1)])
                        p.dma("sy", xres2[ob * P:(ob + 1) * P, n0:n0 + 512],
                              xrb[:, sl * 512:(sl + 1) * 512], sem="st",
                              waits=[("S_dve", p.now("S_dve"))])
                        if ob + 4 < 8:
                            p.dma("sy", xrb[:, sl * 512:(sl + 1) * 512],
                                  xsrc[(ob + 4) * P:(ob + 5) * P, n0:n0 + 512],
                                  waits=[("st", p.now("st"))])
    p.barrier()
    p.op("sy", None, waits=[("st", p.now("st"))])



def build_launch_C(stage=99):
    nc = bass.Bass("TRN2", target_bir_lowering=False)
    xnT = nc.dram_tensor("xnT", [D, TH], BF16, kind="ExternalInput").ap()
    xres = nc.dram_tensor("xres", [D, T], F32, kind="ExternalInput").ap()
    gatew = nc.dram_tensor("gatew", [D, 8], F32, kind="ExternalInput").ap()
    gateb = nc.dram_tensor("gateb", [1, 8], F32, kind="ExternalInput").ap()
    w1 = nc.dram_tensor("w1", [8, D, DI], F32, kind="ExternalInput").ap()
    w2 = nc.dram_tensor("w2", [8, D, DI], F32, kind="ExternalInput").ap()
    wg = nc.dram_tensor("wg", [8, DI, DI], F32, kind="ExternalInput").ap()
    w3 = nc.dram_tensor("w3", [8, DI, D], F32, kind="ExternalInput").ap()
    b1 = nc.dram_tensor("b1", [8, DI], F32, kind="ExternalInput").ap()
    b2 = nc.dram_tensor("b2", [8, DI], F32, kind="ExternalInput").ap()
    bg = nc.dram_tensor("bg", [8, DI], F32, kind="ExternalInput").ap()
    b3 = nc.dram_tensor("b3", [8, D], F32, kind="ExternalInput").ap()
    identf = nc.dram_tensor("identf", [P, P], F32, kind="ExternalInput").ap()
    xres2 = nc.dram_tensor("xres2", [D, T], F32, kind="ExternalOutput").ap()
    wesc = nc.dram_tensor("wesc", [8, T], F32)

    io = {}
    io["xnT"] = xnT
    io["xres"] = xres
    io["gatew"] = gatew
    io["gateb"] = gateb
    io["w1"] = w1
    io["w2"] = w2
    io["wg"] = wg
    io["w3"] = w3
    io["b1"] = b1
    io["b2"] = b2
    io["bg"] = bg
    io["b3"] = b3
    io["identf"] = identf
    io["xres2"] = xres2
    io["wesc"] = wesc
    with ExitStack() as ctx:
        p = Prog(nc, ctx)
        emit_C(nc, p, ctx, io, stage)
        p.finalize()
    return nc




def emit_D(nc, p, ctx, io, stage=99, _DBG_CHUNK=-1):
    E = ctx.enter_context
    szT = io["szT"]
    xcvT = io["xcvT"]
    BTo = io["BTo"]
    CTo = io["CTo"]
    cumo = io["cumo"]
    dtfo = io["dtfo"]
    hino = io["hino"]
    xres2 = io["xres2"]
    woT = io["woT"]
    dvec = io["dvec"]
    identf = io["identf"]
    mask01T = io["mask01T"]
    selT = io["selT"]
    sel2 = io["sel2"]
    xres3 = io["xres3"]
    scr1 = io["scr1"]
    scr2 = io["scr2"]
    woTs = E(nc.sbuf_tensor("woTsD", [P, DIC * D], BF16))
    ygT = E(nc.sbuf_tensor("ygTD", [P, DIC * 512], BF16))
    STb = E(nc.sbuf_tensor("STbD", [P, NH * P], BF16))
    CtT = E(nc.sbuf_tensor("CtTD", [P, NH * P], BF16))
    szc = [E(nc.sbuf_tensor(f"szcD{i}", [P, DIC * P], BF16)) for i in range(2)]
    xcvc = [E(nc.sbuf_tensor(f"xcvcD{i}", [P, DIC * P], BF16)) for i in range(2)]
    BTs = E(nc.sbuf_tensor("BTsD", [P, T], BF16))
    CTs = E(nc.sbuf_tensor("CTsD", [P, T], BF16))
    cums = E(nc.sbuf_tensor("cumsD", [NH, T], F32))
    dtfs = E(nc.sbuf_tensor("dtfsD", [NH, T], F32))
    hT = E(nc.sbuf_tensor("hTD", [P, NH * HD], F32))
    xctok = E(nc.sbuf_tensor("xctokD", [P, DI], BF16))
    dtx = E(nc.sbuf_tensor("dtxD", [P, DI], BF16))
    ddtx = E(nc.sbuf_tensor("ddtxD", [P, DI], BF16))
    btok = E(nc.sbuf_tensor("btokD", [P, P], BF16))
    cumtok = E(nc.sbuf_tensor("cumtokD", [P, NH], F32))
    dttok = E(nc.sbuf_tensor("dttokD", [P, NH], F32))
    edtok = E(nc.sbuf_tensor("edtokD", [P, NH], F32))
    edf2 = E(nc.sbuf_tensor("edf2D", [NH, P], F32))
    edfB = E(nc.sbuf_tensor("edfBD", [NH, P], F32))
    cumflat = E(nc.sbuf_tensor("cumflatD", [1, NH * P], F32))
    ecumflat = E(nc.sbuf_tensor("ecumflatD", [1, NH * P], F32))
    lamexp = E(nc.sbuf_tensor("lamexpD", [1, DI], F32))
    lamrow = E(nc.sbuf_tensor("lamrowD", [1, NH], F32))
    negpc = E(nc.sbuf_tensor("negpcD", [NH, 1], F32))
    zc32 = E(nc.sbuf_tensor("zc32D", [NH, 1], F32))
    GmT = E(nc.sbuf_tensor("GmTD", [P, P], F32))
    t1s = E(nc.sbuf_tensor("t1sD", [P, 512], F32))
    t2s = E(nc.sbuf_tensor("t2sD", [P, 512], F32))
    ytmp = E(nc.sbuf_tensor("ytmpD", [P, P], F32))
    mask = E(nc.sbuf_tensor("maskD", [P, P], F32))
    selTs = E(nc.sbuf_tensor("selTsD", [NH, P], F32))
    sel2s = E(nc.sbuf_tensor("sel2sD", [NH, 16], F32))
    dsel = E(nc.sbuf_tensor("dselD", [NH, 16], F32))
    dcol2 = E(nc.sbuf_tensor("dcol2D", [NH, 1], F32))
    Dcol = E(nc.sbuf_tensor("DcolD", [P, 16], F32))
    ident = E(nc.sbuf_tensor("identD", [P, P], F32))
    identb = E(nc.sbuf_tensor("identbD", [P, P], BF16))
    onesrow = E(nc.sbuf_tensor("onesrowD", [1, P], F32))
    onesDI = E(nc.sbuf_tensor("onesDID", [1, DI], F32))
    xrs = [E(nc.sbuf_tensor(f"xrsD{i}", [P, 512], F32)) for i in range(2)]
    hT2 = E(nc.sbuf_tensor("hT2D", [P, NH * HD], F32))
    hb = E(nc.sbuf_tensor("hbD", [P, NH * HD], BF16))
    tmpd = E(nc.sbuf_tensor("tmpdD", [P, 512], F32))
    ps4 = [E(nc.psum_tensor(f"ps4D{i}", [P, 512], F32)) for i in range(4)]
    psF = E(nc.psum_tensor("psFD", [P, 512], F32))
    psB = E(nc.psum_tensor("psBD", [P, 512], F32))
    psT = [E(nc.psum_tensor(f"psTD{i}", [P, 1024], BF16)) for i in range(2)]

    p.dma("sy", ident[:, :], identf)
    p.dma("pool", identb[:, :], identf)
    p.dma("sy", mask[:, :], mask01T)
    p.dma("sy", selTs[:, :], selT)
    p.dma("sy", sel2s[:, :], sel2)
    p.dma("sy", dcol2[:, :], dvec.unsqueeze(1))
    p.dma("sy", BTs[:, :], BTo)
    p.dma("sy", CTs[:, :], CTo)
    p.dma("sy", cums[:, :], cumo)
    p.dma("sy", dtfs[:, :], dtfo)
    p.dma("sy", hT[:, :], hino)
    p.op("pool", lambda o: o.memset(onesrow[:, :], 1.0))
    p.op("pool", lambda o: o.memset(onesDI[:, :], 1.0))
    p.op("pool", lambda o: o.memset(zc32[:, :], 0.0))
    p.dma("pool", woTs[:, :].rearrange("p (k o) -> p k o", o=D),
          woT.rearrange("(k p) o -> p k o", p=P))
    ld0 = p.now("ld")
    p.op("dve", lambda o: o.tensor_tensor(dsel[:, :], sel2s[:, :],
                                          dcol2[:, 0:1].broadcast_to([NH, 16]), ALU.mult),
         waits=[("ld", ld0)], incs=[("S_dve", 1)])
    p.op("pe", lambda o: o.matmul(psF[:, 0:16], selTs[:, :], dsel[:, :], start=True, stop=True),
         waits=[("S_dve", p.now("S_dve"))], incs=[("S_pe", 1)])
    p.op("act", lambda o: o.copy(Dcol[:, :], psF[:, 0:16]),
         waits=[("S_pe", p.now("S_pe"))], incs=[("S_act", 1)])
    p.op("dve", lambda o: o.tensor_copy(hb[:, :], hT[:, :]),
         waits=[("ld", p.now("ld"))], incs=[("S_dve", 1)])
    p.barrier()

    for c in range(NTC):
        t0 = c * P
        szb = szc[c % 2]
        xcb = xcvc[c % 2]
        p.dma("pool", szb[:, :].rearrange("p (q t) -> p q t", t=P),
              szT.rearrange("(q p) t -> p q t", p=P)[:, :, t0:t0 + P],
              waits=[("S_dve", p.now("S_dve"))])
        p.dma("pool", xcb[:, :].rearrange("p (q t) -> p q t", t=P),
              xcvT.rearrange("(q p) t -> p q t", p=P)[:, :, t0:t0 + P])
        ldc = p.now("pld")
        p.op("dve", lambda o, c=c, t0=t0:
             o.tensor_scalar(negpc[:, 0:1], (cums[:, t0 - 1:t0] if c > 0 else zc32[:, 0:1]),
                             -1.0, 0.0, ALU.mult, ALU.add), incs=[("S_dve", 1)])
        p.op("act", lambda o, t0=t0:
             o.activation(edf2[:, :], cums[:, t0:t0 + P], AF.Exp, bias=negpc[:, 0:1], scale=1.0),
             waits=[("S_dve", p.now("S_dve"))], incs=[("S_act", 1)])
        p.dma("sy", scr1.ap(), cums[:, t0:t0 + P], sem="st")
        p.dma("sy", scr2.ap(), edf2[:, :], sem="st", waits=[("S_act", p.now("S_act"))])
        p.op("sy", None, waits=[("st", p.now("st"))])
        p.dma("sy", cumflat[:, :], scr1.ap().rearrange("h t -> (h t)").unsqueeze(0))
        p.dma("sy", ecumflat[:, :], scr2.ap().rearrange("h t -> (h t)").unsqueeze(0))
        ldflat = p.now("ld")
        base_act = p.now("S_act")
        for q in range(DIC):
            pb = psT[q % 2]
            p.op("pe", lambda o, q=q, pb=pb, xcb=xcb:
                 o.transpose(pb[:, 0:P], xcb[:, q * P:(q + 1) * P], identb[:, :]),
                 waits=[("S_act", max(0, base_act + q - 1)), ("pld", ldc)], incs=[("S_pe", 1)])
            p.op("act", lambda o, q=q, pb=pb:
                 o.copy(xctok[:, q * P:(q + 1) * P], pb[:, 0:P]),
                 waits=[("S_pe", p.now("S_pe"))], incs=[("S_act", 1)])
        p.op("pe", lambda o, t0=t0:
             o.transpose(psT[0][:, 0:P], BTs[:, t0:t0 + P], identb[:, :]),
             waits=[("S_act", p.now("S_act"))], incs=[("S_pe", 1)])
        p.op("act", lambda o: o.copy(btok[:, :], psT[0][:, 0:P]),
             waits=[("S_pe", p.now("S_pe"))], incs=[("S_act", 1)])
        p.op("act", lambda o, t0=t0:
             o.activation(edfB[:, :], cums[:, t0:t0 + P], AF.Exp,
                          bias=cums[:, t0 + P - 1:t0 + P], scale=-1.0),
             incs=[("S_act", 1)])
        for (dst, srcrow) in ((cumtok, 0), (dttok, 1), (edtok, 2)):
            src = [cums, dtfs, None][srcrow]
            if srcrow < 2:
                p.op("pe", lambda o, t0=t0, src=src:
                     o.transpose(psF[:, 0:NH], src[:, t0:t0 + P], ident[0:NH, 0:NH]),
                     waits=[("S_act", p.now("S_act"))], incs=[("S_pe", 1)], settle=True)
            else:
                p.op("pe", lambda o:
                     o.transpose(psF[:, 0:NH], edfB[:, :], ident[0:NH, 0:NH]),
                     waits=[("S_act", p.now("S_act"))], incs=[("S_pe", 1)], settle=True)
            p.op("act", lambda o, dst=dst: o.copy(dst[:, :], psF[:, 0:NH]),
                 waits=[("S_pe", p.now("S_pe"))], incs=[("S_act", 1)])
        p.op("dve", lambda o:
             o.tensor_tensor(dtx[:, :].rearrange("p (h d) -> p h d", d=HD),
                             xctok[:, :].rearrange("p (h d) -> p h d", d=HD),
                             dttok[:, :].unsqueeze(2).broadcast_to([P, NH, HD]), ALU.mult),
             waits=[("S_act", p.now("S_act"))], incs=[("S_dve", 1)])
        p.op("dve", lambda o:
             o.tensor_tensor(ddtx[:, :].rearrange("p (h d) -> p h d", d=HD),
                             dtx[:, :].rearrange("p (h d) -> p h d", d=HD),
                             edtok[:, :].unsqueeze(2).broadcast_to([P, NH, HD]), ALU.mult),
             incs=[("S_dve", 1)])
        p.op("pe", lambda o, t0=t0:
             o.matmul(psF[:, 0:P], BTs[:, t0:t0 + P], CTs[:, t0:t0 + P], start=True, stop=True),
             waits=[("S_act", p.now("S_act"))], incs=[("S_pe", 1)])
        p.op("dve", lambda o: o.tensor_tensor(GmT[:, :], psF[:, 0:P], mask[:, :], ALU.mult),
             waits=[("S_pe", p.now("S_pe"))], incs=[("S_dve", 1)])
        for s in range(8):
            f0 = s * 512
            h0 = s * 4
            p.op("pe", lambda o, f0=f0:
                 o.matmul(psB[:, :], onesrow[0:1, :], cumflat[0:1, f0:f0 + 512], start=True, stop=True),
                 waits=[("ld", ldflat), ("S_dve", p.now("S_dve"))], incs=[("S_pe", 1)])
            p.op("dve", lambda o, h0=h0:
                 o.tensor_tensor(t1s[:, :].rearrange("p (a b) -> p a b", a=4),
                                 psB[:, :].rearrange("p (a b) -> p a b", a=4),
                                 cumtok[:, h0:h0 + 4].unsqueeze(2).broadcast_to([P, 4, P]),
                                 ALU.subtract),
                 waits=[("S_pe", p.now("S_pe"))], incs=[("S_dve", 1)])
            p.op("dve", lambda o:
                 o.tensor_tensor(t2s[:, :].rearrange("p (a b) -> p a b", a=4),
                                 t1s[:, :].rearrange("p (a b) -> p a b", a=4),
                                 mask[:, :].unsqueeze(1).broadcast_to([P, 4, P]), ALU.mult),
                 incs=[("S_dve", 1)])
            p.op("act", lambda o: o.activation(t1s[:, :], t2s[:, :], AF.Exp),
                 waits=[("S_dve", p.now("S_dve"))], incs=[("S_act", 1)])
            p.op("dve", lambda o, f0=f0:
                 o.tensor_tensor(STb[:, f0:f0 + 512].rearrange("p (a b) -> p a b", a=4),
                                 t1s[:, :].rearrange("p (a b) -> p a b", a=4),
                                 GmT[:, :].unsqueeze(1).broadcast_to([P, 4, P]), ALU.mult),
                 waits=[("S_act", p.now("S_act"))], incs=[("S_dve", 1)])
            p.op("pe", lambda o, f0=f0:
                 o.matmul(psB[:, :], onesrow[0:1, :], ecumflat[0:1, f0:f0 + 512], start=True, stop=True),
                 waits=[("S_dve", p.now("S_dve"))], incs=[("S_pe", 1)])
            p.op("dve", lambda o, f0=f0, t0=t0:
                 o.tensor_tensor(CtT[:, f0:f0 + 512].rearrange("p (a b) -> p a b", a=4),
                                 psB[:, :].rearrange("p (a b) -> p a b", a=4),
                                 CTs[:, t0:t0 + P].unsqueeze(1).broadcast_to([P, 4, P]), ALU.mult),
                 waits=[("S_pe", p.now("S_pe"))], incs=[("S_dve", 1)])
        std = p.now("S_dve")
        if stage == 1 and c == 0:
            for q in range(DIC):
                p.dma("pool", xres3[(q % 8) * P:(q % 8 + 1) * P, (q // 8) * 256:(q // 8) * 256 + 128],
                      STb[:, q * P:(q + 1) * P], sem="st", waits=[("S_dve", std)])
                p.dma("pool", xres3[(q % 8) * P:(q % 8 + 1) * P, 512 + (q // 8) * 256:512 + (q // 8) * 256 + 128],
                      CtT[:, q * P:(q + 1) * P], sem="st")
            p.op("sy", None, waits=[("st", p.now("st"))])
            return
        for h in range(NH):
            bank = ps4[(h // 2) % 4]
            coff = ((h // 2) // 4) * P
            g = h // 2
            bank = ps4[g // 4]
            coff = (g % 4) * P
            r0 = (h % 2) * HD
            p.op("pe", lambda o, h=h, bank=bank, coff=coff, r0=r0:
                 o.matmul(bank[r0:r0 + HD, coff:coff + P], dtx[:, h * HD:(h + 1) * HD],
                          STb[:, h * P:(h + 1) * P], start=True, stop=False),
                 waits=([("S_dve", std)] if h == 0 else []))
            p.op("pe", lambda o, h=h, bank=bank, coff=coff, r0=r0:
                 o.matmul(bank[r0:r0 + HD, coff:coff + P], hb[:, h * HD:(h + 1) * HD],
                          CtT[:, h * P:(h + 1) * P], start=False, stop=True),
                 incs=([("S_pe", 1)] if h == NH - 1 else []))
        yd = p.now("S_pe")
        for q in range(DIC):
            bank = ps4[q // 4]
            coff = (q % 4) * P
            p.op("dve", lambda o, q=q, bank=bank, coff=coff, xcb=xcb:
                 o.scalar_tensor_tensor(ytmp[:, :], xcb[:, q * P:(q + 1) * P], Dcol[:, q:q + 1],
                                        bank[:, coff:coff + P], ALU.mult, ALU.add),
                 waits=[("S_pe", yd)], incs=[("S_dve", 1)])
            p.op("dve", lambda o, q=q, c=c, szb=szb:
                 o.tensor_tensor(ygT[:, q * 512 + (c % 4) * P: q * 512 + (c % 4) * P + P],
                                 ytmp[:, :], szb[:, q * P:(q + 1) * P], ALU.mult),
                 incs=[("S_dve", 1)])
        epd = p.now("S_dve")
        if stage == 2 and c == _DBG_CHUNK:
            for q in range(DIC):
                p.dma("pool", xres3[(q % 8) * P:(q % 8 + 1) * P, (q // 8) * 128:(q // 8) * 128 + P],
                      ygT[:, q * 512:q * 512 + P], sem="st", waits=[("S_dve", epd)])
            p.op("sy", None, waits=[("st", p.now("st"))])
            return
        for g in range(4):
            p.op("pe", lambda o, g=g:
                 o.matmul(ps4[g][:, :], btok[:, :], ddtx[:, g * 512:(g + 1) * 512],
                          start=True, stop=True),
                 waits=[("S_dve", epd)],
                 incs=([("S_pe", 1)] if g == 3 else []))
        injd = p.now("S_pe")
        p.op("act", lambda o, t0=t0:
             o.activation(negpc[:, 0:1], cums[:, t0 + P - 1:t0 + P], AF.Exp,
                          bias=negpc[:, 0:1], scale=1.0), incs=[("S_act", 1)])
        p.op("pe", lambda o:
             o.transpose(psF[0:1, 64:64 + NH], negpc[:, 0:1], ident[0:NH, 0:NH]),
             waits=[("S_act", p.now("S_act"))], incs=[("S_pe", 1)], settle=True)
        p.op("act", lambda o: o.copy(lamrow[0:1, :], psF[0:1, 64:64 + NH]),
             waits=[("S_pe", p.now("S_pe"))], incs=[("S_act", 1)])
        p.op("dve", lambda o:
             o.tensor_tensor(lamexp[:, :].rearrange("o (h d) -> o h d", d=HD),
                             onesDI[0:1, :].rearrange("o (h d) -> o h d", d=HD),
                             lamrow[0:1, :].unsqueeze(2).broadcast_to([1, NH, HD]), ALU.mult),
             waits=[("S_act", p.now("S_act"))], incs=[("S_dve", 1)])
        lamd = p.now("S_dve")
        for g in range(4):
            p.op("pe", lambda o, g=g:
                 o.matmul(psB[:, :], onesrow[0:1, :], lamexp[0:1, g * 512:(g + 1) * 512],
                          start=True, stop=True),
                 waits=[("S_dve", lamd), ("S_dve", p.now("S_dve"))], incs=[("S_pe", 1)])
            p.op("dve", lambda o, g=g:
                 o.tensor_tensor(tmpd[:, :], hT[:, g * 512:(g + 1) * 512], psB[:, :], ALU.mult),
                 waits=[("S_pe", p.now("S_pe"))], incs=[("S_dve", 1)])
            p.op("dve", lambda o, g=g:
                 o.tensor_tensor(hT2[:, g * 512:(g + 1) * 512], tmpd[:, :], ps4[g][:, :], ALU.add),
                 waits=[("S_pe", injd)], incs=[("S_dve", 1)])
        p.op("dve", lambda o: o.tensor_copy(hT[:, :], hT2[:, :]), incs=[("S_dve", 1)])
        p.op("dve", lambda o: o.tensor_copy(hb[:, :], hT2[:, :]), incs=[("S_dve", 1)])
        if c % 4 == 3:
            grp = c // 4
            gt0 = grp * 512
            for oc in range(DC):
                bank = ps4[oc % 2]
                xb = xrs[oc % 2]
                p.dma("sy", xb[:, :], xres2[oc * P:(oc + 1) * P, gt0:gt0 + 512],
                      waits=[("st", p.now("st"))])
                ldx = p.now("ld")
                for q in range(DIC):
                    p.op("pe", lambda o, oc=oc, q=q, bank=bank:
                         o.matmul(bank[:, :], woTs[:, q * D + oc * P: q * D + (oc + 1) * P],
                                  ygT[:, q * 512:(q + 1) * 512],
                                  start=(q == 0), stop=(q == DIC - 1)),
                         waits=([("S_dve", p.now("S_dve")), ("S_act", p.now("S_act"))] if q == 0 else []),
                         incs=([("S_pe", 1)] if q == DIC - 1 else []))
                mmd = p.now("S_pe")
                p.op("dve", lambda o, bank=bank, xb=xb:
                     o.tensor_tensor(xb[:, :], xb[:, :], bank[:, :], ALU.add),
                     waits=[("S_pe", mmd), ("ld", ldx)], incs=[("S_dve", 1)])
                p.dma("sy", xres3[oc * P:(oc + 1) * P, gt0:gt0 + 512], xb[:, :],
                      sem="st", waits=[("S_dve", p.now("S_dve"))])
        p.barrier()
    p.op("sy", None, waits=[("st", p.now("st"))])



def build_launch_D(stage=99, _DBG_CHUNK=0):
    nc = bass.Bass("TRN2", target_bir_lowering=False)
    szT = nc.dram_tensor("szT", [DI, T], BF16, kind="ExternalInput").ap()
    xcvT = nc.dram_tensor("xcvT", [DI, T], BF16, kind="ExternalInput").ap()
    BTo = nc.dram_tensor("BTo", [DS, T], BF16, kind="ExternalInput").ap()
    CTo = nc.dram_tensor("CTo", [DS, T], BF16, kind="ExternalInput").ap()
    cumo = nc.dram_tensor("cumo", [NH, T], F32, kind="ExternalInput").ap()
    dtfo = nc.dram_tensor("dtfo", [NH, T], F32, kind="ExternalInput").ap()
    hino = nc.dram_tensor("hino", [DS, NH * HD], F32, kind="ExternalInput").ap()
    xres2 = nc.dram_tensor("xres2", [D, T], F32, kind="ExternalInput").ap()
    woT = nc.dram_tensor("woT", [DI, D], F32, kind="ExternalInput").ap()
    dvec = nc.dram_tensor("dvec", [NH], F32, kind="ExternalInput").ap()
    identf = nc.dram_tensor("identf", [P, P], F32, kind="ExternalInput").ap()
    mask01T = nc.dram_tensor("mask01T", [P, P], F32, kind="ExternalInput").ap()
    selT = nc.dram_tensor("selT", [NH, P], F32, kind="ExternalInput").ap()
    sel2 = nc.dram_tensor("sel2", [NH, 16], F32, kind="ExternalInput").ap()
    xres3 = nc.dram_tensor("xres3", [D, T], F32, kind="ExternalOutput").ap()
    scr1 = nc.dram_tensor("scr1D", [NH, P], F32)
    scr2 = nc.dram_tensor("scr2D", [NH, P], F32)

    io = {}
    io["szT"] = szT
    io["xcvT"] = xcvT
    io["BTo"] = BTo
    io["CTo"] = CTo
    io["cumo"] = cumo
    io["dtfo"] = dtfo
    io["hino"] = hino
    io["xres2"] = xres2
    io["woT"] = woT
    io["dvec"] = dvec
    io["identf"] = identf
    io["mask01T"] = mask01T
    io["selT"] = selT
    io["sel2"] = sel2
    io["xres3"] = xres3
    io["scr1"] = scr1
    io["scr2"] = scr2
    with ExitStack() as ctx:
        p = Prog(nc, ctx)
        emit_D(nc, p, ctx, io, stage, _DBG_CHUNK)
        p.finalize()
    return nc




def emit_E(nc, p, ctx, io, stage=99):
    E = ctx.enter_context
    xres3 = io["xres3"]
    xn1T = io["xn1T"]
    xres4 = io["xres4"]
    xs = E(nc.sbuf_tensor("xsE", [P, DC * T], F32))
    sq = E(nc.sbuf_tensor("sqE", [P, 1024], F32))
    onescol = E(nc.sbuf_tensor("onescolE", [P, 1], F32))
    onesrow = E(nc.sbuf_tensor("onesrowE", [1, P], F32))
    s1r = E(nc.sbuf_tensor("s1rE", [1, T], F32))
    s2r = E(nc.sbuf_tensor("s2rE", [1, T], F32))
    varr = E(nc.sbuf_tensor("varrE", [1, T], F32))
    rrow = E(nc.sbuf_tensor("rrowE", [1, T], F32))
    mrow = E(nc.sbuf_tensor("mrowE", [1, T], F32))
    epsc = E(nc.sbuf_tensor("epscE", [1, 1], F32))
    tmp = E(nc.sbuf_tensor("tmpE", [P, 512], F32))
    xnf = E(nc.sbuf_tensor("xnfE", [P, 512], F32))
    xnb = E(nc.sbuf_tensor("xnbE", [P, 512], BF16))
    psA = E(nc.psum_tensor("psAE", [P, 512], F32))
    psB = E(nc.psum_tensor("psBE", [P, 512], F32))
    psC = E(nc.psum_tensor("psCE", [P, 512], F32))
    psD_ = E(nc.psum_tensor("psDE", [P, 512], F32))

    for c in range(DC):
        p.dma("sy", xs[:, c * T:(c + 1) * T], xres3[c * P:(c + 1) * P, :])
    ld0 = p.now("ld")
    p.op("pool", lambda o: o.memset(onescol[:, :], 1.0))
    p.op("pool", lambda o: o.memset(onesrow[:, :], 1.0))
    p.op("pool", lambda o: o.memset(epsc[:, :], 1e-5), incs=[("S_pool", 1)])
    evacA = evacB = 0
    for a in range(0, T, 512):
        w = 512
        sq_done = []
        mmB_done = []
        for c in range(DC):
            sqh = (c % 2) * 512
            p.op("pe", lambda o, c=c, a=a, w=w:
                 o.matmul(psA[0:1, 0:w], onescol[:, :], xs[:, c * T + a: c * T + a + w],
                          start=(c == 0), stop=(c == DC - 1)),
                 waits=[("ld", ld0), ("S_act", evacA if c == 0 else 0)], incs=[("S_pe", 1)])
            p.op("act", lambda o, c=c, a=a, w=w, sqh=sqh:
                 o.activation(sq[:, sqh:sqh + w], xs[:, c * T + a: c * T + a + w], AF.Square),
                 waits=[("ld", ld0), ("S_pe", mmB_done[c - 2] if c >= 2 else 0)],
                 incs=[("S_act", 1)])
            sq_done.append(p.now("S_act"))
            p.op("pe", lambda o, c=c, a=a, w=w, sqh=sqh:
                 o.matmul(psB[0:1, 0:w], onescol[:, :], sq[:, sqh:sqh + w],
                          start=(c == 0), stop=(c == DC - 1)),
                 waits=[("S_act", sq_done[c]), ("S_act", evacB if c == 0 else 0)],
                 incs=[("S_pe", 1)], settle=(w < 64))
            mmB_done.append(p.now("S_pe"))
        p.op("act", lambda o, a=a, w=w: o.copy(s1r[:, a:a + w], psA[0:1, 0:w]),
             waits=[("S_pe", p.now("S_pe"))], incs=[("S_act", 1)])
        evacA = p.now("S_act")
        p.op("act", lambda o, a=a, w=w: o.copy(s2r[:, a:a + w], psB[0:1, 0:w]),
             incs=[("S_act", 1)])
        evacB = p.now("S_act")
    p.barrier()
    if stage == 4:
        p.dma("sy", xres4[0:1, :], s1r[0:1, :], sem="st")
        p.dma("sy", xres4[1:2, :], s2r[0:1, :], sem="st")
        p.op("sy", None, waits=[("st", p.now("st"))])
        return
    p.op("dve", lambda o: o.tensor_scalar(mrow[:, :], s1r[:, :], 1.0 / D, 0.0, ALU.mult, ALU.add))
    p.op("dve", lambda o: o.tensor_scalar(varr[:, :], s2r[:, :], 1.0 / D, 0.0, ALU.mult, ALU.add),
         incs=[("S_dve", 1)])
    p.op("act", lambda o: o.activation(s2r[:, :], mrow[:, :], AF.Square),
         waits=[("S_dve", p.now("S_dve"))], incs=[("S_act", 1)])
    p.op("dve", lambda o: o.tensor_tensor(s1r[:, :], varr[:, :], s2r[:, :], ALU.subtract),
         waits=[("S_act", p.now("S_act")), ("S_pool", p.now("S_pool"))], incs=[("S_dve", 1)])
    p.op("act", lambda o: o.activation(varr[:, :], s1r[:, :], AF.Sqrt, bias=epsc[0:1, 0:1], scale=1.0),
         waits=[("S_dve", p.now("S_dve"))], incs=[("S_act", 1)])
    p.op("dve", lambda o: o.reciprocal(rrow[:, :], varr[:, :]),
         waits=[("S_act", p.now("S_act"))], incs=[("S_dve", 1)])
    p.op("dve", lambda o: o.scalar_tensor_tensor(s2r[:, :], mrow[:, :], -1.0, rrow[:, :],
                                                 ALU.mult, ALU.mult), incs=[("S_dve", 1)])
    p.op("dve", lambda o: o.tensor_copy(mrow[:, :], s2r[:, :]), incs=[("S_dve", 1)])
    p.barrier()
    if stage == 4:
        pass
    if stage == 3:
        p.dma("sy", xres4[0:1, :], rrow[0:1, :], sem="st")
        p.dma("sy", xres4[1:2, :], mrow[0:1, :], sem="st")
        p.dma("sy", xres4[2:3, :], s1r[0:1, :], sem="st")
        p.dma("sy", xres4[3:4, :], varr[0:1, :], sem="st")
        p.op("sy", None, waits=[("st", p.now("st"))])
        return
    for a in range(0, T, 512):
        w = 512
        p.op("pe", lambda o, a=a:
             o.matmul(psC[:, :], onesrow[0:1, :], rrow[0:1, a:a + 512], start=True, stop=True),
             waits=[("st", p.now("st")), ("S_dve", p.now("S_dve"))], incs=[("S_pe", 1)])
        p.op("pe", lambda o, a=a:
             o.matmul(psD_[:, :], onesrow[0:1, :], mrow[0:1, a:a + 512], start=True, stop=True),
             incs=[("S_pe", 1)])
        bd = p.now("S_pe")
        for c in range(DC):
            p.op("dve", lambda o, c=c, a=a:
                 o.tensor_tensor(tmp[:, :], xs[:, c * T + a: c * T + a + 512], psC[:, :], ALU.mult),
                 waits=[("S_pe", bd), ("st", p.now("st"))], incs=[("S_dve", 1)])
            p.op("dve", lambda o:
                 o.tensor_tensor(xnf[:, :], tmp[:, :], psD_[:, :], ALU.add), incs=[("S_dve", 1)])
            p.op("dve", lambda o:
                 o.tensor_copy(xnb[:, :], xnf[:, :]), incs=[("S_dve", 1)])
            p.op("dve", lambda o, c=c, a=a:
                 o.tensor_tensor(tmp[:, :], xs[:, c * T + a: c * T + a + 512], xnf[:, :], ALU.add),
                 incs=[("S_dve", 1)])
            p.dma("sy", xn1T[c * P:(c + 1) * P, a:a + 512], xnb[:, :], sem="st",
                  waits=[("S_dve", p.now("S_dve"))])
            p.dma("sy", xres4[c * P:(c + 1) * P, a:a + 512], tmp[:, :], sem="st")
    p.op("sy", None, waits=[("st", p.now("st"))])



def build_launch_E(stage=99):
    nc = bass.Bass("TRN2", target_bir_lowering=False)
    xres3 = nc.dram_tensor("xres3", [D, T], F32, kind="ExternalInput").ap()
    xn1T = nc.dram_tensor("xn1T", [D, T], BF16, kind="ExternalOutput").ap()
    xres4 = nc.dram_tensor("xres4", [D, T], F32, kind="ExternalOutput").ap()
    io = {}
    io["xres3"] = xres3
    io["xn1T"] = xn1T
    io["xres4"] = xres4
    with ExitStack() as ctx:
        p = Prog(nc, ctx)
        emit_E(nc, p, ctx, io, stage)
        p.finalize()
    return nc


def emit_F(nc, p, ctx, io, stage=99):
    E = ctx.enter_context
    xn1T = io["xn1T"]
    xres4 = io["xres4"]
    projT = io["projT"]
    fc1T = io["fc1T"]
    fc2T = io["fc2T"]
    vecs8 = io["vecs8"]
    vecs32 = io["vecs32"]
    identf = io["identf"]
    xres5 = io["xres5"]
    xr4b = io["xr4b"]
    HID = 4 * D
    HC = 32
    fc1s = E(nc.sbuf_tensor("fc1sF", [P, DC * HID], BF16))
    n2T = E(nc.sbuf_tensor("n2TF", [P, DC * T], BF16))
    hTt = E(nc.sbuf_tensor("hTtF", [P, HC * 512], BF16))
    hnTt = hTt
    xn1s = E(nc.sbuf_tensor("xn1sF", [P, DC * T], BF16))
    x1Ts = fc1s[:, 0:DC * T]
    projTs = fc1s[:, DC * T:DC * T + DC * D]
    fc2g = [xn1s[:, i * 4 * D:(i + 1) * 4 * D] for i in range(2)]
    fc2r = [xn1s[:, 8 * D + i * D:8 * D + (i + 1) * D] for i in range(2)]
    vc = E(nc.sbuf_tensor("vcF", [P, 64], F32))
    hc = E(nc.sbuf_tensor("hcF", [P, 96], F32))
    v8 = E(nc.sbuf_tensor("v8F", [64, P], F32))
    v32 = E(nc.sbuf_tensor("v32F", [96, P], F32))
    vcb = E(nc.sbuf_tensor("vcbF", [P, 64], BF16))
    hcb = E(nc.sbuf_tensor("hcbF", [P, 96], BF16))
    bias1 = E(nc.sbuf_tensor("bias1F", [P, 8], F32))
    bias2 = E(nc.sbuf_tensor("bias2F", [P, 32], F32))
    bias3 = E(nc.sbuf_tensor("bias3F", [P, 8], F32))
    ident = E(nc.sbuf_tensor("identF", [P, P], F32))
    onescol = E(nc.sbuf_tensor("onescolF", [P, 1], F32))
    onescolb = E(nc.sbuf_tensor("onescolbF", [P, 1], BF16))
    onesrow = E(nc.sbuf_tensor("onesrowF", [1, P], F32))
    sq = E(nc.sbuf_tensor("sqF", [P, 1024], F32))
    rbase = DC * T + DC * D
    s1r = fc1s[0:1, rbase:rbase + 4096].bitcast(F32)
    s2r = fc1s[0:1, rbase + 4096:rbase + 8192].bitcast(F32)
    varr = E(nc.sbuf_tensor("varrF", [1, T], F32))
    rrow = E(nc.sbuf_tensor("rrowF", [1, T], F32))
    mrow = E(nc.sbuf_tensor("mrowF", [1, T], F32))
    srow = E(nc.sbuf_tensor("srowF", [1, 512], F32))
    srow2 = E(nc.sbuf_tensor("srow2F", [1, 512], F32))
    epsc = E(nc.sbuf_tensor("epscF", [1, 1], F32))
    a1s = E(nc.sbuf_tensor("a1sF", [P, 512], F32))
    xrs = [E(nc.sbuf_tensor(f"xrsF{i}", [P, 512], F32)) for i in range(2)]
    tmp = E(nc.sbuf_tensor("tmpF", [P, 512], F32))
    ps8 = [E(nc.psum_tensor(f"ps8F{i}", [P, 512], F32)) for i in range(8)]

    p.dma("sy", ident[:, :], identf)
    p.dma("sy", v8[:, :], vecs8)
    p.dma("sy", v32[:, :], vecs32)
    p.op("pool", lambda o: o.memset(onescol[:, :], 1.0))
    p.op("pool", lambda o: o.memset(onescolb[:, :], 1.0))
    p.op("pool", lambda o: o.memset(onesrow[:, :], 1.0))
    p.op("pool", lambda o: o.memset(epsc[:, :], 1e-5), incs=[("S_pool", 1)])
    for c in range(DC):
        p.dma("pool", xn1s[:, c * T:(c + 1) * T], xn1T[c * P:(c + 1) * P, :])
    p.dma("pool", projTs[:, :].rearrange("p (k o) -> p k o", o=D),
          projT.rearrange("(k p) o -> p k o", p=P))
    ld0 = p.now("ld")
    pld0 = p.now("pld")
    p.op("pe", lambda o: o.transpose(ps8[0][:, 0:64], v8[:, :], ident[0:64, 0:64]),
         waits=[("ld", ld0)], incs=[("S_pe", 1)])
    p.op("act", lambda o: o.copy(vc[:, :], ps8[0][:, 0:64]),
         waits=[("S_pe", p.now("S_pe"))], incs=[("S_act", 1)])
    p.op("pe", lambda o: o.transpose(ps8[0][:, 64:64 + 96], v32[:, :], ident[0:96, 0:96]),
         waits=[("S_act", p.now("S_act"))], incs=[("S_pe", 1)])
    p.op("act", lambda o: o.copy(hc[:, :], ps8[0][:, 64:64 + 96]),
         waits=[("S_pe", p.now("S_pe"))], incs=[("S_act", 1)])
    p.op("dve", lambda o: o.tensor_copy(vcb[:, :], vc[:, :]),
         waits=[("S_act", p.now("S_act"))], incs=[("S_dve", 1)])
    p.op("dve", lambda o: o.tensor_copy(hcb[:, :], hc[:, :]), incs=[("S_dve", 1)])
    p.barrier()

    def ln_stats_bf(src, colw, with_mean=True):
        evacA = [0]
        evacB = [0]
        for a in range(0, T, 512):
            w = 512
            mmB_done = []
            for c in range(DC):
                sqh = (c % 2) * 512
                if with_mean:
                    p.op("pe", lambda o, c=c, a=a, w=w, src=src:
                         o.matmul(ps8[6][0:1, 0:w], onescolb[:, :], src[:, c * T + a: c * T + a + w],
                                  start=(c == 0), stop=(c == DC - 1)),
                         waits=[("S_act", evacA[0] if c == 0 else 0)], incs=[("S_pe", 1)])
                p.op("act", lambda o, c=c, a=a, w=w, sqh=sqh, src=src:
                     o.activation(sq[:, sqh:sqh + w], src[:, c * T + a: c * T + a + w], AF.Square),
                     waits=[("S_pe", mmB_done[c - 2] if c >= 2 else 0)], incs=[("S_act", 1)])
                sd = p.now("S_act")
                p.op("pe", lambda o, c=c, a=a, w=w, sqh=sqh:
                     o.matmul(ps8[7][0:1, 0:w], onescol[:, :], sq[:, sqh:sqh + w],
                              start=(c == 0), stop=(c == DC - 1)),
                     waits=[("S_act", sd), ("S_act", evacB[0] if c == 0 else 0)],
                     incs=[("S_pe", 1)])
                mmB_done.append(p.now("S_pe"))
            if with_mean:
                p.op("act", lambda o, a=a, w=w: o.copy(s1r[:, a:a + w], ps8[6][0:1, 0:w]),
                     waits=[("S_pe", p.now("S_pe"))], incs=[("S_act", 1)])
                evacA[0] = p.now("S_act")
            p.op("act", lambda o, a=a, w=w: o.copy(s2r[:, a:a + w], ps8[7][0:1, 0:w]),
                 waits=[("S_pe", p.now("S_pe"))], incs=[("S_act", 1)])
            evacB[0] = p.now("S_act")
        p.barrier()
        p.op("dve", lambda o: o.tensor_scalar(mrow[:, :], s1r[:, :], 1.0 / D, 0.0, ALU.mult, ALU.add))
        p.op("dve", lambda o: o.tensor_scalar(varr[:, :], s2r[:, :], 1.0 / D, 0.0, ALU.mult, ALU.add),
             incs=[("S_dve", 1)])
        p.op("act", lambda o: o.activation(s2r[:, :], mrow[:, :], AF.Square),
             waits=[("S_dve", p.now("S_dve"))], incs=[("S_act", 1)])
        p.op("dve", lambda o: o.tensor_tensor(s1r[:, :], varr[:, :], s2r[:, :], ALU.subtract),
             waits=[("S_act", p.now("S_act"))], incs=[("S_dve", 1)])
        p.op("act", lambda o: o.activation(varr[:, :], s1r[:, :], AF.Sqrt, bias=epsc[0:1, 0:1], scale=1.0),
             waits=[("S_dve", p.now("S_dve")), ("S_pool", p.now("S_pool"))], incs=[("S_act", 1)])
        p.op("dve", lambda o: o.reciprocal(rrow[:, :], varr[:, :]),
             waits=[("S_act", p.now("S_act"))], incs=[("S_dve", 1)])
        p.op("dve", lambda o: o.scalar_tensor_tensor(s2r[:, :], mrow[:, :], -1.0, rrow[:, :],
                                                     ALU.mult, ALU.mult), incs=[("S_dve", 1)])
        p.op("dve", lambda o: o.tensor_copy(mrow[:, :], s2r[:, :]), incs=[("S_dve", 1)])
        p.barrier()

    def ln_apply_bf(src, dst):
        for a in range(0, T, 512):
            p.op("pe", lambda o, a=a:
                 o.matmul(ps8[6][:, :], onesrow[0:1, :], rrow[0:1, a:a + 512], start=True, stop=True),
                 waits=[("S_dve", p.now("S_dve"))], incs=[("S_pe", 1)])
            p.op("pe", lambda o, a=a:
                 o.matmul(ps8[7][:, :], onesrow[0:1, :], mrow[0:1, a:a + 512], start=True, stop=True),
                 incs=[("S_pe", 1)])
            bd = p.now("S_pe")
            for c in range(DC):
                p.op("dve", lambda o, c=c, a=a, src=src:
                     o.tensor_tensor(tmp[:, :], src[:, c * T + a: c * T + a + 512], ps8[6][:, :], ALU.mult),
                     waits=[("S_pe", bd)], incs=[("S_dve", 1)])
                p.op("dve", lambda o, c=c, a=a, dst=dst:
                     o.tensor_tensor(dst[:, c * T + a: c * T + a + 512], tmp[:, :], ps8[7][:, :], ALU.add),
                     incs=[("S_dve", 1)])

    ln_stats_bf(xn1s, None)
    n1T = hTt[:, 0:DC * T // 2]
    n1T = hTt
    ln_apply_bf(xn1s, n1T)
    p.barrier()

    for ob in range(DC):
        for k in range(DC):
            p.op("pe", lambda o, ob=ob, k=k:
                 o.matmul(ps8[0][0:P, ob:ob + 1], projTs[:, k * D + ob * P:k * D + (ob + 1) * P],
                          vcb[:, 8 + k:8 + k + 1], start=(k == 0), stop=(k == DC - 1)),
                 waits=([("pld", pld0), ("S_act", p.now("S_act"))] if ob == 0 and k == 0 else []),
                 incs=([("S_pe", 1)] if k == DC - 1 else []))
    p.op("act", lambda o: o.copy(bias1[:, :], ps8[0][:, 0:8]),
         waits=[("S_pe", p.now("S_pe"))], incs=[("S_act", 1)])
    p.op("dve", lambda o: o.tensor_tensor(bias1[:, :], bias1[:, :], vc[:, 48:56], ALU.add),
         waits=[("S_act", p.now("S_act"))], incs=[("S_dve", 1)])
    for k in range(DC):
        p.op("dve", lambda o, k=k:
             o.tensor_scalar(projTs[:, k * D:(k + 1) * D], projTs[:, k * D:(k + 1) * D],
                             vc[:, k:k + 1], 0.0, ALU.mult, ALU.add), incs=[("S_dve", 1)])
    p.barrier()
    for oc in range(DC):
        for a in range(0, T, 512):
            bank = ps8[(a // 512) % 2]
            xb = xrs[(a // 512) % 2]
            p.dma("sy", xb[:, :], xres4[oc * P:(oc + 1) * P, a:a + 512],
                  waits=[("st", p.now("st"))])
            ldx = p.now("ld")
            for k in range(DC):
                p.op("pe", lambda o, oc=oc, a=a, k=k, bank=bank:
                     o.matmul(bank[:, :], projTs[:, k * D + oc * P:k * D + (oc + 1) * P],
                              n1T[:, k * T + a:k * T + a + 512], start=(k == 0), stop=(k == DC - 1)),
                     waits=([("S_dve", p.now("S_dve")), ("S_act", p.now("S_act"))] if k == 0 else []),
                     incs=([("S_pe", 1)] if k == DC - 1 else []))
            mmd = p.now("S_pe")
            p.op("act", lambda o, oc=oc, bank=bank:
                 o.activation(a1s[:, :], bank[:, :], AF.Identity, bias=bias1[:, oc:oc + 1], scale=1.0),
                 waits=[("S_pe", mmd)], incs=[("S_act", 1)])
            p.op("dve", lambda o, oc=oc, a=a:
                 o.scalar_tensor_tensor(x1Ts[:, oc * T + a:oc * T + a + 512], a1s[:, :],
                                        vc[:, 32 + oc:32 + oc + 1], xn1s[:, oc * T + a:oc * T + a + 512],
                                        ALU.mult, ALU.add),
                 waits=[("S_act", p.now("S_act"))], incs=[("S_dve", 1)])
            p.op("dve", lambda o, oc=oc, xb=xb:
                 o.scalar_tensor_tensor(xb[:, :], a1s[:, :], vc[:, 32 + oc:32 + oc + 1], xb[:, :],
                                        ALU.mult, ALU.add),
                 waits=[("ld", ldx)], incs=[("S_dve", 1)])
            p.dma("sy", xr4b.ap()[oc * P:(oc + 1) * P, a:a + 512], xb[:, :], sem="st",
                  waits=[("S_dve", p.now("S_dve"))])
    p.barrier()
    if stage == 1:
        for c in range(DC):
            p.dma("pool", xres5[c * P:(c + 1) * P, :], x1Ts[:, c * T:(c + 1) * T], sem="st")
        p.op("sy", None, waits=[("st", p.now("st"))])
        return

    ln_stats_bf(x1Ts, None)
    ln_apply_bf(x1Ts, n2T)
    p.barrier()

    for k in range(DC):
        p.dma("pool", fc1s[:, k * HID:(k + 1) * HID], fc1T[k * P:(k + 1) * P, :])
    pldf = p.now("pld")
    p.barrier()
    if stage == 2:
        for c in range(DC):
            p.dma("pool", xres5[c * P:(c + 1) * P, :], n2T[:, c * T:(c + 1) * T], sem="st")
        p.op("sy", None, waits=[("st", p.now("st"))])
        return

    for hb in range(HC):
        for k in range(DC):
            p.op("pe", lambda o, hb=hb, k=k:
                 o.matmul(ps8[5][0:P, hb:hb + 1], fc1s[:, k * HID + hb * P:k * HID + (hb + 1) * P],
                          vcb[:, 24 + k:24 + k + 1], start=(k == 0), stop=(k == DC - 1)),
                 waits=([("pld", pldf)] if hb == 0 and k == 0 else []),
                 incs=([("S_pe", 1)] if k == DC - 1 else []))
    p.op("act", lambda o: o.copy(bias2[:, :], ps8[5][:, 0:32]),
         waits=[("S_pe", p.now("S_pe"))], incs=[("S_act", 1)])
    p.op("dve", lambda o: o.tensor_tensor(bias2[:, :], bias2[:, :], hc[:, 64:96], ALU.add),
         waits=[("S_act", p.now("S_act"))], incs=[("S_dve", 1)])
    for k in range(DC):
        p.op("dve", lambda o, k=k:
             o.tensor_scalar(fc1s[:, k * HID:(k + 1) * HID], fc1s[:, k * HID:(k + 1) * HID],
                             vc[:, 16 + k:16 + k + 1], 0.0, ALU.mult, ALU.add),
             waits=[("S_pe", p.now("S_pe"))], incs=[("S_dve", 1)])
    p.barrier()

    for k in range(HC):
        wr = fc2r[k % 2]
        p.dma("pool", wr[:, :], fc2T[k * P:(k + 1) * P, :],
              waits=[("S_pe", p.now("S_pe"))] if k >= 2 else [])
        ldk = p.now("pld")
        for ob in range(DC):
            p.op("pe", lambda o, k=k, ob=ob, wr=wr:
                 o.matmul(ps8[6][0:P, ob:ob + 1], wr[:, ob * P:(ob + 1) * P],
                          hcb[:, 32 + k:32 + k + 1], start=(k == 0), stop=(k == HC - 1)),
                 waits=([("pld", ldk)] if ob == 0 else []),
                 incs=([("S_pe", 1)] if ob == DC - 1 else []))
    p.op("pe", lambda o: o.drain(), incs=[("S_pe", 1)], settle=False)
    p.op("act", lambda o: o.copy(bias3[:, :], ps8[6][:, 0:8]),
         waits=[("S_pe", p.now("S_pe"))], incs=[("S_act", 1)])
    p.op("dve", lambda o: o.tensor_tensor(bias3[:, :], bias3[:, :], vc[:, 56:64], ALU.add),
         waits=[("S_act", p.now("S_act"))], incs=[("S_dve", 1)])
    p.barrier()

    for tg in range(4):
        a = tg * 512
        for hb in range(HC):
            bank = ps8[hb % 2]
            for k in range(DC):
                p.op("pe", lambda o, hb=hb, k=k, a=a, bank=bank:
                     o.matmul(bank[:, :], fc1s[:, k * HID + hb * P:k * HID + (hb + 1) * P],
                              n2T[:, k * T + a:k * T + a + 512], start=(k == 0), stop=(k == DC - 1)),
                     waits=([("S_act", p.now("S_act")), ("S_dve", p.now("S_dve"))] if k == 0 else []),
                     incs=([("S_pe", 1)] if k == DC - 1 else []))
            mmd = p.now("S_pe")
            p.op("act", lambda o, hb=hb, bank=bank:
                 o.activation(hTt[:, hb * 512:(hb + 1) * 512], bank[:, :], AF.Gelu,
                              bias=bias2[:, hb:hb + 1], scale=1.0),
                 waits=[("S_pe", mmd)], incs=[("S_act", 1)])
        p.barrier()
        mmB_done = []
        for k in range(HC):
            sqh = (k % 2) * 512
            p.op("pe", lambda o, k=k:
                 o.matmul(ps8[6][0:1, 0:512], onescolb[:, :], hTt[:, k * 512:(k + 1) * 512],
                          start=(k == 0), stop=(k == HC - 1)), incs=[("S_pe", 1)])
            p.op("act", lambda o, k=k, sqh=sqh:
                 o.activation(sq[:, sqh:sqh + 512], hTt[:, k * 512:(k + 1) * 512], AF.Square),
                 waits=[("S_pe", mmB_done[k - 2] if k >= 2 else 0)], incs=[("S_act", 1)])
            sd = p.now("S_act")
            p.op("pe", lambda o, k=k, sqh=sqh:
                 o.matmul(ps8[7][0:1, 0:512], onescol[:, :], sq[:, sqh:sqh + 512],
                          start=(k == 0), stop=(k == HC - 1)),
                 waits=[("S_act", sd)], incs=[("S_pe", 1)])
            mmB_done.append(p.now("S_pe"))
        p.op("act", lambda o: o.copy(srow[:, :], ps8[6][0:1, 0:512]),
             waits=[("S_pe", p.now("S_pe"))], incs=[("S_act", 1)])
        p.op("act", lambda o: o.copy(srow2[:, :], ps8[7][0:1, 0:512]), incs=[("S_act", 1)])
        p.op("dve", lambda o: o.tensor_scalar(srow[:, :], srow[:, :], 1.0 / HID, 0.0, ALU.mult, ALU.add),
             waits=[("S_act", p.now("S_act"))], incs=[("S_dve", 1)])
        p.op("dve", lambda o: o.tensor_scalar(srow2[:, :], srow2[:, :], 1.0 / HID, 0.0, ALU.mult, ALU.add),
             incs=[("S_dve", 1)])
        p.op("act", lambda o: o.activation(sq[0:1, 0:512], srow[:, :], AF.Square),
             waits=[("S_dve", p.now("S_dve"))], incs=[("S_act", 1)])
        p.op("dve", lambda o: o.tensor_tensor(srow2[:, :], srow2[:, :], sq[0:1, 0:512], ALU.subtract),
             waits=[("S_act", p.now("S_act"))], incs=[("S_dve", 1)])
        p.op("act", lambda o: o.activation(srow2[:, :], srow2[:, :], AF.Sqrt, bias=epsc[0:1, 0:1], scale=1.0),
             incs=[("S_act", 1)])
        p.op("dve", lambda o: o.reciprocal(sq[0:1, 512:1024], srow2[:, :]),
             waits=[("S_act", p.now("S_act"))], incs=[("S_dve", 1)])
        p.op("dve", lambda o: o.scalar_tensor_tensor(srow[:, :], srow[:, :], -1.0, sq[0:1, 512:1024],
                                                     ALU.mult, ALU.mult), incs=[("S_dve", 1)])
        p.barrier()
        p.op("pe", lambda o: o.matmul(ps8[6][:, :], onesrow[0:1, :], sq[0:1, 512:1024],
                                      start=True, stop=True), incs=[("S_pe", 1)])
        p.op("pe", lambda o: o.matmul(ps8[7][:, :], onesrow[0:1, :], srow[0:1, :],
                                      start=True, stop=True), incs=[("S_pe", 1)])
        bd = p.now("S_pe")
        for k in range(HC):
            p.op("dve", lambda o, k=k:
                 o.tensor_tensor(tmp[:, :], hTt[:, k * 512:(k + 1) * 512], ps8[6][:, :], ALU.mult),
                 waits=[("S_pe", bd)], incs=[("S_dve", 1)])
            p.op("dve", lambda o, k=k:
                 o.tensor_tensor(hTt[:, k * 512:(k + 1) * 512], tmp[:, :], ps8[7][:, :], ALU.add),
                 incs=[("S_dve", 1)])
            p.op("dve", lambda o, k=k:
                 o.tensor_scalar(hTt[:, k * 512:(k + 1) * 512], hTt[:, k * 512:(k + 1) * 512],
                                 hc[:, k:k + 1], 0.0, ALU.mult, ALU.add), incs=[("S_dve", 1)])
        p.barrier()
        hd_ = p.now("S_dve")
        kg_done = {}
        for kg in range(HC // 4):
            wr = fc2g[kg % 2]
            p.dma("pool", wr[:, :].rearrange("p (q o) -> p q o", o=D),
                  fc2T[kg * 4 * P:(kg + 1) * 4 * P, :].rearrange("(q p) o -> p q o", p=P),
                  waits=[("S_pe", kg_done.get(kg - 2, 0))] if kg >= 2 else [("S_dve", hd_)])
            ldk = p.now("pld")
            for kq in range(4):
                k = kg * 4 + kq
                for ob in range(DC):
                    p.op("pe", lambda o, k=k, kq=kq, ob=ob, wr=wr:
                         o.matmul(ps8[ob][:, :], wr[:, kq * D + ob * P:kq * D + (ob + 1) * P],
                                  hTt[:, k * 512:(k + 1) * 512],
                                  start=(k == 0), stop=(k == HC - 1)),
                         waits=([("pld", ldk), ("S_dve", hd_)] if ob == 0 and kq == 0 else []),
                         incs=([("S_pe", 1)] if ob == DC - 1 and kq == 3 else []), settle=False)
            kg_done[kg] = p.now("S_pe")
        p.op("pe", lambda o: o.drain(), incs=[("S_pe", 1)], settle=False)
        mmd = p.now("S_pe")
        for ob in range(DC):
            xb = xrs[ob % 2]
            p.dma("sy", xb[:, :], xr4b.ap()[ob * P:(ob + 1) * P, a:a + 512],
                  waits=[("st", p.now("st"))])
            ldx = p.now("ld")
            p.op("act", lambda o, ob=ob:
                 o.activation(a1s[:, :], ps8[ob][:, :], AF.Identity, bias=bias3[:, ob:ob + 1], scale=1.0),
                 waits=[("S_pe", mmd)], incs=[("S_act", 1)])
            p.op("dve", lambda o, ob=ob, xb=xb:
                 o.scalar_tensor_tensor(xb[:, :], a1s[:, :], vc[:, 40 + ob:40 + ob + 1], xb[:, :],
                                        ALU.mult, ALU.add),
                 waits=[("ld", ldx), ("S_act", p.now("S_act"))], incs=[("S_dve", 1)])
            p.dma("sy", xres5[ob * P:(ob + 1) * P, a:a + 512], xb[:, :], sem="st",
                  waits=[("S_dve", p.now("S_dve"))])
        p.barrier()
    p.op("sy", None, waits=[("st", p.now("st"))])



def build_launch_F(stage=99):
    nc = bass.Bass("TRN2", target_bir_lowering=False)
    xn1T = nc.dram_tensor("xn1T", [D, T], BF16, kind="ExternalInput").ap()
    xres4 = nc.dram_tensor("xres4", [D, T], F32, kind="ExternalInput").ap()
    projT = nc.dram_tensor("projT", [D, D], F32, kind="ExternalInput").ap()
    fc1T = nc.dram_tensor("fc1T", [D, 4 * D], F32, kind="ExternalInput").ap()
    fc2T = nc.dram_tensor("fc2T", [4 * D, D], F32, kind="ExternalInput").ap()
    vecs8 = nc.dram_tensor("vecs8", [64, P], F32, kind="ExternalInput").ap()
    vecs32 = nc.dram_tensor("vecs32", [96, P], F32, kind="ExternalInput").ap()
    identf = nc.dram_tensor("identf", [P, P], F32, kind="ExternalInput").ap()
    xres5 = nc.dram_tensor("xres5", [D, T], F32, kind="ExternalOutput").ap()
    xr4b = nc.dram_tensor("xr4bF", [D, T], F32)

    HID = 4 * D
    HC = 32
    io = {}
    io["xn1T"] = xn1T
    io["xres4"] = xres4
    io["projT"] = projT
    io["fc1T"] = fc1T
    io["fc2T"] = fc2T
    io["vecs8"] = vecs8
    io["vecs32"] = vecs32
    io["identf"] = identf
    io["xres5"] = xres5
    io["xr4b"] = xr4b
    with ExitStack() as ctx:
        p = Prog(nc, ctx)
        emit_F(nc, p, ctx, io, stage)
        p.finalize()
    return nc


def emit_G(nc, p, ctx, io, stage=99):
    E = ctx.enter_context
    xn1T = io["xn1T"]
    xres5 = io["xres5"]
    Wg = io["Wg"]
    bg = io["bg1"]
    WeT = io["WeT"]
    be = io["be"]
    identf = io["identf"]
    yout = io["yout"]
    wesc = io["wesc"]
    xn1s = E(nc.sbuf_tensor("xn1sG", [P, DC * T], BF16))
    macc = E(nc.sbuf_tensor("maccG", [P, DC * T], F32))
    xnw = E(nc.sbuf_tensor("xnwG", [P, DC * T], BF16))
    wes = [E(nc.sbuf_tensor(f"wesG{i}", [P, DC * D], BF16)) for i in range(2)]
    ident = E(nc.sbuf_tensor("identG", [P, P], F32))
    onesrow = E(nc.sbuf_tensor("onesrowG", [1, P], F32))
    gw = E(nc.sbuf_tensor("gwG", [P, DC * 8], BF16))
    gbr = E(nc.sbuf_tensor("gbrG", [1, 8], F32))
    bes = E(nc.sbuf_tensor("besG", [8, D], F32))
    wefm = E(nc.sbuf_tensor("wefmG", [8, T], F32))
    werow = E(nc.sbuf_tensor("werowG", [1, T], F32))
    lg = E(nc.sbuf_tensor("lgG", [P, 8], F32))
    pex = E(nc.sbuf_tensor("pexG", [P, 8], F32))
    m1 = E(nc.sbuf_tensor("m1G", [P, 1], F32))
    t1g = E(nc.sbuf_tensor("t1gG", [P, 8], F32))
    wet = E(nc.sbuf_tensor("wetG", [P, 8], F32))
    xg = [E(nc.sbuf_tensor(f"xgG{i}", [P, DC * P], BF16)) for i in range(2)]
    outst = [E(nc.sbuf_tensor(f"outstG{i}", [P, D], F32)) for i in range(2)]
    ps8 = [E(nc.psum_tensor(f"ps8G{i}", [P, 512], F32)) for i in range(8)]

    p.dma("sy", ident[:, :], identf)
    p.op("pool", lambda o: o.memset(onesrow[:, :], 1.0), incs=[("S_pool", 1)])
    p.dma("pool", gw[:, :].rearrange("p (k e) -> p k e", e=8), Wg.rearrange("(k p) e -> p k e", p=P))
    p.dma("sy", gbr[:, :], bg)
    p.dma("sy", bes[:, :], be)
    for c in range(DC):
        p.dma("pool", xn1s[:, c * T:(c + 1) * T], xn1T[c * P:(c + 1) * P, :])
        p.dma("sy", macc[:, c * T:(c + 1) * T], xres5[c * P:(c + 1) * P, :])
    ldg = p.now("ld")
    pldg = p.now("pld")
    for tt in range(NTC):
        t0 = tt * P
        for k in range(DC):
            p.op("pe", lambda o, k=k, t0=t0:
                 o.matmul(ps8[1][0:P, 0:8], xn1s[:, k * T + t0:k * T + t0 + P],
                          gw[:, k * 8:(k + 1) * 8], start=(k == 0), stop=False),
                 waits=([("pld", pldg), ("ld", ldg), ("S_act", p.now("S_act"))] if k == 0 else []))
        p.op("pe", lambda o:
             o.matmul(ps8[1][0:P, 0:8], onesrow[0:1, 0:P], gbr[0:1, :], start=False, stop=True),
             incs=[("S_pe", 1)])
        p.op("act", lambda o: o.copy(lg[:, :], ps8[1][0:P, 0:8]),
             waits=[("S_pe", p.now("S_pe"))], incs=[("S_act", 1)])
        p.op("dve", lambda o: o.reduce_max(m1[:, :], lg[:, :], AX.X),
             waits=[("S_act", p.now("S_act"))], incs=[("S_dve", 1)])
        p.op("dve", lambda o: o.tensor_scalar(m1[:, :], m1[:, :], -1.0, 0.0, ALU.mult, ALU.add),
             incs=[("S_dve", 1)])
        p.op("act", lambda o: o.activation(pex[:, :], lg[:, :], AF.Exp, bias=m1[:, 0:1], scale=1.0),
             waits=[("S_dve", p.now("S_dve"))], incs=[("S_act", 1)])
        p.op("dve", lambda o: o.reduce_max(m1[:, :], pex[:, :], AX.X),
             waits=[("S_act", p.now("S_act"))], incs=[("S_dve", 1)])
        p.op("dve", lambda o: o.tensor_scalar(t1g[:, :], pex[:, :], m1[:, 0:1], 0.0, ALU.is_ge, ALU.add),
             incs=[("S_dve", 1)])
        p.op("dve", lambda o: o.tensor_scalar(t1g[:, :], t1g[:, :], -1.0, 1.0, ALU.mult, ALU.add),
             incs=[("S_dve", 1)])
        p.op("dve", lambda o: o.tensor_tensor(t1g[:, :], pex[:, :], t1g[:, :], ALU.mult),
             incs=[("S_dve", 1)])
        p.op("dve", lambda o: o.reduce_max(m1[:, :], t1g[:, :], AX.X), incs=[("S_dve", 1)])
        p.op("dve", lambda o: o.tensor_scalar(t1g[:, :], pex[:, :], m1[:, 0:1], 0.0, ALU.is_ge, ALU.add),
             incs=[("S_dve", 1)])
        p.op("dve", lambda o: o.tensor_tensor(wet[:, :], pex[:, :], t1g[:, :], ALU.mult),
             incs=[("S_dve", 1)])
        p.op("dve", lambda o: o.reduce_sum(m1[:, :], wet[:, :], AX.X), incs=[("S_dve", 1)])
        p.op("dve", lambda o: o.reciprocal(m1[:, :], m1[:, :]), incs=[("S_dve", 1)])
        p.op("dve", lambda o: o.tensor_scalar(wet[:, :], wet[:, :], m1[:, 0:1], 0.0, ALU.mult, ALU.add),
             incs=[("S_dve", 1)])
        p.op("pe", lambda o: o.transpose(ps8[2][0:8, 0:P], wet[:, :], ident[:, :]),
             waits=[("S_dve", p.now("S_dve")), ("S_act", p.now("S_act"))], incs=[("S_pe", 1)])
        p.op("act", lambda o, t0=t0: o.copy(wefm[:, t0:t0 + P], ps8[2][0:8, 0:P]),
             waits=[("S_pe", p.now("S_pe"))], incs=[("S_act", 1)])
    p.barrier()
    p.dma("sy", wesc.ap(), wefm[:, :], sem="st")
    p.op("sy", None, waits=[("st", p.now("st"))])
    for ob in range(DC):
        for a in range(0, T, 512):
            bank = ps8[(a // 512) % 2]
            p.op("pe", lambda o, ob=ob, a=a, bank=bank:
                 o.matmul(bank[:, :], bes[:, ob * P:(ob + 1) * P], wefm[0:8, a:a + 512],
                          start=True, stop=True),
                 waits=[("S_dve", p.now("S_dve"))], incs=[("S_pe", 1)])
            p.op("dve", lambda o, ob=ob, a=a, bank=bank:
                 o.tensor_tensor(macc[:, ob * T + a:ob * T + a + 512],
                                 macc[:, ob * T + a:ob * T + a + 512], bank[:, :], ALU.add),
                 waits=[("S_pe", p.now("S_pe")), ("ld", p.now("ld"))], incs=[("S_dve", 1)])
    p.barrier()
    if stage == 1:
        for c in range(DC):
            p.dma("sy", yout.rearrange("t o -> o t")[c * P:(c + 1) * P, :],
                  macc[:, c * T:(c + 1) * T], sem="st")
        p.op("sy", None, waits=[("st", p.now("st"))])
        return

    lde_marks = {}
    p.dma("pool", wes[0][:, :].rearrange("p (k o) -> p k o", o=D),
          WeT.rearrange("e (k p) o -> e p k o", p=P)[0])
    lde_marks[0] = p.now("pld")
    for e in range(8):
        p.dma("sy", werow[:, :], wesc.ap()[e:e + 1, :], waits=[("S_pe", p.now("S_pe"))])
        if e + 1 < 8:
            p.dma("pool", wes[(e + 1) % 2][:, :].rearrange("p (k o) -> p k o", o=D),
                  WeT.rearrange("e (k p) o -> e p k o", p=P)[e + 1],
                  waits=[("S_pe", p.now("S_pe"))])
            lde_marks[e + 1] = p.now("pld")
        lde = lde_marks[e]
        ldw = p.now("ld")
        wese = wes[e % 2]
        for a in range(0, T, 512):
            p.op("pe", lambda o, a=a:
                 o.matmul(ps8[7][:, :], onesrow[0:1, :], werow[0:1, a:a + 512], start=True, stop=True),
                 waits=[("ld", ldw), ("S_dve", p.now("S_dve"))], incs=[("S_pe", 1)])
            wd = p.now("S_pe")
            for k in range(DC):
                eng = "dve" if k % 2 == 0 else "dve"
                p.op(eng, lambda o, k=k, a=a:
                     o.tensor_tensor(xnw[:, k * T + a:k * T + a + 512],
                                     xn1s[:, k * T + a:k * T + a + 512], ps8[7][:, :], ALU.mult),
                     waits=[("S_pe", wd)], incs=[("S_dve", 1)])
        xd = p.now("S_dve")
        for oc in range(DC):
            for a in range(0, T, 512):
                bank = ps8[(oc * 4 + a // 512) % 4]
                for k in range(DC):
                    p.op("pe", lambda o, oc=oc, a=a, k=k, bank=bank, wese=wese:
                         o.matmul(bank[:, :], wese[:, k * D + oc * P:k * D + (oc + 1) * P],
                                  xnw[:, k * T + a:k * T + a + 512], start=(k == 0), stop=(k == DC - 1)),
                         waits=([("pld", lde), ("S_dve", p.now("S_dve"))] if k == 0 else []),
                         incs=([("S_pe", 1)] if k == DC - 1 else []))
                mmd = p.now("S_pe")
                p.op("dve", lambda o, oc=oc, a=a, bank=bank:
                     o.tensor_tensor(macc[:, oc * T + a:oc * T + a + 512],
                                     macc[:, oc * T + a:oc * T + a + 512], bank[:, :], ALU.add),
                     waits=[("S_pe", mmd)], incs=[("S_dve", 1)])
        p.barrier()
    for tt in range(NTC):
        t0 = tt * P
        ob_ = outst[tt % 2]
        for oc in range(DC):
            bank = ps8[oc % 2]
            p.op("pe", lambda o, oc=oc, t0=t0, bank=bank:
                 o.transpose(bank[:, 0:P], macc[:, oc * T + t0:oc * T + t0 + P], ident[:, :]),
                 waits=[("S_act", p.now("S_act")), ("st", p.now("st"))], incs=[("S_pe", 1)])
            p.op("act", lambda o, oc=oc, bank=bank, ob_=ob_:
                 o.copy(ob_[:, oc * P:(oc + 1) * P], bank[:, 0:P]),
                 waits=[("S_pe", p.now("S_pe"))], incs=[("S_act", 1)])
        p.dma("sy", yout[t0:t0 + P, :], ob_[:, :], sem="st",
              waits=[("S_act", p.now("S_act"))])
    p.op("sy", None, waits=[("st", p.now("st"))])



def build_launch_G(stage=99):
    nc = bass.Bass("TRN2", target_bir_lowering=False)
    xn1T = nc.dram_tensor("xn1T", [D, T], BF16, kind="ExternalInput").ap()
    xres5 = nc.dram_tensor("xres5", [D, T], F32, kind="ExternalInput").ap()
    Wg = nc.dram_tensor("Wg", [D, 8], F32, kind="ExternalInput").ap()
    bg = nc.dram_tensor("bg", [1, 8], F32, kind="ExternalInput").ap()
    WeT = nc.dram_tensor("WeT", [8, D, D], F32, kind="ExternalInput").ap()
    be = nc.dram_tensor("be", [8, D], F32, kind="ExternalInput").ap()
    identf = nc.dram_tensor("identf", [P, P], F32, kind="ExternalInput").ap()
    yout = nc.dram_tensor("yout", [T, D], F32, kind="ExternalOutput").ap()
    wesc = nc.dram_tensor("wescG", [8, T], F32)

    io = {}
    io["xn1T"] = xn1T
    io["xres5"] = xres5
    io["Wg"] = Wg
    io["bg1"] = bg
    io["WeT"] = WeT
    io["be"] = be
    io["identf"] = identf
    io["yout"] = yout
    io["wesc"] = wesc
    with ExitStack() as ctx:
        p = Prog(nc, ctx)
        emit_G(nc, p, ctx, io, stage)
        p.finalize()
    return nc




def build_fused(dump=()):
    nc = bass.Bass("TRN2", target_bir_lowering=False)
    io = {}

    def ein(n, s, dt=F32):
        io[n] = nc.dram_tensor(n, s, dt, kind="ExternalInput").ap()

    def itn(k, n, s, dt):
        kind = "ExternalOutput" if k in dump else "Internal"
        io[k] = nc.dram_tensor(n, s, dt, kind=kind).ap()

    ein("x_in", [TH, D]); ein("identf", [P, P]); ein("mask01T", [P, P])
    ein("selT", [NH, P]); ein("sel2", [NH, 16]); ein("cmask", [1, NCORES])
    ein("winT", [D, 4384]); ein("convw", [DI, 4]); ein("convb", [DI])
    ein("alog", [NH]); ein("dtb", [NH]); ein("nw", [D]); ein("woT", [DI, D])
    ein("dvec", [NH])
    ein("gatew", [D, 8]); ein("gateb", [1, 8])
    ein("w1", [8, D, DI]); ein("w2", [8, D, DI]); ein("wg", [8, DI, DI]); ein("w3", [8, DI, D])
    ein("b1", [8, DI]); ein("b2", [8, DI]); ein("bg", [8, DI]); ein("b3", [8, D])
    ein("projT", [D, D]); ein("fc1T", [D, 4 * D]); ein("fc2T", [4 * D, D])
    ein("vecs8", [64, P]); ein("vecs32", [96, P])
    ein("Wg", [D, 8]); ein("bg1", [1, 8]); ein("WeT", [8, D, D]); ein("be", [8, D])
    itn("xnT", "i_xnT", [D, TH], BF16)
    itn("xres", "i_xres", [D, T], F32)
    itn("szT", "i_szT", [DI, T], BF16)
    itn("xcvT", "i_xcvT", [DI, T], BF16)
    itn("BTo", "i_BTo", [DS, T], BF16)
    itn("CTo", "i_CTo", [DS, T], BF16)
    itn("cumo", "i_cumo", [NH, T], F32)
    itn("dtfo", "i_dtfo", [NH, T], F32)
    itn("hino", "i_hino", [DS, NH * HD], F32)
    itn("xres2", "i_xres2", [D, T], F32)
    itn("xres3", "i_xres3", [D, T], F32)
    itn("xn1T", "i_xn1T", [D, T], BF16)
    itn("xres4", "i_xres4", [D, T], F32)
    itn("xres5", "i_xres5", [D, T], F32)
    io["xr4b"] = nc.dram_tensor("i_xr4b", [D, T], F32)
    io["wesc"] = nc.dram_tensor("i_wesc", [8, T], F32)
    io["scr1"] = nc.dram_tensor("i_scr1", [NH, P], F32)
    io["scr2"] = nc.dram_tensor("i_scr2", [NH, P], F32)
    io["cc_in_s"] = nc.dram_tensor("cc_in_s", [DS, NH * HD], F32)
    io["cc_out_s"] = nc.dram_tensor("cc_out_s", [NCORES * DS, NH * HD], F32, addr_space="Shared")
    io["cc_in_l"] = nc.dram_tensor("cc_in_l", [NH, 1], F32)
    io["cc_out_l"] = nc.dram_tensor("cc_out_l", [NCORES * NH, 1], F32, addr_space="Shared")
    io["yout"] = nc.dram_tensor("yout", [T, D], F32, kind="ExternalOutput").ap()

    with ExitStack() as ctx:
        p = Prog(nc, ctx)
        for emit in (emit_A, emit_B, emit_C, emit_D, emit_E, emit_F, emit_G):
            with ExitStack() as c2:
                emit(nc, p, c2, io)
            p.barrier()
        p.op("sy", None, waits=[("st", p.now("st")), ("cc", p.now("cc"))])
        p.finalize()
    return nc


def shard_inputs(L0_attn_x):
    x = np.asarray(L0_attn_x).reshape(16384, D)
    shards = []
    for i in range(NCORES):
        lo = i * T
        halo = np.zeros((3, D), np.float32) if i == 0 else x[lo - 3:lo]
        shards.append(np.ascontiguousarray(np.concatenate([halo, x[lo:lo + T]], axis=0)))
    return shards


def host_consts():
    identf = np.eye(P, dtype=np.float32)
    mask01T = np.triu(np.ones((P, P), np.float32))
    selT = np.zeros((NH, P), np.float32)
    for h in range(NH):
        selT[h, (h % 2) * HD:(h % 2 + 1) * HD] = 1.0
    sel2 = np.zeros((NH, 16), np.float32)
    for h in range(NH):
        sel2[h, h // 2] = 1.0
    return dict(identf=identf, mask01T=mask01T, selT=selT, sel2=sel2)


def _f32(a):
    return np.ascontiguousarray(np.asarray(a, dtype=np.float32))


def _pack_rows(inputs, names):
    return np.ascontiguousarray(np.concatenate(
        [_f32(inputs[n]).reshape(-1, P) for n in names], 0))


def kernel(**inputs):
    consts = host_consts()
    identf = consts["identf"]
    shards = shard_inputs(inputs["L0_attn_x"])
    cores = list(range(NCORES))

    winT = np.ascontiguousarray(_f32(inputs["L0_attn_in_proj_w"]).T)
    convw = _f32(inputs["L0_attn_conv_w"]).reshape(DI, 4)
    woT = np.ascontiguousarray(_f32(inputs["L0_attn_out_proj_w"]).T)
    projT = np.ascontiguousarray(_f32(inputs["L1_attn_proj_w"]).T)
    fc1T = np.ascontiguousarray(_f32(inputs["L1_attn_fc1_w"]).T)
    fc2T = np.ascontiguousarray(_f32(inputs["L1_attn_fc2_w"]).T)
    WeT = np.ascontiguousarray(np.transpose(_f32(inputs["L1_ffn_We"]), (0, 2, 1)))
    vecs8 = _pack_rows(inputs, ["L1_attn_ln1_g", "L1_attn_ln1_b", "L1_attn_ln2_g",
                                "L1_attn_ln2_b", "L1_attn_gamma1", "L1_attn_gamma2",
                                "L1_attn_proj_b", "L1_attn_fc2_b"])
    vecs32 = _pack_rows(inputs, ["L1_attn_lnh_g", "L1_attn_lnh_b", "L1_attn_fc1_b"])

    nc = build_fused()
    in_maps = []
    for i in cores:
        cm = np.zeros((1, NCORES), np.float32)
        cm[0, :i] = 1.0
        in_maps.append(dict(
            x_in=shards[i], identf=identf, mask01T=consts["mask01T"],
            selT=consts["selT"], sel2=consts["sel2"], cmask=cm,
            winT=winT, convw=convw, convb=_f32(inputs["L0_attn_conv_b"]),
            alog=_f32(inputs["L0_attn_A_log"]), dtb=_f32(inputs["L0_attn_dt_bias"]),
            nw=_f32(inputs["L0_attn_norm_w"]), woT=woT, dvec=_f32(inputs["L0_attn_D"]),
            gatew=_f32(inputs["L0_ffn_gate_w"]), gateb=_f32(inputs["L0_ffn_gate_b"]).reshape(1, 8),
            w1=_f32(inputs["L0_ffn_w1"]), w2=_f32(inputs["L0_ffn_w2"]),
            wg=_f32(inputs["L0_ffn_wg"]), w3=_f32(inputs["L0_ffn_w3"]),
            b1=_f32(inputs["L0_ffn_b1"]), b2=_f32(inputs["L0_ffn_b2"]),
            bg=_f32(inputs["L0_ffn_bg"]), b3=_f32(inputs["L0_ffn_b3"]),
            projT=projT, fc1T=fc1T, fc2T=fc2T, vecs8=vecs8, vecs32=vecs32,
            Wg=_f32(inputs["L1_ffn_Wg"]), bg1=_f32(inputs["L1_ffn_bg"]).reshape(1, 8),
            WeT=WeT, be=_f32(inputs["L1_ffn_be"])))
    res = run_bass_kernel_spmd(nc, in_maps, core_ids=cores)
    out = np.concatenate([np.asarray(res.results[i]["yout"]) for i in cores], axis=0)
    return out.reshape(1, 16384, D).astype(np.float32)


# version 07
# speedup vs baseline: 1.2424x; 1.1147x over previous
import sys
sys.path.insert(0, '/opt/trn_rl_repo')
from contextlib import ExitStack

import numpy as np
import ml_dtypes
import concourse.bass as bass
import concourse.mybir as mybir
from concourse.bass_utils import run_bass_kernel_spmd

F32 = mybir.dt.float32
BF16 = mybir.dt.bfloat16
AF = mybir.ActivationFunctionType
ALU = mybir.AluOpType
AX = mybir.AxisListType

P = 128
T = 2048
TH = 2051
NCORES = 8
D = 1024
DC = 8
DI = 2048
DIC = 16
NH = 32
HD = 64
DS = 128
NTC = 16
BF = ml_dtypes.bfloat16


class Prog:

    ENGINES = ("pe", "act", "dve", "pool", "sy")

    def __init__(self, nc, ctx):
        self.nc = nc
        self.ctx = ctx
        self.steps = {e: [] for e in self.ENGINES}
        self.sems = {}
        self.cnt = {}
        for s in ("ld", "pld", "st", "S_pe", "S_act", "S_dve", "S_pool", "S_sy", "cc"):
            self.sems[s] = ctx.enter_context(nc.semaphore(s))
            self.cnt[s] = 0

    def now(self, sem):
        return self.cnt[sem]

    def op(self, eng, emit, waits=(), incs=(), settle=None):
        waits = tuple(waits)
        incs = tuple(incs)
        if settle is None:
            if eng in ("act", "dve", "pool"):
                settle = emit is not None
            else:
                settle = any(s.startswith("S_") for s, _ in incs)
        for s, a in incs:
            self.cnt[s] += a
        if settle and emit is not None:
            self.steps[eng].append((emit, waits, ()))
            self.steps[eng].append((lambda o: o.drain(), (), incs))
        else:
            self.steps[eng].append((emit, waits, incs))

    def dma(self, qeng, out, in_, sem=None, waits=()):
        if sem is None:
            sem = "pld" if qeng == "pool" else ("ld" if qeng == "sy" else "ld")
        self.op(qeng, lambda o, out=out, in_=in_: o.dma_start(out=out, in_=in_),
                waits=waits, incs=[(sem, 16)], settle=False)
        return self.cnt[sem]

    def barrier(self):
        marks = {}
        for e in self.ENGINES:
            s = {"pe": "S_pe", "act": "S_act", "dve": "S_dve", "pool": "S_pool", "sy": "S_sy"}[e]
            if e in ("act", "dve", "pool", "pe"):
                self.op(e, lambda o: o.drain(), incs=[(s, 1)], settle=False)
            else:
                self.op(e, lambda o: o.nop(), incs=[(s, 1)], settle=False)
            marks[s] = self.cnt[s]
        dma_marks = [(s, self.cnt[s]) for s in ("ld", "pld", "st")]
        waits = [(s, v) for s, v in marks.items()] + dma_marks
        for e in self.ENGINES:
            self.op(e, None, waits=waits)

    def finalize(self):
        nc = self.nc
        block = self.ctx.enter_context(nc.Block())
        prog = self

        def runner(eng_name):
            def _(obj):
                for emit, waits, incs in prog.steps[eng_name]:
                    for s, v in waits:
                        if v > 0:
                            obj.wait_ge(prog.sems[s], v)
                    if emit is None:
                        if incs:
                            inst = obj.nop()
                            for s, a in incs:
                                inst.then_inc(prog.sems[s], a)
                        continue
                    inst = emit(obj)
                    for s, a in incs:
                        inst.then_inc(prog.sems[s], a)
            return _

        block.tensor(runner("pe"))
        block.scalar(runner("act"))
        block.vector(runner("dve"))
        block.gpsimd(runner("pool"))
        block.sync(runner("sy"))


def ceil_div(a, b):
    return -(-a // b)


def col_groups(total, width=512):
    out = []
    a = 0
    while a < total:
        out.append((a, min(a + width, total)))
        a = out[-1][1]
    return out


def _dump_A(p, xnT, xres, xfm, xnbf):
    for c in range(DC):
        p.dma("sy", xres[c * P:(c + 1) * P, 0:T], xfm[:, c * TH + 3: c * TH + 3 + T], sem="st")
    p.op("sy", None, waits=[("st", p.now("st"))])


def _dump_rows_A(p, xres, r1, r2):
    p.dma("sy", xres[0:1, 0:T], r1[0:1, 3:3 + T], sem="st")
    p.dma("sy", xres[1:2, 0:T], r2[0:1, 3:3 + T], sem="st")
    p.op("sy", None, waits=[("st", p.now("st"))])


def emit_A(nc, p, ctx, io, stage=99):
    E = ctx.enter_context
    x_in = io["x_in"]
    identf = io["identf"]
    xnT = io["xnT"]
    xres = io["xres"]
    xfm = E(nc.sbuf_tensor("xfm", [P, DC * TH], F32))
    xnbf = E(nc.sbuf_tensor("xnbf", [P, DC * TH], BF16))
    xt0 = E(nc.sbuf_tensor("xt0", [P, D], F32))
    xt1 = E(nc.sbuf_tensor("xt1", [P, D], F32))
    ident = E(nc.sbuf_tensor("ident", [P, P], F32))
    sq = E(nc.sbuf_tensor("sq", [P, 1024], F32))
    onescol = E(nc.sbuf_tensor("onescol", [P, 1], F32))
    onesrow = E(nc.sbuf_tensor("onesrow", [1, P], F32))
    s1r = E(nc.sbuf_tensor("s1r", [1, TH], F32))
    s2r = E(nc.sbuf_tensor("s2r", [1, TH], F32))
    varr = E(nc.sbuf_tensor("varr", [1, TH], F32))
    rrow = E(nc.sbuf_tensor("rrow", [1, TH], F32))
    mrow = E(nc.sbuf_tensor("mrow", [1, TH], F32))
    epsc = E(nc.sbuf_tensor("epsc", [1, 1], F32))
    bc1 = E(nc.sbuf_tensor("bc1", [P, TH], F32))
    bc2 = E(nc.sbuf_tensor("bc2", [P, TH], F32))
    tmp = E(nc.sbuf_tensor("tmpx", [P, 512], F32))
    xrst = E(nc.sbuf_tensor("xrst", [P, 512], F32))
    pst = [E(nc.psum_tensor(f"pst{i}", [P, 512], F32)) for i in range(2)]
    psA = E(nc.psum_tensor("psA", [P, 512], F32))
    psB = E(nc.psum_tensor("psB", [P, 512], F32))
    psC = E(nc.psum_tensor("psC", [P, 512], F32))

    p.dma("sy", ident[:, :], identf)
    p.op("pool", lambda o: o.memset(onescol[:, :], 1.0))
    p.op("pool", lambda o: o.memset(onesrow[:, :], 1.0))
    p.op("pool", lambda o: o.memset(epsc[:, :], 1e-5), incs=[("S_pool", 1)])
    ldw = p.now("ld")

    tiles = [(3, 0)] + [(128, 3 + 128 * i) for i in range(NTC)]
    tbuf = [xt0, xt1]
    tld = [0, 0]
    tfree = [0, 0]
    ti = 0
    for rows, coff in tiles:
        b = ti % 2
        src = x_in[0:3, :] if rows == 3 else x_in[coff:coff + 128, :]
        tld[b] = p.dma("sy", tbuf[b][0:rows, :], src,
                       waits=[("S_act", tfree[b]), ("ld", ldw)])
        act_base = p.now("S_act")
        for c in range(DC):
            pb = pst[c % 2]
            need_act = max(0, act_base + c - 1)
            p.op("pe", lambda o, b=b, c=c, rows=rows, pb=pb:
                 o.transpose(pb[:, 0:rows], tbuf[b][0:rows, c * P:(c + 1) * P],
                             ident[0:rows, 0:rows]),
                 waits=[("ld", tld[b]), ("S_act", need_act)],
                 incs=[("S_pe", 1)], settle=(rows < 64))
            p.op("act", lambda o, c=c, rows=rows, coff=coff, pb=pb:
                 o.copy(xfm[:, c * TH + coff: c * TH + coff + rows], pb[:, 0:rows]),
                 waits=[("S_pe", p.now("S_pe"))], incs=[("S_act", 1)])
        tfree[b] = p.now("S_act")
        ti += 1
    p.barrier()
    if stage == 1:
        _dump_A(p, xnT, xres, xfm, xnbf)
        return

    evacA = p.now("S_act")
    evacB = p.now("S_act")
    for a, bnd in col_groups(TH):
        w = bnd - a
        sq_done = []
        mmB_done = []
        for c in range(DC):
            p.op("pe", lambda o, c=c, a=a, w=w:
                 o.matmul(psA[0:1, 0:w], onescol[:, :], xfm[:, c * TH + a: c * TH + a + w],
                          start=(c == 0), stop=(c == DC - 1)),
                 waits=[("S_act", evacA if c == 0 else 0)], incs=[("S_pe", 1)],
                 settle=(w < 64))
            sqh = (c % 2) * 512
            p.op("act", lambda o, c=c, a=a, w=w, sqh=sqh:
                 o.activation(sq[:, sqh:sqh + w], xfm[:, c * TH + a: c * TH + a + w], AF.Square),
                 waits=[("S_pe", mmB_done[c - 2] if c >= 2 else 0)],
                 incs=[("S_act", 1)])
            sq_done.append(p.now("S_act"))
            p.op("pe", lambda o, c=c, a=a, w=w, sqh=sqh:
                 o.matmul(psB[0:1, 0:w], onescol[:, :], sq[:, sqh:sqh + w],
                          start=(c == 0), stop=(c == DC - 1)),
                 waits=[("S_act", sq_done[c]), ("S_act", evacB if c == 0 else 0)],
                 incs=[("S_pe", 1)], settle=(w < 64))
            mmB_done.append(p.now("S_pe"))
        p.op("act", lambda o, a=a, w=w: o.copy(s1r[:, a:a + w], psA[0:1, 0:w]),
             waits=[("S_pe", p.now("S_pe"))], incs=[("S_act", 1)])
        evacA = p.now("S_act")
        p.op("act", lambda o, a=a, w=w: o.copy(s2r[:, a:a + w], psB[0:1, 0:w]),
             incs=[("S_act", 1)])
        evacB = p.now("S_act")
    p.barrier()
    if stage == 2:
        _dump_rows_A(p, xres, s1r, s2r)
        return

    p.op("dve", lambda o: o.tensor_scalar(mrow[:, :], s1r[:, :], 1.0 / D, 0.0, ALU.mult, ALU.add))
    p.op("dve", lambda o: o.tensor_scalar(varr[:, :], s2r[:, :], 1.0 / D, 0.0, ALU.mult, ALU.add), incs=[("S_dve", 1)])
    p.op("act", lambda o: o.activation(s2r[:, :], mrow[:, :], AF.Square),
         waits=[("S_dve", p.now("S_dve"))], incs=[("S_act", 1)])
    p.op("dve", lambda o: o.tensor_tensor(s1r[:, :], varr[:, :], s2r[:, :], ALU.subtract),
         waits=[("S_act", p.now("S_act"))], incs=[("S_dve", 1)])
    p.op("act", lambda o: o.activation(varr[:, :], s1r[:, :], AF.Sqrt, bias=epsc[0:1, 0:1], scale=1.0),
         waits=[("S_dve", p.now("S_dve")), ("S_pool", p.now("S_pool"))], incs=[("S_act", 1)])
    p.op("dve", lambda o: o.reciprocal(rrow[:, :], varr[:, :]),
         waits=[("S_act", p.now("S_act"))], incs=[("S_dve", 1)])
    p.op("dve", lambda o: o.scalar_tensor_tensor(s2r[:, :], mrow[:, :], -1.0, rrow[:, :], ALU.mult, ALU.mult),
         incs=[("S_dve", 1)])
    p.op("dve", lambda o: o.tensor_copy(mrow[:, :], s2r[:, :]), incs=[("S_dve", 1)])
    p.barrier()
    if stage == 4:
        pass
    if stage == 3:
        p.dma("sy", xres[0:1, 0:T], rrow[0:1, 3:3 + T], sem="st")
        p.dma("sy", xres[1:2, 0:T], mrow[0:1, 3:3 + T], sem="st")
        p.dma("sy", xres[2:3, 0:T], s1r[0:1, 3:3 + T], sem="st")
        p.dma("sy", xres[3:4, 0:T], varr[0:1, 3:3 + T], sem="st")
        p.dma("sy", xres[4:5, 0:T], s2r[0:1, 3:3 + T], sem="st")
        p.op("sy", None, waits=[("st", p.now("st"))])
        return

    for a, bnd in col_groups(TH):
        w = bnd - a
        p.op("pe", lambda o, a=a, w=w:
             o.matmul(psC[:, 0:w], onesrow[0:1, :], rrow[0:1, a:a + w], start=True, stop=True),
             waits=[("S_act", p.now("S_act"))], incs=[("S_pe", 1)], settle=(w < 64))
        p.op("act", lambda o, a=a, w=w: o.copy(bc1[:, a:a + w], psC[:, 0:w]),
             waits=[("S_pe", p.now("S_pe"))], incs=[("S_act", 1)])
        p.op("pe", lambda o, a=a, w=w:
             o.matmul(psC[:, 0:w], onesrow[0:1, :], mrow[0:1, a:a + w], start=True, stop=True),
             waits=[("S_act", p.now("S_act"))], incs=[("S_pe", 1)], settle=(w < 64))
        p.op("act", lambda o, a=a, w=w: o.copy(bc2[:, a:a + w], psC[:, 0:w]),
             waits=[("S_pe", p.now("S_pe"))], incs=[("S_act", 1)])
    p.barrier()
    if stage == 4:
        for c in range(4):
            p.dma("sy", xres[c * P:(c + 1) * P, 0:T], bc1[:, 3:3 + T], sem="st")
            p.dma("sy", xres[(4 + c) * P:(5 + c) * P, 0:T], bc2[:, 3:3 + T], sem="st")
        p.op("sy", None, waits=[("st", p.now("st"))])
        return

    for c in range(DC):
        for a, bnd in col_groups(TH):
            w = bnd - a
            p.op("dve", lambda o, c=c, a=a, w=w:
                 o.tensor_tensor(tmp[:, 0:w], xfm[:, c * TH + a: c * TH + a + w], bc1[:, a:a + w], ALU.mult),
                 waits=[("st", p.now("st"))], incs=[("S_dve", 1)])
            p.op("dve", lambda o, c=c, a=a, w=w:
                 o.tensor_tensor(xnbf[:, c * TH + a: c * TH + a + w], tmp[:, 0:w], bc2[:, a:a + w], ALU.add),
                 incs=[("S_dve", 1)])
        p.dma("sy", xnT[c * P:(c + 1) * P, :], xnbf[:, c * TH: (c + 1) * TH],
              sem="st", waits=[("S_dve", p.now("S_dve"))])
    p.barrier()
    for c in range(DC):
        for a in range(0, T, 512):
            w = 512
            p.op("dve", lambda o, c=c, a=a, w=w:
                 o.tensor_tensor(xrst[:, 0:w], xfm[:, c * TH + 3 + a: c * TH + 3 + a + w],
                                 xnbf[:, c * TH + 3 + a: c * TH + 3 + a + w], ALU.add),
                 waits=[("st", p.now("st"))], incs=[("S_dve", 1)])
            p.dma("sy", xres[c * P:(c + 1) * P, a:a + w], xrst[:, 0:w],
                  sem="st", waits=[("S_dve", p.now("S_dve"))])
    p.op("sy", None, waits=[("st", p.now("st"))])



def build_launch_A(stage=99):
    nc = bass.Bass("TRN2", target_bir_lowering=False)
    x_in = nc.dram_tensor("x_in", [TH, D], F32, kind="ExternalInput").ap()
    identf = nc.dram_tensor("identf", [P, P], F32, kind="ExternalInput").ap()
    xnT = nc.dram_tensor("xnT", [D, TH], BF16, kind="ExternalOutput").ap()
    xres = nc.dram_tensor("xres", [D, T], F32, kind="ExternalOutput").ap()

    io = {}
    io["x_in"] = x_in
    io["identf"] = identf
    io["xnT"] = xnT
    io["xres"] = xres
    with ExitStack() as ctx:
        p = Prog(nc, ctx)
        emit_A(nc, p, ctx, io, stage)
        p.finalize()
    return nc


def emit_B(nc, p, ctx, io, stage=99):
    E = ctx.enter_context
    xnT = io["xnT"]
    winT = io["winT"]
    convw = io["convw"]
    convb = io["convb"]
    alog = io["alog"]
    dtb = io["dtb"]
    nw = io["nw"]
    cmask = io["cmask"]
    identf = io["identf"]
    szT = io["szT"]
    xcvT = io["xcvT"]
    BTo = io["BTo"]
    CTo = io["CTo"]
    cumo = io["cumo"]
    dtfo = io["dtfo"]
    hino = io["hino"]
    cc_in_s = io["cc_in_s"]
    cc_out_s = io["cc_out_s"]
    cc_in_l = io["cc_in_l"]
    cc_out_l = io["cc_out_l"]
    big = E(nc.sbuf_tensor("bigB", [P, DC * 4384], BF16))
    xnTs = big[:, 0:DC * TH]
    winTs = big[:, :]
    xcv = big[:, 0:DIC * T]
    big2 = E(nc.sbuf_tensor("big2B", [P, DIC * TH], BF16))
    xssmT = big2[:, :]
    hTa = big2[:, 0:4096].bitcast(F32)
    hTb = big2[:, 4096:8192].bitcast(F32)
    xctok = big2[:, 8192:10240]
    dtx = big2[:, 10240:12288]
    ddtx = big2[:, 12288:14336]
    tmpd = big2[:, 14336:15360].bitcast(F32)
    sgr = big2[:, 15360:19456].bitcast(F32)
    hins = big2[:, 19456:23552].bitcast(F32)
    xn2T = E(nc.sbuf_tensor("xn2T", [P, DC * TH], BF16))
    BTs = E(nc.sbuf_tensor("BTs", [P, TH], BF16))
    ident = E(nc.sbuf_tensor("identB", [P, P], F32))
    identb = E(nc.sbuf_tensor("identbB", [P, P], BF16))
    onescol = E(nc.sbuf_tensor("onescolB", [P, 1], F32))
    onesrow = E(nc.sbuf_tensor("onesrowB", [1, P], F32))
    dtraw = E(nc.sbuf_tensor("dtrawB", [NH, TH], F32))
    dtfs = E(nc.sbuf_tensor("dtfsB", [NH, T], F32))
    cums = E(nc.sbuf_tensor("cumsB", [NH, T], F32))
    rowsA = xn2T[0:32, 0:4096].bitcast(F32)
    rowsB = xn2T[0:32, 4096:8192].bitcast(F32)
    onesTrow = E(nc.sbuf_tensor("onesTrowB", [1, T], F32))
    onecol32 = E(nc.sbuf_tensor("onecol32B", [NH, 1], F32))
    zcol32 = E(nc.sbuf_tensor("zcol32B", [NH, 1], F32))
    nwcol = E(nc.sbuf_tensor("nwcolB", [P, DC], F32))
    wcv = E(nc.sbuf_tensor("wcvB", [P, DIC * 4], F32))
    cbcol = E(nc.sbuf_tensor("cbcolB", [P, DIC], F32))
    dtbc = E(nc.sbuf_tensor("dtbcB", [NH, 1], F32))
    expac = E(nc.sbuf_tensor("expacB", [NH, 1], F32))
    expa2 = E(nc.sbuf_tensor("expa2B", [NH, 1], F32))
    s2r = big2[0:1, 0:4102].bitcast(F32)
    varr = big2[0:1, 4102:8204].bitcast(F32)
    rrow = big2[0:1, 8204:12306].bitcast(F32)
    sq = xn2T[:, 4 * T:4 * T + 2048].bitcast(F32)
    stg = [E(nc.sbuf_tensor(f"stgB{i}", [P, 512], BF16)) for i in range(2)]
    btok = big2[:, 23552:23680]
    dttok = big2[:, 23680:23744].bitcast(F32)
    edtok = big2[:, 23744:23808].bitcast(F32)
    edf = big2[0:32, 23808:24064].bitcast(F32)
    lamrow = big2[0:1, 24064:24128].bitcast(F32)
    lamexp = xn2T[0:1, 10240:14336].bitcast(F32)
    wrow = big2[0:1, 0:4096].bitcast(F32)
    ccrow = big2[0:1, 4096:4608].bitcast(F32)
    trow = big2[0:1, 4608:5120].bitcast(F32)
    prow = big2[0:1, 5120:5632].bitcast(F32)
    cmrow = E(nc.sbuf_tensor("cmrowB", [1, NCORES], F32))
    invrow = E(nc.sbuf_tensor("invrowB", [1, NCORES], F32))
    nwr = big[0:DC, 0:256].bitcast(F32)
    cbr = big[0:DIC, 256:512].bitcast(F32)
    cv1 = xn2T[:, 0:2 * T].bitcast(F32)
    cv2 = xn2T[:, 2 * T:4 * T].bitcast(F32)
    ps4 = [E(nc.psum_tensor(f"ps4_{i}", [P, 512], F32)) for i in range(4)]
    psL = E(nc.psum_tensor("psL", [P, 512], F32))
    psF = E(nc.psum_tensor("psF", [P, 512], F32))
    psT = [E(nc.psum_tensor(f"psT{i}", [P, 1024], BF16)) for i in range(2)]

    p.dma("sy", ident[:, :], identf)
    p.dma("pool", identb[:, :], identf)
    p.dma("pool", wcv[:, :].rearrange("p (c k) -> p c k", k=4),
          convw.rearrange("(c p) k -> p c k", p=P))
    p.dma("sy", cmrow[:, :], cmask)
    p.op("pool", lambda o: o.memset(onescol[:, :], 1.0))
    p.op("pool", lambda o: o.memset(onesrow[:, :], 1.0))
    p.op("pool", lambda o: o.memset(onesTrow[:, :], 1.0))
    p.op("pool", lambda o: o.memset(onecol32[:, :], 1.0))
    p.op("pool", lambda o: o.memset(zcol32[:, :], 0.0), incs=[("S_pool", 1)])
    p.dma("sy", nwr[:, :], nw.rearrange("(c p) -> c p", p=P))
    p.dma("sy", cbr[:, :], convb.rearrange("(c p) -> c p", p=P))
    p.dma("sy", dtbc[:, :], dtb.unsqueeze(1))
    p.dma("sy", expac[:, :], alog.unsqueeze(1))
    ld0 = p.now("ld")
    pld0 = p.now("pld")
    p.op("pe", lambda o: o.transpose(psF[0:P, 0:DC], nwr[:, :], ident[0:DC, 0:DC]),
         waits=[("ld", ld0), ("pld", pld0), ("S_pool", p.now("S_pool"))], incs=[("S_pe", 1)],
         settle=True)
    p.op("act", lambda o: o.copy(nwcol[:, :], psF[:, 0:DC]),
         waits=[("S_pe", p.now("S_pe"))], incs=[("S_act", 1)])
    p.op("pe", lambda o: o.transpose(psF[:, 64:64 + DIC], cbr[:, :], ident[0:DIC, 0:DIC]),
         waits=[("S_act", p.now("S_act"))], incs=[("S_pe", 1)], settle=True)
    p.op("act", lambda o: o.copy(cbcol[:, :], psF[:, 64:64 + DIC]),
         waits=[("S_pe", p.now("S_pe"))], incs=[("S_act", 1)])
    p.op("act", lambda o: o.activation(expa2[:, :], expac[:, :], AF.Exp),
         waits=[("ld", ld0)], incs=[("S_act", 1)])
    p.barrier()

    for c in range(DC):
        p.dma("sy", xnTs[:, c * TH:(c + 1) * TH], xnT[c * P:(c + 1) * P, :])
    ld1 = p.now("ld")
    evacB = 0
    for a, bnd in col_groups(TH):
        w = bnd - a
        sq_done = []
        mmB_done = []
        for c in range(DC):
            sqh = (c % 2) * 512
            p.op("act", lambda o, c=c, a=a, w=w, sqh=sqh:
                 o.activation(sq[:, sqh:sqh + w], xnTs[:, c * TH + a: c * TH + a + w], AF.Square),
                 waits=[("ld", ld1), ("S_pe", mmB_done[c - 2] if c >= 2 else 0)],
                 incs=[("S_act", 1)])
            sq_done.append(p.now("S_act"))
            p.op("pe", lambda o, c=c, a=a, w=w, sqh=sqh:
                 o.matmul(ps4[0][0:1, 0:w], onescol[:, :], sq[:, sqh:sqh + w],
                          start=(c == 0), stop=(c == DC - 1)),
                 waits=[("S_act", sq_done[c]), ("S_act", evacB if c == 0 else 0)],
                 incs=[("S_pe", 1)], settle=(w < 64))
            mmB_done.append(p.now("S_pe"))
        p.op("act", lambda o, a=a, w=w: o.copy(s2r[:, a:a + w], ps4[0][0:1, 0:w]),
             waits=[("S_pe", p.now("S_pe"))], incs=[("S_act", 1)])
        evacB = p.now("S_act")
    p.barrier()
    p.op("dve", lambda o: o.tensor_scalar(varr[:, :], s2r[:, :], 1.0 / D, 1e-6, ALU.mult, ALU.add),
         incs=[("S_dve", 1)])
    p.op("act", lambda o: o.activation(s2r[:, :], varr[:, :], AF.Sqrt, bias=0.0, scale=1.0),
         waits=[("S_dve", p.now("S_dve"))], incs=[("S_act", 1)])
    p.op("dve", lambda o: o.reciprocal(rrow[:, :], s2r[:, :]),
         waits=[("S_act", p.now("S_act"))], incs=[("S_dve", 1)])
    p.barrier()
    for a, bnd in col_groups(TH):
        w = bnd - a
        p.op("pe", lambda o, a=a, w=w:
             o.matmul(psL[:, 0:w], onesrow[0:1, :], rrow[0:1, a:a + w], start=True, stop=True),
             waits=[("S_dve", p.now("S_dve"))], incs=[("S_pe", 1)], settle=(w < 64))
        mmd = p.now("S_pe")
        for c in range(DC):
            p.op("dve", lambda o, c=c, a=a, w=w:
                 o.scalar_tensor_tensor(xn2T[:, c * TH + a: c * TH + a + w],
                                        xnTs[:, c * TH + a: c * TH + a + w],
                                        nwcol[:, c:c + 1], psL[:, 0:w], ALU.mult, ALU.mult),
                 waits=[("S_pe", mmd)], incs=[("S_dve", 1)] if c == DC - 1 else [])
    p.barrier()
    if stage == 1:
        for c in range(DC):
            p.dma("sy", szT[c * P:(c + 1) * P, 0:T], xn2T[:, c * TH + 3: c * TH + 3 + T], sem="st")
        p.op("sy", None, waits=[("st", p.now("st"))])
        return

    for k in range(DC):
        p.dma("pool", winTs[:, k * 4384:(k + 1) * 4384], winT[k * P:(k + 1) * P, :])
    ld2 = p.now("pld")
    bank_free = [("S_act", 0)] * 4
    bi = 0
    for a, bnd in col_groups(TH):
        w = bnd - a
        for m in range(35):
            rows = 32 if m == 34 else P
            bank = ps4[bi % 4]
            bwait = bank_free[bi % 4]
            for k in range(DC):
                p.op("pe", lambda o, m=m, k=k, a=a, w=w, rows=rows, bank=bank:
                     o.matmul(bank[0:rows, 0:w],
                              winTs[:, k * 4384 + m * P: k * 4384 + m * P + rows],
                              xn2T[:, k * TH + a: k * TH + a + w],
                              start=(k == 0), stop=(k == DC - 1)),
                     waits=([bwait, ("pld", ld2)] if k == 0 else []),
                     incs=([("S_pe", 1)] if k == DC - 1 else []),
                     settle=(w < 64 and k == DC - 1))
            mmdone = p.now("S_pe")
            sg = stg[bi % 2]
            lo = max(a, 3)
            if m < 16:
                p.op("act", lambda o, w=w, bank=bank, sg=sg:
                     o.activation(sg[:, 0:w], bank[:, 0:w], AF.Silu),
                     waits=[("S_pe", mmdone), ("st", p.now("st"))], incs=[("S_act", 1)])
                if lo < bnd:
                    p.dma("sy", szT[m * P:(m + 1) * P, lo - 3:bnd - 3], sg[:, lo - a:w],
                          sem="st", waits=[("S_act", p.now("S_act"))])
            elif m < 32:
                p.op("act", lambda o, m=m, a=a, w=w, bank=bank:
                     o.activation(xssmT[:, (m - 16) * TH + a:(m - 16) * TH + a + w],
                                  bank[:, 0:w], AF.Identity, bias=0.0, scale=1.0),
                     waits=[("S_pe", mmdone)], incs=[("S_act", 1)])
            elif m == 32:
                p.op("act", lambda o, a=a, w=w, bank=bank:
                     o.activation(BTs[:, a:a + w], bank[:, 0:w], AF.Identity, bias=0.0, scale=1.0),
                     waits=[("S_pe", mmdone), ("st", p.now("st"))], incs=[("S_act", 1)])
                if lo < bnd:
                    p.dma("sy", BTo[:, lo - 3:bnd - 3], BTs[:, lo:bnd],
                          sem="st", waits=[("S_act", p.now("S_act"))])
            elif m == 33:
                p.op("act", lambda o, a=a, w=w, bank=bank, sg=sg:
                     o.activation(sg[:, 0:w], bank[:, 0:w], AF.Identity, bias=0.0, scale=1.0),
                     waits=[("S_pe", mmdone), ("st", p.now("st"))], incs=[("S_act", 1)])
                if lo < bnd:
                    p.dma("sy", CTo[:, lo - 3:bnd - 3], sg[:, lo - a:w],
                          sem="st", waits=[("S_act", p.now("S_act"))])
            else:
                p.op("act", lambda o, a=a, w=w, bank=bank:
                     o.copy(dtraw[:, a:a + w], bank[0:32, 0:w]),
                     waits=[("S_pe", mmdone)], incs=[("S_act", 1)])
            bank_free[bi % 4] = ("S_act", p.now("S_act"))
            bi += 1
    p.barrier()
    if stage == 2:
        for c in range(DIC):
            p.dma("sy", xcvT[c * P:(c + 1) * P, :], xssmT[:, c * TH + 3: c * TH + 3 + T], sem="st")
        p.dma("sy", cumo[:, :], dtraw[:, 3:], sem="st")
        p.op("sy", None, waits=[("st", p.now("st"))])
        return

    for c in range(DIC):
        base = c * TH
        p.op("dve", lambda o, c=c, base=base:
             o.tensor_scalar(cv1[:, :], xssmT[:, base:base + T], wcv[:, 4 * c:4 * c + 1],
                             0.0, ALU.mult, ALU.add))
        p.op("dve", lambda o, c=c, base=base:
             o.scalar_tensor_tensor(cv2[:, :], xssmT[:, base + 1:base + 1 + T],
                                    wcv[:, 4 * c + 1:4 * c + 2], cv1[:, :], ALU.mult, ALU.add))
        p.op("dve", lambda o, c=c, base=base:
             o.scalar_tensor_tensor(cv1[:, :], xssmT[:, base + 2:base + 2 + T],
                                    wcv[:, 4 * c + 2:4 * c + 3], cv2[:, :], ALU.mult, ALU.add))
        p.op("dve", lambda o, c=c, base=base:
             o.scalar_tensor_tensor(cv2[:, :], xssmT[:, base + 3:base + 3 + T],
                                    wcv[:, 4 * c + 3:4 * c + 4], cv1[:, :], ALU.mult, ALU.add),
             incs=[("S_dve", 1)])
        p.op("act", lambda o, c=c:
             o.activation(xcv[:, c * T:(c + 1) * T], cv2[:, :], AF.Silu,
                          bias=cbcol[:, c:c + 1], scale=1.0),
             waits=[("S_dve", p.now("S_dve")), ("st", p.now("st"))], incs=[("S_act", 1)])
        p.dma("sy", xcvT[c * P:(c + 1) * P, :], xcv[:, c * T:(c + 1) * T],
              sem="st", waits=[("S_act", p.now("S_act"))])
        p.op("dve", None, waits=[("S_act", p.now("S_act"))])
    p.barrier()

    p.op("dve", lambda o: o.tensor_scalar(rowsA[:, :], dtraw[:, 3:], dtbc[:, 0:1], 0.0, ALU.add, ALU.add),
         incs=[("S_dve", 1)])
    p.op("act", lambda o: o.activation(rowsB[:, :], rowsA[:, :], AF.Abs),
         waits=[("S_dve", p.now("S_dve"))], incs=[("S_act", 1)])
    p.op("act", lambda o: o.activation(rowsB[:, :], rowsB[:, :], AF.Exp, bias=0.0, scale=-1.0),
         incs=[("S_act", 1)])
    p.op("act", lambda o: o.activation(rowsB[:, :], rowsB[:, :], AF.Ln, bias=onecol32[:, 0:1], scale=1.0),
         incs=[("S_act", 1)])
    p.op("dve", lambda o: o.tensor_scalar(rowsA[:, :], rowsA[:, :], 0.0, 0.0, ALU.max, ALU.add),
         waits=[("S_act", p.now("S_act"))])
    p.op("dve", lambda o: o.tensor_tensor(dtfs[:, :], rowsA[:, :], rowsB[:, :], ALU.add),
         incs=[("S_dve", 1)])
    p.dma("sy", dtfo[:, :], dtfs[:, :], sem="st", waits=[("S_dve", p.now("S_dve"))])
    p.op("dve", lambda o: o.tensor_scalar(rowsA[:, :], dtfs[:, :], expa2[:, 0:1], -1.0, ALU.mult, ALU.mult),
         incs=[("S_dve", 1)])
    p.op("dve", lambda o: o.tensor_tensor_scan(cums[:, :], rowsA[:, :],
                                               rowsA[:, :], 0.0, ALU.add, ALU.bypass),
         incs=[("S_dve", 1)])
    p.dma("sy", cumo[:, :], cums[:, :], sem="st", waits=[("S_dve", p.now("S_dve"))])
    p.barrier()
    if stage == 3:
        p.op("sy", None, waits=[("st", p.now("st"))])
        return

    p.op("pool", lambda o: o.memset(hTa[:, :], 0.0))
    p.op("pool", lambda o: o.memset(hTb[:, :], 0.0), incs=[("S_pool", 1)])
    hbufs = [hTa, hTb]
    for c in range(NTC):
        t0 = c * P
        hold = hbufs[c % 2]
        hnew = hbufs[(c + 1) % 2]
        base_act = p.now("S_act")
        for q in range(DIC):
            pb = psT[q % 2]
            p.op("pe", lambda o, q=q, t0=t0, pb=pb:
                 o.transpose(pb[:, 0:P], xcv[:, q * T + t0: q * T + t0 + P], identb[:, :]),
                 waits=[("S_act", max(0, base_act + q - 1)), ("S_pool", p.now("S_pool"))],
                 incs=[("S_pe", 1)])
            p.op("act", lambda o, q=q, pb=pb:
                 o.copy(xctok[:, q * P:(q + 1) * P], pb[:, 0:P]),
                 waits=[("S_pe", p.now("S_pe"))], incs=[("S_act", 1)])
        p.op("pe", lambda o, t0=t0:
             o.transpose(psT[0][:, 0:P], BTs[:, 3 + t0: 3 + t0 + P], identb[:, :]),
             waits=[("S_act", p.now("S_act"))], incs=[("S_pe", 1)])
        p.op("act", lambda o: o.copy(btok[:, :], psT[0][:, 0:P]),
             waits=[("S_pe", p.now("S_pe"))], incs=[("S_act", 1)])
        p.op("act", lambda o, t0=t0:
             o.activation(edf[:, :], cums[:, t0:t0 + P], AF.Exp,
                          bias=cums[:, t0 + P - 1:t0 + P], scale=-1.0),
             incs=[("S_act", 1)])
        p.op("pe", lambda o, t0=t0:
             o.transpose(psF[:, 0:NH], dtfs[:, t0:t0 + P], ident[0:NH, 0:NH]),
             waits=[("S_act", p.now("S_act"))], incs=[("S_pe", 1)], settle=True)
        p.op("act", lambda o: o.copy(dttok[:, :], psF[:, 0:NH]),
             waits=[("S_pe", p.now("S_pe"))], incs=[("S_act", 1)])
        p.op("pe", lambda o:
             o.transpose(psF[:, 64:64 + NH], edf[:, :], ident[0:NH, 0:NH]),
             waits=[("S_act", p.now("S_act"))], incs=[("S_pe", 1)], settle=True)
        p.op("act", lambda o: o.copy(edtok[:, :], psF[:, 64:64 + NH]),
             waits=[("S_pe", p.now("S_pe"))], incs=[("S_act", 1)])
        p.op("dve", lambda o:
             o.tensor_tensor(dtx[:, :].rearrange("p (h d) -> p h d", d=HD),
                             xctok[:, :].rearrange("p (h d) -> p h d", d=HD),
                             dttok[:, :].unsqueeze(2).broadcast_to([P, NH, HD]), ALU.mult),
             waits=[("S_act", p.now("S_act"))], incs=[("S_dve", 1)])
        p.op("dve", lambda o:
             o.tensor_tensor(ddtx[:, :].rearrange("p (h d) -> p h d", d=HD),
                             dtx[:, :].rearrange("p (h d) -> p h d", d=HD),
                             edtok[:, :].unsqueeze(2).broadcast_to([P, NH, HD]), ALU.mult),
             incs=[("S_dve", 1)])
        dd = p.now("S_dve")
        for g in range(4):
            p.op("pe", lambda o, g=g:
                 o.matmul(ps4[g][:, :], btok[:, :], ddtx[:, g * 512:(g + 1) * 512],
                          start=True, stop=True),
                 waits=[("S_dve", dd)],
                 incs=([("S_pe", 1)] if g == 3 else []), settle=(g == 3))
        inj_done = p.now("S_pe")
        p.op("dve", lambda o, c=c, t0=t0:
             o.tensor_scalar(rowsB[:, 0:1], (cums[:, t0 - 1:t0] if c > 0 else zcol32[:, 0:1]),
                             -1.0, 0.0, ALU.mult, ALU.add), incs=[("S_dve", 1)])
        p.op("act", lambda o, t0=t0:
             o.activation(rowsB[:, 1:2], cums[:, t0 + P - 1:t0 + P], AF.Exp,
                          bias=rowsB[:, 0:1], scale=1.0),
             waits=[("S_dve", p.now("S_dve"))], incs=[("S_act", 1)])
        p.op("pe", lambda o:
             o.transpose(psF[0:1, 64:64 + NH], rowsB[:, 1:2], ident[0:NH, 0:NH]),
             waits=[("S_act", p.now("S_act"))], incs=[("S_pe", 1)], settle=True)
        p.op("act", lambda o: o.copy(lamrow[0:1, :], psF[0:1, 64:64 + NH]),
             waits=[("S_pe", p.now("S_pe"))], incs=[("S_act", 1)])
        p.op("dve", lambda o:
             o.tensor_tensor(lamexp[:, :].rearrange("o (h d) -> o h d", d=HD),
                             onesTrow[0:1, 0:DI].rearrange("o (h d) -> o h d", d=HD),
                             lamrow[0:1, :].unsqueeze(2).broadcast_to([1, NH, HD]), ALU.mult),
             waits=[("S_act", p.now("S_act"))], incs=[("S_dve", 1)])
        lam_done = p.now("S_dve")
        for g in range(4):
            p.op("pe", lambda o, g=g:
                 o.matmul(psL[:, :], onesrow[0:1, :], lamexp[0:1, g * 512:(g + 1) * 512],
                          start=True, stop=True),
                 waits=[("S_dve", lam_done), ("S_dve", p.now("S_dve"))],
                 incs=[("S_pe", 1)], settle=True)
            p.op("dve", lambda o, g=g, hold=hold:
                 o.tensor_tensor(tmpd[:, :], hold[:, g * 512:(g + 1) * 512], psL[:, :], ALU.mult),
                 waits=[("S_pe", p.now("S_pe")), ("S_pool", p.now("S_pool"))], incs=[("S_dve", 1)])
            p.op("dve", lambda o, g=g, hnew=hnew:
                 o.tensor_tensor(hnew[:, g * 512:(g + 1) * 512], tmpd[:, :], ps4[g][:, :], ALU.add),
                 waits=[("S_pe", inj_done)], incs=[("S_dve", 1)])
        p.barrier()
    hfin = hbufs[NTC % 2]
    p.op("act", lambda o: o.activation(rowsB[:, 2:3], cums[:, T - 1:T], AF.Exp),
         incs=[("S_act", 1)])
    p.dma("sy", cc_in_s.ap(), hfin[:, :], sem="st", waits=[("S_dve", p.now("S_dve"))])
    p.dma("sy", cc_in_l.ap(), rowsB[:, 2:3], sem="st", waits=[("S_act", p.now("S_act"))])
    p.op("pool", lambda o: o.collective_compute(
        "AllGather", ALU.bypass, replica_groups=[list(range(NCORES))],
        ins=[cc_in_s.ap().opt()], outs=[cc_out_s.ap().opt()]),
        waits=[("st", p.now("st"))], incs=[("cc", 1)], settle=False)
    p.op("pool", lambda o: o.collective_compute(
        "AllGather", ALU.bypass, replica_groups=[list(range(NCORES))],
        ins=[cc_in_l.ap().opt()], outs=[cc_out_l.ap().opt()]),
        incs=[("cc", 1)], settle=False)
    if stage == 4:
        for g in range(4):
            p.dma("sy", hino[:, g * 512:(g + 1) * 512], hfin[:, g * 512:(g + 1) * 512], sem="st")
        p.op("sy", None, waits=[("st", p.now("st")), ("cc", p.now("cc"))])
        return

    p.dma("sy", ccrow[:, :], cc_out_l.ap().rearrange("(r h) one -> one (r h)", h=NH),
          waits=[("cc", p.now("cc"))])
    ld3 = p.now("ld")
    p.op("pool", lambda o: o.memset(hins[:, :], 0.0), incs=[("S_pool", 1)])
    p.op("dve", lambda o: o.tensor_scalar(invrow[:, :], cmrow[:, :], -1.0, 1.0, ALU.mult, ALU.add),
         incs=[("S_dve", 1)])
    for q in range(NCORES):
        p.op("dve", lambda o, q=q:
             o.tensor_scalar(trow[:, q * NH:(q + 1) * NH], ccrow[:, q * NH:(q + 1) * NH],
                             cmrow[0:1, q:q + 1], invrow[0:1, q:q + 1], ALU.mult, ALU.add),
             waits=[("ld", ld3)], incs=[("S_dve", 1)])
    p.op("dve", lambda o: o.tensor_scalar(prow[:, 7 * NH:8 * NH], onesTrow[0:1, 0:NH], 1.0, 0.0,
                                          ALU.mult, ALU.add), incs=[("S_dve", 1)])
    for r in range(NCORES - 2, -1, -1):
        p.op("dve", lambda o, r=r:
             o.tensor_tensor(prow[:, r * NH:(r + 1) * NH], trow[:, (r + 1) * NH:(r + 2) * NH],
                             prow[:, (r + 1) * NH:(r + 2) * NH], ALU.mult),
             incs=[("S_dve", 1)])
    p.barrier()
    for r in range(NCORES):
        p.dma("sy", sgr[:, :], cc_out_s.ap()[r * DS:(r + 1) * DS, :],
              waits=[("S_dve", p.now("S_dve"))])
        ldr = p.now("ld")
        p.op("dve", lambda o, r=r:
             o.tensor_scalar(prow[:, r * NH:(r + 1) * NH], prow[:, r * NH:(r + 1) * NH],
                             cmrow[0:1, r:r + 1], 0.0, ALU.mult, ALU.add), incs=[("S_dve", 1)])
        p.op("dve", lambda o, r=r:
             o.tensor_tensor(wrow[:, :].rearrange("o (h d) -> o h d", d=HD),
                             onesTrow[0:1, 0:DI].rearrange("o (h d) -> o h d", d=HD),
                             prow[:, r * NH:(r + 1) * NH].unsqueeze(2).broadcast_to([1, NH, HD]),
                             ALU.mult), incs=[("S_dve", 1)])
        wd = p.now("S_dve")
        for g in range(4):
            p.op("pe", lambda o, g=g:
                 o.matmul(psL[:, :], onesrow[0:1, :], wrow[0:1, g * 512:(g + 1) * 512],
                          start=True, stop=True),
                 waits=[("S_dve", wd), ("S_dve", p.now("S_dve"))], incs=[("S_pe", 1)])
            p.op("dve", lambda o, g=g:
                 o.tensor_tensor(tmpd[:, :], sgr[:, g * 512:(g + 1) * 512], psL[:, :], ALU.mult),
                 waits=[("S_pe", p.now("S_pe")), ("ld", ldr), ("S_pool", p.now("S_pool"))],
                 incs=[("S_dve", 1)])
            p.op("dve", lambda o, g=g:
                 o.tensor_tensor(hins[:, g * 512:(g + 1) * 512], hins[:, g * 512:(g + 1) * 512],
                                 tmpd[:, :], ALU.add),
                 incs=[("S_dve", 1)])
    p.dma("sy", hino, hins[:, :], sem="st", waits=[("S_dve", p.now("S_dve"))])
    p.op("sy", None, waits=[("st", p.now("st"))])



def build_launch_B(stage=99):
    nc = bass.Bass("TRN2", target_bir_lowering=False)
    xnT = nc.dram_tensor("xnT", [D, TH], BF16, kind="ExternalInput").ap()
    winT = nc.dram_tensor("winT", [D, 4384], F32, kind="ExternalInput").ap()
    convw = nc.dram_tensor("convw", [DI, 4], F32, kind="ExternalInput").ap()
    convb = nc.dram_tensor("convb", [DI], F32, kind="ExternalInput").ap()
    alog = nc.dram_tensor("alog", [NH], F32, kind="ExternalInput").ap()
    dtb = nc.dram_tensor("dtb", [NH], F32, kind="ExternalInput").ap()
    nw = nc.dram_tensor("nw", [D], F32, kind="ExternalInput").ap()
    cmask = nc.dram_tensor("cmask", [1, NCORES], F32, kind="ExternalInput").ap()
    identf = nc.dram_tensor("identf", [P, P], F32, kind="ExternalInput").ap()
    szT = nc.dram_tensor("szT", [DI, T], BF16, kind="ExternalOutput").ap()
    xcvT = nc.dram_tensor("xcvT", [DI, T], BF16, kind="ExternalOutput").ap()
    BTo = nc.dram_tensor("BTo", [DS, T], BF16, kind="ExternalOutput").ap()
    CTo = nc.dram_tensor("CTo", [DS, T], BF16, kind="ExternalOutput").ap()
    cumo = nc.dram_tensor("cumo", [NH, T], F32, kind="ExternalOutput").ap()
    dtfo = nc.dram_tensor("dtfo", [NH, T], F32, kind="ExternalOutput").ap()
    hino = nc.dram_tensor("hino", [DS, NH * HD], F32, kind="ExternalOutput").ap()
    cc_in_s = nc.dram_tensor("cc_in_s", [DS, NH * HD], F32)
    cc_out_s = nc.dram_tensor("cc_out_s", [NCORES * DS, NH * HD], F32, addr_space="Shared")
    cc_in_l = nc.dram_tensor("cc_in_l", [NH, 1], F32)
    cc_out_l = nc.dram_tensor("cc_out_l", [NCORES * NH, 1], F32, addr_space="Shared")

    io = {}
    io["xnT"] = xnT
    io["winT"] = winT
    io["convw"] = convw
    io["convb"] = convb
    io["alog"] = alog
    io["dtb"] = dtb
    io["nw"] = nw
    io["cmask"] = cmask
    io["identf"] = identf
    io["szT"] = szT
    io["xcvT"] = xcvT
    io["BTo"] = BTo
    io["CTo"] = CTo
    io["cumo"] = cumo
    io["dtfo"] = dtfo
    io["hino"] = hino
    io["cc_in_s"] = cc_in_s
    io["cc_out_s"] = cc_out_s
    io["cc_in_l"] = cc_in_l
    io["cc_out_l"] = cc_out_l
    with ExitStack() as ctx:
        p = Prog(nc, ctx)
        emit_B(nc, p, ctx, io, stage)
        p.finalize()
    return nc




def emit_C(nc, p, ctx, io, stage=99):
    E = ctx.enter_context
    xnT = io["xnT"]
    xres = io["xres"]
    gatew = io["gatew"]
    gateb = io["gateb"]
    w1 = io["w1"]
    w2 = io["w2"]
    wg = io["wg"]
    w3 = io["w3"]
    b1 = io["b1"]
    b2 = io["b2"]
    bg = io["bg"]
    b3 = io["b3"]
    identf = io["identf"]
    xres2 = io["xres2"]
    wesc = io["wesc"]
    wbig = E(nc.sbuf_tensor("wbigC", [P, 2 * DC * DI], BF16))
    w1s = wbig[:, 0:DC * DI]
    w2s = wbig[:, DC * DI:2 * DC * DI]
    wgs = wbig[:, :]
    zT = E(nc.sbuf_tensor("zTC", [P, DIC * 1024], BF16))
    xnTs = E(nc.sbuf_tensor("xnTsC", [P, DC * T], BF16))
    w3s = E(nc.sbuf_tensor("w3sC", [P, DIC * D], BF16))
    swT = E(nc.sbuf_tensor("swTC", [P, DIC * 512], BF16))
    xrb = E(nc.sbuf_tensor("xrbC", [P, 4 * 512], F32))
    ident = E(nc.sbuf_tensor("identC", [P, P], F32))
    onesrow = E(nc.sbuf_tensor("onesrowC", [1, P], F32))
    gw = E(nc.sbuf_tensor("gwC", [P, DC * 8], BF16))
    gbr = E(nc.sbuf_tensor("gbrC", [1, 8], F32))
    b3row = E(nc.sbuf_tensor("b3rowC", [1, D], F32))
    bcolr = E(nc.sbuf_tensor("bcolrC", [P, P], F32))
    b1col = E(nc.sbuf_tensor("b1colC", [P, P], F32))
    b2col = E(nc.sbuf_tensor("b2colC", [P, P], F32))
    bgcol = E(nc.sbuf_tensor("bgcolC", [P, P], F32))
    wefm = zT[0:8, 0:4096].bitcast(F32)
    lg = E(nc.sbuf_tensor("lgC", [P, 8], F32))
    pex = E(nc.sbuf_tensor("pexC", [P, 8], F32))
    m1 = E(nc.sbuf_tensor("m1C", [P, 1], F32))
    t1g = E(nc.sbuf_tensor("t1gC", [P, 8], F32))
    wet = E(nc.sbuf_tensor("wetC", [P, 8], F32))
    werow = E(nc.sbuf_tensor("werowC", [1, T], F32))
    x1tmp = [swT[:, i * 1024:(i + 1) * 1024].bitcast(F32) for i in range(3)]
    stmp = E(nc.sbuf_tensor("stmpC", [P, 3 * 512], F32))
    ps8 = [E(nc.psum_tensor(f"ps8C{i}", [P, 512], F32)) for i in range(8)]

    p.dma("sy", ident[:, :], identf)
    p.op("pool", lambda o: o.memset(onesrow[:, :], 1.0), incs=[("S_pool", 1)])
    p.dma("pool", gw[:, :].rearrange("p (k e) -> p k e", e=8), gatew.rearrange("(k p) e -> p k e", p=P))
    p.dma("sy", gbr[:, :], gateb)
    for name, src_ap, dst in ((0, b1, b1col), (1, b2, b2col), (2, bg, bgcol)):
        p.dma("sy", bcolr[:, :], src_ap.rearrange("e (c p) -> (e c) p", p=P),
              waits=[("S_act", p.now("S_act"))])
        p.op("pe", lambda o: o.transpose(ps8[0][:, 0:P], bcolr[:, :], ident[:, :]),
             waits=[("ld", p.now("ld")), ("S_act", p.now("S_act"))], incs=[("S_pe", 1)])
        p.op("act", lambda o, dst=dst: o.copy(dst[:, :], ps8[0][:, 0:P]),
             waits=[("S_pe", p.now("S_pe"))], incs=[("S_act", 1)])
    for c in range(DC):
        p.dma("pool", xnTs[:, c * T:(c + 1) * T], xnT[c * P:(c + 1) * P, 3:])
    xg = [zT[:, 4096 + i * DC * P:4096 + (i + 1) * DC * P] for i in range(2)]
    if True:
        ldg = p.now("ld")
        for tt in range(NTC):
            t0 = tt * P
            xgb = xg[tt % 2]
            p.dma("pool", xgb[:, :].rearrange("p (k t) -> p k t", t=P),
                  xnT.rearrange("(k p) t -> p k t", p=P)[:, :, 3 + t0:3 + t0 + P],
                  waits=[("S_pe", p.now("S_pe"))])
            ldx = p.now("pld")
            for k in range(DC):
                p.op("pe", lambda o, k=k, xgb=xgb:
                     o.matmul(ps8[1][0:P, 0:8], xgb[:, k * P:(k + 1) * P],
                              gw[:, k * 8:(k + 1) * 8], start=(k == 0), stop=False),
                     waits=([("pld", ldx), ("ld", ldg), ("S_act", p.now("S_act"))] if k == 0 else []))
            p.op("pe", lambda o, t0=t0:
                 o.matmul(ps8[1][0:P, 0:8], onesrow[0:1, 0:P], gbr[0:1, :], start=False, stop=True),
                 incs=[("S_pe", 1)])
            p.op("act", lambda o: o.copy(lg[:, :], ps8[1][0:P, 0:8]),
                 waits=[("S_pe", p.now("S_pe"))], incs=[("S_act", 1)])
            p.op("dve", lambda o: o.reduce_max(m1[:, :], lg[:, :], AX.X),
                 waits=[("S_act", p.now("S_act"))])
            p.op("dve", lambda o: o.tensor_scalar(m1[:, :], m1[:, :], -1.0, 0.0, ALU.mult, ALU.add),
                 incs=[("S_dve", 1)])
            p.op("act", lambda o: o.activation(pex[:, :], lg[:, :], AF.Exp, bias=m1[:, 0:1], scale=1.0),
                 waits=[("S_dve", p.now("S_dve"))], incs=[("S_act", 1)])
            p.op("dve", lambda o: o.reduce_max(m1[:, :], pex[:, :], AX.X),
                 waits=[("S_act", p.now("S_act"))])
            p.op("dve", lambda o: o.tensor_scalar(t1g[:, :], pex[:, :], m1[:, 0:1], 0.0, ALU.is_ge, ALU.add))
            p.op("dve", lambda o: o.tensor_scalar(t1g[:, :], t1g[:, :], -1.0, 1.0, ALU.mult, ALU.add))
            p.op("dve", lambda o: o.tensor_tensor(t1g[:, :], pex[:, :], t1g[:, :], ALU.mult))
            p.op("dve", lambda o: o.reduce_max(m1[:, :], t1g[:, :], AX.X))
            p.op("dve", lambda o: o.tensor_scalar(t1g[:, :], pex[:, :], m1[:, 0:1], 0.0, ALU.is_ge, ALU.add))
            p.op("dve", lambda o: o.tensor_tensor(wet[:, :], pex[:, :], t1g[:, :], ALU.mult))
            p.op("dve", lambda o: o.reduce_sum(m1[:, :], wet[:, :], AX.X))
            p.op("dve", lambda o: o.reciprocal(m1[:, :], m1[:, :]))
            p.op("dve", lambda o: o.tensor_scalar(wet[:, :], wet[:, :], m1[:, 0:1], 0.0, ALU.mult, ALU.add),
                 incs=[("S_dve", 1)])
            p.op("pe", lambda o: o.transpose(ps8[2][0:8, 0:P], wet[:, :], ident[:, :]),
                 waits=[("S_dve", p.now("S_dve")), ("S_act", p.now("S_act"))], incs=[("S_pe", 1)])
            p.op("act", lambda o, t0=t0: o.copy(wefm[:, t0:t0 + P], ps8[2][0:8, 0:P]),
                 waits=[("S_pe", p.now("S_pe"))], incs=[("S_act", 1)])
            if stage == 0 and t0 == 0:
                p.barrier()
                p.dma("sy", xres2[0:P, 0:8], lg[:, :], sem="st")
                p.dma("sy", xres2[0:P, 8:16], pex[:, :], sem="st")
                p.dma("sy", xres2[0:P, 16:24], wet[:, :], sem="st")
                p.dma("sy", xres2[128:136, 24:152], wefm[0:8, 0:128], sem="st")
                p.op("sy", None, waits=[("st", p.now("st"))])
                return
        p.barrier()
        p.dma("sy", wesc.ap(), wefm[:, :], sem="st")
        p.op("sy", None, waits=[("st", p.now("st"))])
        p.barrier()
        if stage == 1:
            for c in range(DC):
                p.dma("sy", xres2[c * P:(c + 1) * P, :], xracc[:, c * T:(c + 1) * T], sem="st")
            p.dma("sy", xres2[0:8, 0:T], wefm[:, :], sem="st")
            p.op("sy", None, waits=[("st", p.now("st"))])
            return

        bankmark = {i: ("S_dve", p.now("S_dve")) for i in range(8)}
        for e in range(8):
            p.dma("sy", werow[:, :], wesc.ap()[e:e + 1, :], waits=[("S_pe", p.now("S_pe"))])
            wemark = p.now("ld")
            p.dma("sy", b3row[:, :], b3[e:e + 1, :])
            b3mark = p.now("ld")
            for th in range(2):
                wmark = {}
                ztmark = {}
                wbfree = getattr(p, "_c_wg_end", p.now("S_pe"))
                for k in range(DC):
                    p.dma("pool", w1s[:, k * DI:(k + 1) * DI], w1[e, k * P:(k + 1) * P, :],
                          waits=[("S_pe", wbfree)] if k == 0 else [])
                    wmark[("w1", k)] = p.now("pld")
                    p.dma("pool", w2s[:, k * DI:(k + 1) * DI], w2[e, k * P:(k + 1) * P, :])
                    wmark[("w2", k)] = p.now("pld")
                if th == 0:
                    w3mark = {}
                    w3free = p.now("S_pe")
                    for gb in range(DIC):
                        p.dma("pool", w3s[:, gb * D:(gb + 1) * D], w3[e, gb * P:(gb + 1) * P, :],
                              waits=[("S_pe", w3free)] if gb == 0 else [])
                        w3mark[gb] = p.now("pld")
                for ng in range(2):
                    n0 = th * 1024 + ng * 512
                    nn = ng * 512
                    for fb in range(DIC):
                        iA = (fb % 3) * 2
                        iB = iA + 1
                        bnkA = ps8[iA]
                        bnkB = ps8[iB]
                        xt = x1tmp[fb % 3]
                        for k in range(DC):
                            w_ = [("pld", wmark[("w1", k)])] if fb == 0 else []
                            if k == 0:
                                w_ = w_ + [bankmark[iA]]
                            p.op("pe", lambda o, k=k, fb=fb, bnkA=bnkA, n0=n0:
                                 o.matmul(bnkA[:, :], w1s[:, k * DI + fb * P: k * DI + (fb + 1) * P],
                                          xnTs[:, k * T + n0:k * T + n0 + 512],
                                          start=(k == 0), stop=(k == DC - 1)),
                                 waits=w_)
                        for k in range(DC):
                            w_ = [("pld", wmark[("w2", k)])] if fb == 0 else []
                            if k == 0:
                                w_ = w_ + [bankmark[iB]]
                            p.op("pe", lambda o, k=k, fb=fb, bnkB=bnkB, n0=n0:
                                 o.matmul(bnkB[:, :], w2s[:, k * DI + fb * P: k * DI + (fb + 1) * P],
                                          xnTs[:, k * T + n0:k * T + n0 + 512],
                                          start=(k == 0), stop=(k == DC - 1)),
                                 waits=w_,
                                 incs=([("S_pe", 1)] if k == DC - 1 else []))
                        mmd = p.now("S_pe")
                        p.op("act", lambda o, e=e, fb=fb, bnkA=bnkA, xt=xt:
                             o.activation(xt[:, :], bnkA[:, :], AF.Identity,
                                          bias=b1col[:, e * DIC + fb:e * DIC + fb + 1], scale=1.0),
                             waits=[("S_pe", mmd)], incs=[("S_act", 1)])
                        p.op("dve", lambda o, e=e, fb=fb, nn=nn, bnkB=bnkB, xt=xt:
                             o.scalar_tensor_tensor(zT[:, fb * 1024 + nn: fb * 1024 + nn + 512],
                                                    bnkB[:, :], b2col[:, e * DIC + fb:e * DIC + fb + 1],
                                                    xt[:, :], ALU.add, ALU.mult),
                             waits=[("S_act", p.now("S_act"))], incs=[("S_dve", 1)])
                        bankmark[iA] = bankmark[iB] = ("S_dve", p.now("S_dve"))
                        ztmark[(fb, ng)] = bankmark[iA]
                if stage == 2 and e == 0 and th == 0:
                    p.barrier()
                    for fb in range(DIC):
                        p.dma("pool", xres2[(fb % 8) * P:(fb % 8 + 1) * P,
                                          (fb // 8) * 1024:(fb // 8 + 1) * 1024],
                              zT[:, fb * 1024:(fb + 1) * 1024], sem="st")
                    p.op("sy", None, waits=[("st", p.now("st"))])
                    return
                gmark = {}
                wgfree = p.now("S_pe")
                for k in range(DIC):
                    p.dma("pool", wgs[:, k * DI:(k + 1) * DI], wg[e, k * P:(k + 1) * P, :],
                          waits=[("S_pe", wgfree)] if k == 0 else [])
                    gmark[k] = p.now("pld")
                for ng in range(2):
                    n0 = th * 1024 + ng * 512
                    nn = ng * 512
                    p.op("pe", lambda o, n0=n0:
                         o.matmul(ps8[7][:, :], onesrow[0:1, :], werow[0:1, n0:n0 + 512],
                                  start=True, stop=True),
                         waits=[bankmark[7], ("ld", wemark)], incs=[("S_pe", 1)])
                    webd = p.now("S_pe")
                    for gb in range(DIC):
                        ibk = 4 + (gb % 3)
                        bnk = ps8[ibk]
                        for k in range(DIC):
                            w_ = [("pld", gmark[k])] if gb == 0 and ng == 0 else []
                            if gb == 0:
                                w_ = w_ + [ztmark[(k, ng)]]
                            if k == 0:
                                w_ = w_ + [bankmark[ibk]]
                            p.op("pe", lambda o, k=k, gb=gb, nn=nn, bnk=bnk:
                                 o.matmul(bnk[:, :], wgs[:, k * DI + gb * P: k * DI + (gb + 1) * P],
                                          zT[:, k * 1024 + nn: k * 1024 + nn + 512],
                                          start=(k == 0), stop=(k == DIC - 1)),
                                 waits=w_,
                                 incs=([("S_pe", 1)] if k == DIC - 1 else []))
                        mmd = p.now("S_pe")
                        sm = stmp[:, (gb % 3) * 512:(gb % 3 + 1) * 512]
                        p.op("act", lambda o, e=e, gb=gb, bnk=bnk, sm=sm:
                             o.activation(sm[:, :], bnk[:, :], AF.Silu,
                                          bias=bgcol[:, e * DIC + gb:e * DIC + gb + 1], scale=1.0),
                             waits=[("S_pe", mmd)], incs=[("S_act", 1)])
                        p.op("dve", lambda o, gb=gb, sm=sm:
                             o.tensor_tensor(swT[:, gb * 512:(gb + 1) * 512], sm[:, :], ps8[7][:, :], ALU.mult),
                             waits=[("S_act", p.now("S_act")), ("S_pe", webd)], incs=[("S_dve", 1)])
                        bankmark[ibk] = ("S_dve", p.now("S_dve"))
                    if ng == 1:
                        p._c_wg_end = p.now("S_pe")
                    swd = p.now("S_dve")
                    if stage == 3 and e == 0 and th == 0 and ng == 0:
                        for gb in range(DIC):
                            p.dma("pool", xres2[(gb % 8) * P:(gb % 8 + 1) * P,
                                              (gb // 8) * 512:(gb // 8 + 1) * 512],
                                  swT[:, gb * 512:(gb + 1) * 512], sem="st",
                                  waits=[("S_dve", swd)])
                        p.op("sy", None, waits=[("st", p.now("st"))])
                        return
                    xsrc = xres if e == 0 else xres2
                    for ob in range(4):
                        p.dma("sy", xrb[:, ob * 512:(ob + 1) * 512],
                              xsrc[ob * P:(ob + 1) * P, n0:n0 + 512],
                              waits=[("st", p.now("st"))] if ob == 0 else [])
                    for gb in range(DIC):
                        for ob in range(8):
                            w_ = [("pld", w3mark[gb])] if ob == 0 else []
                            if gb == 0:
                                w_ = w_ + [("S_dve", swd)]
                            p.op("pe", lambda o, gb=gb, ob=ob:
                                 o.matmul(ps8[ob][:, :], w3s[:, gb * D + ob * P:gb * D + (ob + 1) * P],
                                          swT[:, gb * 512:(gb + 1) * 512],
                                          start=(gb == 0), stop=False),
                                 waits=w_, settle=False)
                    for ob in range(8):
                        p.op("pe", lambda o, ob=ob, n0=n0:
                             o.matmul(ps8[ob][:, :], b3row[0:1, ob * P:(ob + 1) * P],
                                      werow[0:1, n0:n0 + 512],
                                      start=False, stop=True),
                             incs=([("S_pe", 1)] if ob == 7 else []))
                    mmd = p.now("S_pe")
                    for ob in range(8):
                        sl = ob % 4
                        ldx = p.now("ld")
                        p.op("dve", lambda o, ob=ob, sl=sl:
                             o.tensor_tensor(xrb[:, sl * 512:(sl + 1) * 512],
                                             xrb[:, sl * 512:(sl + 1) * 512],
                                             ps8[ob][:, :], ALU.add),
                             waits=[("S_pe", mmd), ("ld", ldx)], incs=[("S_dve", 1)])
                        bankmark[ob] = ("S_dve", p.now("S_dve"))
                        p.dma("sy", xres2[ob * P:(ob + 1) * P, n0:n0 + 512],
                              xrb[:, sl * 512:(sl + 1) * 512], sem="st",
                              waits=[("S_dve", p.now("S_dve"))])
                        if ob + 4 < 8:
                            p.dma("sy", xrb[:, sl * 512:(sl + 1) * 512],
                                  xsrc[(ob + 4) * P:(ob + 5) * P, n0:n0 + 512],
                                  waits=[("st", p.now("st"))])
    p.barrier()
    p.op("sy", None, waits=[("st", p.now("st"))])



def build_launch_C(stage=99):
    nc = bass.Bass("TRN2", target_bir_lowering=False)
    xnT = nc.dram_tensor("xnT", [D, TH], BF16, kind="ExternalInput").ap()
    xres = nc.dram_tensor("xres", [D, T], F32, kind="ExternalInput").ap()
    gatew = nc.dram_tensor("gatew", [D, 8], F32, kind="ExternalInput").ap()
    gateb = nc.dram_tensor("gateb", [1, 8], F32, kind="ExternalInput").ap()
    w1 = nc.dram_tensor("w1", [8, D, DI], F32, kind="ExternalInput").ap()
    w2 = nc.dram_tensor("w2", [8, D, DI], F32, kind="ExternalInput").ap()
    wg = nc.dram_tensor("wg", [8, DI, DI], F32, kind="ExternalInput").ap()
    w3 = nc.dram_tensor("w3", [8, DI, D], F32, kind="ExternalInput").ap()
    b1 = nc.dram_tensor("b1", [8, DI], F32, kind="ExternalInput").ap()
    b2 = nc.dram_tensor("b2", [8, DI], F32, kind="ExternalInput").ap()
    bg = nc.dram_tensor("bg", [8, DI], F32, kind="ExternalInput").ap()
    b3 = nc.dram_tensor("b3", [8, D], F32, kind="ExternalInput").ap()
    identf = nc.dram_tensor("identf", [P, P], F32, kind="ExternalInput").ap()
    xres2 = nc.dram_tensor("xres2", [D, T], F32, kind="ExternalOutput").ap()
    wesc = nc.dram_tensor("wesc", [8, T], F32)

    io = {}
    io["xnT"] = xnT
    io["xres"] = xres
    io["gatew"] = gatew
    io["gateb"] = gateb
    io["w1"] = w1
    io["w2"] = w2
    io["wg"] = wg
    io["w3"] = w3
    io["b1"] = b1
    io["b2"] = b2
    io["bg"] = bg
    io["b3"] = b3
    io["identf"] = identf
    io["xres2"] = xres2
    io["wesc"] = wesc
    with ExitStack() as ctx:
        p = Prog(nc, ctx)
        emit_C(nc, p, ctx, io, stage)
        p.finalize()
    return nc




def emit_D(nc, p, ctx, io, stage=99, _DBG_CHUNK=-1):
    E = ctx.enter_context
    szT = io["szT"]
    xcvT = io["xcvT"]
    BTo = io["BTo"]
    CTo = io["CTo"]
    cumo = io["cumo"]
    dtfo = io["dtfo"]
    hino = io["hino"]
    xres2 = io["xres2"]
    woT = io["woT"]
    dvec = io["dvec"]
    identf = io["identf"]
    mask01T = io["mask01T"]
    selT = io["selT"]
    sel2 = io["sel2"]
    xres3 = io["xres3"]
    scr1 = io["scr1"]
    scr2 = io["scr2"]
    woTs = E(nc.sbuf_tensor("woTsD", [P, DIC * D], BF16))
    ygT = E(nc.sbuf_tensor("ygTD", [P, DIC * 512], BF16))
    STb = E(nc.sbuf_tensor("STbD", [P, NH * P], BF16))
    CtT = E(nc.sbuf_tensor("CtTD", [P, NH * P], BF16))
    szc = [E(nc.sbuf_tensor(f"szcD{i}", [P, DIC * P], BF16)) for i in range(2)]
    xcvc = [E(nc.sbuf_tensor(f"xcvcD{i}", [P, DIC * P], BF16)) for i in range(2)]
    BTs = E(nc.sbuf_tensor("BTsD", [P, T], BF16))
    CTs = E(nc.sbuf_tensor("CTsD", [P, T], BF16))
    cums = E(nc.sbuf_tensor("cumsD", [NH, T], F32))
    dtfs = E(nc.sbuf_tensor("dtfsD", [NH, T], F32))
    hT = E(nc.sbuf_tensor("hTD", [P, NH * HD], F32))
    xctok = E(nc.sbuf_tensor("xctokD", [P, DI], BF16))
    dtx = E(nc.sbuf_tensor("dtxD", [P, DI], BF16))
    ddtx = E(nc.sbuf_tensor("ddtxD", [P, DI], BF16))
    btok = E(nc.sbuf_tensor("btokD", [P, P], BF16))
    cumtok = E(nc.sbuf_tensor("cumtokD", [P, NH], F32))
    dttok = E(nc.sbuf_tensor("dttokD", [P, NH], F32))
    edtok = E(nc.sbuf_tensor("edtokD", [P, NH], F32))
    edf2 = E(nc.sbuf_tensor("edf2D", [NH, P], F32))
    edfB = E(nc.sbuf_tensor("edfBD", [NH, P], F32))
    cumflat = E(nc.sbuf_tensor("cumflatD", [1, NH * P], F32))
    ecumflat = E(nc.sbuf_tensor("ecumflatD", [1, NH * P], F32))
    lamexp = E(nc.sbuf_tensor("lamexpD", [1, DI], F32))
    lamrow = E(nc.sbuf_tensor("lamrowD", [1, NH], F32))
    negpc = E(nc.sbuf_tensor("negpcD", [NH, 1], F32))
    zc32 = E(nc.sbuf_tensor("zc32D", [NH, 1], F32))
    GmT = E(nc.sbuf_tensor("GmTD", [P, P], F32))
    t1s = E(nc.sbuf_tensor("t1sD", [P, 512], F32))
    t2s = E(nc.sbuf_tensor("t2sD", [P, 512], F32))
    ytmp = E(nc.sbuf_tensor("ytmpD", [P, P], F32))
    mask = E(nc.sbuf_tensor("maskD", [P, P], F32))
    selTs = E(nc.sbuf_tensor("selTsD", [NH, P], F32))
    sel2s = E(nc.sbuf_tensor("sel2sD", [NH, 16], F32))
    dsel = E(nc.sbuf_tensor("dselD", [NH, 16], F32))
    dcol2 = E(nc.sbuf_tensor("dcol2D", [NH, 1], F32))
    Dcol = E(nc.sbuf_tensor("DcolD", [P, 16], F32))
    ident = E(nc.sbuf_tensor("identD", [P, P], F32))
    identb = E(nc.sbuf_tensor("identbD", [P, P], BF16))
    onesrow = E(nc.sbuf_tensor("onesrowD", [1, P], F32))
    onesDI = E(nc.sbuf_tensor("onesDID", [1, DI], F32))
    xrs = [E(nc.sbuf_tensor(f"xrsD{i}", [P, 512], F32)) for i in range(2)]
    hT2 = E(nc.sbuf_tensor("hT2D", [P, NH * HD], F32))
    hb = E(nc.sbuf_tensor("hbD", [P, NH * HD], BF16))
    tmpd = E(nc.sbuf_tensor("tmpdD", [P, 512], F32))
    ps4 = [E(nc.psum_tensor(f"ps4D{i}", [P, 512], F32)) for i in range(4)]
    psF = E(nc.psum_tensor("psFD", [P, 512], F32))
    psB = E(nc.psum_tensor("psBD", [P, 512], F32))
    psT = [E(nc.psum_tensor(f"psTD{i}", [P, 1024], BF16)) for i in range(2)]

    p.dma("sy", ident[:, :], identf)
    p.dma("pool", identb[:, :], identf)
    p.dma("sy", mask[:, :], mask01T)
    p.dma("sy", selTs[:, :], selT)
    p.dma("sy", sel2s[:, :], sel2)
    p.dma("sy", dcol2[:, :], dvec.unsqueeze(1))
    p.dma("sy", BTs[:, :], BTo)
    p.dma("sy", CTs[:, :], CTo)
    p.dma("sy", cums[:, :], cumo)
    p.dma("sy", dtfs[:, :], dtfo)
    p.dma("sy", hT[:, :], hino)
    p.op("pool", lambda o: o.memset(onesrow[:, :], 1.0))
    p.op("pool", lambda o: o.memset(onesDI[:, :], 1.0))
    p.op("pool", lambda o: o.memset(zc32[:, :], 0.0))
    p.dma("pool", woTs[:, :].rearrange("p (k o) -> p k o", o=D),
          woT.rearrange("(k p) o -> p k o", p=P))
    ld0 = p.now("ld")
    p.op("dve", lambda o: o.tensor_tensor(dsel[:, :], sel2s[:, :],
                                          dcol2[:, 0:1].broadcast_to([NH, 16]), ALU.mult),
         waits=[("ld", ld0)], incs=[("S_dve", 1)])
    p.op("pe", lambda o: o.matmul(psF[:, 0:16], selTs[:, :], dsel[:, :], start=True, stop=True),
         waits=[("S_dve", p.now("S_dve"))], incs=[("S_pe", 1)])
    p.op("act", lambda o: o.copy(Dcol[:, :], psF[:, 0:16]),
         waits=[("S_pe", p.now("S_pe"))], incs=[("S_act", 1)])
    p.op("dve", lambda o: o.tensor_copy(hb[:, :], hT[:, :]),
         waits=[("ld", p.now("ld"))], incs=[("S_dve", 1)])
    p.barrier()

    for c in range(NTC):
        t0 = c * P
        szb = szc[c % 2]
        xcb = xcvc[c % 2]
        p.dma("pool", szb[:, :].rearrange("p (q t) -> p q t", t=P),
              szT.rearrange("(q p) t -> p q t", p=P)[:, :, t0:t0 + P],
              waits=[("S_dve", p.now("S_dve"))])
        p.dma("pool", xcb[:, :].rearrange("p (q t) -> p q t", t=P),
              xcvT.rearrange("(q p) t -> p q t", p=P)[:, :, t0:t0 + P])
        ldc = p.now("pld")
        p.op("dve", lambda o, c=c, t0=t0:
             o.tensor_scalar(negpc[:, 0:1], (cums[:, t0 - 1:t0] if c > 0 else zc32[:, 0:1]),
                             -1.0, 0.0, ALU.mult, ALU.add), incs=[("S_dve", 1)])
        p.op("act", lambda o, t0=t0:
             o.activation(edf2[:, :], cums[:, t0:t0 + P], AF.Exp, bias=negpc[:, 0:1], scale=1.0),
             waits=[("S_dve", p.now("S_dve"))], incs=[("S_act", 1)])
        p.dma("sy", scr1.ap(), cums[:, t0:t0 + P], sem="st")
        p.dma("sy", scr2.ap(), edf2[:, :], sem="st", waits=[("S_act", p.now("S_act"))])
        p.op("sy", None, waits=[("st", p.now("st"))])
        p.dma("sy", cumflat[:, :], scr1.ap().rearrange("h t -> (h t)").unsqueeze(0))
        p.dma("sy", ecumflat[:, :], scr2.ap().rearrange("h t -> (h t)").unsqueeze(0))
        ldflat = p.now("ld")
        base_act = p.now("S_act")
        for q in range(DIC):
            pb = psT[q % 2]
            p.op("pe", lambda o, q=q, pb=pb, xcb=xcb:
                 o.transpose(pb[:, 0:P], xcb[:, q * P:(q + 1) * P], identb[:, :]),
                 waits=[("S_act", max(0, base_act + q - 1)), ("pld", ldc)], incs=[("S_pe", 1)])
            p.op("act", lambda o, q=q, pb=pb:
                 o.copy(xctok[:, q * P:(q + 1) * P], pb[:, 0:P]),
                 waits=[("S_pe", p.now("S_pe"))], incs=[("S_act", 1)])
        p.op("pe", lambda o, t0=t0:
             o.transpose(psT[0][:, 0:P], BTs[:, t0:t0 + P], identb[:, :]),
             waits=[("S_act", p.now("S_act"))], incs=[("S_pe", 1)])
        p.op("act", lambda o: o.copy(btok[:, :], psT[0][:, 0:P]),
             waits=[("S_pe", p.now("S_pe"))], incs=[("S_act", 1)])
        p.op("act", lambda o, t0=t0:
             o.activation(edfB[:, :], cums[:, t0:t0 + P], AF.Exp,
                          bias=cums[:, t0 + P - 1:t0 + P], scale=-1.0),
             incs=[("S_act", 1)])
        for (dst, srcrow) in ((cumtok, 0), (dttok, 1), (edtok, 2)):
            src = [cums, dtfs, None][srcrow]
            if srcrow < 2:
                p.op("pe", lambda o, t0=t0, src=src:
                     o.transpose(psF[:, 0:NH], src[:, t0:t0 + P], ident[0:NH, 0:NH]),
                     waits=[("S_act", p.now("S_act"))], incs=[("S_pe", 1)], settle=True)
            else:
                p.op("pe", lambda o:
                     o.transpose(psF[:, 0:NH], edfB[:, :], ident[0:NH, 0:NH]),
                     waits=[("S_act", p.now("S_act"))], incs=[("S_pe", 1)], settle=True)
            p.op("act", lambda o, dst=dst: o.copy(dst[:, :], psF[:, 0:NH]),
                 waits=[("S_pe", p.now("S_pe"))], incs=[("S_act", 1)])
        p.op("dve", lambda o:
             o.tensor_tensor(dtx[:, :].rearrange("p (h d) -> p h d", d=HD),
                             xctok[:, :].rearrange("p (h d) -> p h d", d=HD),
                             dttok[:, :].unsqueeze(2).broadcast_to([P, NH, HD]), ALU.mult),
             waits=[("S_act", p.now("S_act"))], incs=[("S_dve", 1)])
        p.op("dve", lambda o:
             o.tensor_tensor(ddtx[:, :].rearrange("p (h d) -> p h d", d=HD),
                             dtx[:, :].rearrange("p (h d) -> p h d", d=HD),
                             edtok[:, :].unsqueeze(2).broadcast_to([P, NH, HD]), ALU.mult),
             incs=[("S_dve", 1)])
        p.op("pe", lambda o, t0=t0:
             o.matmul(psF[:, 0:P], BTs[:, t0:t0 + P], CTs[:, t0:t0 + P], start=True, stop=True),
             waits=[("S_act", p.now("S_act"))], incs=[("S_pe", 1)])
        p.op("dve", lambda o: o.tensor_tensor(GmT[:, :], psF[:, 0:P], mask[:, :], ALU.mult),
             waits=[("S_pe", p.now("S_pe"))], incs=[("S_dve", 1)])
        for s in range(8):
            f0 = s * 512
            h0 = s * 4
            p.op("pe", lambda o, f0=f0:
                 o.matmul(psB[:, :], onesrow[0:1, :], cumflat[0:1, f0:f0 + 512], start=True, stop=True),
                 waits=[("ld", ldflat), ("S_dve", p.now("S_dve"))], incs=[("S_pe", 1)])
            p.op("dve", lambda o, h0=h0:
                 o.tensor_tensor(t1s[:, :].rearrange("p (a b) -> p a b", a=4),
                                 psB[:, :].rearrange("p (a b) -> p a b", a=4),
                                 cumtok[:, h0:h0 + 4].unsqueeze(2).broadcast_to([P, 4, P]),
                                 ALU.subtract),
                 waits=[("S_pe", p.now("S_pe"))], incs=[("S_dve", 1)])
            p.op("dve", lambda o:
                 o.tensor_tensor(t2s[:, :].rearrange("p (a b) -> p a b", a=4),
                                 t1s[:, :].rearrange("p (a b) -> p a b", a=4),
                                 mask[:, :].unsqueeze(1).broadcast_to([P, 4, P]), ALU.mult),
                 incs=[("S_dve", 1)])
            p.op("act", lambda o: o.activation(t1s[:, :], t2s[:, :], AF.Exp),
                 waits=[("S_dve", p.now("S_dve"))], incs=[("S_act", 1)])
            p.op("dve", lambda o, f0=f0:
                 o.tensor_tensor(STb[:, f0:f0 + 512].rearrange("p (a b) -> p a b", a=4),
                                 t1s[:, :].rearrange("p (a b) -> p a b", a=4),
                                 GmT[:, :].unsqueeze(1).broadcast_to([P, 4, P]), ALU.mult),
                 waits=[("S_act", p.now("S_act"))], incs=[("S_dve", 1)])
            p.op("pe", lambda o, f0=f0:
                 o.matmul(psB[:, :], onesrow[0:1, :], ecumflat[0:1, f0:f0 + 512], start=True, stop=True),
                 waits=[("S_dve", p.now("S_dve"))], incs=[("S_pe", 1)])
            p.op("dve", lambda o, f0=f0, t0=t0:
                 o.tensor_tensor(CtT[:, f0:f0 + 512].rearrange("p (a b) -> p a b", a=4),
                                 psB[:, :].rearrange("p (a b) -> p a b", a=4),
                                 CTs[:, t0:t0 + P].unsqueeze(1).broadcast_to([P, 4, P]), ALU.mult),
                 waits=[("S_pe", p.now("S_pe"))], incs=[("S_dve", 1)])
        std = p.now("S_dve")
        if stage == 1 and c == 0:
            for q in range(DIC):
                p.dma("pool", xres3[(q % 8) * P:(q % 8 + 1) * P, (q // 8) * 256:(q // 8) * 256 + 128],
                      STb[:, q * P:(q + 1) * P], sem="st", waits=[("S_dve", std)])
                p.dma("pool", xres3[(q % 8) * P:(q % 8 + 1) * P, 512 + (q // 8) * 256:512 + (q // 8) * 256 + 128],
                      CtT[:, q * P:(q + 1) * P], sem="st")
            p.op("sy", None, waits=[("st", p.now("st"))])
            return
        for h in range(NH):
            bank = ps4[(h // 2) % 4]
            coff = ((h // 2) // 4) * P
            g = h // 2
            bank = ps4[g // 4]
            coff = (g % 4) * P
            r0 = (h % 2) * HD
            p.op("pe", lambda o, h=h, bank=bank, coff=coff, r0=r0:
                 o.matmul(bank[r0:r0 + HD, coff:coff + P], dtx[:, h * HD:(h + 1) * HD],
                          STb[:, h * P:(h + 1) * P], start=True, stop=False),
                 waits=([("S_dve", std)] if h == 0 else []))
            p.op("pe", lambda o, h=h, bank=bank, coff=coff, r0=r0:
                 o.matmul(bank[r0:r0 + HD, coff:coff + P], hb[:, h * HD:(h + 1) * HD],
                          CtT[:, h * P:(h + 1) * P], start=False, stop=True),
                 incs=([("S_pe", 1)] if h == NH - 1 else []))
        yd = p.now("S_pe")
        for q in range(DIC):
            bank = ps4[q // 4]
            coff = (q % 4) * P
            p.op("dve", lambda o, q=q, bank=bank, coff=coff, xcb=xcb:
                 o.scalar_tensor_tensor(ytmp[:, :], xcb[:, q * P:(q + 1) * P], Dcol[:, q:q + 1],
                                        bank[:, coff:coff + P], ALU.mult, ALU.add),
                 waits=[("S_pe", yd)], incs=[("S_dve", 1)])
            p.op("dve", lambda o, q=q, c=c, szb=szb:
                 o.tensor_tensor(ygT[:, q * 512 + (c % 4) * P: q * 512 + (c % 4) * P + P],
                                 ytmp[:, :], szb[:, q * P:(q + 1) * P], ALU.mult),
                 incs=[("S_dve", 1)])
        epd = p.now("S_dve")
        if stage == 2 and c == _DBG_CHUNK:
            for q in range(DIC):
                p.dma("pool", xres3[(q % 8) * P:(q % 8 + 1) * P, (q // 8) * 128:(q // 8) * 128 + P],
                      ygT[:, q * 512:q * 512 + P], sem="st", waits=[("S_dve", epd)])
            p.op("sy", None, waits=[("st", p.now("st"))])
            return
        for g in range(4):
            p.op("pe", lambda o, g=g:
                 o.matmul(ps4[g][:, :], btok[:, :], ddtx[:, g * 512:(g + 1) * 512],
                          start=True, stop=True),
                 waits=[("S_dve", epd)],
                 incs=([("S_pe", 1)] if g == 3 else []))
        injd = p.now("S_pe")
        p.op("act", lambda o, t0=t0:
             o.activation(negpc[:, 0:1], cums[:, t0 + P - 1:t0 + P], AF.Exp,
                          bias=negpc[:, 0:1], scale=1.0), incs=[("S_act", 1)])
        p.op("pe", lambda o:
             o.transpose(psF[0:1, 64:64 + NH], negpc[:, 0:1], ident[0:NH, 0:NH]),
             waits=[("S_act", p.now("S_act"))], incs=[("S_pe", 1)], settle=True)
        p.op("act", lambda o: o.copy(lamrow[0:1, :], psF[0:1, 64:64 + NH]),
             waits=[("S_pe", p.now("S_pe"))], incs=[("S_act", 1)])
        p.op("dve", lambda o:
             o.tensor_tensor(lamexp[:, :].rearrange("o (h d) -> o h d", d=HD),
                             onesDI[0:1, :].rearrange("o (h d) -> o h d", d=HD),
                             lamrow[0:1, :].unsqueeze(2).broadcast_to([1, NH, HD]), ALU.mult),
             waits=[("S_act", p.now("S_act"))], incs=[("S_dve", 1)])
        lamd = p.now("S_dve")
        for g in range(4):
            p.op("pe", lambda o, g=g:
                 o.matmul(psB[:, :], onesrow[0:1, :], lamexp[0:1, g * 512:(g + 1) * 512],
                          start=True, stop=True),
                 waits=[("S_dve", lamd), ("S_dve", p.now("S_dve"))], incs=[("S_pe", 1)])
            p.op("dve", lambda o, g=g:
                 o.tensor_tensor(tmpd[:, :], hT[:, g * 512:(g + 1) * 512], psB[:, :], ALU.mult),
                 waits=[("S_pe", p.now("S_pe"))], incs=[("S_dve", 1)])
            p.op("dve", lambda o, g=g:
                 o.tensor_tensor(hT2[:, g * 512:(g + 1) * 512], tmpd[:, :], ps4[g][:, :], ALU.add),
                 waits=[("S_pe", injd)], incs=[("S_dve", 1)])
        p.op("dve", lambda o: o.tensor_copy(hT[:, :], hT2[:, :]), incs=[("S_dve", 1)])
        p.op("dve", lambda o: o.tensor_copy(hb[:, :], hT2[:, :]), incs=[("S_dve", 1)])
        if c % 4 == 3:
            grp = c // 4
            gt0 = grp * 512
            for oc in range(DC):
                bank = ps4[oc % 2]
                xb = xrs[oc % 2]
                p.dma("sy", xb[:, :], xres2[oc * P:(oc + 1) * P, gt0:gt0 + 512],
                      waits=[("st", p.now("st"))])
                ldx = p.now("ld")
                for q in range(DIC):
                    p.op("pe", lambda o, oc=oc, q=q, bank=bank:
                         o.matmul(bank[:, :], woTs[:, q * D + oc * P: q * D + (oc + 1) * P],
                                  ygT[:, q * 512:(q + 1) * 512],
                                  start=(q == 0), stop=(q == DIC - 1)),
                         waits=([("S_dve", p.now("S_dve")), ("S_act", p.now("S_act"))] if q == 0 else []),
                         incs=([("S_pe", 1)] if q == DIC - 1 else []))
                mmd = p.now("S_pe")
                p.op("dve", lambda o, bank=bank, xb=xb:
                     o.tensor_tensor(xb[:, :], xb[:, :], bank[:, :], ALU.add),
                     waits=[("S_pe", mmd), ("ld", ldx)], incs=[("S_dve", 1)])
                p.dma("sy", xres3[oc * P:(oc + 1) * P, gt0:gt0 + 512], xb[:, :],
                      sem="st", waits=[("S_dve", p.now("S_dve"))])
        p.barrier()
    p.op("sy", None, waits=[("st", p.now("st"))])



def build_launch_D(stage=99, _DBG_CHUNK=0):
    nc = bass.Bass("TRN2", target_bir_lowering=False)
    szT = nc.dram_tensor("szT", [DI, T], BF16, kind="ExternalInput").ap()
    xcvT = nc.dram_tensor("xcvT", [DI, T], BF16, kind="ExternalInput").ap()
    BTo = nc.dram_tensor("BTo", [DS, T], BF16, kind="ExternalInput").ap()
    CTo = nc.dram_tensor("CTo", [DS, T], BF16, kind="ExternalInput").ap()
    cumo = nc.dram_tensor("cumo", [NH, T], F32, kind="ExternalInput").ap()
    dtfo = nc.dram_tensor("dtfo", [NH, T], F32, kind="ExternalInput").ap()
    hino = nc.dram_tensor("hino", [DS, NH * HD], F32, kind="ExternalInput").ap()
    xres2 = nc.dram_tensor("xres2", [D, T], F32, kind="ExternalInput").ap()
    woT = nc.dram_tensor("woT", [DI, D], F32, kind="ExternalInput").ap()
    dvec = nc.dram_tensor("dvec", [NH], F32, kind="ExternalInput").ap()
    identf = nc.dram_tensor("identf", [P, P], F32, kind="ExternalInput").ap()
    mask01T = nc.dram_tensor("mask01T", [P, P], F32, kind="ExternalInput").ap()
    selT = nc.dram_tensor("selT", [NH, P], F32, kind="ExternalInput").ap()
    sel2 = nc.dram_tensor("sel2", [NH, 16], F32, kind="ExternalInput").ap()
    xres3 = nc.dram_tensor("xres3", [D, T], F32, kind="ExternalOutput").ap()
    scr1 = nc.dram_tensor("scr1D", [NH, P], F32)
    scr2 = nc.dram_tensor("scr2D", [NH, P], F32)

    io = {}
    io["szT"] = szT
    io["xcvT"] = xcvT
    io["BTo"] = BTo
    io["CTo"] = CTo
    io["cumo"] = cumo
    io["dtfo"] = dtfo
    io["hino"] = hino
    io["xres2"] = xres2
    io["woT"] = woT
    io["dvec"] = dvec
    io["identf"] = identf
    io["mask01T"] = mask01T
    io["selT"] = selT
    io["sel2"] = sel2
    io["xres3"] = xres3
    io["scr1"] = scr1
    io["scr2"] = scr2
    with ExitStack() as ctx:
        p = Prog(nc, ctx)
        emit_D(nc, p, ctx, io, stage, _DBG_CHUNK)
        p.finalize()
    return nc




def emit_E(nc, p, ctx, io, stage=99):
    E = ctx.enter_context
    xres3 = io["xres3"]
    xn1T = io["xn1T"]
    xres4 = io["xres4"]
    xs = E(nc.sbuf_tensor("xsE", [P, DC * T], F32))
    sq = E(nc.sbuf_tensor("sqE", [P, 1024], F32))
    onescol = E(nc.sbuf_tensor("onescolE", [P, 1], F32))
    onesrow = E(nc.sbuf_tensor("onesrowE", [1, P], F32))
    s1r = E(nc.sbuf_tensor("s1rE", [1, T], F32))
    s2r = E(nc.sbuf_tensor("s2rE", [1, T], F32))
    varr = E(nc.sbuf_tensor("varrE", [1, T], F32))
    rrow = E(nc.sbuf_tensor("rrowE", [1, T], F32))
    mrow = E(nc.sbuf_tensor("mrowE", [1, T], F32))
    epsc = E(nc.sbuf_tensor("epscE", [1, 1], F32))
    tmp = E(nc.sbuf_tensor("tmpE", [P, 512], F32))
    xnf = E(nc.sbuf_tensor("xnfE", [P, 512], F32))
    xnb = E(nc.sbuf_tensor("xnbE", [P, 512], BF16))
    psA = E(nc.psum_tensor("psAE", [P, 512], F32))
    psB = E(nc.psum_tensor("psBE", [P, 512], F32))
    psC = E(nc.psum_tensor("psCE", [P, 512], F32))
    psD_ = E(nc.psum_tensor("psDE", [P, 512], F32))

    for c in range(DC):
        p.dma("sy", xs[:, c * T:(c + 1) * T], xres3[c * P:(c + 1) * P, :])
    ld0 = p.now("ld")
    p.op("pool", lambda o: o.memset(onescol[:, :], 1.0))
    p.op("pool", lambda o: o.memset(onesrow[:, :], 1.0))
    p.op("pool", lambda o: o.memset(epsc[:, :], 1e-5), incs=[("S_pool", 1)])
    evacA = evacB = 0
    for a in range(0, T, 512):
        w = 512
        sq_done = []
        mmB_done = []
        for c in range(DC):
            sqh = (c % 2) * 512
            p.op("pe", lambda o, c=c, a=a, w=w:
                 o.matmul(psA[0:1, 0:w], onescol[:, :], xs[:, c * T + a: c * T + a + w],
                          start=(c == 0), stop=(c == DC - 1)),
                 waits=[("ld", ld0), ("S_act", evacA if c == 0 else 0)], incs=[("S_pe", 1)])
            p.op("act", lambda o, c=c, a=a, w=w, sqh=sqh:
                 o.activation(sq[:, sqh:sqh + w], xs[:, c * T + a: c * T + a + w], AF.Square),
                 waits=[("ld", ld0), ("S_pe", mmB_done[c - 2] if c >= 2 else 0)],
                 incs=[("S_act", 1)])
            sq_done.append(p.now("S_act"))
            p.op("pe", lambda o, c=c, a=a, w=w, sqh=sqh:
                 o.matmul(psB[0:1, 0:w], onescol[:, :], sq[:, sqh:sqh + w],
                          start=(c == 0), stop=(c == DC - 1)),
                 waits=[("S_act", sq_done[c]), ("S_act", evacB if c == 0 else 0)],
                 incs=[("S_pe", 1)], settle=(w < 64))
            mmB_done.append(p.now("S_pe"))
        p.op("act", lambda o, a=a, w=w: o.copy(s1r[:, a:a + w], psA[0:1, 0:w]),
             waits=[("S_pe", p.now("S_pe"))], incs=[("S_act", 1)])
        evacA = p.now("S_act")
        p.op("act", lambda o, a=a, w=w: o.copy(s2r[:, a:a + w], psB[0:1, 0:w]),
             incs=[("S_act", 1)])
        evacB = p.now("S_act")
    p.barrier()
    if stage == 4:
        p.dma("sy", xres4[0:1, :], s1r[0:1, :], sem="st")
        p.dma("sy", xres4[1:2, :], s2r[0:1, :], sem="st")
        p.op("sy", None, waits=[("st", p.now("st"))])
        return
    p.op("dve", lambda o: o.tensor_scalar(mrow[:, :], s1r[:, :], 1.0 / D, 0.0, ALU.mult, ALU.add))
    p.op("dve", lambda o: o.tensor_scalar(varr[:, :], s2r[:, :], 1.0 / D, 0.0, ALU.mult, ALU.add),
         incs=[("S_dve", 1)])
    p.op("act", lambda o: o.activation(s2r[:, :], mrow[:, :], AF.Square),
         waits=[("S_dve", p.now("S_dve"))], incs=[("S_act", 1)])
    p.op("dve", lambda o: o.tensor_tensor(s1r[:, :], varr[:, :], s2r[:, :], ALU.subtract),
         waits=[("S_act", p.now("S_act")), ("S_pool", p.now("S_pool"))], incs=[("S_dve", 1)])
    p.op("act", lambda o: o.activation(varr[:, :], s1r[:, :], AF.Sqrt, bias=epsc[0:1, 0:1], scale=1.0),
         waits=[("S_dve", p.now("S_dve"))], incs=[("S_act", 1)])
    p.op("dve", lambda o: o.reciprocal(rrow[:, :], varr[:, :]),
         waits=[("S_act", p.now("S_act"))], incs=[("S_dve", 1)])
    p.op("dve", lambda o: o.scalar_tensor_tensor(s2r[:, :], mrow[:, :], -1.0, rrow[:, :],
                                                 ALU.mult, ALU.mult), incs=[("S_dve", 1)])
    p.op("dve", lambda o: o.tensor_copy(mrow[:, :], s2r[:, :]), incs=[("S_dve", 1)])
    p.barrier()
    if stage == 4:
        pass
    if stage == 3:
        p.dma("sy", xres4[0:1, :], rrow[0:1, :], sem="st")
        p.dma("sy", xres4[1:2, :], mrow[0:1, :], sem="st")
        p.dma("sy", xres4[2:3, :], s1r[0:1, :], sem="st")
        p.dma("sy", xres4[3:4, :], varr[0:1, :], sem="st")
        p.op("sy", None, waits=[("st", p.now("st"))])
        return
    for a in range(0, T, 512):
        w = 512
        p.op("pe", lambda o, a=a:
             o.matmul(psC[:, :], onesrow[0:1, :], rrow[0:1, a:a + 512], start=True, stop=True),
             waits=[("st", p.now("st")), ("S_dve", p.now("S_dve"))], incs=[("S_pe", 1)])
        p.op("pe", lambda o, a=a:
             o.matmul(psD_[:, :], onesrow[0:1, :], mrow[0:1, a:a + 512], start=True, stop=True),
             incs=[("S_pe", 1)])
        bd = p.now("S_pe")
        for c in range(DC):
            p.op("dve", lambda o, c=c, a=a:
                 o.tensor_tensor(tmp[:, :], xs[:, c * T + a: c * T + a + 512], psC[:, :], ALU.mult),
                 waits=[("S_pe", bd), ("st", p.now("st"))], incs=[("S_dve", 1)])
            p.op("dve", lambda o:
                 o.tensor_tensor(xnf[:, :], tmp[:, :], psD_[:, :], ALU.add), incs=[("S_dve", 1)])
            p.op("dve", lambda o:
                 o.tensor_copy(xnb[:, :], xnf[:, :]), incs=[("S_dve", 1)])
            p.op("dve", lambda o, c=c, a=a:
                 o.tensor_tensor(tmp[:, :], xs[:, c * T + a: c * T + a + 512], xnf[:, :], ALU.add),
                 incs=[("S_dve", 1)])
            p.dma("sy", xn1T[c * P:(c + 1) * P, a:a + 512], xnb[:, :], sem="st",
                  waits=[("S_dve", p.now("S_dve"))])
            p.dma("sy", xres4[c * P:(c + 1) * P, a:a + 512], tmp[:, :], sem="st")
    p.op("sy", None, waits=[("st", p.now("st"))])



def build_launch_E(stage=99):
    nc = bass.Bass("TRN2", target_bir_lowering=False)
    xres3 = nc.dram_tensor("xres3", [D, T], F32, kind="ExternalInput").ap()
    xn1T = nc.dram_tensor("xn1T", [D, T], BF16, kind="ExternalOutput").ap()
    xres4 = nc.dram_tensor("xres4", [D, T], F32, kind="ExternalOutput").ap()
    io = {}
    io["xres3"] = xres3
    io["xn1T"] = xn1T
    io["xres4"] = xres4
    with ExitStack() as ctx:
        p = Prog(nc, ctx)
        emit_E(nc, p, ctx, io, stage)
        p.finalize()
    return nc


def emit_F(nc, p, ctx, io, stage=99):
    E = ctx.enter_context
    xn1T = io["xn1T"]
    xres4 = io["xres4"]
    projT = io["projT"]
    fc1T = io["fc1T"]
    fc2T = io["fc2T"]
    vecs8 = io["vecs8"]
    vecs32 = io["vecs32"]
    identf = io["identf"]
    xres5 = io["xres5"]
    xr4b = io["xr4b"]
    HID = 4 * D
    HC = 32
    fc1s = E(nc.sbuf_tensor("fc1sF", [P, DC * HID], BF16))
    n2T = E(nc.sbuf_tensor("n2TF", [P, DC * T], BF16))
    hTt = E(nc.sbuf_tensor("hTtF", [P, HC * 512], BF16))
    hnTt = hTt
    xn1s = E(nc.sbuf_tensor("xn1sF", [P, DC * T], BF16))
    x1Ts = fc1s[:, 0:DC * T]
    projTs = fc1s[:, DC * T:DC * T + DC * D]
    fc2g = [xn1s[:, i * 4 * D:(i + 1) * 4 * D] for i in range(2)]
    fc2r = [xn1s[:, 8 * D + i * D:8 * D + (i + 1) * D] for i in range(2)]
    vc = E(nc.sbuf_tensor("vcF", [P, 64], F32))
    hc = E(nc.sbuf_tensor("hcF", [P, 96], F32))
    v8 = E(nc.sbuf_tensor("v8F", [64, P], F32))
    v32 = E(nc.sbuf_tensor("v32F", [96, P], F32))
    vcb = E(nc.sbuf_tensor("vcbF", [P, 64], BF16))
    hcb = E(nc.sbuf_tensor("hcbF", [P, 96], BF16))
    bias1 = E(nc.sbuf_tensor("bias1F", [P, 8], F32))
    bias2 = E(nc.sbuf_tensor("bias2F", [P, 32], F32))
    bias3 = E(nc.sbuf_tensor("bias3F", [P, 8], F32))
    ident = E(nc.sbuf_tensor("identF", [P, P], F32))
    onescol = E(nc.sbuf_tensor("onescolF", [P, 1], F32))
    onescolb = E(nc.sbuf_tensor("onescolbF", [P, 1], BF16))
    onesrow = E(nc.sbuf_tensor("onesrowF", [1, P], F32))
    sq = E(nc.sbuf_tensor("sqF", [P, 1024], F32))
    rbase = DC * T + DC * D
    s1r = fc1s[0:1, rbase:rbase + 4096].bitcast(F32)
    s2r = fc1s[0:1, rbase + 4096:rbase + 8192].bitcast(F32)
    varr = E(nc.sbuf_tensor("varrF", [1, T], F32))
    rrow = E(nc.sbuf_tensor("rrowF", [1, T], F32))
    mrow = E(nc.sbuf_tensor("mrowF", [1, T], F32))
    srow = E(nc.sbuf_tensor("srowF", [1, 512], F32))
    srow2 = E(nc.sbuf_tensor("srow2F", [1, 512], F32))
    epsc = E(nc.sbuf_tensor("epscF", [1, 1], F32))
    a1s = E(nc.sbuf_tensor("a1sF", [P, 512], F32))
    xrs = [E(nc.sbuf_tensor(f"xrsF{i}", [P, 512], F32)) for i in range(2)]
    tmp = E(nc.sbuf_tensor("tmpF", [P, 512], F32))
    ps8 = [E(nc.psum_tensor(f"ps8F{i}", [P, 512], F32)) for i in range(8)]

    p.dma("sy", ident[:, :], identf)
    p.dma("sy", v8[:, :], vecs8)
    p.dma("sy", v32[:, :], vecs32)
    p.op("pool", lambda o: o.memset(onescol[:, :], 1.0))
    p.op("pool", lambda o: o.memset(onescolb[:, :], 1.0))
    p.op("pool", lambda o: o.memset(onesrow[:, :], 1.0))
    p.op("pool", lambda o: o.memset(epsc[:, :], 1e-5), incs=[("S_pool", 1)])
    for c in range(DC):
        p.dma("pool", xn1s[:, c * T:(c + 1) * T], xn1T[c * P:(c + 1) * P, :])
    p.dma("pool", projTs[:, :].rearrange("p (k o) -> p k o", o=D),
          projT.rearrange("(k p) o -> p k o", p=P))
    ld0 = p.now("ld")
    pld0 = p.now("pld")
    p.op("pe", lambda o: o.transpose(ps8[0][:, 0:64], v8[:, :], ident[0:64, 0:64]),
         waits=[("ld", ld0)], incs=[("S_pe", 1)])
    p.op("act", lambda o: o.copy(vc[:, :], ps8[0][:, 0:64]),
         waits=[("S_pe", p.now("S_pe"))], incs=[("S_act", 1)])
    p.op("pe", lambda o: o.transpose(ps8[0][:, 64:64 + 96], v32[:, :], ident[0:96, 0:96]),
         waits=[("S_act", p.now("S_act"))], incs=[("S_pe", 1)])
    p.op("act", lambda o: o.copy(hc[:, :], ps8[0][:, 64:64 + 96]),
         waits=[("S_pe", p.now("S_pe"))], incs=[("S_act", 1)])
    p.op("dve", lambda o: o.tensor_copy(vcb[:, :], vc[:, :]),
         waits=[("S_act", p.now("S_act"))], incs=[("S_dve", 1)])
    p.op("dve", lambda o: o.tensor_copy(hcb[:, :], hc[:, :]), incs=[("S_dve", 1)])
    p.barrier()

    def ln_stats_bf(src, colw, with_mean=True):
        evacA = [0]
        evacB = [0]
        for a in range(0, T, 512):
            w = 512
            mmB_done = []
            for c in range(DC):
                sqh = (c % 2) * 512
                if with_mean:
                    p.op("pe", lambda o, c=c, a=a, w=w, src=src:
                         o.matmul(ps8[6][0:1, 0:w], onescolb[:, :], src[:, c * T + a: c * T + a + w],
                                  start=(c == 0), stop=(c == DC - 1)),
                         waits=[("S_act", evacA[0] if c == 0 else 0)], incs=[("S_pe", 1)])
                p.op("act", lambda o, c=c, a=a, w=w, sqh=sqh, src=src:
                     o.activation(sq[:, sqh:sqh + w], src[:, c * T + a: c * T + a + w], AF.Square),
                     waits=[("S_pe", mmB_done[c - 2] if c >= 2 else 0)], incs=[("S_act", 1)])
                sd = p.now("S_act")
                p.op("pe", lambda o, c=c, a=a, w=w, sqh=sqh:
                     o.matmul(ps8[7][0:1, 0:w], onescol[:, :], sq[:, sqh:sqh + w],
                              start=(c == 0), stop=(c == DC - 1)),
                     waits=[("S_act", sd), ("S_act", evacB[0] if c == 0 else 0)],
                     incs=[("S_pe", 1)])
                mmB_done.append(p.now("S_pe"))
            if with_mean:
                p.op("act", lambda o, a=a, w=w: o.copy(s1r[:, a:a + w], ps8[6][0:1, 0:w]),
                     waits=[("S_pe", p.now("S_pe"))], incs=[("S_act", 1)])
                evacA[0] = p.now("S_act")
            p.op("act", lambda o, a=a, w=w: o.copy(s2r[:, a:a + w], ps8[7][0:1, 0:w]),
                 waits=[("S_pe", p.now("S_pe"))], incs=[("S_act", 1)])
            evacB[0] = p.now("S_act")
        p.barrier()
        p.op("dve", lambda o: o.tensor_scalar(mrow[:, :], s1r[:, :], 1.0 / D, 0.0, ALU.mult, ALU.add))
        p.op("dve", lambda o: o.tensor_scalar(varr[:, :], s2r[:, :], 1.0 / D, 0.0, ALU.mult, ALU.add),
             incs=[("S_dve", 1)])
        p.op("act", lambda o: o.activation(s2r[:, :], mrow[:, :], AF.Square),
             waits=[("S_dve", p.now("S_dve"))], incs=[("S_act", 1)])
        p.op("dve", lambda o: o.tensor_tensor(s1r[:, :], varr[:, :], s2r[:, :], ALU.subtract),
             waits=[("S_act", p.now("S_act"))], incs=[("S_dve", 1)])
        p.op("act", lambda o: o.activation(varr[:, :], s1r[:, :], AF.Sqrt, bias=epsc[0:1, 0:1], scale=1.0),
             waits=[("S_dve", p.now("S_dve")), ("S_pool", p.now("S_pool"))], incs=[("S_act", 1)])
        p.op("dve", lambda o: o.reciprocal(rrow[:, :], varr[:, :]),
             waits=[("S_act", p.now("S_act"))], incs=[("S_dve", 1)])
        p.op("dve", lambda o: o.scalar_tensor_tensor(s2r[:, :], mrow[:, :], -1.0, rrow[:, :],
                                                     ALU.mult, ALU.mult), incs=[("S_dve", 1)])
        p.op("dve", lambda o: o.tensor_copy(mrow[:, :], s2r[:, :]), incs=[("S_dve", 1)])
        p.barrier()

    def ln_apply_bf(src, dst):
        for a in range(0, T, 512):
            p.op("pe", lambda o, a=a:
                 o.matmul(ps8[6][:, :], onesrow[0:1, :], rrow[0:1, a:a + 512], start=True, stop=True),
                 waits=[("S_dve", p.now("S_dve"))], incs=[("S_pe", 1)])
            p.op("pe", lambda o, a=a:
                 o.matmul(ps8[7][:, :], onesrow[0:1, :], mrow[0:1, a:a + 512], start=True, stop=True),
                 incs=[("S_pe", 1)])
            bd = p.now("S_pe")
            for c in range(DC):
                p.op("dve", lambda o, c=c, a=a, src=src:
                     o.tensor_tensor(tmp[:, :], src[:, c * T + a: c * T + a + 512], ps8[6][:, :], ALU.mult),
                     waits=[("S_pe", bd)], incs=[("S_dve", 1)])
                p.op("dve", lambda o, c=c, a=a, dst=dst:
                     o.tensor_tensor(dst[:, c * T + a: c * T + a + 512], tmp[:, :], ps8[7][:, :], ALU.add),
                     incs=[("S_dve", 1)])

    ln_stats_bf(xn1s, None)
    n1T = hTt[:, 0:DC * T // 2]
    n1T = hTt
    ln_apply_bf(xn1s, n1T)
    p.barrier()

    for ob in range(DC):
        for k in range(DC):
            p.op("pe", lambda o, ob=ob, k=k:
                 o.matmul(ps8[0][0:P, ob:ob + 1], projTs[:, k * D + ob * P:k * D + (ob + 1) * P],
                          vcb[:, 8 + k:8 + k + 1], start=(k == 0), stop=(k == DC - 1)),
                 waits=([("pld", pld0), ("S_act", p.now("S_act"))] if ob == 0 and k == 0 else []),
                 incs=([("S_pe", 1)] if k == DC - 1 else []))
    p.op("act", lambda o: o.copy(bias1[:, :], ps8[0][:, 0:8]),
         waits=[("S_pe", p.now("S_pe"))], incs=[("S_act", 1)])
    p.op("dve", lambda o: o.tensor_tensor(bias1[:, :], bias1[:, :], vc[:, 48:56], ALU.add),
         waits=[("S_act", p.now("S_act"))], incs=[("S_dve", 1)])
    for k in range(DC):
        p.op("dve", lambda o, k=k:
             o.tensor_scalar(projTs[:, k * D:(k + 1) * D], projTs[:, k * D:(k + 1) * D],
                             vc[:, k:k + 1], 0.0, ALU.mult, ALU.add), incs=[("S_dve", 1)])
    p.barrier()
    for oc in range(DC):
        for a in range(0, T, 512):
            bank = ps8[(a // 512) % 2]
            xb = xrs[(a // 512) % 2]
            p.dma("sy", xb[:, :], xres4[oc * P:(oc + 1) * P, a:a + 512],
                  waits=[("st", p.now("st"))])
            ldx = p.now("ld")
            for k in range(DC):
                p.op("pe", lambda o, oc=oc, a=a, k=k, bank=bank:
                     o.matmul(bank[:, :], projTs[:, k * D + oc * P:k * D + (oc + 1) * P],
                              n1T[:, k * T + a:k * T + a + 512], start=(k == 0), stop=(k == DC - 1)),
                     waits=([("S_dve", p.now("S_dve")), ("S_act", p.now("S_act"))] if k == 0 else []),
                     incs=([("S_pe", 1)] if k == DC - 1 else []))
            mmd = p.now("S_pe")
            p.op("act", lambda o, oc=oc, bank=bank:
                 o.activation(a1s[:, :], bank[:, :], AF.Identity, bias=bias1[:, oc:oc + 1], scale=1.0),
                 waits=[("S_pe", mmd)], incs=[("S_act", 1)])
            p.op("dve", lambda o, oc=oc, a=a:
                 o.scalar_tensor_tensor(x1Ts[:, oc * T + a:oc * T + a + 512], a1s[:, :],
                                        vc[:, 32 + oc:32 + oc + 1], xn1s[:, oc * T + a:oc * T + a + 512],
                                        ALU.mult, ALU.add),
                 waits=[("S_act", p.now("S_act"))], incs=[("S_dve", 1)])
            p.op("dve", lambda o, oc=oc, xb=xb:
                 o.scalar_tensor_tensor(xb[:, :], a1s[:, :], vc[:, 32 + oc:32 + oc + 1], xb[:, :],
                                        ALU.mult, ALU.add),
                 waits=[("ld", ldx)], incs=[("S_dve", 1)])
            p.dma("sy", xr4b.ap()[oc * P:(oc + 1) * P, a:a + 512], xb[:, :], sem="st",
                  waits=[("S_dve", p.now("S_dve"))])
    p.barrier()
    if stage == 1:
        for c in range(DC):
            p.dma("pool", xres5[c * P:(c + 1) * P, :], x1Ts[:, c * T:(c + 1) * T], sem="st")
        p.op("sy", None, waits=[("st", p.now("st"))])
        return

    ln_stats_bf(x1Ts, None)
    ln_apply_bf(x1Ts, n2T)
    p.barrier()

    for k in range(DC):
        p.dma("pool", fc1s[:, k * HID:(k + 1) * HID], fc1T[k * P:(k + 1) * P, :])
    pldf = p.now("pld")
    p.barrier()
    if stage == 2:
        for c in range(DC):
            p.dma("pool", xres5[c * P:(c + 1) * P, :], n2T[:, c * T:(c + 1) * T], sem="st")
        p.op("sy", None, waits=[("st", p.now("st"))])
        return

    for hb in range(HC):
        for k in range(DC):
            p.op("pe", lambda o, hb=hb, k=k:
                 o.matmul(ps8[5][0:P, hb:hb + 1], fc1s[:, k * HID + hb * P:k * HID + (hb + 1) * P],
                          vcb[:, 24 + k:24 + k + 1], start=(k == 0), stop=(k == DC - 1)),
                 waits=([("pld", pldf)] if hb == 0 and k == 0 else []),
                 incs=([("S_pe", 1)] if k == DC - 1 else []))
    p.op("act", lambda o: o.copy(bias2[:, :], ps8[5][:, 0:32]),
         waits=[("S_pe", p.now("S_pe"))], incs=[("S_act", 1)])
    p.op("dve", lambda o: o.tensor_tensor(bias2[:, :], bias2[:, :], hc[:, 64:96], ALU.add),
         waits=[("S_act", p.now("S_act"))], incs=[("S_dve", 1)])
    for k in range(DC):
        p.op("dve", lambda o, k=k:
             o.tensor_scalar(fc1s[:, k * HID:(k + 1) * HID], fc1s[:, k * HID:(k + 1) * HID],
                             vc[:, 16 + k:16 + k + 1], 0.0, ALU.mult, ALU.add),
             waits=[("S_pe", p.now("S_pe"))], incs=[("S_dve", 1)])
    p.barrier()

    for k in range(HC):
        wr = fc2r[k % 2]
        p.dma("pool", wr[:, :], fc2T[k * P:(k + 1) * P, :],
              waits=[("S_pe", p.now("S_pe"))] if k >= 2 else [])
        ldk = p.now("pld")
        for ob in range(DC):
            p.op("pe", lambda o, k=k, ob=ob, wr=wr:
                 o.matmul(ps8[6][0:P, ob:ob + 1], wr[:, ob * P:(ob + 1) * P],
                          hcb[:, 32 + k:32 + k + 1], start=(k == 0), stop=(k == HC - 1)),
                 waits=([("pld", ldk)] if ob == 0 else []),
                 incs=([("S_pe", 1)] if ob == DC - 1 else []))
    p.op("pe", lambda o: o.drain(), incs=[("S_pe", 1)], settle=False)
    p.op("act", lambda o: o.copy(bias3[:, :], ps8[6][:, 0:8]),
         waits=[("S_pe", p.now("S_pe"))], incs=[("S_act", 1)])
    p.op("dve", lambda o: o.tensor_tensor(bias3[:, :], bias3[:, :], vc[:, 56:64], ALU.add),
         waits=[("S_act", p.now("S_act"))], incs=[("S_dve", 1)])
    p.barrier()

    for tg in range(4):
        a = tg * 512
        for hb in range(HC):
            bank = ps8[hb % 2]
            for k in range(DC):
                p.op("pe", lambda o, hb=hb, k=k, a=a, bank=bank:
                     o.matmul(bank[:, :], fc1s[:, k * HID + hb * P:k * HID + (hb + 1) * P],
                              n2T[:, k * T + a:k * T + a + 512], start=(k == 0), stop=(k == DC - 1)),
                     waits=([("S_act", p.now("S_act")), ("S_dve", p.now("S_dve"))] if k == 0 else []),
                     incs=([("S_pe", 1)] if k == DC - 1 else []))
            mmd = p.now("S_pe")
            p.op("act", lambda o, hb=hb, bank=bank:
                 o.activation(hTt[:, hb * 512:(hb + 1) * 512], bank[:, :], AF.Gelu,
                              bias=bias2[:, hb:hb + 1], scale=1.0),
                 waits=[("S_pe", mmd)], incs=[("S_act", 1)])
        p.barrier()
        mmB_done = []
        for k in range(HC):
            sqh = (k % 2) * 512
            p.op("pe", lambda o, k=k:
                 o.matmul(ps8[6][0:1, 0:512], onescolb[:, :], hTt[:, k * 512:(k + 1) * 512],
                          start=(k == 0), stop=(k == HC - 1)), incs=[("S_pe", 1)])
            p.op("act", lambda o, k=k, sqh=sqh:
                 o.activation(sq[:, sqh:sqh + 512], hTt[:, k * 512:(k + 1) * 512], AF.Square),
                 waits=[("S_pe", mmB_done[k - 2] if k >= 2 else 0)], incs=[("S_act", 1)])
            sd = p.now("S_act")
            p.op("pe", lambda o, k=k, sqh=sqh:
                 o.matmul(ps8[7][0:1, 0:512], onescol[:, :], sq[:, sqh:sqh + 512],
                          start=(k == 0), stop=(k == HC - 1)),
                 waits=[("S_act", sd)], incs=[("S_pe", 1)])
            mmB_done.append(p.now("S_pe"))
        p.op("act", lambda o: o.copy(srow[:, :], ps8[6][0:1, 0:512]),
             waits=[("S_pe", p.now("S_pe"))], incs=[("S_act", 1)])
        p.op("act", lambda o: o.copy(srow2[:, :], ps8[7][0:1, 0:512]), incs=[("S_act", 1)])
        p.op("dve", lambda o: o.tensor_scalar(srow[:, :], srow[:, :], 1.0 / HID, 0.0, ALU.mult, ALU.add),
             waits=[("S_act", p.now("S_act"))], incs=[("S_dve", 1)])
        p.op("dve", lambda o: o.tensor_scalar(srow2[:, :], srow2[:, :], 1.0 / HID, 0.0, ALU.mult, ALU.add),
             incs=[("S_dve", 1)])
        p.op("act", lambda o: o.activation(sq[0:1, 0:512], srow[:, :], AF.Square),
             waits=[("S_dve", p.now("S_dve"))], incs=[("S_act", 1)])
        p.op("dve", lambda o: o.tensor_tensor(srow2[:, :], srow2[:, :], sq[0:1, 0:512], ALU.subtract),
             waits=[("S_act", p.now("S_act"))], incs=[("S_dve", 1)])
        p.op("act", lambda o: o.activation(srow2[:, :], srow2[:, :], AF.Sqrt, bias=epsc[0:1, 0:1], scale=1.0),
             incs=[("S_act", 1)])
        p.op("dve", lambda o: o.reciprocal(sq[0:1, 512:1024], srow2[:, :]),
             waits=[("S_act", p.now("S_act"))], incs=[("S_dve", 1)])
        p.op("dve", lambda o: o.scalar_tensor_tensor(srow[:, :], srow[:, :], -1.0, sq[0:1, 512:1024],
                                                     ALU.mult, ALU.mult), incs=[("S_dve", 1)])
        p.barrier()
        p.op("pe", lambda o: o.matmul(ps8[6][:, :], onesrow[0:1, :], sq[0:1, 512:1024],
                                      start=True, stop=True), incs=[("S_pe", 1)])
        p.op("pe", lambda o: o.matmul(ps8[7][:, :], onesrow[0:1, :], srow[0:1, :],
                                      start=True, stop=True), incs=[("S_pe", 1)])
        bd = p.now("S_pe")
        for k in range(HC):
            p.op("dve", lambda o, k=k:
                 o.tensor_tensor(tmp[:, :], hTt[:, k * 512:(k + 1) * 512], ps8[6][:, :], ALU.mult),
                 waits=[("S_pe", bd)], incs=[("S_dve", 1)])
            p.op("dve", lambda o, k=k:
                 o.tensor_tensor(hTt[:, k * 512:(k + 1) * 512], tmp[:, :], ps8[7][:, :], ALU.add),
                 incs=[("S_dve", 1)])
            p.op("dve", lambda o, k=k:
                 o.tensor_scalar(hTt[:, k * 512:(k + 1) * 512], hTt[:, k * 512:(k + 1) * 512],
                                 hc[:, k:k + 1], 0.0, ALU.mult, ALU.add), incs=[("S_dve", 1)])
        p.barrier()
        hd_ = p.now("S_dve")
        kg_done = {}
        for kg in range(HC // 4):
            wr = fc2g[kg % 2]
            p.dma("pool", wr[:, :].rearrange("p (q o) -> p q o", o=D),
                  fc2T[kg * 4 * P:(kg + 1) * 4 * P, :].rearrange("(q p) o -> p q o", p=P),
                  waits=[("S_pe", kg_done.get(kg - 2, 0))] if kg >= 2 else [("S_dve", hd_)])
            ldk = p.now("pld")
            for kq in range(4):
                k = kg * 4 + kq
                for ob in range(DC):
                    p.op("pe", lambda o, k=k, kq=kq, ob=ob, wr=wr:
                         o.matmul(ps8[ob][:, :], wr[:, kq * D + ob * P:kq * D + (ob + 1) * P],
                                  hTt[:, k * 512:(k + 1) * 512],
                                  start=(k == 0), stop=(k == HC - 1)),
                         waits=([("pld", ldk), ("S_dve", hd_)] if ob == 0 and kq == 0 else []),
                         incs=([("S_pe", 1)] if ob == DC - 1 and kq == 3 else []), settle=False)
            kg_done[kg] = p.now("S_pe")
        p.op("pe", lambda o: o.drain(), incs=[("S_pe", 1)], settle=False)
        mmd = p.now("S_pe")
        for ob in range(DC):
            xb = xrs[ob % 2]
            p.dma("sy", xb[:, :], xr4b.ap()[ob * P:(ob + 1) * P, a:a + 512],
                  waits=[("st", p.now("st"))])
            ldx = p.now("ld")
            p.op("act", lambda o, ob=ob:
                 o.activation(a1s[:, :], ps8[ob][:, :], AF.Identity, bias=bias3[:, ob:ob + 1], scale=1.0),
                 waits=[("S_pe", mmd)], incs=[("S_act", 1)])
            p.op("dve", lambda o, ob=ob, xb=xb:
                 o.scalar_tensor_tensor(xb[:, :], a1s[:, :], vc[:, 40 + ob:40 + ob + 1], xb[:, :],
                                        ALU.mult, ALU.add),
                 waits=[("ld", ldx), ("S_act", p.now("S_act"))], incs=[("S_dve", 1)])
            p.dma("sy", xres5[ob * P:(ob + 1) * P, a:a + 512], xb[:, :], sem="st",
                  waits=[("S_dve", p.now("S_dve"))])
        p.barrier()
    p.op("sy", None, waits=[("st", p.now("st"))])



def build_launch_F(stage=99):
    nc = bass.Bass("TRN2", target_bir_lowering=False)
    xn1T = nc.dram_tensor("xn1T", [D, T], BF16, kind="ExternalInput").ap()
    xres4 = nc.dram_tensor("xres4", [D, T], F32, kind="ExternalInput").ap()
    projT = nc.dram_tensor("projT", [D, D], F32, kind="ExternalInput").ap()
    fc1T = nc.dram_tensor("fc1T", [D, 4 * D], F32, kind="ExternalInput").ap()
    fc2T = nc.dram_tensor("fc2T", [4 * D, D], F32, kind="ExternalInput").ap()
    vecs8 = nc.dram_tensor("vecs8", [64, P], F32, kind="ExternalInput").ap()
    vecs32 = nc.dram_tensor("vecs32", [96, P], F32, kind="ExternalInput").ap()
    identf = nc.dram_tensor("identf", [P, P], F32, kind="ExternalInput").ap()
    xres5 = nc.dram_tensor("xres5", [D, T], F32, kind="ExternalOutput").ap()
    xr4b = nc.dram_tensor("xr4bF", [D, T], F32)

    HID = 4 * D
    HC = 32
    io = {}
    io["xn1T"] = xn1T
    io["xres4"] = xres4
    io["projT"] = projT
    io["fc1T"] = fc1T
    io["fc2T"] = fc2T
    io["vecs8"] = vecs8
    io["vecs32"] = vecs32
    io["identf"] = identf
    io["xres5"] = xres5
    io["xr4b"] = xr4b
    with ExitStack() as ctx:
        p = Prog(nc, ctx)
        emit_F(nc, p, ctx, io, stage)
        p.finalize()
    return nc


def emit_G(nc, p, ctx, io, stage=99):
    E = ctx.enter_context
    xn1T = io["xn1T"]
    xres5 = io["xres5"]
    Wg = io["Wg"]
    bg = io["bg1"]
    WeT = io["WeT"]
    be = io["be"]
    identf = io["identf"]
    yout = io["yout"]
    wesc = io["wesc"]
    xn1s = E(nc.sbuf_tensor("xn1sG", [P, DC * T], BF16))
    macc = E(nc.sbuf_tensor("maccG", [P, DC * T], F32))
    xnw = E(nc.sbuf_tensor("xnwG", [P, DC * T], BF16))
    wes = [E(nc.sbuf_tensor(f"wesG{i}", [P, DC * D], BF16)) for i in range(2)]
    ident = E(nc.sbuf_tensor("identG", [P, P], F32))
    onesrow = E(nc.sbuf_tensor("onesrowG", [1, P], F32))
    gw = E(nc.sbuf_tensor("gwG", [P, DC * 8], BF16))
    gbr = E(nc.sbuf_tensor("gbrG", [1, 8], F32))
    bes = E(nc.sbuf_tensor("besG", [8, D], F32))
    wefm = E(nc.sbuf_tensor("wefmG", [8, T], F32))
    werow = E(nc.sbuf_tensor("werowG", [1, T], F32))
    lg = E(nc.sbuf_tensor("lgG", [P, 8], F32))
    pex = E(nc.sbuf_tensor("pexG", [P, 8], F32))
    m1 = E(nc.sbuf_tensor("m1G", [P, 1], F32))
    t1g = E(nc.sbuf_tensor("t1gG", [P, 8], F32))
    wet = E(nc.sbuf_tensor("wetG", [P, 8], F32))
    xg = [E(nc.sbuf_tensor(f"xgG{i}", [P, DC * P], BF16)) for i in range(2)]
    outst = [E(nc.sbuf_tensor(f"outstG{i}", [P, D], F32)) for i in range(2)]
    ps8 = [E(nc.psum_tensor(f"ps8G{i}", [P, 512], F32)) for i in range(8)]

    p.dma("sy", ident[:, :], identf)
    p.op("pool", lambda o: o.memset(onesrow[:, :], 1.0), incs=[("S_pool", 1)])
    p.dma("pool", gw[:, :].rearrange("p (k e) -> p k e", e=8), Wg.rearrange("(k p) e -> p k e", p=P))
    p.dma("sy", gbr[:, :], bg)
    p.dma("sy", bes[:, :], be)
    for c in range(DC):
        p.dma("pool", xn1s[:, c * T:(c + 1) * T], xn1T[c * P:(c + 1) * P, :])
        p.dma("sy", macc[:, c * T:(c + 1) * T], xres5[c * P:(c + 1) * P, :])
    ldg = p.now("ld")
    pldg = p.now("pld")
    for tt in range(NTC):
        t0 = tt * P
        for k in range(DC):
            p.op("pe", lambda o, k=k, t0=t0:
                 o.matmul(ps8[1][0:P, 0:8], xn1s[:, k * T + t0:k * T + t0 + P],
                          gw[:, k * 8:(k + 1) * 8], start=(k == 0), stop=False),
                 waits=([("pld", pldg), ("ld", ldg), ("S_act", p.now("S_act"))] if k == 0 else []))
        p.op("pe", lambda o:
             o.matmul(ps8[1][0:P, 0:8], onesrow[0:1, 0:P], gbr[0:1, :], start=False, stop=True),
             incs=[("S_pe", 1)])
        p.op("act", lambda o: o.copy(lg[:, :], ps8[1][0:P, 0:8]),
             waits=[("S_pe", p.now("S_pe"))], incs=[("S_act", 1)])
        p.op("dve", lambda o: o.reduce_max(m1[:, :], lg[:, :], AX.X),
             waits=[("S_act", p.now("S_act"))], incs=[("S_dve", 1)])
        p.op("dve", lambda o: o.tensor_scalar(m1[:, :], m1[:, :], -1.0, 0.0, ALU.mult, ALU.add),
             incs=[("S_dve", 1)])
        p.op("act", lambda o: o.activation(pex[:, :], lg[:, :], AF.Exp, bias=m1[:, 0:1], scale=1.0),
             waits=[("S_dve", p.now("S_dve"))], incs=[("S_act", 1)])
        p.op("dve", lambda o: o.reduce_max(m1[:, :], pex[:, :], AX.X),
             waits=[("S_act", p.now("S_act"))], incs=[("S_dve", 1)])
        p.op("dve", lambda o: o.tensor_scalar(t1g[:, :], pex[:, :], m1[:, 0:1], 0.0, ALU.is_ge, ALU.add),
             incs=[("S_dve", 1)])
        p.op("dve", lambda o: o.tensor_scalar(t1g[:, :], t1g[:, :], -1.0, 1.0, ALU.mult, ALU.add),
             incs=[("S_dve", 1)])
        p.op("dve", lambda o: o.tensor_tensor(t1g[:, :], pex[:, :], t1g[:, :], ALU.mult),
             incs=[("S_dve", 1)])
        p.op("dve", lambda o: o.reduce_max(m1[:, :], t1g[:, :], AX.X), incs=[("S_dve", 1)])
        p.op("dve", lambda o: o.tensor_scalar(t1g[:, :], pex[:, :], m1[:, 0:1], 0.0, ALU.is_ge, ALU.add),
             incs=[("S_dve", 1)])
        p.op("dve", lambda o: o.tensor_tensor(wet[:, :], pex[:, :], t1g[:, :], ALU.mult),
             incs=[("S_dve", 1)])
        p.op("dve", lambda o: o.reduce_sum(m1[:, :], wet[:, :], AX.X), incs=[("S_dve", 1)])
        p.op("dve", lambda o: o.reciprocal(m1[:, :], m1[:, :]), incs=[("S_dve", 1)])
        p.op("dve", lambda o: o.tensor_scalar(wet[:, :], wet[:, :], m1[:, 0:1], 0.0, ALU.mult, ALU.add),
             incs=[("S_dve", 1)])
        p.op("pe", lambda o: o.transpose(ps8[2][0:8, 0:P], wet[:, :], ident[:, :]),
             waits=[("S_dve", p.now("S_dve")), ("S_act", p.now("S_act"))], incs=[("S_pe", 1)])
        p.op("act", lambda o, t0=t0: o.copy(wefm[:, t0:t0 + P], ps8[2][0:8, 0:P]),
             waits=[("S_pe", p.now("S_pe"))], incs=[("S_act", 1)])
    p.barrier()
    p.dma("sy", wesc.ap(), wefm[:, :], sem="st")
    p.op("sy", None, waits=[("st", p.now("st"))])
    for ob in range(DC):
        for a in range(0, T, 512):
            bank = ps8[(a // 512) % 2]
            p.op("pe", lambda o, ob=ob, a=a, bank=bank:
                 o.matmul(bank[:, :], bes[:, ob * P:(ob + 1) * P], wefm[0:8, a:a + 512],
                          start=True, stop=True),
                 waits=[("S_dve", p.now("S_dve"))], incs=[("S_pe", 1)])
            p.op("dve", lambda o, ob=ob, a=a, bank=bank:
                 o.tensor_tensor(macc[:, ob * T + a:ob * T + a + 512],
                                 macc[:, ob * T + a:ob * T + a + 512], bank[:, :], ALU.add),
                 waits=[("S_pe", p.now("S_pe")), ("ld", p.now("ld"))], incs=[("S_dve", 1)])
    p.barrier()
    if stage == 1:
        for c in range(DC):
            p.dma("sy", yout.rearrange("t o -> o t")[c * P:(c + 1) * P, :],
                  macc[:, c * T:(c + 1) * T], sem="st")
        p.op("sy", None, waits=[("st", p.now("st"))])
        return

    lde_marks = {}
    p.dma("pool", wes[0][:, :].rearrange("p (k o) -> p k o", o=D),
          WeT.rearrange("e (k p) o -> e p k o", p=P)[0])
    lde_marks[0] = p.now("pld")
    for e in range(8):
        p.dma("sy", werow[:, :], wesc.ap()[e:e + 1, :], waits=[("S_pe", p.now("S_pe"))])
        if e + 1 < 8:
            p.dma("pool", wes[(e + 1) % 2][:, :].rearrange("p (k o) -> p k o", o=D),
                  WeT.rearrange("e (k p) o -> e p k o", p=P)[e + 1],
                  waits=[("S_pe", p.now("S_pe"))])
            lde_marks[e + 1] = p.now("pld")
        lde = lde_marks[e]
        ldw = p.now("ld")
        wese = wes[e % 2]
        for a in range(0, T, 512):
            p.op("pe", lambda o, a=a:
                 o.matmul(ps8[7][:, :], onesrow[0:1, :], werow[0:1, a:a + 512], start=True, stop=True),
                 waits=[("ld", ldw), ("S_dve", p.now("S_dve"))], incs=[("S_pe", 1)])
            wd = p.now("S_pe")
            for k in range(DC):
                eng = "dve" if k % 2 == 0 else "dve"
                p.op(eng, lambda o, k=k, a=a:
                     o.tensor_tensor(xnw[:, k * T + a:k * T + a + 512],
                                     xn1s[:, k * T + a:k * T + a + 512], ps8[7][:, :], ALU.mult),
                     waits=[("S_pe", wd)], incs=[("S_dve", 1)])
        xd = p.now("S_dve")
        for oc in range(DC):
            for a in range(0, T, 512):
                bank = ps8[(oc * 4 + a // 512) % 4]
                for k in range(DC):
                    p.op("pe", lambda o, oc=oc, a=a, k=k, bank=bank, wese=wese:
                         o.matmul(bank[:, :], wese[:, k * D + oc * P:k * D + (oc + 1) * P],
                                  xnw[:, k * T + a:k * T + a + 512], start=(k == 0), stop=(k == DC - 1)),
                         waits=([("pld", lde), ("S_dve", p.now("S_dve"))] if k == 0 else []),
                         incs=([("S_pe", 1)] if k == DC - 1 else []))
                mmd = p.now("S_pe")
                p.op("dve", lambda o, oc=oc, a=a, bank=bank:
                     o.tensor_tensor(macc[:, oc * T + a:oc * T + a + 512],
                                     macc[:, oc * T + a:oc * T + a + 512], bank[:, :], ALU.add),
                     waits=[("S_pe", mmd)], incs=[("S_dve", 1)])
        p.barrier()
    for tt in range(NTC):
        t0 = tt * P
        ob_ = outst[tt % 2]
        for oc in range(DC):
            bank = ps8[oc % 2]
            p.op("pe", lambda o, oc=oc, t0=t0, bank=bank:
                 o.transpose(bank[:, 0:P], macc[:, oc * T + t0:oc * T + t0 + P], ident[:, :]),
                 waits=[("S_act", p.now("S_act")), ("st", p.now("st"))], incs=[("S_pe", 1)])
            p.op("act", lambda o, oc=oc, bank=bank, ob_=ob_:
                 o.copy(ob_[:, oc * P:(oc + 1) * P], bank[:, 0:P]),
                 waits=[("S_pe", p.now("S_pe"))], incs=[("S_act", 1)])
        p.dma("sy", yout[t0:t0 + P, :], ob_[:, :], sem="st",
              waits=[("S_act", p.now("S_act"))])
    p.op("sy", None, waits=[("st", p.now("st"))])



def build_launch_G(stage=99):
    nc = bass.Bass("TRN2", target_bir_lowering=False)
    xn1T = nc.dram_tensor("xn1T", [D, T], BF16, kind="ExternalInput").ap()
    xres5 = nc.dram_tensor("xres5", [D, T], F32, kind="ExternalInput").ap()
    Wg = nc.dram_tensor("Wg", [D, 8], F32, kind="ExternalInput").ap()
    bg = nc.dram_tensor("bg", [1, 8], F32, kind="ExternalInput").ap()
    WeT = nc.dram_tensor("WeT", [8, D, D], F32, kind="ExternalInput").ap()
    be = nc.dram_tensor("be", [8, D], F32, kind="ExternalInput").ap()
    identf = nc.dram_tensor("identf", [P, P], F32, kind="ExternalInput").ap()
    yout = nc.dram_tensor("yout", [T, D], F32, kind="ExternalOutput").ap()
    wesc = nc.dram_tensor("wescG", [8, T], F32)

    io = {}
    io["xn1T"] = xn1T
    io["xres5"] = xres5
    io["Wg"] = Wg
    io["bg1"] = bg
    io["WeT"] = WeT
    io["be"] = be
    io["identf"] = identf
    io["yout"] = yout
    io["wesc"] = wesc
    with ExitStack() as ctx:
        p = Prog(nc, ctx)
        emit_G(nc, p, ctx, io, stage)
        p.finalize()
    return nc




def build_fused(dump=()):
    nc = bass.Bass("TRN2", target_bir_lowering=False)
    io = {}

    def ein(n, s, dt=F32):
        io[n] = nc.dram_tensor(n, s, dt, kind="ExternalInput").ap()

    def itn(k, n, s, dt):
        kind = "ExternalOutput" if k in dump else "Internal"
        io[k] = nc.dram_tensor(n, s, dt, kind=kind).ap()

    ein("x_in", [TH, D]); ein("identf", [P, P]); ein("mask01T", [P, P])
    ein("selT", [NH, P]); ein("sel2", [NH, 16]); ein("cmask", [1, NCORES])
    ein("winT", [D, 4384]); ein("convw", [DI, 4]); ein("convb", [DI])
    ein("alog", [NH]); ein("dtb", [NH]); ein("nw", [D]); ein("woT", [DI, D])
    ein("dvec", [NH])
    ein("gatew", [D, 8]); ein("gateb", [1, 8])
    ein("w1", [8, D, DI]); ein("w2", [8, D, DI]); ein("wg", [8, DI, DI]); ein("w3", [8, DI, D])
    ein("b1", [8, DI]); ein("b2", [8, DI]); ein("bg", [8, DI]); ein("b3", [8, D])
    ein("projT", [D, D]); ein("fc1T", [D, 4 * D]); ein("fc2T", [4 * D, D])
    ein("vecs8", [64, P]); ein("vecs32", [96, P])
    ein("Wg", [D, 8]); ein("bg1", [1, 8]); ein("WeT", [8, D, D]); ein("be", [8, D])
    itn("xnT", "i_xnT", [D, TH], BF16)
    itn("xres", "i_xres", [D, T], F32)
    itn("szT", "i_szT", [DI, T], BF16)
    itn("xcvT", "i_xcvT", [DI, T], BF16)
    itn("BTo", "i_BTo", [DS, T], BF16)
    itn("CTo", "i_CTo", [DS, T], BF16)
    itn("cumo", "i_cumo", [NH, T], F32)
    itn("dtfo", "i_dtfo", [NH, T], F32)
    itn("hino", "i_hino", [DS, NH * HD], F32)
    itn("xres2", "i_xres2", [D, T], F32)
    itn("xres3", "i_xres3", [D, T], F32)
    itn("xn1T", "i_xn1T", [D, T], BF16)
    itn("xres4", "i_xres4", [D, T], F32)
    itn("xres5", "i_xres5", [D, T], F32)
    io["xr4b"] = nc.dram_tensor("i_xr4b", [D, T], F32)
    io["wesc"] = nc.dram_tensor("i_wesc", [8, T], F32)
    io["scr1"] = nc.dram_tensor("i_scr1", [NH, P], F32)
    io["scr2"] = nc.dram_tensor("i_scr2", [NH, P], F32)
    io["cc_in_s"] = nc.dram_tensor("cc_in_s", [DS, NH * HD], F32)
    io["cc_out_s"] = nc.dram_tensor("cc_out_s", [NCORES * DS, NH * HD], F32, addr_space="Shared")
    io["cc_in_l"] = nc.dram_tensor("cc_in_l", [NH, 1], F32)
    io["cc_out_l"] = nc.dram_tensor("cc_out_l", [NCORES * NH, 1], F32, addr_space="Shared")
    io["yout"] = nc.dram_tensor("yout", [T, D], F32, kind="ExternalOutput").ap()

    with ExitStack() as ctx:
        p = Prog(nc, ctx)
        for emit in (emit_A, emit_B, emit_C, emit_D, emit_E, emit_F, emit_G):
            with ExitStack() as c2:
                emit(nc, p, c2, io)
            p.barrier()
        p.op("sy", None, waits=[("st", p.now("st")), ("cc", p.now("cc"))])
        p.finalize()
    return nc


def shard_inputs(L0_attn_x):
    x = np.asarray(L0_attn_x).reshape(16384, D)
    shards = []
    for i in range(NCORES):
        lo = i * T
        halo = np.zeros((3, D), np.float32) if i == 0 else x[lo - 3:lo]
        shards.append(np.ascontiguousarray(np.concatenate([halo, x[lo:lo + T]], axis=0)))
    return shards


def host_consts():
    identf = np.eye(P, dtype=np.float32)
    mask01T = np.triu(np.ones((P, P), np.float32))
    selT = np.zeros((NH, P), np.float32)
    for h in range(NH):
        selT[h, (h % 2) * HD:(h % 2 + 1) * HD] = 1.0
    sel2 = np.zeros((NH, 16), np.float32)
    for h in range(NH):
        sel2[h, h // 2] = 1.0
    return dict(identf=identf, mask01T=mask01T, selT=selT, sel2=sel2)


def _f32(a):
    return np.ascontiguousarray(np.asarray(a, dtype=np.float32))


def _pack_rows(inputs, names):
    return np.ascontiguousarray(np.concatenate(
        [_f32(inputs[n]).reshape(-1, P) for n in names], 0))


def kernel(**inputs):
    consts = host_consts()
    identf = consts["identf"]
    shards = shard_inputs(inputs["L0_attn_x"])
    cores = list(range(NCORES))

    winT = np.ascontiguousarray(_f32(inputs["L0_attn_in_proj_w"]).T)
    convw = _f32(inputs["L0_attn_conv_w"]).reshape(DI, 4)
    woT = np.ascontiguousarray(_f32(inputs["L0_attn_out_proj_w"]).T)
    projT = np.ascontiguousarray(_f32(inputs["L1_attn_proj_w"]).T)
    fc1T = np.ascontiguousarray(_f32(inputs["L1_attn_fc1_w"]).T)
    fc2T = np.ascontiguousarray(_f32(inputs["L1_attn_fc2_w"]).T)
    WeT = np.ascontiguousarray(np.transpose(_f32(inputs["L1_ffn_We"]), (0, 2, 1)))
    vecs8 = _pack_rows(inputs, ["L1_attn_ln1_g", "L1_attn_ln1_b", "L1_attn_ln2_g",
                                "L1_attn_ln2_b", "L1_attn_gamma1", "L1_attn_gamma2",
                                "L1_attn_proj_b", "L1_attn_fc2_b"])
    vecs32 = _pack_rows(inputs, ["L1_attn_lnh_g", "L1_attn_lnh_b", "L1_attn_fc1_b"])

    nc = build_fused()
    in_maps = []
    for i in cores:
        cm = np.zeros((1, NCORES), np.float32)
        cm[0, :i] = 1.0
        in_maps.append(dict(
            x_in=shards[i], identf=identf, mask01T=consts["mask01T"],
            selT=consts["selT"], sel2=consts["sel2"], cmask=cm,
            winT=winT, convw=convw, convb=_f32(inputs["L0_attn_conv_b"]),
            alog=_f32(inputs["L0_attn_A_log"]), dtb=_f32(inputs["L0_attn_dt_bias"]),
            nw=_f32(inputs["L0_attn_norm_w"]), woT=woT, dvec=_f32(inputs["L0_attn_D"]),
            gatew=_f32(inputs["L0_ffn_gate_w"]), gateb=_f32(inputs["L0_ffn_gate_b"]).reshape(1, 8),
            w1=_f32(inputs["L0_ffn_w1"]), w2=_f32(inputs["L0_ffn_w2"]),
            wg=_f32(inputs["L0_ffn_wg"]), w3=_f32(inputs["L0_ffn_w3"]),
            b1=_f32(inputs["L0_ffn_b1"]), b2=_f32(inputs["L0_ffn_b2"]),
            bg=_f32(inputs["L0_ffn_bg"]), b3=_f32(inputs["L0_ffn_b3"]),
            projT=projT, fc1T=fc1T, fc2T=fc2T, vecs8=vecs8, vecs32=vecs32,
            Wg=_f32(inputs["L1_ffn_Wg"]), bg1=_f32(inputs["L1_ffn_bg"]).reshape(1, 8),
            WeT=WeT, be=_f32(inputs["L1_ffn_be"])))
    res = run_bass_kernel_spmd(nc, in_maps, core_ids=cores)
    out = np.concatenate([np.asarray(res.results[i]["yout"]) for i in cores], axis=0)
    return out.reshape(1, 16384, D).astype(np.float32)
